# version 03
# baseline (speedup 1.0000x reference)
import numpy as np
import ml_dtypes
from contextlib import ExitStack
import concourse.bass as bass
import concourse.mybir as mybir
from concourse.bass_utils import run_bass_kernel_spmd

F32 = mybir.dt.float32
BF16 = mybir.dt.bfloat16
AF = mybir.ActivationFunctionType
ALU = mybir.AluOpType
AX = mybir.AxisListType
NBF = ml_dtypes.bfloat16

NCORE = 8
B, N, DM = 4, 4096, 4096
NT_OWN = 16
T_OWN = NT_OWN * 128
HD = 128
EPS_LN = 1e-5
CSC = float(1.0 / np.sqrt(128.0))

ENGINES = ("tensor", "vector", "scalar", "gpsimd", "sync")


def host_consts(parity):
    c = {}
    c["c_idf"] = np.eye(128, dtype=np.float32)
    c["c_idb"] = np.eye(128, dtype=np.float32).astype(NBF)
    c["c_U"] = np.triu(np.ones((128, 128), np.float32))
    c["c_L"] = np.tril(np.ones((128, 128), np.float32), -1)
    pos = np.arange(N, dtype=np.float64)
    half = np.arange(64, dtype=np.float64)
    inv = 1.0 / (10000.0 ** (2.0 * half / 128.0))
    fr = pos[:, None] * inv[None, :]
    emb = np.concatenate([fr, fr], -1)
    cos = np.cos(emb).astype(np.float32).reshape(32, 128, 128)
    sin = np.sin(emb).astype(np.float32).reshape(32, 128, 128)
    c["c_cos"] = np.ascontiguousarray(cos.transpose(1, 0, 2).reshape(128, 4096))
    c["c_sin"] = np.ascontiguousarray(sin.transpose(1, 0, 2).reshape(128, 4096))
    qi = np.arange(128)[:, None]
    ji = np.arange(256)[None, :]
    ok = (ji >= qi + 64) & (ji <= qi + 128)
    c["c_swm"] = np.where(ok, 0.0, -1e9).astype(np.float32)
    c["c_swm0"] = np.where(ok & (ji >= 128), 0.0, -1e9).astype(np.float32)
    tri_kq = (np.arange(128)[:, None] <= np.arange(128)[None, :])
    if parity == 0:
        mA, mB2 = tri_kq.astype(np.float32), np.zeros((128, 128), np.float32)
    else:
        mA, mB2 = np.ones((128, 128), np.float32), tri_kq.astype(np.float32)
    c["c_dmA"], c["c_dmB"] = mA, mB2
    c["c_dmAb"], c["c_dmBb"] = mA.astype(NBF), mB2.astype(NBF)
    c["c_odd"] = np.ascontiguousarray(
        (np.arange(128) % 2 == 1).astype(np.float32)[:, None])
    return c

CONST_SPECS = {
    "c_idf": ((128, 128), F32), "c_idb": ((128, 128), BF16),
    "c_U": ((128, 128), F32), "c_L": ((128, 128), F32),
    "c_cos": ((128, 4096), F32), "c_sin": ((128, 4096), F32),
    "c_swm": ((128, 256), F32), "c_swm0": ((128, 256), F32),
    "c_dmA": ((128, 128), F32), "c_dmB": ((128, 128), F32),
    "c_dmAb": ((128, 128), BF16), "c_dmBb": ((128, 128), BF16),
    "c_odd": ((128, 1), F32),
}


MAX_SEM_TRACK = {}

class Sem:
    def __init__(self, ctx, name):
        self.name = name
        self.h = ctx.st.enter_context(ctx.nc.semaphore(name))
        self.n = 0
        _SEM_VALUES[id(self.h)] = 0
    def tick(self, k):
        self.n += k
        _SEM_VALUES[id(self.h)] = self.n
        MAX_SEM_TRACK[self.name if hasattr(self, "name") else id(self)] = self.n
        return (self.h, k)

_KW_WRITE = {"accum_out"}
_KW_READ = {"bias", "scale", "alpha"}

def _ap_name(a):
    if isinstance(a, bass.AP):
        return a.tensor.name
    if hasattr(a, "ap") and hasattr(a, "dtype") and hasattr(a, "name"):
        return a.name
    return None


class Ctx:
    def __init__(self, nc):
        self.nc = nc
        self.st = ExitStack()
        self.scopes = [self.st]
        self.q = {e: [] for e in ENGINES}
        self.auto = {}
        self.qst = {}
        self.last_w = {}
        self.readers = {}
        self.known = {e: {} for e in ENGINES}

    def _autosem(self, eng, kind="c"):
        key = (eng, kind)
        if key not in self.auto:
            self.auto[key] = Sem(self, "auto_%s_%s" % (kind, eng))
        return self.auto[key]

    def push_scope(self):
        self.scopes.append(ExitStack())

    def pop_scope(self):
        if not hasattr(self, "_barsem"):
            self._barsem = Sem(self, "scopebar")
        self.barrier(self._barsem)
        self.scopes.pop().close()

    def sem(self, name):
        return Sem(self, name)

    def sb(self, name, shape, dt, persist=False):
        stk = self.scopes[0] if persist else self.scopes[-1]
        return stk.enter_context(self.nc.sbuf_tensor(name, list(shape), dt))

    def ps(self, name, shape, dt=F32):
        return self.scopes[-1].enter_context(
            self.nc.psum_tensor(name, list(shape), dt))

    def _need(self, eng, dep):
        if dep is None:
            return
        sem, val = dep
        if self.known[eng].get(id(sem), -1) >= val:
            return
        self.known[eng][id(sem)] = val
        self.q[eng].append(("wait", sem, val))

    def wait(self, eng, sem, val=None):
        v = sem.n if val is None else val
        self._need(eng, (sem.h, v))

    def op(self, eng, meth, *args, inc=None, **kw):
        writes, reads = [], []
        if meth == "collective_compute":
            for a in kw.get("ins", []):
                reads.append(_ap_name(a))
            for a in kw.get("outs", []):
                writes.append(_ap_name(a))
        else:
            first = True
            for a in args:
                nm = _ap_name(a)
                if nm is not None:
                    (writes if first else reads).append(nm)
                    first = False
                elif first and meth in ("dma_start",):
                    pass
            for k, a in kw.items():
                nm = _ap_name(a)
                if nm is None:
                    continue
                (writes if k in _KW_WRITE else reads).append(nm)
        for nm in reads:
            self._dep(eng, self.last_w.get(nm))
        for nm in writes:
            self._dep(eng, self.last_w.get(nm))
            for dep in self.readers.get(nm, ()):
                self._dep(eng, dep)
        async_op = meth in ("dma_start", "collective_compute")
        if async_op:
            kind = "cc" if meth == "collective_compute" else "d"
            s = self._autosem(eng, kind)
            k = 1 if meth == "collective_compute" else 16
            s.tick(k)
            inc2 = (s.h, None) if meth == "collective_compute" else (s.h, k)
            st = self._qstate(eng, kind)
            st["issued"] += 1
            stamp = ("pend", (eng, kind), st["issued"])
        else:
            if inc is None:
                s = self._autosem(eng)
                s.tick(1)
                inc2 = (s.h, 1)
                stamp = (s.h, s.n, eng)
            else:
                inc2 = inc
                stamp = (inc2[0], self._manual_val(inc2[0]), eng)
        for nm in writes:
            self.last_w[nm] = stamp
            self.readers[nm] = []
        for nm in reads:
            self.readers.setdefault(nm, []).append(stamp)
        self.q[eng].append(("op", meth, args, kw, inc2))

    def _qstate(self, eng, kind):
        key = (eng, kind)
        if key not in self.qst:
            self.qst[key] = {"issued": 0, "confirms": [],
                             "ready": Sem(self, "rdy_%s_%s" % (kind, eng))}
        return self.qst[key]

    def _confirm(self, qkey, idx):
        eng, kind = qkey
        st = self._qstate(eng, kind)
        for upto, val in st["confirms"]:
            if upto >= idx:
                return (st["ready"].h, val)
        a = self._autosem(eng, kind)
        self._need(eng, (a.h, a.n))
        t = st["ready"].tick(1)
        self.q[eng].append(("inc", t[0], t[1]))
        st["confirms"].append((st["issued"], st["ready"].n))
        return (st["ready"].h, st["ready"].n)

    def _dep(self, eng, dep):
        if dep is None:
            return
        if dep[0] == "pend":
            _, qkey, idx = dep
            h, v = self._confirm(qkey, idx)
            self._need(eng, (h, v))
            return
        if eng == "tensor" and dep[2] == "tensor":
            return
        self._need(eng, (dep[0], dep[1]))

    def _manual_val(self, handle):
        return _SEM_VALUES[id(handle)]

    def barrier(self, bsem, engines=ENGINES):
        for e in engines:
            t = bsem.tick(1)
            self.q[e].append(("inc", t[0], t[1]))
        for e in engines:
            self.wait(e, bsem)

    def finalize(self, block):
        def runner(items):
            def run(eng):
                for it in items:
                    if it[0] == "op":
                        _, meth, args, kw, inc = it
                        r = getattr(eng, meth)(*args, **kw)
                        if inc is not None:
                            if inc[1] is None:
                                r.then_inc(inc[0])
                            else:
                                r.then_inc(inc[0], inc[1])
                    elif it[0] == "wait":
                        eng.wait_ge(it[1], it[2])
                    elif it[0] == "inc":
                        eng.sem_inc(it[1], it[2])
            return run
        for e in ENGINES:
            getattr(block, e)(runner(self.q[e]))


_SEM_VALUES = {}


def make_sems(ctx):
    return {"d": ctx.sem("dsem"), "t": ctx.sem("tsem"), "v": ctx.sem("vsem"),
            "a": ctx.sem("asem"), "e": ctx.sem("esem"), "s": ctx.sem("ssem"),
            "bar": ctx.sem("barsem")}


def load_consts(ctx, names, sems):
    nc = ctx.nc
    out = {}
    for name in names:
        shape, dt = CONST_SPECS[name]
        d = nc.dram_tensor(name, list(shape), dt, kind="ExternalInput")
        t = ctx.sb("s" + name, shape, dt)
        ctx.op("sync", "dma_start", t[:], d[:, :])
        out[name] = t
    return out


def emit_stats(ctx, x_rows, n_tiles, sems, consts, tag=""):
    dsem, tsem, vsem, asem = sems["d"], sems["t"], sems["v"], sems["a"]
    rs_cols = ctx.sb("strs" + tag, (128, n_tiles), F32)
    negm_rows = ctx.sb("stnm" + tag, (1, n_tiles * 128), BF16)
    std_rows = ctx.sb("stsd" + tag, (1, n_tiles * 128), BF16)
    ctx.push_scope()
    xt = ctx.sb("stx" + tag, (128, 4096), F32)
    sq = ctx.sb("stq" + tag, (128, 4096), F32)
    s1 = ctx.sb("st1" + tag, (128, 1), F32)
    ss = ctx.sb("st2" + tag, (128, 1), F32)
    mvar = ctx.sb("st3" + tag, (128, 1), F32)
    mstd = ctx.sb("st4" + tag, (128, 1), F32)
    mm_ = ctx.sb("st5" + tag, (128, 1), F32)
    msq = ctx.sb("st6" + tag, (128, 1), F32)
    pkf = ctx.sb("st7" + tag, (128, 2), F32)
    pk = ctx.sb("st8" + tag, (128, 2), BF16)
    rows_ev = ctx.sb("st9" + tag, (2, 128), BF16)
    pT = ctx.ps("stpT" + tag, (2, 128), BF16)
    idb = consts["c_idb"]

    for tt in range(n_tiles):
        if tt > 0:
            ctx.wait("sync", vsem)
            ctx.wait("sync", asem)
        ctx.op("sync", "dma_start", xt[:], x_rows[tt * 128:(tt + 1) * 128, :])
        ctx.wait("vector", dsem)
        ctx.op("vector", "reduce_sum", s1[:], xt[:], axis=AX.X, inc=vsem.tick(1))
        ctx.wait("scalar", dsem)
        ctx.op("scalar", "activation", sq[:], xt[:], AF.Square,
               accum_out=ss[:], inc=asem.tick(1))
        ctx.wait("vector", asem)
        ctx.op("vector", "tensor_scalar_mul", mm_[:], s1[:], 1.0 / 4096.0)
        ctx.op("vector", "tensor_mul", msq[:], mm_[:], mm_[:])
        ctx.op("vector", "scalar_tensor_tensor", mvar[:], ss[:], 1.0 / 4096.0,
               msq[:], op0=ALU.mult, op1=ALU.subtract)
        ctx.op("vector", "tensor_scalar_add", mvar[:], mvar[:], EPS_LN,
               inc=vsem.tick(1))
        ctx.wait("scalar", vsem)
        ctx.op("scalar", "activation", mstd[:], mvar[:], AF.Sqrt,
               inc=asem.tick(1))
        ctx.wait("vector", asem)
        ctx.op("vector", "reciprocal", rs_cols[:, tt:tt + 1], mstd[:])
        ctx.op("vector", "tensor_scalar_mul", pkf[:, 0:1], mm_[:], -1.0)
        ctx.op("vector", "tensor_copy", pkf[:, 1:2], mstd[:])
        ctx.op("vector", "tensor_copy", pk[:], pkf[:], inc=vsem.tick(1))
        ctx.wait("tensor", vsem)
        ctx.op("tensor", "transpose", pT[:], pk[:], idb[:], inc=tsem.tick(1))
        ctx.wait("vector", tsem)
        ctx.op("vector", "tensor_copy", rows_ev[:], pT[:], inc=vsem.tick(1))
        ctx.wait("gpsimd", vsem)
        ctx.op("gpsimd", "dma_start", negm_rows[:, tt * 128:(tt + 1) * 128],
               rows_ev[0:1, :])
        ctx.op("gpsimd", "dma_start", std_rows[:, tt * 128:(tt + 1) * 128],
               rows_ev[1:2, :])
    for e in ENGINES:
        ctx.wait(e, dsem)
        ctx.wait(e, vsem)
    ctx.pop_scope()
    return {"rs": rs_cols, "negm": negm_rows, "std": std_rows}


def emit_proj(ctx, *, xT_planes, n_tok, w_dram, n_out, stats, sems, consts,
              bias_dram=None, epilogue=None, tag="", nfc=32, x_loader=None):
    dsem, tsem, vsem, esem = sems["d"], sems["t"], sems["v"], sems["e"]
    ctx.push_scope()
    NP = (n_out + 511) // 512
    n_tb = n_tok // 512
    wpan = [ctx.sb("pjw%d" % i + tag, (128, nfc * 512), BF16) for i in range(2)]
    xtb = [ctx.sb("pjx%d" % i + tag, (128, nfc * 512), BF16) for i in range(2)]
    onesc = ctx.sb("pj1c" + tag, (128, 1), BF16)
    ones_row = ctx.sb("pj1r" + tag, (1, 128), BF16)
    wsum = [ctx.sb("pjws%d" % i + tag, (1, 512), BF16) for i in range(2)]
    brow = [ctx.sb("pjbr%d" % i + tag, (1, 512), BF16) for i in range(2)]
    pP = [ctx.ps("pjp%d" % i + tag, (128, 512), F32) for i in range(4)]
    pW = ctx.ps("pjpw" + tag, (1, 512), F32)
    gsem = ctx.sem("pjg" + tag)
    ctx.op("gpsimd", "memset", onesc[:], 1.0)
    ctx.op("gpsimd", "memset", ones_row[:], 1.0, inc=gsem.tick(1))
    fold = stats is not None
    ng = max(1, nfc // 8)

    def load_w(pan):
        wp = wpan[pan % 2]
        o0 = pan * 512
        n_o = min(512, n_out - o0)
        for g in range(ng):
            f0 = g * 8
            nf = min(8, nfc - f0)
            ctx.op("gpsimd", "dma_start",
                   wp[:, f0 * 512:f0 * 512 + nf * 512]
                   .rearrange("p (fc t) -> p fc t", t=512)[:, :, 0:n_o],
                   w_dram[f0 * 128:(f0 + nf) * 128, o0:o0 + n_o]
                   .rearrange("(fc p) t -> p fc t", p=128))
        if bias_dram is not None:
            ctx.op("gpsimd", "dma_start", brow[pan % 2][:, 0:n_o],
                   bias_dram[0:1, o0:o0 + n_o])

    def load_x(tb):
        xp = xtb[tb % 2]
        if x_loader is not None:
            x_loader(ctx, xp, tb)
            return
        for g in range(ng):
            f0 = g * 8
            nf = min(8, nfc - f0)
            ctx.op("gpsimd", "dma_start",
                   xp[:, f0 * 512:(f0 + nf) * 512]
                   .rearrange("p (fc t) -> p fc t", t=512),
                   xT_planes[f0:f0 + nf, :, tb * 512:(tb + 1) * 512]
                   .rearrange("fc p t -> p fc t"))

    load_w(0)
    for pan in range(NP):
        o0 = pan * 512
        n_o = min(512, n_out - o0)
        wp = wpan[pan % 2]
        if pan + 1 < NP:
            load_w(pan + 1)
        if fold:
            ctx.wait("tensor", gsem)
            for fc in range(nfc):
                ctx.op("tensor", "matmul", pW[0:1, 0:n_o], onesc[:],
                       wp[:, fc * 512:fc * 512 + n_o],
                       start=(fc == 0), stop=(fc == nfc - 1),
                       inc=(tsem.tick(1) if fc == nfc - 1 else None))
            ctx.op("vector", "tensor_copy", wsum[pan % 2][:, 0:n_o],
                   pW[0:1, 0:n_o])
        load_x(0)
        for tb in range(n_tb):
            xp = xtb[tb % 2]
            if tb + 1 < n_tb:
                load_x(tb + 1)
            for tt in range(4):
                tg = tb * 4 + tt
                par = tg % 4
                n_extra = (1 if fold else 0) + (1 if bias_dram is not None else 0)
                for fc in range(nfc):
                    ctx.op("tensor", "matmul", pP[par][:, 0:n_o],
                           xp[:, fc * 512 + tt * 128:fc * 512 + (tt + 1) * 128],
                           wp[:, fc * 512:fc * 512 + n_o],
                           start=(fc == 0),
                           stop=(fc == nfc - 1 and n_extra == 0),
                           inc=(tsem.tick(1)
                                if fc == nfc - 1 and n_extra == 0 else None))
                if fold:
                    lastf = bias_dram is None
                    ctx.op("tensor", "matmul", pP[par][:, 0:n_o],
                           stats["negm"][0:1, tg * 128:(tg + 1) * 128],
                           wsum[pan % 2][0:1, 0:n_o], start=False, stop=lastf,
                           inc=(tsem.tick(1) if lastf else None))
                if bias_dram is not None:
                    row = (stats["std"][0:1, tg * 128:(tg + 1) * 128]
                           if fold else ones_row[0:1, 0:128])
                    ctx.op("tensor", "matmul", pP[par][:, 0:n_o],
                           row, brow[pan % 2][0:1, 0:n_o], start=False,
                           stop=True, inc=tsem.tick(1))
                epilogue(tg, o0, n_o, pP[par], tsem.n)
    for e in ENGINES:
        ctx.wait(e, esem)
    ctx.pop_scope()


def emit_softmax_free(ctx, dst, src, tmp, sems, pre_scale=1.0):
    vsem, asem = sems["v"], sems["a"]
    ctx.op("vector", "reduce_max", tmp["nm"][:], src, axis=AX.X, negate=True,
           inc=vsem.tick(1))
    ctx.wait("scalar", vsem)
    ctx.wait("scalar", sems["t"])
    ctx.op("scalar", "activation", dst, src, AF.Exp, bias=tmp["nm"][:],
           scale=pre_scale, accum_out=tmp["sm"][:], inc=asem.tick(1))
    ctx.wait("vector", asem)
    ctx.op("vector", "reciprocal", tmp["rs"][:], tmp["sm"][:], inc=vsem.tick(1))
    ctx.op("vector", "tensor_scalar_mul", dst, dst, tmp["rs"][:],
           inc=vsem.tick(1))


def emit_rope(ctx, dst, src, cosc, sinc, tmp64, sems):
    vsem = sems["v"]
    ctx.op("vector", "tensor_mul", dst[:, 0:64], src[:, 0:64], cosc[:, 0:64])
    ctx.op("vector", "tensor_mul", tmp64[:], src[:, 64:128], sinc[:, 0:64])
    ctx.op("vector", "tensor_sub", dst[:, 0:64], dst[:, 0:64], tmp64[:])
    ctx.op("vector", "tensor_mul", dst[:, 64:128], src[:, 64:128], cosc[:, 64:128])
    ctx.op("vector", "tensor_mul", tmp64[:], src[:, 0:64], sinc[:, 64:128])
    ctx.op("vector", "tensor_add", dst[:, 64:128], dst[:, 64:128], tmp64[:],
           inc=vsem.tick(1))


def emit_mixer(ctx, *, qf, kf, vf, oT_out, sems, consts, tag="",
               oT_writer=None):
    dsem, tsem, vsem, asem = sems["d"], sems["t"], sems["v"], sems["a"]
    ctx.push_scope()
    bar = sems["bar"]
    idf, idb = consts["c_idf"], consts["c_idb"]
    cU, cL = consts["c_U"], consts["c_L"]
    ccos, csin = consts["c_cos"], consts["c_sin"]

    qc_t = ctx.sb("mxq" + tag, (128, 2048), F32)
    kc_t = ctx.sb("mxk" + tag, (128, 512), F32)
    vc_t = ctx.sb("mxv" + tag, (128, 512), F32)
    vb = [ctx.sb("mxvb%d" % i + tag, (128, 512), BF16) for i in range(2)]
    sA = ctx.sb("mxsA" + tag, (128, 128), F32)
    sB = ctx.sb("mxsB" + tag, (128, 128), F32)
    gs = ctx.sb("mxgs" + tag, (128, 128), F32)
    ks = ctx.sb("mxks" + tag, (128, 128), F32)
    nm = ctx.sb("mxnm" + tag, (128, 1), F32)
    sm = ctx.sb("mxsm" + tag, (128, 1), F32)
    rsc = ctx.sb("mxrs" + tag, (128, 1), F32)
    tmp = {"nm": nm, "sm": sm, "rs": rsc}
    tmp64 = ctx.sb("mxt64" + tag, (128, 64), F32)
    eGT = ctx.sb("mxeg" + tag, (128, 128), F32)
    enGT = ctx.sb("mxen" + tag, (128, 128), F32)
    eRv = ctx.sb("mxer" + tag, (128, 128), F32)
    kh_bf = ctx.sb("mxkh" + tag, (128, 128), BF16)
    kgT_bf = ctx.sb("mxkg" + tag, (128, 128), BF16)
    sk = ctx.sb("mxsk" + tag, (128, 128), F32)
    win = ctx.sb("mxwin" + tag, (128, 4 * 256), BF16)
    Sf = ctx.sb("mxSf" + tag, (128, 4 * 128), F32)
    Sb = ctx.sb("mxSb" + tag, (128, 4 * 128), BF16)
    qs = ctx.sb("mxqs" + tag, (128, 128), F32)
    qgT_bf = ctx.sb("mxqg" + tag, (128, 128), BF16)
    sq_ = ctx.sb("mxsq" + tag, (128, 128), F32)
    sqT_bf = ctx.sb("mxsqT" + tag, (128, 128), BF16)
    At_bf = ctx.sb("mxAt" + tag, (128, 128), BF16)
    olh = ctx.sb("mxol" + tag, (128, 128), F32)
    msk = ctx.sb("mxmsk" + tag, (128, 256), F32)
    Pt = ctx.sb("mxP" + tag, (128, 256), F32)
    P_bf = ctx.sb("mxPb" + tag, (128, 256), BF16)
    PT1 = ctx.sb("mxPT1" + tag, (128, 128), BF16)
    PT2 = ctx.sb("mxPT2" + tag, (128, 128), BF16)
    oT_stage = ctx.sb("mxoT" + tag, (128, 2048), BF16)

    pAux = ctx.ps("mpAux" + tag, (128, 512), F32)
    pG, pR = pAux[:, 0:128], pAux[:, 128:256]
    pA, pD = pAux[:, 256:384], pAux[:, 384:512]
    pTr = ctx.ps("mpT" + tag, (128, 128), F32)
    pO = ctx.ps("mpO" + tag, (128, 128), F32)
    pSO = ctx.ps("mpSO" + tag, (128, 512), F32)
    pS, pO2 = pSO[:, 0:256], pSO[:, 256:384]
    pPT = ctx.ps("mpPT" + tag, (128, 256), BF16)

    ctx.op("gpsimd", "memset", Sf[:], 0.0)
    ctx.op("gpsimd", "memset", Sb[:], 0.0)
    ctx.op("gpsimd", "memset", win[:], 0.0)
    ctx.op("gpsimd", "memset", vb[1][:], 0.0)

    oT2d = (oT_out.rearrange("d (h t) -> d h t", t=4096)
            if oT_writer is None else None)

    for c in range(32):
        par = c & 1
        csl = slice(c * 128, (c + 1) * 128)
        ctx.op("sync", "dma_start", qc_t[:], qf[csl, :])
        ctx.op("sync", "dma_start", kc_t[:], kf[csl, :])
        ctx.op("sync", "dma_start", vc_t[:], vf[csl, :])
        ctx.wait("vector", dsem)
        ctx.wait("scalar", dsem)
        ctx.wait("tensor", dsem)
        cosc = ccos[:, csl]
        sinc = csin[:, csl]
        for kvh in range(4):
            kc = kc_t[:, kvh * 128:(kvh + 1) * 128]
            vc = vc_t[:, kvh * 128:(kvh + 1) * 128]
            vbsl = vb[par][:, kvh * 128:(kvh + 1) * 128]
            wsl = win[:, kvh * 256:(kvh + 1) * 256]
            Ssl_f = Sf[:, kvh * 128:(kvh + 1) * 128]
            Ssl_b = Sb[:, kvh * 128:(kvh + 1) * 128]
            ctx.op("vector", "tensor_copy", vbsl, vc, inc=vsem.tick(1))
            ctx.op("scalar", "activation", sA[:], kc, AF.Abs)
            ctx.op("scalar", "activation", sB[:], sA[:], AF.Exp, scale=-1.0)
            ctx.op("scalar", "activation", sA[:], sB[:], AF.Ln, bias=1.0,
                   inc=asem.tick(1))
            ctx.op("vector", "tensor_scalar_min", sB[:], kc, 0.0)
            ctx.wait("vector", asem)
            ctx.op("vector", "tensor_sub", gs[:], sB[:], sA[:])
            ctx.op("vector", "tensor_scalar_mul", gs[:], gs[:], 1.0 / 16.0,
                   inc=vsem.tick(1))
            emit_softmax_free(ctx, ks[:], kc, tmp, sems)
            ctx.wait("tensor", vsem)
            ctx.op("tensor", "matmul", pG, gs[:], cU[:], start=True,
                   stop=True, inc=tsem.tick(1))
            ctx.op("tensor", "matmul", pR, cL[:], gs[:], start=True,
                   stop=True, inc=tsem.tick(1))
            ctx.wait("scalar", tsem)
            ctx.op("scalar", "activation", eGT[:], pG, AF.Exp)
            ctx.op("scalar", "activation", enGT[:], pG, AF.Exp, scale=-1.0)
            ctx.op("scalar", "activation", eRv[:], pR, AF.Exp,
                   inc=asem.tick(1))
            ctx.wait("vector", asem)
            ctx.op("vector", "tensor_mul", kh_bf[:], ks[:], eRv[:],
                   inc=vsem.tick(1))
            ctx.wait("tensor", asem)
            ctx.op("tensor", "transpose", pTr[:], ks[:], idf[:],
                   inc=tsem.tick(1))
            ctx.wait("vector", tsem)
            ctx.op("vector", "tensor_mul", kgT_bf[:], pTr[:], enGT[:],
                   inc=vsem.tick(1))
            emit_rope(ctx, sk[:], kc, cosc, sinc, tmp64, sems)
            ctx.op("vector", "tensor_copy", wsl[:, 0:128], wsl[:, 128:256])
            ctx.wait("tensor", vsem)
            ctx.op("tensor", "transpose", pTr[:], sk[:], idf[:],
                   inc=tsem.tick(1))
            ctx.wait("vector", tsem)
            ctx.op("vector", "tensor_copy", wsl[:, 128:256], pTr[:, 0:128],
                   inc=vsem.tick(1))
            dec = eGT[:, 127:128]

            for hq in range(4):
                h = kvh * 4 + hq
                qc = qc_t[:, h * 128:(h + 1) * 128]
                emit_softmax_free(ctx, qs[:], qc, tmp, sems)
                ctx.wait("tensor", vsem)
                ctx.op("tensor", "transpose", pTr[:], qs[:], idf[:],
                       inc=tsem.tick(1))
                ctx.wait("vector", tsem)
                ctx.op("vector", "tensor_mul", qgT_bf[:], pTr[:], eGT[:],
                       inc=vsem.tick(1))
                ctx.wait("tensor", vsem)
                ctx.op("tensor", "matmul", pA, kgT_bf[:], qgT_bf[:],
                       start=True, stop=True, inc=tsem.tick(1))
                ctx.wait("vector", tsem)
                ctx.op("vector", "tensor_mul", At_bf[:], pA, cU[:],
                       inc=vsem.tick(1))
                ctx.wait("tensor", vsem)
                ctx.op("tensor", "matmul", pO[:], Ssl_b, qgT_bf[:],
                       start=True, stop=False)
                ctx.op("tensor", "matmul", pO[:], vbsl, At_bf[:],
                       start=False, stop=True, inc=tsem.tick(1))
                ctx.wait("scalar", tsem)
                ctx.op("scalar", "activation", olh[:], pO[:], AF.Copy,
                       scale=0.5, inc=asem.tick(1))
                emit_rope(ctx, sq_[:], qc, cosc, sinc, tmp64, sems)
                ctx.wait("tensor", vsem)
                ctx.op("tensor", "transpose", pTr[:], sq_[:], idf[:],
                       inc=tsem.tick(1))
                ctx.wait("vector", tsem)
                ctx.op("vector", "tensor_copy", sqT_bf[:], pTr[:],
                       inc=vsem.tick(1))
                ctx.wait("tensor", vsem)
                ctx.op("tensor", "matmul", pS, sqT_bf[:], wsl[:, 0:256],
                       start=True, stop=True, inc=tsem.tick(1))
                ctx.wait("vector", tsem)
                ctx.wait("vector", asem)
                swm = consts["c_swm0"] if c == 0 else consts["c_swm"]
                ctx.op("vector", "tensor_add", msk[:], pS, swm[:])
                ctx.op("vector", "reduce_max", nm[:], msk[:], axis=AX.X)
                ctx.op("vector", "tensor_scalar_mul", nm[:], nm[:], -CSC,
                       inc=vsem.tick(1))
                ctx.wait("scalar", vsem)
                ctx.op("scalar", "activation", Pt[:], msk[:], AF.Exp,
                       bias=nm[:], scale=CSC, accum_out=sm[:],
                       inc=asem.tick(1))
                ctx.wait("vector", asem)
                ctx.op("vector", "reciprocal", rsc[:], sm[:])
                ctx.op("vector", "tensor_scalar_mul", P_bf[:], Pt[:], rsc[:],
                       inc=vsem.tick(1))
                ctx.wait("tensor", vsem)
                ctx.op("tensor", "transpose", pPT[:, 0:128], P_bf[:, 0:128],
                       idb[:], inc=tsem.tick(1))
                ctx.op("tensor", "transpose", pPT[:, 128:256],
                       P_bf[:, 128:256], idb[:], inc=tsem.tick(1))
                ctx.wait("vector", tsem)
                ctx.op("vector", "tensor_copy", PT1[:], pPT[:, 0:128])
                ctx.op("vector", "tensor_copy", PT2[:], pPT[:, 128:256],
                       inc=vsem.tick(1))
                ctx.wait("tensor", vsem)
                ctx.op("tensor", "matmul", pO2,
                       vb[1 - par][:, kvh * 128:(kvh + 1) * 128],
                       PT1[:], start=True, stop=False)
                ctx.op("tensor", "matmul", pO2,
                       vb[par][:, kvh * 128:(kvh + 1) * 128],
                       PT2[:], start=False, stop=True,
                       inc=tsem.tick(1))
                ctx.wait("vector", tsem)
                ctx.wait("vector", asem)
                ctx.op("vector", "scalar_tensor_tensor",
                       oT_stage[:, h * 128:(h + 1) * 128], pO2, 0.5,
                       olh[:], op0=ALU.mult, op1=ALU.add, inc=vsem.tick(1))
            ctx.wait("tensor", vsem)
            ctx.op("tensor", "matmul", pD, kh_bf[:], vbsl,
                   start=True, stop=True, inc=tsem.tick(1))
            ctx.wait("vector", tsem)
            ctx.op("vector", "scalar_tensor_tensor", Ssl_f, Ssl_f, dec,
                   pD, op0=ALU.mult, op1=ALU.add)
            ctx.op("vector", "tensor_copy", Ssl_b, Ssl_f, inc=vsem.tick(1))
        ctx.wait("gpsimd", vsem)
        if oT_writer is None:
            for g in range(2):
                ctx.op("gpsimd", "dma_start",
                       oT2d[:, g * 8:(g + 1) * 8, c * 128:(c + 1) * 128],
                       oT_stage[:, g * 1024:(g + 1) * 1024]
                       .rearrange("d (h t) -> d h t", t=128))
        else:
            oT_writer(c, oT_stage)
        ctx.barrier(bar)
    a = ctx._autosem("gpsimd", "d")
    ctx._need("gpsimd", (a.h, a.n))
    ctx.pop_scope()


L1_CONSTS = ["c_idf", "c_idb", "c_U", "c_L", "c_cos", "c_sin", "c_swm",
             "c_swm0"]

def build_L1(debug=False):
    nc = bass.Bass("TRN2", target_bir_lowering=False, debug=False)
    ctx = Ctx(nc)
    sems = make_sems(ctx)
    kind = "ExternalOutput" if debug else "Internal"
    x0_rows = nc.dram_tensor("x0_rows", [4096, 4096], F32, kind="ExternalInput")
    x0Tp = nc.dram_tensor("x0Tp", [32, 128, 4096], F32, kind="ExternalInput")
    wqkv = nc.dram_tensor("wqkv", [4096, 3072], F32, kind="ExternalInput")
    qf = nc.dram_tensor("qf", [4096, 2048], F32, kind=kind)
    kf = nc.dram_tensor("kf", [4096, 512], F32, kind=kind)
    vf = nc.dram_tensor("vf", [4096, 512], F32, kind=kind)
    oT_out = nc.dram_tensor("oT", [128, 16 * 4096], BF16, kind="ExternalOutput")

    with nc.Block() as block:
        consts = load_consts(ctx, L1_CONSTS, sems)
        stats = emit_stats(ctx, x0_rows, 32, sems, consts)
        ctx.barrier(sems["bar"])

        out_sb = [ctx.sb("l1o%d" % i, (128, 512), F32) for i in range(2)]
        epar = [0]

        def epi(tg, o0, n_o, psum, tval):
            par = epar[0]
            epar[0] = 1 - par
            ctx.op("scalar", "activation", out_sb[par][:, 0:n_o],
                   psum[:, 0:n_o], AF.Copy,
                   scale=stats["rs"][:, tg:tg + 1], inc=sems["e"].tick(1))
            if o0 < 2048:
                dest = qf[tg * 128:(tg + 1) * 128, o0:o0 + n_o]
            elif o0 < 2560:
                dest = kf[tg * 128:(tg + 1) * 128, 0:n_o]
            else:
                dest = vf[tg * 128:(tg + 1) * 128, 0:n_o]
            ctx.op("sync", "dma_start", dest, out_sb[par][:, 0:n_o])

        emit_proj(ctx, xT_planes=x0Tp, n_tok=4096, w_dram=wqkv, n_out=3072,
                  stats=stats, sems=sems, consts=consts, epilogue=epi,
                  tag="L1")
        ctx.barrier(sems["bar"])
        emit_mixer(ctx, qf=qf, kf=kf, vf=vf, oT_out=oT_out.ap(), sems=sems,
                   consts=consts)
        ctx.barrier(sems["bar"])
        ctx.finalize(block)
    ctx.st.close()
    return nc


def host_inputs_L1(inputs, core):
    b, p = core // 2, core % 2
    x0 = np.ascontiguousarray(inputs["L0_attn_hidden_states"][b])
    hs = slice(p * 2048, (p + 1) * 2048)
    ks = slice(p * 512, (p + 1) * 512)
    wqkv = np.concatenate([inputs["L0_attn_Wq"][:, hs],
                           inputs["L0_attn_Wk"][:, ks],
                           inputs["L0_attn_Wv"][:, ks]], axis=1)
    d = {"x0_rows": x0,
         "x0Tp": np.ascontiguousarray(x0.T.reshape(32, 128, 4096)),
         "wqkv": np.ascontiguousarray(wqkv)}
    d.update({k: v for k, v in host_consts(p).items() if k in L1_CONSTS})
    return d


def build_L2a():
    nc = bass.Bass("TRN2", target_bir_lowering=False, debug=False)
    ctx = Ctx(nc)
    sems = make_sems(ctx)
    x0_own = nc.dram_tensor("x0_own", [T_OWN, 4096], F32, kind="ExternalInput")
    oT_own = nc.dram_tensor("oT_own", [32, 128, T_OWN], BF16,
                            kind="ExternalInput")
    Wo = nc.dram_tensor("Wo", [4096, 4096], F32, kind="ExternalInput")
    x1_rows = nc.dram_tensor("x1_rows", [T_OWN, 4096], F32,
                             kind="ExternalOutput")
    with nc.Block() as block:
        x0t = ctx.sb("l2x0", (128, 512), F32)
        out_sb = [ctx.sb("l2o%d" % i, (128, 512), F32) for i in range(2)]
        epar = [0]

        def epi(tg, o0, n_o, psum, tval):
            par = epar[0]
            epar[0] = 1 - par
            rows = slice(tg * 128, (tg + 1) * 128)
            ctx.op("sync", "dma_start", x0t[:, 0:n_o], x0_own[rows, o0:o0 + n_o])
            ctx.op("vector", "tensor_add", out_sb[par][:, 0:n_o],
                   psum[:, 0:n_o], x0t[:, 0:n_o], inc=sems["e"].tick(1))
            ctx.op("sync", "dma_start", x1_rows[rows, o0:o0 + n_o],
                   out_sb[par][:, 0:n_o])

        emit_proj(ctx, xT_planes=oT_own, n_tok=T_OWN, w_dram=Wo, n_out=4096,
                  stats=None, sems=sems, consts=None, epilogue=epi, tag="L2a")
        a = ctx._autosem("sync", "d")
        ctx._need("sync", (a.h, a.n))
        ctx.finalize(block)
    ctx.st.close()
    return nc


def host_inputs_L2a(inputs, core, oT_pair):
    b, p = core // 2, core % 2
    own = own_rows(p)
    x0 = inputs["L0_attn_hidden_states"][b][own]
    o0 = np.asarray(oT_pair[0]).reshape(128, 16, 4096)
    o1 = np.asarray(oT_pair[1]).reshape(128, 16, 4096)
    full = np.concatenate([o0, o1], axis=1)
    oT_own = np.ascontiguousarray(full.transpose(1, 0, 2)[:, :, own])
    return {"x0_own": np.ascontiguousarray(x0), "oT_own": oT_own,
            "Wo": inputs["L0_attn_Wo"]}


def own_rows(p):
    idx = np.arange(T_OWN)
    tile = idx // 128
    return (2 * tile + p) * 128 + (idx % 128)


L2B_CONSTS = ["c_idf", "c_idb", "c_odd"]

def build_L2b():
    nc = bass.Bass("TRN2", target_bir_lowering=False, debug=False)
    ctx = Ctx(nc)
    sems = make_sems(ctx)
    x1_rows = nc.dram_tensor("x1_rows", [T_OWN, 4096], F32, kind="ExternalInput")
    x1Tp = nc.dram_tensor("x1Tp", [32, 128, T_OWN], F32, kind="ExternalInput")
    WT = nc.dram_tensor("WT", [4096, 4096], F32, kind="ExternalInput")
    b_row = nc.dram_tensor("b_row", [1, 4096], F32, kind="ExternalInput")
    A4 = nc.dram_tensor("A4", [4096, 32], F32, kind="ExternalInput")
    LiT = nc.dram_tensor("LiT", [32, 8], F32, kind="ExternalInput")
    H_rows = nc.dram_tensor("H_rows", [T_OWN, 4096], F32, kind="ExternalOutput")
    z_rows = nc.dram_tensor("z_rows", [T_OWN, 32], F32, kind="ExternalOutput")
    ds_rows = nc.dram_tensor("ds_rows", [T_OWN, 8], F32, kind="ExternalOutput")
    scales_loc = nc.dram_tensor("scales_loc", [1, 2], F32, kind="ExternalOutput")
    with nc.Block() as block:
        consts = load_consts(ctx, L2B_CONSTS, sems)
        stats = emit_stats(ctx, x1_rows, NT_OWN, sems, consts, tag="b")
        hmax = ctx.sb("l2hm", (128, 1), F32)
        dmax = ctx.sb("l2dm", (128, 1), F32)
        tmp8 = ctx.sb("l2t8", (128, 8), F32)
        m8 = ctx.sb("l2m8", (128, 1), F32)
        ctx.op("gpsimd", "memset", hmax[:], 0.0)
        ctx.op("gpsimd", "memset", dmax[:], 0.0)
        out_sb = [ctx.sb("l2bo%d" % i, (128, 512), F32) for i in range(2)]
        epar = [0]

        def epiH(tg, o0, n_o, psum, tval):
            par = epar[0]
            epar[0] = 1 - par
            rows = slice(tg * 128, (tg + 1) * 128)
            ctx.op("scalar", "activation", out_sb[par][:, 0:n_o],
                   psum[:, 0:n_o], AF.Copy, scale=stats["rs"][:, tg:tg + 1],
                   inc=sems["e"].tick(1))
            ctx.op("sync", "dma_start", H_rows[rows, o0:o0 + n_o],
                   out_sb[par][:, 0:n_o])
            if o0 == 0:
                ctx.op("vector", "tensor_scalar_mul", tmp8[:],
                       out_sb[par][:, 0:8], consts["c_odd"][:])
                ctx.op("vector", "reduce_max", m8[:], tmp8[:], axis=AX.X,
                       apply_absolute_value=True)
                ctx.op("vector", "tensor_max", hmax[:], hmax[:], m8[:])

        emit_proj(ctx, xT_planes=x1Tp, n_tok=T_OWN, w_dram=WT, n_out=4096,
                  stats=stats, sems=sems, consts=consts, bias_dram=b_row,
                  epilogue=epiH, tag="L2bH")

        z_sb = ctx.sb("l2z", (128, 32), F32)

        def epiZ(tg, o0, n_o, psum, tval):
            rows = slice(tg * 128, (tg + 1) * 128)
            ctx.op("scalar", "activation", z_sb[:, 0:n_o], psum[:, 0:n_o],
                   AF.Copy, scale=stats["rs"][:, tg:tg + 1],
                   inc=sems["e"].tick(1))
            ctx.op("sync", "dma_start", z_rows[rows, 0:n_o], z_sb[:, 0:n_o])

        emit_proj(ctx, xT_planes=x1Tp, n_tok=T_OWN, w_dram=A4, n_out=32,
                  stats=stats, sems=sems, consts=consts, epilogue=epiZ,
                  tag="L2bz")

        ctx.push_scope()
        zt = ctx.sb("l2zt", (128, 32), F32)
        ztb = ctx.sb("l2ztb", (128, 32), BF16)
        zTb = ctx.sb("l2zT", (32, 128), BF16)
        litb = ctx.sb("l2lit", (32, 8), BF16)
        ds_sb = ctx.sb("l2ds", (128, 8), F32)
        pT = ctx.ps("l2pT", (32, 128), BF16)
        pD = ctx.ps("l2pD", (128, 8), F32)
        ctx.op("gpsimd", "dma_start", litb[:], LiT[:, :])
        for tg in range(NT_OWN):
            rows = slice(tg * 128, (tg + 1) * 128)
            ctx.op("sync", "dma_start", zt[:], z_rows[rows, :])
            ctx.op("vector", "tensor_copy", ztb[:], zt[:])
            ctx.op("tensor", "transpose", pT[:], ztb[:], consts["c_idb"][:],
                   inc=sems["t"].tick(1))
            ctx.op("vector", "tensor_copy", zTb[:], pT[:])
            ctx.op("tensor", "matmul", pD[:], zTb[:], litb[:], start=True,
                   stop=True, inc=sems["t"].tick(1))
            ctx.op("vector", "tensor_copy", ds_sb[:], pD[:])
            ctx.op("sync", "dma_start", ds_rows[rows, :], ds_sb[:])
            ctx.op("vector", "tensor_scalar_mul", tmp8[:], ds_sb[:],
                   consts["c_odd"][:])
            ctx.op("vector", "reduce_max", m8[:], tmp8[:], axis=AX.X,
                   apply_absolute_value=True)
            ctx.op("vector", "tensor_max", dmax[:], dmax[:], m8[:])
        pR = ctx.ps("l2pR", (1, 128), F32)
        srow = ctx.sb("l2sr", (1, 128), F32)
        sc2 = ctx.sb("l2sc", (1, 2), F32)
        for i, acc in enumerate((hmax, dmax)):
            ctx.op("tensor", "transpose", pR[:], acc[:], consts["c_idf"][:],
                   inc=sems["t"].tick(1))
            ctx.op("vector", "tensor_copy", srow[:], pR[:])
            ctx.op("vector", "reduce_max", sc2[:, i:i + 1], srow[:], axis=AX.X)
        ctx.op("sync", "dma_start", scales_loc[:, :], sc2[:])
        a = ctx._autosem("sync", "d")
        ctx._need("sync", (a.h, a.n))
        ctx.pop_scope()
        ctx.finalize(block)
    ctx.st.close()
    return nc


def host_inputs_L2b(inputs, core, x1_own):
    d = {"x1_rows": np.ascontiguousarray(x1_own),
         "x1Tp": np.ascontiguousarray(x1_own.T.reshape(32, 128, T_OWN)),
         "WT": np.ascontiguousarray(inputs["L0_ffn_W"].T),
         "b_row": np.ascontiguousarray(inputs["L0_ffn_b"][None, :]),
         "A4": inputs["L0_ffn_A"],
         "LiT": np.ascontiguousarray(inputs["L0_ffn_LiMEs"].T)}
    d.update({k: v for k, v in host_consts(core % 2).items() if k in L2B_CONSTS})
    return d


L3_CONSTS = ["c_idf", "c_idb"]

def build_L3():
    nc = bass.Bass("TRN2", target_bir_lowering=False, debug=False)
    ctx = Ctx(nc)
    sems = make_sems(ctx)
    x1_rows = nc.dram_tensor("x1_rows", [T_OWN, 4096], F32, kind="ExternalInput")
    H_rows = nc.dram_tensor("H_rows", [T_OWN, 4096], F32, kind="ExternalInput")
    z_rows = nc.dram_tensor("z_rows", [T_OWN, 32], F32, kind="ExternalInput")
    ds_rows = nc.dram_tensor("ds_rows", [T_OWN, 8], F32, kind="ExternalInput")
    scales_all = nc.dram_tensor("scales_all", [1, 16], F32, kind="ExternalInput")
    LiM = nc.dram_tensor("LiM", [8, 32], F32, kind="ExternalInput")
    Bm = nc.dram_tensor("Bm", [32, 4096], F32, kind="ExternalInput")
    x2_rows = nc.dram_tensor("x2_rows", [T_OWN, 4096], F32,
                             kind="ExternalOutput")
    with nc.Block() as block:
        consts = load_consts(ctx, L3_CONSTS, sems)
        sall = ctx.sb("l3sa", (1, 16), F32)
        s11 = ctx.sb("l3s1", (1, 2), F32)
        inv2 = ctx.sb("l3inv", (1, 2), F32)
        invc = ctx.sb("l3ic", (128, 2), F32)
        ones_r = ctx.sb("l3or", (1, 128), F32)
        limb = ctx.sb("l3lim", (8, 32), BF16)
        bmb = ctx.sb("l3bm", (32, 4096), BF16)
        pB = ctx.ps("l3pB", (128, 2), F32)
        ctx.op("sync", "dma_start", sall[:], scales_all[:, :])
        ctx.op("gpsimd", "dma_start", limb[:], LiM[:, :])
        for fc in range(1):
            ctx.op("gpsimd", "dma_start", bmb[:], Bm[:, :])
        ctx.op("gpsimd", "memset", ones_r[:], 1.0)
        s8x2 = ctx.sb("l3s8", (1, 16), F32)
        sv = sall.ap().rearrange("a (p two) -> a two p", two=2)
        ctx.op("vector", "tensor_copy",
               s8x2.ap().rearrange("a (two p) -> a two p", two=2), sv)
        ctx.op("vector", "reduce_max", s11[:, 0:1], s8x2[:, 0:8], axis=AX.X)
        ctx.op("vector", "reduce_max", s11[:, 1:2], s8x2[:, 8:16], axis=AX.X)
        ctx.op("vector", "tensor_scalar_max", s11[:], s11[:], 1e-6)
        ctx.op("vector", "reciprocal", inv2[:], s11[:])
        ctx.op("vector", "tensor_scalar_mul", inv2[:], inv2[:], 0.5)
        ctx.op("tensor", "matmul", pB[:], ones_r[:], inv2[:], start=True,
               stop=True, inc=sems["t"].tick(1))
        ctx.op("vector", "tensor_copy", invc[:], pB[:])

        Ht = ctx.sb("l3H", (128, 4096), F32)
        x1t = ctx.sb("l3x1", (128, 4096), F32)
        hs8 = ctx.sb("l3h8", (128, 8), F32)
        ds8 = ctx.sb("l3d8", (128, 8), F32)
        zt = ctx.sb("l3z", (128, 32), F32)
        lg = ctx.sb("l3lg", (128, 8), F32)
        t8 = ctx.sb("l3t8", (128, 8), F32)
        nm = ctx.sb("l3nm", (128, 1), F32)
        sm = ctx.sb("l3sm", (128, 1), F32)
        rsc = ctx.sb("l3rs", (128, 1), F32)
        probs = ctx.sb("l3pr", (128, 8), F32)
        eqm = ctx.sb("l3eq", (128, 8), F32)
        m2 = ctx.sb("l3m2", (128, 1), F32)
        sig = ctx.sb("l3sg", (128, 8), F32)
        wts = ctx.sb("l3wt", (128, 8), F32)
        wsh = ctx.sb("l3ws", (128, 8), F32)
        wshb = ctx.sb("l3wsb", (128, 8), BF16)
        wT = ctx.sb("l3wT", (8, 128), BF16)
        zpb = ctx.sb("l3zp", (128, 32), BF16)
        zpT = ctx.sb("l3zpT", (32, 128), BF16)
        osb = ctx.sb("l3ou", (128, 512), F32)
        pT8 = ctx.ps("l3pT", (8, 128), BF16)
        pM = ctx.ps("l3pM", (128, 32), F32)
        pT32 = ctx.ps("l3pT2", (32, 128), BF16)
        pDl = ctx.ps("l3pD", (128, 512), F32)
        SHUF = [i | 1 for i in range(32)]

        for tg in range(NT_OWN):
            rows = slice(tg * 128, (tg + 1) * 128)
            ctx.op("sync", "dma_start", hs8[:], H_rows[rows, 0:8])
            ctx.op("sync", "dma_start", ds8[:], ds_rows[rows, :])
            ctx.op("sync", "dma_start", zt[:], z_rows[rows, :])
            ctx.op("sync", "dma_start", x1t[:], x1_rows[rows, :])
            ctx.op("sync", "dma_start", Ht[:], H_rows[rows, :])
            ctx.op("vector", "tensor_scalar_mul", t8[:], ds8[:], invc[:, 1:2])
            ctx.op("vector", "scalar_tensor_tensor", lg[:], hs8[:],
                   invc[:, 0:1], t8[:], op0=ALU.mult, op1=ALU.add)
            ctx.op("vector", "reduce_max", nm[:], lg[:], axis=AX.X, negate=True)
            ctx.op("scalar", "activation", probs[:], lg[:], AF.Exp,
                   bias=nm[:], scale=1.0, accum_out=sm[:])
            ctx.op("vector", "reciprocal", rsc[:], sm[:])
            ctx.op("vector", "tensor_scalar_mul", probs[:], probs[:], rsc[:])
            ctx.op("vector", "reduce_max", nm[:], probs[:], axis=AX.X)
            ctx.op("vector", "tensor_scalar", eqm[:], probs[:], nm[:], None,
                   op0=ALU.is_ge)
            ctx.op("vector", "scalar_tensor_tensor", t8[:], eqm[:], -1e30,
                   probs[:], op0=ALU.mult, op1=ALU.add)
            ctx.op("vector", "reduce_max", m2[:], t8[:], axis=AX.X)
            ctx.op("vector", "tensor_scalar_mul", m2[:], m2[:], -2.0)
            ctx.op("scalar", "activation", sig[:], probs[:], AF.Sigmoid,
                   bias=m2[:], scale=2.0)
            ctx.op("vector", "tensor_mul", wts[:], probs[:], sig[:])
            ctx.op("vector", "reduce_sum", sm[:], wts[:], axis=AX.X)
            ctx.op("vector", "tensor_scalar_add", sm[:], sm[:], 1e-9)
            ctx.op("vector", "reciprocal", rsc[:], sm[:])
            ctx.op("vector", "tensor_scalar_mul", wts[:], wts[:], rsc[:])
            ctx.op("vector", "stream_shuffle", wsh[:], wts[:], SHUF)
            ctx.op("vector", "tensor_copy", wshb[:], wsh[:])
            ctx.op("tensor", "transpose", pT8[:], wshb[:], consts["c_idb"][:],
                   inc=sems["t"].tick(1))
            ctx.op("vector", "tensor_copy", wT[:], pT8[:])
            ctx.op("tensor", "matmul", pM[:], wT[:], limb[:], start=True,
                   stop=True, inc=sems["t"].tick(1))
            ctx.op("vector", "tensor_mul", zpb[:], zt[:], pM[:])
            ctx.op("tensor", "transpose", pT32[:], zpb[:], consts["c_idb"][:],
                   inc=sems["t"].tick(1))
            ctx.op("vector", "tensor_copy", zpT[:], pT32[:])
            for oc in range(8):
                osl = slice(oc * 512, (oc + 1) * 512)
                ctx.op("tensor", "matmul", pDl[:], zpT[:], bmb[:, osl],
                       start=True, stop=True, inc=sems["t"].tick(1))
                ctx.op("vector", "scalar_tensor_tensor", osb[:], pDl[:], 0.5,
                       Ht[:, osl], op0=ALU.mult, op1=ALU.add)
                ctx.op("vector", "tensor_add", osb[:], osb[:], x1t[:, osl])
                ctx.op("sync", "dma_start", x2_rows[rows, osl], osb[:])
        a = ctx._autosem("sync", "d")
        ctx._need("sync", (a.h, a.n))
        ctx.finalize(block)
    ctx.st.close()
    return nc


def host_inputs_L3(inputs, core, x1_own, H_own, z_own, ds_own, scales8):
    d = {"x1_rows": np.ascontiguousarray(x1_own),
         "H_rows": np.ascontiguousarray(H_own),
         "z_rows": np.ascontiguousarray(z_own),
         "ds_rows": np.ascontiguousarray(ds_own),
         "scales_all": np.ascontiguousarray(
             np.concatenate(scales8, 1).reshape(1, 16)),
         "LiM": inputs["L0_ffn_LiMEs"],
         "Bm": inputs["L0_ffn_Bmat"]}
    d.update({k: v for k, v in host_consts(core % 2).items() if k in L3_CONSTS})
    return d


L4_CONSTS = ["c_idf", "c_idb"]

def build_L4():
    nc = bass.Bass("TRN2", target_bir_lowering=False, debug=False)
    ctx = Ctx(nc)
    sems = make_sems(ctx)
    x2_rows = nc.dram_tensor("x2_rows", [T_OWN, 4096], F32, kind="ExternalInput")
    x2Tp = nc.dram_tensor("x2Tp", [32, 128, T_OWN], F32, kind="ExternalInput")
    Wcat = nc.dram_tensor("Wcat", [4096, 12288], F32, kind="ExternalInput")
    bcat = nc.dram_tensor("bcat", [1, 12288], F32, kind="ExternalInput")
    q_rows = nc.dram_tensor("q_rows", [T_OWN, 4096], BF16, kind="ExternalOutput")
    k_rows = nc.dram_tensor("k_rows", [T_OWN, 4096], BF16, kind="ExternalOutput")
    v_rows = nc.dram_tensor("v_rows", [T_OWN, 4096], BF16, kind="ExternalOutput")
    kmax_loc = nc.dram_tensor("kmax_loc", [1, 1], F32, kind="ExternalOutput")
    with nc.Block() as block:
        consts = load_consts(ctx, L4_CONSTS, sems)
        stats = emit_stats(ctx, x2_rows, NT_OWN, sems, consts, tag="4")
        out_sb = [ctx.sb("l4o%d" % i, (128, 512), BF16) for i in range(2)]
        sqs = ctx.sb("l4sq", (128, 128), F32)
        kn = ctx.sb("l4kn", (128, 1), F32)
        knacc = ctx.sb("l4ka", (128, 1), F32)
        ctx.op("gpsimd", "memset", knacc[:], 0.0)
        epar = [0]

        def epi(tg, o0, n_o, psum, tval):
            par = epar[0]
            epar[0] = 1 - par
            rows = slice(tg * 128, (tg + 1) * 128)
            ctx.op("scalar", "activation", out_sb[par][:, 0:n_o],
                   psum[:, 0:n_o], AF.Copy, scale=stats["rs"][:, tg:tg + 1],
                   inc=sems["e"].tick(1))
            which, off = divmod(o0, 4096)
            dest = (q_rows, k_rows, v_rows)[which]
            ctx.op("sync", "dma_start", dest[rows, off:off + n_o],
                   out_sb[par][:, 0:n_o])
            if which == 1:
                for hh in range(4):
                    sl = slice(hh * 128, (hh + 1) * 128)
                    ctx.op("scalar", "activation", sqs[:],
                           out_sb[par][:, sl], AF.Square, accum_out=kn[:])
                    ctx.op("vector", "tensor_max", knacc[:], knacc[:], kn[:])

        emit_proj(ctx, xT_planes=x2Tp, n_tok=T_OWN, w_dram=Wcat, n_out=12288,
                  stats=stats, sems=sems, consts=consts, bias_dram=bcat,
                  epilogue=epi, tag="L4")
        pR = ctx.ps("l4pR", (1, 128), F32)
        srow = ctx.sb("l4sr", (1, 128), F32)
        km = ctx.sb("l4km", (1, 1), F32)
        ctx.op("tensor", "transpose", pR[:], knacc[:], consts["c_idf"][:],
               inc=sems["t"].tick(1))
        ctx.op("vector", "tensor_copy", srow[:], pR[:])
        ctx.op("vector", "reduce_max", km[:], srow[:], axis=AX.X)
        ctx.op("sync", "dma_start", kmax_loc[:, :], km[:])
        a = ctx._autosem("sync", "d")
        ctx._need("sync", (a.h, a.n))
        ctx.finalize(block)
    ctx.st.close()
    return nc


def host_inputs_L4(inputs, core, x2_own):
    Wcat = np.concatenate([inputs["L1_attn_Wq"].T, inputs["L1_attn_Wk"].T,
                           inputs["L1_attn_Wv"].T], axis=1)
    bcat = np.concatenate([inputs["L1_attn_bq"], inputs["L1_attn_bk"],
                           inputs["L1_attn_bv"]])[None, :]
    d = {"x2_rows": np.ascontiguousarray(x2_own),
         "x2Tp": np.ascontiguousarray(x2_own.T.reshape(32, 128, T_OWN)),
         "Wcat": np.ascontiguousarray(Wcat),
         "bcat": np.ascontiguousarray(bcat)}
    d.update({k: v for k, v in host_consts(core % 2).items() if k in L4_CONSTS})
    return d


L5_CONSTS = ["c_idf", "c_idb", "c_dmA", "c_dmB", "c_dmAb", "c_dmBb"]

def emit_attn_l1(ctx, *, qT, kT, vpl, kmax_all, oT_l1, sems, consts,
                 rank_major=False):
    tsem = sems["t"]
    if rank_major:
        kcoff = lambda kc: (kc & 1) * 2048 + (kc >> 1) * 128
    else:
        kcoff = lambda kc: kc * 128
    ctx.push_scope()
    kT_sb = ctx.sb("a5kT", (128, 4096), BF16)
    v_sb = ctx.sb("a5v", (128, 32 * 128), BF16)
    qT_sb = ctx.sb("a5qT", (128, 2048), BF16)
    sq_s = ctx.sb("a5sq", (128, 512), F32)
    Brow = ctx.sb("a5Br", (1, 512), F32)
    BoC = ctx.sb("a5Bc", (1, 512), F32)
    negBoC = ctx.sb("a5nB", (1, 512), F32)
    lnS = ctx.sb("a5ln", (1, 512), F32)
    negLoC = ctx.sb("a5nL", (1, 512), F32)
    onesr = ctx.sb("a5or", (1, 128), F32)
    onesc = ctx.sb("a5oc", (128, 1), F32)
    kmx = ctx.sb("a5km", (1, 8), F32)
    ksc = ctx.sb("a5ks", (1, 1), F32)
    P0 = ctx.sb("a5P0", (128, 512), F32)
    Pbf = ctx.sb("a5Pb", (128, 512), BF16)
    osb = ctx.sb("a5os", (128, 512), BF16)
    pS = ctx.ps("a5pS", (128, 512), F32)
    pSum = ctx.ps("a5pSm", (1, 512), F32)
    pO = ctx.ps("a5pO", (128, 512), F32)
    ctx.op("gpsimd", "memset", onesr[:], 1.0)
    ctx.op("gpsimd", "memset", onesc[:], 1.0)
    ctx.op("sync", "dma_start", kmx[:], kmax_all[:, :])
    ctx.op("vector", "reduce_max", ksc[:], kmx[:], axis=AX.X)
    ctx.op("vector", "tensor_scalar_mul", ksc[:], ksc[:], 1.0 / 128.0)

    for h in range(32):
        if rank_major:
            for r in range(2):
                ctx.op("sync", "dma_start", kT_sb[:, r * 2048:(r + 1) * 2048],
                       kT[r, h, :, :])
                for kg in range(2):
                    ctx.op("sync", "dma_start",
                           v_sb[:, r * 2048 + kg * 1024:
                                r * 2048 + (kg + 1) * 1024]
                           .rearrange("t (kc d) -> t kc d", d=128),
                           vpl[r, h, kg * 1024:(kg + 1) * 1024, :]
                           .rearrange("(kc t) d -> t kc d", t=128))
        else:
            ctx.op("sync", "dma_start", kT_sb[:], kT[h, :, :])
            for kg in range(4):
                ctx.op("sync", "dma_start",
                       v_sb[:, kg * 1024:(kg + 1) * 1024]
                       .rearrange("t (kc d) -> t kc d", d=128),
                       vpl[h, kg * 1024:(kg + 1) * 1024, :]
                       .rearrange("(kc t) d -> t kc d", t=128))
        ctx.op("sync", "dma_start", qT_sb[:], qT[h, :, :])
        for g in range(4):
            gq = slice(g * 512, (g + 1) * 512)
            qTg = qT_sb[:, gq]
            ctx.op("scalar", "activation", sq_s[:], qTg, AF.Square)
            ctx.op("tensor", "matmul", pSum[:], onesc[:], sq_s[:],
                   start=True, stop=True, inc=tsem.tick(1))
            ctx.op("scalar", "activation", Brow[:], pSum[:], AF.Sqrt,
                   scale=ksc[:])
            ctx.op("vector", "tensor_scalar_mul", BoC[:], Brow[:], 1.0 / CSC)
            ctx.op("vector", "tensor_scalar_mul", negBoC[:], BoC[:], -1.0)
            nkc = 8 * g + 8
            for phase in range(2):
                rowv = negBoC if phase == 0 else negLoC
                for kc in range(nkc):
                    j0 = max(0, (kc - 8 * g - 1 + 1) // 2)
                    cols = slice(j0 * 128, 512)
                    ko = kcoff(kc)
                    ctx.op("tensor", "matmul", pS[:, cols],
                           kT_sb[:, ko:ko + 128],
                           qT_sb[:, g * 512 + j0 * 128:(g + 1) * 512],
                           start=True, stop=False)
                    ctx.op("tensor", "matmul", pS[:, cols], onesr[:],
                           rowv[0:1, cols], start=False, stop=True,
                           inc=tsem.tick(1))
                    if phase == 0:
                        ctx.op("scalar", "activation", P0[:, cols], pS[:, cols],
                               AF.Exp, scale=CSC)
                        for j in range(j0, 4):
                            lt = 4 * g + j
                            bsl = slice(j * 128, (j + 1) * 128)
                            if kc == 2 * lt:
                                ctx.op("vector", "tensor_mul", P0[:, bsl],
                                       P0[:, bsl], consts["c_dmA"][:])
                            elif kc == 2 * lt + 1:
                                ctx.op("vector", "tensor_mul", P0[:, bsl],
                                       P0[:, bsl], consts["c_dmB"][:])
                        ctx.op("tensor", "matmul", pSum[0:1, cols], onesc[:],
                               P0[:, cols], start=(kc == 0), stop=(kc == nkc - 1),
                               inc=(tsem.tick(1) if kc == nkc - 1 else None))
                    else:
                        ctx.op("scalar", "activation", Pbf[:, cols], pS[:, cols],
                               AF.Exp, scale=CSC)
                        for j in range(j0, 4):
                            lt = 4 * g + j
                            bsl = slice(j * 128, (j + 1) * 128)
                            if kc == 2 * lt:
                                ctx.op("vector", "tensor_mul", Pbf[:, bsl],
                                       Pbf[:, bsl], consts["c_dmAb"][:])
                            elif kc == 2 * lt + 1:
                                ctx.op("vector", "tensor_mul", Pbf[:, bsl],
                                       Pbf[:, bsl], consts["c_dmBb"][:])
                        ctx.op("tensor", "matmul", pO[:, cols],
                               v_sb[:, kcoff(kc):kcoff(kc) + 128],
                               Pbf[:, cols], start=(kc == 0), stop=(kc == nkc - 1),
                               inc=(tsem.tick(1) if kc == nkc - 1 else None))
                if phase == 0:
                    ctx.op("scalar", "activation", lnS[:], pSum[:], AF.Ln)
                    ctx.op("vector", "scalar_tensor_tensor", negLoC[:], lnS[:],
                           1.0 / CSC, BoC[:], op0=ALU.mult, op1=ALU.add)
                    ctx.op("vector", "tensor_scalar_mul", negLoC[:], negLoC[:],
                           -1.0)
            ctx.op("scalar", "activation", osb[:], pO[:], AF.Copy,
                   inc=sems["e"].tick(1))
            ctx.op("sync", "dma_start", oT_l1[h, :, gq], osb[:])
    a = ctx._autosem("sync", "d")
    ctx._need("sync", (a.h, a.n))
    ctx.pop_scope()


def build_L5(debug=False):
    nc = bass.Bass("TRN2", target_bir_lowering=False, debug=False)
    ctx = Ctx(nc)
    sems = make_sems(ctx)
    kind = "ExternalOutput" if debug else "Internal"
    x2_rows = nc.dram_tensor("x2_rows", [T_OWN, 4096], F32, kind="ExternalInput")
    qT = nc.dram_tensor("qT", [32, 128, T_OWN], BF16, kind="ExternalInput")
    kT = nc.dram_tensor("kT", [32, 128, 4096], BF16, kind="ExternalInput")
    vpl = nc.dram_tensor("vpl", [32, 4096, 128], BF16, kind="ExternalInput")
    kmax_all = nc.dram_tensor("kmax_all", [1, 8], F32, kind="ExternalInput")
    WoT = nc.dram_tensor("WoT", [4096, 4096], F32, kind="ExternalInput")
    bo_row = nc.dram_tensor("bo_row", [1, 4096], F32, kind="ExternalInput")
    Acat = nc.dram_tensor("Acat", [4096, 64], F32, kind="ExternalInput")
    Bcat = nc.dram_tensor("Bcat", [64, 4096], F32, kind="ExternalInput")
    WrT = nc.dram_tensor("WrT", [4096, 4], F32, kind="ExternalInput")
    oT_l1 = nc.dram_tensor("oT_l1", [32, 128, T_OWN], BF16, kind=kind)
    x3_rows = nc.dram_tensor("x3_rows", [T_OWN, 4096], F32, kind=kind)
    x4_out = nc.dram_tensor("x4_out", [T_OWN, 4096], F32, kind="ExternalOutput")
    with nc.Block() as block:
        consts = load_consts(ctx, L5_CONSTS, sems)
        emit_attn_l1(ctx, qT=qT, kT=kT, vpl=vpl, kmax_all=kmax_all,
                     oT_l1=oT_l1, sems=sems, consts=consts)
        ctx.push_scope()
        x2t = ctx.sb("l5x2", (128, 512), F32)
        out_sb = [ctx.sb("l5o%d" % i, (128, 512), F32) for i in range(2)]
        epar = [0]

        def epiO(tg, o0, n_o, psum, tval):
            par = epar[0]
            epar[0] = 1 - par
            rows = slice(tg * 128, (tg + 1) * 128)
            ctx.op("sync", "dma_start", x2t[:, 0:n_o], x2_rows[rows, o0:o0 + n_o])
            ctx.op("vector", "tensor_add", out_sb[par][:, 0:n_o],
                   psum[:, 0:n_o], x2t[:, 0:n_o], inc=sems["e"].tick(1))
            ctx.op("sync", "dma_start", x3_rows[rows, o0:o0 + n_o],
                   out_sb[par][:, 0:n_o])

        emit_proj(ctx, xT_planes=oT_l1, n_tok=T_OWN, w_dram=WoT, n_out=4096,
                  stats=None, sems=sems, consts=consts, bias_dram=bo_row,
                  epilogue=epiO, tag="L5o")
        ctx.pop_scope()
        stats = emit_stats(ctx, x3_rows, NT_OWN, sems, consts, tag="5")
        ctx.push_scope()
        acatb = ctx.sb("l5Ac", (128, 32 * 64), BF16)
        bcatb = ctx.sb("l5Bc", (64, 4096), BF16)
        wrtb = ctx.sb("l5Wr", (128, 32 * 4), BF16)
        onesc = ctx.sb("l5oc", (128, 1), BF16)
        wsA = ctx.sb("l5wA", (1, 64), BF16)
        wsR = ctx.sb("l5wR", (1, 4), BF16)
        x3t = ctx.sb("l5x3", (128, 4096), F32)
        x3b = ctx.sb("l5x3b", (128, 4096), BF16)
        x3T = ctx.sb("l5x3T", (128, 32 * 128), BF16)
        lgt = ctx.sb("l5lg", (128, 4), F32)
        mid = ctx.sb("l5md", (128, 64), F32)
        nm = ctx.sb("l5nm", (128, 1), F32)
        m2 = ctx.sb("l5m2", (128, 1), F32)
        eqm = ctx.sb("l5eq", (128, 4), F32)
        t4 = ctx.sb("l5t4", (128, 4), F32)
        gat = ctx.sb("l5ga", (128, 4), F32)
        sm = ctx.sb("l5sm", (128, 1), F32)
        rsc = ctx.sb("l5rs", (128, 1), F32)
        midg = ctx.sb("l5mg", (128, 64), BF16)
        mgT = ctx.sb("l5mT", (64, 128), BF16)
        osb2 = ctx.sb("l5ob", (128, 512), F32)
        pP = ctx.ps("l5pP", (128, 512), F32)
        pL = ctx.ps("l5pL", (128, 68), F32)
        pT = ctx.ps("l5pT", (128, 128), BF16)
        pW = ctx.ps("l5pW", (1, 68), F32)
        for fc in range(32):
            ctx.op("gpsimd", "dma_start", acatb[:, fc * 64:(fc + 1) * 64],
                   Acat[fc * 128:(fc + 1) * 128, :])
            ctx.op("gpsimd", "dma_start", wrtb[:, fc * 4:(fc + 1) * 4],
                   WrT[fc * 128:(fc + 1) * 128, :])
        ctx.op("gpsimd", "dma_start", bcatb[:], Bcat[:, :])
        ctx.op("gpsimd", "memset", onesc[:], 1.0)
        for fc in range(32):
            ctx.op("tensor", "matmul", pW[0:1, 0:64], onesc[:],
                   acatb[:, fc * 64:(fc + 1) * 64], start=(fc == 0), stop=False)
            ctx.op("tensor", "matmul", pW[0:1, 64:68], onesc[:],
                   wrtb[:, fc * 4:(fc + 1) * 4], start=False,
                   stop=(fc == 31), inc=(sems["t"].tick(1) if fc == 31 else None))
        ctx.op("vector", "tensor_copy", wsA[:], pW[0:1, 0:64])
        ctx.op("vector", "tensor_copy", wsR[:], pW[0:1, 64:68])

        for tg in range(NT_OWN):
            rows = slice(tg * 128, (tg + 1) * 128)
            ctx.op("sync", "dma_start", x3t[:], x3_rows[rows, :])
            ctx.op("vector", "tensor_copy", x3b[:], x3t[:])
            for fc in range(32):
                ctx.op("tensor", "transpose", pT[:],
                       x3b[:, fc * 128:(fc + 1) * 128], consts["c_idb"][:],
                       inc=sems["t"].tick(1))
                ctx.op("vector", "tensor_copy",
                       x3T[:, fc * 128:(fc + 1) * 128], pT[:])
            for fc in range(32):
                ctx.op("tensor", "matmul", pL[:, 0:64],
                       x3T[:, fc * 128:(fc + 1) * 128],
                       acatb[:, fc * 64:(fc + 1) * 64],
                       start=(fc == 0), stop=False)
                ctx.op("tensor", "matmul", pL[:, 64:68],
                       x3T[:, fc * 128:(fc + 1) * 128],
                       wrtb[:, fc * 4:(fc + 1) * 4], start=False, stop=False)
            ctx.op("tensor", "matmul", pL[:, 0:64],
                   stats["negm"][0:1, tg * 128:(tg + 1) * 128], wsA[:],
                   start=False, stop=False)
            ctx.op("tensor", "matmul", pL[:, 64:68],
                   stats["negm"][0:1, tg * 128:(tg + 1) * 128], wsR[:],
                   start=False, stop=True, inc=sems["t"].tick(1))
            ctx.op("scalar", "activation", mid[:], pL[:, 0:64], AF.Copy,
                   scale=stats["rs"][:, tg:tg + 1])
            ctx.op("scalar", "activation", lgt[:], pL[:, 64:68], AF.Copy,
                   scale=stats["rs"][:, tg:tg + 1])
            ctx.op("vector", "reduce_max", nm[:], lgt[:], axis=AX.X)
            ctx.op("vector", "tensor_scalar", eqm[:], lgt[:], nm[:], None,
                   op0=ALU.is_ge)
            ctx.op("vector", "scalar_tensor_tensor", t4[:], eqm[:], -1e30,
                   lgt[:], op0=ALU.mult, op1=ALU.add)
            ctx.op("vector", "reduce_max", m2[:], t4[:], axis=AX.X)
            ctx.op("vector", "tensor_scalar", eqm[:], lgt[:], m2[:], None,
                   op0=ALU.is_ge)
            ctx.op("vector", "tensor_scalar_mul", nm[:], nm[:], -1.0)
            ctx.op("scalar", "activation", gat[:], lgt[:], AF.Exp, bias=nm[:])
            ctx.op("vector", "tensor_mul", gat[:], gat[:], eqm[:])
            ctx.op("vector", "reduce_sum", sm[:], gat[:], axis=AX.X)
            ctx.op("vector", "reciprocal", rsc[:], sm[:])
            ctx.op("vector", "tensor_scalar_mul", gat[:], gat[:], rsc[:])
            for e in range(4):
                ctx.op("vector", "tensor_scalar", midg[:, e * 16:(e + 1) * 16],
                       mid[:, e * 16:(e + 1) * 16], gat[:, e:e + 1], 4.0,
                       op0=ALU.mult, op1=ALU.mult)
            ctx.op("tensor", "transpose", pT[0:64, 0:128], midg[:],
                   consts["c_idb"][:], inc=sems["t"].tick(1))
            ctx.op("vector", "tensor_copy", mgT[:], pT[0:64, 0:128])
            for oc in range(8):
                osl = slice(oc * 512, (oc + 1) * 512)
                ctx.op("tensor", "matmul", pP[:], mgT[:], bcatb[:, osl],
                       start=True, stop=True, inc=sems["t"].tick(1))
                ctx.op("vector", "tensor_add", osb2[:], pP[:], x3t[:, osl])
                ctx.op("sync", "dma_start", x4_out[rows, osl], osb2[:])
        a = ctx._autosem("sync", "d")
        ctx._need("sync", (a.h, a.n))
        ctx.pop_scope()
        ctx.finalize(block)
    ctx.st.close()
    return nc


def host_inputs_L5(inputs, core, x2_own, qkv_pair, kmax8):
    p = core % 2
    q_own, _, _ = qkv_pair[p]
    kT_full = np.zeros((32, 128, 4096), dtype=NBF)
    v_full = np.zeros((32, 4096, 128), dtype=NBF)
    for pp in range(2):
        own = own_rows(pp)
        _, k_pp, v_pp = qkv_pair[pp]
        for h in range(32):
            kT_full[h][:, own] = k_pp[:, h * 128:(h + 1) * 128].T
            v_full[h][own, :] = v_pp[:, h * 128:(h + 1) * 128]
    qTl = np.stack([np.ascontiguousarray(q_own[:, h * 128:(h + 1) * 128].T)
                    for h in range(32)])
    E, R = 4, 16
    Acat = inputs["L1_ffn_A"].transpose(2, 0, 1).reshape(4096, 64)
    Bcat = inputs["L1_ffn_B"].transpose(0, 2, 1).reshape(64, 4096)
    d = {"x2_rows": np.ascontiguousarray(x2_own),
         "qT": np.ascontiguousarray(qTl),
         "kT": np.ascontiguousarray(kT_full),
         "vpl": np.ascontiguousarray(v_full),
         "kmax_all": np.ascontiguousarray(np.array(kmax8, np.float32).reshape(1, 8)),
         "WoT": np.ascontiguousarray(inputs["L1_attn_Wo"].T),
         "bo_row": np.ascontiguousarray(inputs["L1_attn_bo"][None, :]),
         "Acat": np.ascontiguousarray(Acat),
         "Bcat": np.ascontiguousarray(Bcat),
         "WrT": np.ascontiguousarray(inputs["L1_ffn_Wr"].T)}
    d.update({k: v for k, v in host_consts(p).items() if k in L5_CONSTS})
    return d


_NC_CACHE = {}

def _get(name, fn):
    if name not in _NC_CACHE:
        _NC_CACHE[name] = fn()
    return _NC_CACHE[name]


def kernel(**inputs):
    return kernel_fused(**inputs)


def kernel_multilaunch(**inputs):
    inputs = {k: np.ascontiguousarray(np.asarray(v)) for k, v in inputs.items()}
    cores = list(range(NCORE))

    nc1 = _get("L1", build_L1)
    r1 = run_bass_kernel_spmd(nc1, [host_inputs_L1(inputs, c) for c in cores],
                              core_ids=cores)
    oT = [np.asarray(r1.results[c]["oT"]) for c in cores]

    nc2a = _get("L2a", build_L2a)
    r2a = run_bass_kernel_spmd(
        nc2a, [host_inputs_L2a(inputs, c, (oT[c // 2 * 2], oT[c // 2 * 2 + 1]))
               for c in cores], core_ids=cores)
    x1 = [np.asarray(r2a.results[c]["x1_rows"]) for c in cores]

    nc2b = _get("L2b", build_L2b)
    r2b = run_bass_kernel_spmd(
        nc2b, [host_inputs_L2b(inputs, c, x1[c]) for c in cores],
        core_ids=cores)
    scales8 = [np.asarray(r2b.results[c]["scales_loc"]) for c in cores]

    nc3 = _get("L3", build_L3)
    r3 = run_bass_kernel_spmd(
        nc3, [host_inputs_L3(inputs, c, x1[c],
                             np.asarray(r2b.results[c]["H_rows"]),
                             np.asarray(r2b.results[c]["z_rows"]),
                             np.asarray(r2b.results[c]["ds_rows"]), scales8)
              for c in cores], core_ids=cores)
    x2 = [np.asarray(r3.results[c]["x2_rows"]) for c in cores]

    nc4 = _get("L4", build_L4)
    r4 = run_bass_kernel_spmd(
        nc4, [host_inputs_L4(inputs, c, x2[c]) for c in cores],
        core_ids=cores)
    kmax8 = [float(np.asarray(r4.results[c]["kmax_loc"])[0, 0]) for c in cores]
    qkv = [(np.asarray(r4.results[c]["q_rows"]),
            np.asarray(r4.results[c]["k_rows"]),
            np.asarray(r4.results[c]["v_rows"])) for c in cores]

    nc5 = _get("L5", build_L5)
    r5 = run_bass_kernel_spmd(
        nc5, [host_inputs_L5(inputs, c, x2[c],
                             (qkv[c // 2 * 2], qkv[c // 2 * 2 + 1]), kmax8)
              for c in cores], core_ids=cores)

    out = np.empty((B, N, DM), np.float32)
    for c in cores:
        b, p = c // 2, c % 2
        out[b][own_rows(p)] = np.asarray(r5.results[c]["x4_out"])
    return out


def _emit_gating(ctx, sems, consts, x1_rows, H_rows, z_rows, ds_rows,
                 scales_all, LiM, Bm, x2_rows):
    sall = ctx.sb("g3sa", (1, 16), F32)
    s11 = ctx.sb("g3s1", (1, 2), F32)
    inv2 = ctx.sb("g3inv", (1, 2), F32)
    invc = ctx.sb("g3ic", (128, 2), F32)
    ones_r = ctx.sb("g3or", (1, 128), F32)
    limb = ctx.sb("g3lim", (8, 32), BF16)
    bmb = ctx.sb("g3bm", (32, 4096), BF16)
    pB = ctx.ps("g3pB", (128, 2), F32)
    ctx.op("sync", "dma_start", sall[:], scales_all[:, :])
    ctx.op("gpsimd", "dma_start", limb[:], LiM[:, :])
    ctx.op("gpsimd", "dma_start", bmb[:], Bm[:, :])
    ctx.op("gpsimd", "memset", ones_r[:], 1.0)
    s8x2 = ctx.sb("g3s8", (1, 16), F32)
    sv = sall.ap().rearrange("a (p two) -> a two p", two=2)
    ctx.op("vector", "tensor_copy",
           s8x2.ap().rearrange("a (two p) -> a two p", two=2), sv)
    ctx.op("vector", "reduce_max", s11[:, 0:1], s8x2[:, 0:8], axis=AX.X)
    ctx.op("vector", "reduce_max", s11[:, 1:2], s8x2[:, 8:16], axis=AX.X)
    ctx.op("vector", "tensor_scalar_max", s11[:], s11[:], 1e-6)
    ctx.op("vector", "reciprocal", inv2[:], s11[:])
    ctx.op("vector", "tensor_scalar_mul", inv2[:], inv2[:], 0.5)
    ctx.op("tensor", "matmul", pB[:], ones_r[:], inv2[:], start=True,
           stop=True, inc=sems["t"].tick(1))
    ctx.op("vector", "tensor_copy", invc[:], pB[:])

    Ht = ctx.sb("g3H", (128, 4096), F32)
    x1t = ctx.sb("g3x1", (128, 4096), F32)
    hs8 = ctx.sb("g3h8", (128, 8), F32)
    ds8 = ctx.sb("g3d8", (128, 8), F32)
    zt = ctx.sb("g3z", (128, 32), F32)
    lg = ctx.sb("g3lg", (128, 8), F32)
    t8 = ctx.sb("g3t8", (128, 8), F32)
    nm = ctx.sb("g3nm", (128, 1), F32)
    sm = ctx.sb("g3sm", (128, 1), F32)
    rsc = ctx.sb("g3rs", (128, 1), F32)
    probs = ctx.sb("g3pr", (128, 8), F32)
    eqm = ctx.sb("g3eq", (128, 8), F32)
    m2 = ctx.sb("g3m2", (128, 1), F32)
    sig = ctx.sb("g3sg", (128, 8), F32)
    wts = ctx.sb("g3wt", (128, 8), F32)
    wsh = ctx.sb("g3ws", (128, 8), F32)
    wshb = ctx.sb("g3wsb", (128, 8), BF16)
    wT = ctx.sb("g3wT", (8, 128), BF16)
    zpb = ctx.sb("g3zp", (128, 32), BF16)
    zpT = ctx.sb("g3zpT", (32, 128), BF16)
    osb = ctx.sb("g3ou", (128, 512), F32)
    pT8 = ctx.ps("g3pT", (8, 128), BF16)
    pM = ctx.ps("g3pM", (128, 32), F32)
    pT32 = ctx.ps("g3pT2", (32, 128), BF16)
    pDl = ctx.ps("g3pD", (128, 512), F32)
    SHUF = [i | 1 for i in range(32)]

    for tg in range(NT_OWN):
        rows = slice(tg * 128, (tg + 1) * 128)
        ctx.op("sync", "dma_start", hs8[:], H_rows[rows, 0:8])
        ctx.op("sync", "dma_start", ds8[:], ds_rows[rows, :])
        ctx.op("sync", "dma_start", zt[:], z_rows[rows, :])
        ctx.op("sync", "dma_start", x1t[:], x1_rows[rows, :])
        ctx.op("sync", "dma_start", Ht[:], H_rows[rows, :])
        ctx.op("vector", "tensor_scalar_mul", t8[:], ds8[:], invc[:, 1:2])
        ctx.op("vector", "scalar_tensor_tensor", lg[:], hs8[:],
               invc[:, 0:1], t8[:], op0=ALU.mult, op1=ALU.add)
        ctx.op("vector", "reduce_max", nm[:], lg[:], axis=AX.X, negate=True)
        ctx.op("scalar", "activation", probs[:], lg[:], AF.Exp,
               bias=nm[:], scale=1.0, accum_out=sm[:])
        ctx.op("vector", "reciprocal", rsc[:], sm[:])
        ctx.op("vector", "tensor_scalar_mul", probs[:], probs[:], rsc[:])
        ctx.op("vector", "reduce_max", nm[:], probs[:], axis=AX.X)
        ctx.op("vector", "tensor_scalar", eqm[:], probs[:], nm[:], None,
               op0=ALU.is_ge)
        ctx.op("vector", "scalar_tensor_tensor", t8[:], eqm[:], -1e30,
               probs[:], op0=ALU.mult, op1=ALU.add)
        ctx.op("vector", "reduce_max", m2[:], t8[:], axis=AX.X)
        ctx.op("vector", "tensor_scalar_mul", m2[:], m2[:], -2.0)
        ctx.op("scalar", "activation", sig[:], probs[:], AF.Sigmoid,
               bias=m2[:], scale=2.0)
        ctx.op("vector", "tensor_mul", wts[:], probs[:], sig[:])
        ctx.op("vector", "reduce_sum", sm[:], wts[:], axis=AX.X)
        ctx.op("vector", "tensor_scalar_add", sm[:], sm[:], 1e-9)
        ctx.op("vector", "reciprocal", rsc[:], sm[:])
        ctx.op("vector", "tensor_scalar_mul", wts[:], wts[:], rsc[:])
        ctx.op("vector", "stream_shuffle", wsh[:], wts[:], SHUF)
        ctx.op("vector", "tensor_copy", wshb[:], wsh[:])
        ctx.op("tensor", "transpose", pT8[:], wshb[:], consts["c_idb"][:],
               inc=sems["t"].tick(1))
        ctx.op("vector", "tensor_copy", wT[:], pT8[:])
        ctx.op("tensor", "matmul", pM[:], wT[:], limb[:], start=True,
               stop=True, inc=sems["t"].tick(1))
        ctx.op("vector", "tensor_mul", zpb[:], zt[:], pM[:])
        ctx.op("tensor", "transpose", pT32[:], zpb[:], consts["c_idb"][:],
               inc=sems["t"].tick(1))
        ctx.op("vector", "tensor_copy", zpT[:], pT32[:])
        for oc in range(8):
            osl = slice(oc * 512, (oc + 1) * 512)
            ctx.op("tensor", "matmul", pDl[:], zpT[:], bmb[:, osl],
                   start=True, stop=True, inc=sems["t"].tick(1))
            ctx.op("vector", "scalar_tensor_tensor", osb[:], pDl[:], 0.5,
                   Ht[:, osl], op0=ALU.mult, op1=ALU.add)
            ctx.op("vector", "tensor_add", osb[:], osb[:], x1t[:, osl])
            ctx.op("sync", "dma_start", x2_rows[rows, osl], osb[:])


def _emit_ffn1(ctx, sems, consts, stats, x3_rows, Acat, Bcat, WrT, x4_out):
    acatb = ctx.sb("d5Ac", (128, 32 * 64), BF16)
    bcatb = ctx.sb("d5Bc", (64, 4096), BF16)
    wrtb = ctx.sb("d5Wr", (128, 32 * 4), BF16)
    onesc = ctx.sb("d5oc", (128, 1), BF16)
    wsA = ctx.sb("d5wA", (1, 64), BF16)
    wsR = ctx.sb("d5wR", (1, 4), BF16)
    x3t = ctx.sb("d5x3", (128, 4096), F32)
    x3b = ctx.sb("d5x3b", (128, 4096), BF16)
    x3T = ctx.sb("d5x3T", (128, 32 * 128), BF16)
    lgt = ctx.sb("d5lg", (128, 4), F32)
    mid = ctx.sb("d5md", (128, 64), F32)
    nm = ctx.sb("d5nm", (128, 1), F32)
    m2 = ctx.sb("d5m2", (128, 1), F32)
    eqm = ctx.sb("d5eq", (128, 4), F32)
    t4 = ctx.sb("d5t4", (128, 4), F32)
    gat = ctx.sb("d5ga", (128, 4), F32)
    sm = ctx.sb("d5sm", (128, 1), F32)
    rsc = ctx.sb("d5rs", (128, 1), F32)
    midg = ctx.sb("d5mg", (128, 64), BF16)
    mgT = ctx.sb("d5mT", (64, 128), BF16)
    osb2 = ctx.sb("d5ob", (128, 512), F32)
    pP = ctx.ps("d5pP", (128, 512), F32)
    pL = ctx.ps("d5pL", (128, 68), F32)
    pT = ctx.ps("d5pT", (128, 128), BF16)
    pW = ctx.ps("d5pW", (1, 68), F32)
    for fc in range(32):
        ctx.op("gpsimd", "dma_start", acatb[:, fc * 64:(fc + 1) * 64],
               Acat[fc * 128:(fc + 1) * 128, :])
        ctx.op("gpsimd", "dma_start", wrtb[:, fc * 4:(fc + 1) * 4],
               WrT[fc * 128:(fc + 1) * 128, :])
    ctx.op("gpsimd", "dma_start", bcatb[:], Bcat[:, :])
    ctx.op("gpsimd", "memset", onesc[:], 1.0)
    for fc in range(32):
        ctx.op("tensor", "matmul", pW[0:1, 0:64], onesc[:],
               acatb[:, fc * 64:(fc + 1) * 64], start=(fc == 0), stop=False)
        ctx.op("tensor", "matmul", pW[0:1, 64:68], onesc[:],
               wrtb[:, fc * 4:(fc + 1) * 4], start=False,
               stop=(fc == 31), inc=(sems["t"].tick(1) if fc == 31 else None))
    ctx.op("vector", "tensor_copy", wsA[:], pW[0:1, 0:64])
    ctx.op("vector", "tensor_copy", wsR[:], pW[0:1, 64:68])

    for tg in range(NT_OWN):
        rows = slice(tg * 128, (tg + 1) * 128)
        ctx.op("sync", "dma_start", x3t[:], x3_rows[rows, :])
        ctx.op("vector", "tensor_copy", x3b[:], x3t[:])
        for fc in range(32):
            ctx.op("tensor", "transpose", pT[:],
                   x3b[:, fc * 128:(fc + 1) * 128], consts["c_idb"][:],
                   inc=sems["t"].tick(1))
            ctx.op("vector", "tensor_copy",
                   x3T[:, fc * 128:(fc + 1) * 128], pT[:])
        for fc in range(32):
            ctx.op("tensor", "matmul", pL[:, 0:64],
                   x3T[:, fc * 128:(fc + 1) * 128],
                   acatb[:, fc * 64:(fc + 1) * 64],
                   start=(fc == 0), stop=False)
            ctx.op("tensor", "matmul", pL[:, 64:68],
                   x3T[:, fc * 128:(fc + 1) * 128],
                   wrtb[:, fc * 4:(fc + 1) * 4], start=False, stop=False)
        ctx.op("tensor", "matmul", pL[:, 0:64],
               stats["negm"][0:1, tg * 128:(tg + 1) * 128], wsA[:],
               start=False, stop=False)
        ctx.op("tensor", "matmul", pL[:, 64:68],
               stats["negm"][0:1, tg * 128:(tg + 1) * 128], wsR[:],
               start=False, stop=True, inc=sems["t"].tick(1))
        ctx.op("scalar", "activation", mid[:], pL[:, 0:64], AF.Copy,
               scale=stats["rs"][:, tg:tg + 1])
        ctx.op("scalar", "activation", lgt[:], pL[:, 64:68], AF.Copy,
               scale=stats["rs"][:, tg:tg + 1])
        ctx.op("vector", "reduce_max", nm[:], lgt[:], axis=AX.X)
        ctx.op("vector", "tensor_scalar", eqm[:], lgt[:], nm[:], None,
               op0=ALU.is_ge)
        ctx.op("vector", "scalar_tensor_tensor", t4[:], eqm[:], -1e30,
               lgt[:], op0=ALU.mult, op1=ALU.add)
        ctx.op("vector", "reduce_max", m2[:], t4[:], axis=AX.X)
        ctx.op("vector", "tensor_scalar", eqm[:], lgt[:], m2[:], None,
               op0=ALU.is_ge)
        ctx.op("vector", "tensor_scalar_mul", nm[:], nm[:], -1.0)
        ctx.op("scalar", "activation", gat[:], lgt[:], AF.Exp, bias=nm[:])
        ctx.op("vector", "tensor_mul", gat[:], gat[:], eqm[:])
        ctx.op("vector", "reduce_sum", sm[:], gat[:], axis=AX.X)
        ctx.op("vector", "reciprocal", rsc[:], sm[:])
        ctx.op("vector", "tensor_scalar_mul", gat[:], gat[:], rsc[:])
        for e in range(4):
            ctx.op("vector", "tensor_scalar", midg[:, e * 16:(e + 1) * 16],
                   mid[:, e * 16:(e + 1) * 16], gat[:, e:e + 1], 4.0,
                   op0=ALU.mult, op1=ALU.mult)
        ctx.op("tensor", "transpose", pT[0:64, 0:128], midg[:],
               consts["c_idb"][:], inc=sems["t"].tick(1))
        ctx.op("vector", "tensor_copy", mgT[:], pT[0:64, 0:128])
        for oc in range(8):
            osl = slice(oc * 512, (oc + 1) * 512)
            ctx.op("tensor", "matmul", pP[:], mgT[:], bcatb[:, osl],
                   start=True, stop=True, inc=sems["t"].tick(1))
            ctx.op("vector", "tensor_add", osb2[:], pP[:], x3t[:, osl])
            ctx.op("sync", "dma_start", x4_out[rows, osl], osb2[:])


FUSED_CONSTS = ["c_idf", "c_idb", "c_U", "c_L", "c_cos", "c_sin", "c_swm",
                "c_swm0", "c_dmA", "c_dmB", "c_dmAb", "c_dmBb", "c_odd"]
RG_PAIR = [[0, 1], [2, 3], [4, 5], [6, 7]]
RG_ALL = [[0, 1, 2, 3, 4, 5, 6, 7]]


def emit_transpose_planes(ctx, rows_dram, planes_bf, n_tiles, sems, consts, tag):
    ctx.push_scope()
    xt = ctx.sb("tp_x" + tag, (128, 4096), F32)
    xb = ctx.sb("tp_b" + tag, (128, 4096), BF16)
    stg = ctx.sb("tp_s" + tag, (128, 4096), BF16)
    pT = ctx.ps("tp_p" + tag, (128, 128), BF16)
    idb = consts["c_idb"]
    for tg in range(n_tiles):
        ctx.op("sync", "dma_start", xt[:], rows_dram[tg * 128:(tg + 1) * 128, :])
        ctx.op("vector", "tensor_copy", xb[:], xt[:])
        for fc in range(32):
            ctx.op("tensor", "transpose", pT[:], xb[:, fc * 128:(fc + 1) * 128],
                   idb[:], inc=sems["t"].tick(1))
            ctx.op("vector", "tensor_copy", stg[:, fc * 128:(fc + 1) * 128],
                   pT[:])
        for g8 in range(4):
            ctx.op("gpsimd", "dma_start",
                   planes_bf[g8 * 8:(g8 + 1) * 8, :, tg * 128:(tg + 1) * 128]
                   .rearrange("fc d t -> d fc t"),
                   stg[:, g8 * 1024:(g8 + 1) * 1024]
                   .rearrange("d (fc t) -> d fc t", t=128))
    ctx.pop_scope()


def build_fused(upto="full"):
    nc = bass.Bass("TRN2", target_bir_lowering=False, debug=False)
    ctx = Ctx(nc)
    sems = make_sems(ctx)
    EI = "ExternalInput"
    x0_rows = nc.dram_tensor("x0_rows", [4096, 4096], F32, kind=EI)
    x0Tp = nc.dram_tensor("x0Tp", [32, 128, 4096], F32, kind=EI)
    x0_own = nc.dram_tensor("x0_own", [T_OWN, 4096], F32, kind=EI)
    wqkv = nc.dram_tensor("wqkv", [4096, 3072], F32, kind=EI)
    Wo_sl = nc.dram_tensor("Wo_sl", [2048, 4096], F32, kind=EI)
    WT = nc.dram_tensor("WT", [4096, 4096], F32, kind=EI)
    b_row = nc.dram_tensor("b_row", [1, 4096], F32, kind=EI)
    A4 = nc.dram_tensor("A4", [4096, 32], F32, kind=EI)
    LiT = nc.dram_tensor("LiT", [32, 8], F32, kind=EI)
    LiM = nc.dram_tensor("LiM", [8, 32], F32, kind=EI)
    Bm = nc.dram_tensor("Bm", [32, 4096], F32, kind=EI)
    Wcat = nc.dram_tensor("Wcat", [4096, 12288], F32, kind=EI)
    bcat = nc.dram_tensor("bcat", [1, 12288], F32, kind=EI)
    WoT = nc.dram_tensor("WoT", [4096, 4096], F32, kind=EI)
    bo_row = nc.dram_tensor("bo_row", [1, 4096], F32, kind=EI)
    Acat = nc.dram_tensor("Acat", [4096, 64], F32, kind=EI)
    Bcat = nc.dram_tensor("Bcat", [64, 4096], F32, kind=EI)
    WrT = nc.dram_tensor("WrT", [4096, 4], F32, kind=EI)
    qf = nc.dram_tensor("qf", [4096, 2048], F32)
    kf = nc.dram_tensor("kf", [4096, 512], F32)
    vf = nc.dram_tensor("vf", [4096, 512], F32)
    oT_send = nc.dram_tensor("oT_send", [2, 16, 128, T_OWN], BF16)
    rs_in = nc.dram_tensor("rs_in", [4096, 4096], BF16)
    rs_out = nc.dram_tensor("rs_out", [T_OWN, 4096], BF16)
    x1_rows = nc.dram_tensor("x1_rows", [T_OWN, 4096], F32)
    x1Tp = nc.dram_tensor("x1Tp", [32, 128, T_OWN], BF16)
    H_rows = nc.dram_tensor("H_rows", [T_OWN, 4096], F32)
    z_rows = nc.dram_tensor("z_rows", [T_OWN, 32], F32)
    ds_rows = nc.dram_tensor("ds_rows", [T_OWN, 8], F32)
    scales_loc = nc.dram_tensor("scales_loc", [1, 2], F32)
    scales_all = nc.dram_tensor("scales_all", [1, 16], F32)
    x2_rows = nc.dram_tensor("x2_rows", [T_OWN, 4096], F32)
    x2Tp = nc.dram_tensor("x2Tp", [32, 128, T_OWN], BF16)
    qT_loc = nc.dram_tensor("qT_loc", [32, 128, T_OWN], BF16)
    kT_loc = nc.dram_tensor("kT_loc", [32, 128, T_OWN], BF16)
    v_loc = nc.dram_tensor("v_loc", [32, T_OWN, 128], BF16)
    kT_rm = nc.dram_tensor("kT_rm", [2, 32, 128, T_OWN], BF16)
    v_rm = nc.dram_tensor("v_rm", [2, 32, T_OWN, 128], BF16)
    kmax_loc = nc.dram_tensor("kmax_loc", [1, 1], F32)
    kmax_all = nc.dram_tensor("kmax_all", [1, 8], F32)
    oT_l1 = nc.dram_tensor("oT_l1", [32, 128, T_OWN], BF16)
    x3_rows = nc.dram_tensor("x3_rows", [T_OWN, 4096], F32)
    x4_out = nc.dram_tensor("x4_out", [T_OWN, 4096], F32, kind="ExternalOutput")

    def _stop_here(ctx_, name, src_dram, shape):
        t = ctx_.sb("dbgcp" + name, (128, 4096), F32)
        for tg in range(NT_OWN):
            rows = slice(tg * 128, (tg + 1) * 128)
            ctx_.op("sync", "dma_start", t[:], src_dram[rows, :])
            ctx_.op("sync", "dma_start", x4_out[rows, :], t[:])
        a = ctx_._autosem("sync", "d")
        ctx_._need("sync", (a.h, a.n))

    with nc.Block() as block:
        consts = load_consts(ctx, FUSED_CONSTS, sems)

        stats0 = emit_stats(ctx, x0_rows, 32, sems, consts, tag="f0")
        ctx.push_scope()
        out_sb = [ctx.sb("f1o%d" % i, (128, 512), F32) for i in range(2)]
        epar = [0]

        def epi1(tg, o0, n_o, psum, tval):
            par = epar[0]
            epar[0] = 1 - par
            ctx.op("scalar", "activation", out_sb[par][:, 0:n_o],
                   psum[:, 0:n_o], AF.Copy, scale=stats0["rs"][:, tg:tg + 1],
                   inc=sems["e"].tick(1))
            if o0 < 2048:
                dest = qf[tg * 128:(tg + 1) * 128, o0:o0 + n_o]
            elif o0 < 2560:
                dest = kf[tg * 128:(tg + 1) * 128, 0:n_o]
            else:
                dest = vf[tg * 128:(tg + 1) * 128, 0:n_o]
            ctx.op("sync", "dma_start", dest, out_sb[par][:, 0:n_o])

        emit_proj(ctx, xT_planes=x0Tp, n_tok=4096, w_dram=wqkv, n_out=3072,
                  stats=stats0, sems=sems, consts=consts, epilogue=epi1,
                  tag="f1")
        ctx.pop_scope()

        def oT_writer(c, oT_stage):
            r, i = c & 1, c >> 1
            for g in range(2):
                ctx.op("gpsimd", "dma_start",
                       oT_send[r, g * 8:(g + 1) * 8, :, i * 128:(i + 1) * 128]
                       .rearrange("h d t -> d h t"),
                       oT_stage[:, g * 1024:(g + 1) * 1024]
                       .rearrange("d (h t) -> d h t", t=128))

        emit_mixer(ctx, qf=qf, kf=kf, vf=vf, oT_out=None, sems=sems,
                   consts=consts, tag="f", oT_writer=oT_writer)

        ctx.push_scope()
        out2 = [ctx.sb("f2o%d" % i, (128, 512), BF16) for i in range(2)]
        epar2 = [0]

        def epi2(tg, o0, n_o, psum, tval):
            par = epar2[0]
            epar2[0] = 1 - par
            rows = slice(tg * 128, (tg + 1) * 128)
            ctx.op("scalar", "activation", out2[par][:, 0:n_o],
                   psum[:, 0:n_o], AF.Copy, inc=sems["e"].tick(1))
            ctx.op("sync", "dma_start", rs_in[rows, o0:o0 + n_o],
                   out2[par][:, 0:n_o])

        def oT_loader(c2, xp, tb):
            hf, toff = divmod(tb * 512, 2048)
            for g in range(2):
                f0 = g * 8
                c2.op("gpsimd", "dma_start",
                      xp[:, f0 * 512:(f0 + 8) * 512]
                      .rearrange("p (fc t) -> p fc t", t=512),
                      oT_send[hf, f0:f0 + 8, :, toff:toff + 512]
                      .rearrange("fc p t -> p fc t"))

        emit_proj(ctx, xT_planes=None, n_tok=4096, w_dram=Wo_sl, n_out=4096,
                  stats=None, sems=sems, consts=consts, epilogue=epi2,
                  tag="f2", nfc=16, x_loader=oT_loader)
        ctx.pop_scope()
        ctx.op("gpsimd", "collective_compute", "ReduceScatter", ALU.add,
               replica_groups=RG_PAIR, ins=[rs_in.ap().opt()],
               outs=[rs_out.ap().opt()])
        ctx.push_scope()
        x0t = ctx.sb("f2x0", (128, 4096), F32)
        rst = ctx.sb("f2rs", (128, 4096), BF16)
        x1s = ctx.sb("f2x1", (128, 4096), F32)
        for tg in range(NT_OWN):
            rows = slice(tg * 128, (tg + 1) * 128)
            ctx.op("sync", "dma_start", x0t[:], x0_own[rows, :])
            ctx.op("sync", "dma_start", rst[:], rs_out[rows, :])
            ctx.op("vector", "tensor_add", x1s[:], x0t[:], rst[:])
            ctx.op("sync", "dma_start", x1_rows[rows, :], x1s[:])
        ctx.pop_scope()

        if upto == "x1":
            _stop_here(ctx, "x1", x1_rows, None)
            ctx.finalize(block)
            ctx.st.close()
            return nc

        emit_transpose_planes(ctx, x1_rows, x1Tp, NT_OWN, sems, consts, "f1T")
        statsB = emit_stats(ctx, x1_rows, NT_OWN, sems, consts, tag="fB")
        ctx.push_scope()
        hmax = ctx.sb("fbhm", (128, 1), F32)
        dmax = ctx.sb("fbdm", (128, 1), F32)
        tmp8 = ctx.sb("fbt8", (128, 8), F32)
        m8 = ctx.sb("fbm8", (128, 1), F32)
        ctx.op("gpsimd", "memset", hmax[:], 0.0)
        ctx.op("gpsimd", "memset", dmax[:], 0.0)
        outH = [ctx.sb("fbo%d" % i, (128, 512), F32) for i in range(2)]
        eparH = [0]

        def epiH(tg, o0, n_o, psum, tval):
            par = eparH[0]
            eparH[0] = 1 - par
            rows = slice(tg * 128, (tg + 1) * 128)
            ctx.op("scalar", "activation", outH[par][:, 0:n_o],
                   psum[:, 0:n_o], AF.Copy, scale=statsB["rs"][:, tg:tg + 1],
                   inc=sems["e"].tick(1))
            ctx.op("sync", "dma_start", H_rows[rows, o0:o0 + n_o],
                   outH[par][:, 0:n_o])
            if o0 == 0:
                ctx.op("vector", "tensor_scalar_mul", tmp8[:],
                       outH[par][:, 0:8], consts["c_odd"][:])
                ctx.op("vector", "reduce_max", m8[:], tmp8[:], axis=AX.X,
                       apply_absolute_value=True)
                ctx.op("vector", "tensor_max", hmax[:], hmax[:], m8[:])

        emit_proj(ctx, xT_planes=x1Tp, n_tok=T_OWN, w_dram=WT, n_out=4096,
                  stats=statsB, sems=sems, consts=consts, bias_dram=b_row,
                  epilogue=epiH, tag="fbH")
        z_sb = ctx.sb("fbz", (128, 32), F32)

        def epiZ(tg, o0, n_o, psum, tval):
            rows = slice(tg * 128, (tg + 1) * 128)
            ctx.op("scalar", "activation", z_sb[:, 0:n_o], psum[:, 0:n_o],
                   AF.Copy, scale=statsB["rs"][:, tg:tg + 1],
                   inc=sems["e"].tick(1))
            ctx.op("sync", "dma_start", z_rows[rows, 0:n_o], z_sb[:, 0:n_o])

        emit_proj(ctx, xT_planes=x1Tp, n_tok=T_OWN, w_dram=A4, n_out=32,
                  stats=statsB, sems=sems, consts=consts, epilogue=epiZ,
                  tag="fbz")
        zt = ctx.sb("fbzt", (128, 32), F32)
        ztb = ctx.sb("fbztb", (128, 32), BF16)
        zTb = ctx.sb("fbzT", (32, 128), BF16)
        litb = ctx.sb("fblit", (32, 8), BF16)
        ds_sb = ctx.sb("fbds", (128, 8), F32)
        pTz = ctx.ps("fbpT", (32, 128), BF16)
        pDz = ctx.ps("fbpD", (128, 8), F32)
        ctx.op("gpsimd", "dma_start", litb[:], LiT[:, :])
        for tg in range(NT_OWN):
            rows = slice(tg * 128, (tg + 1) * 128)
            ctx.op("sync", "dma_start", zt[:], z_rows[rows, :])
            ctx.op("vector", "tensor_copy", ztb[:], zt[:])
            ctx.op("tensor", "transpose", pTz[:], ztb[:], consts["c_idb"][:],
                   inc=sems["t"].tick(1))
            ctx.op("vector", "tensor_copy", zTb[:], pTz[:])
            ctx.op("tensor", "matmul", pDz[:], zTb[:], litb[:], start=True,
                   stop=True, inc=sems["t"].tick(1))
            ctx.op("vector", "tensor_copy", ds_sb[:], pDz[:])
            ctx.op("sync", "dma_start", ds_rows[rows, :], ds_sb[:])
            ctx.op("vector", "tensor_scalar_mul", tmp8[:], ds_sb[:],
                   consts["c_odd"][:])
            ctx.op("vector", "reduce_max", m8[:], tmp8[:], axis=AX.X,
                   apply_absolute_value=True)
            ctx.op("vector", "tensor_max", dmax[:], dmax[:], m8[:])
        pRs = ctx.ps("fbpR", (1, 128), F32)
        srow = ctx.sb("fbsr", (1, 128), F32)
        sc2 = ctx.sb("fbsc", (1, 2), F32)
        for i, acc in enumerate((hmax, dmax)):
            ctx.op("tensor", "transpose", pRs[:], acc[:], consts["c_idf"][:],
                   inc=sems["t"].tick(1))
            ctx.op("vector", "tensor_copy", srow[:], pRs[:])
            ctx.op("vector", "reduce_max", sc2[:, i:i + 1], srow[:], axis=AX.X)
        ctx.op("sync", "dma_start", scales_loc[:, :], sc2[:])
        ctx.pop_scope()

        ctx.op("gpsimd", "collective_compute", "AllGather", ALU.bypass,
               replica_groups=RG_ALL, ins=[scales_loc.ap().opt()],
               outs=[scales_all.ap().opt()])

        ctx.push_scope()
        _emit_gating(ctx, sems, consts, x1_rows, H_rows, z_rows, ds_rows,
                     scales_all, LiM, Bm, x2_rows)
        ctx.pop_scope()

        if upto == "x2":
            _stop_here(ctx, "x2", x2_rows, None)
            ctx.finalize(block)
            ctx.st.close()
            return nc

        emit_transpose_planes(ctx, x2_rows, x2Tp, NT_OWN, sems, consts, "f2T")
        statsC = emit_stats(ctx, x2_rows, NT_OWN, sems, consts, tag="fC")
        ctx.push_scope()
        outq = [ctx.sb("fco%d" % i, (128, 512), BF16) for i in range(2)]
        stgq = ctx.sb("fcst", (128, 128), BF16)
        sqs = ctx.sb("fcsq", (128, 128), F32)
        kn = ctx.sb("fckn", (128, 1), F32)
        knacc = ctx.sb("fcka", (128, 1), F32)
        pTq = ctx.ps("fcpT", (128, 128), BF16)
        ctx.op("gpsimd", "memset", knacc[:], 0.0)
        eparq = [0]

        def epiQ(tg, o0, n_o, psum, tval):
            par = eparq[0]
            eparq[0] = 1 - par
            rows = slice(tg * 128, (tg + 1) * 128)
            ctx.op("scalar", "activation", outq[par][:, 0:n_o],
                   psum[:, 0:n_o], AF.Copy, scale=statsC["rs"][:, tg:tg + 1],
                   inc=sems["e"].tick(1))
            which, off = divmod(o0, 4096)
            for j in range(4):
                h = off // 128 + j
                sl = slice(j * 128, (j + 1) * 128)
                if which < 2:
                    dst = (qT_loc, kT_loc)[which]
                    ctx.op("tensor", "transpose", pTq[:], outq[par][:, sl],
                           consts["c_idb"][:], inc=sems["t"].tick(1))
                    ctx.op("vector", "tensor_copy", stgq[:], pTq[:])
                    ctx.op("sync", "dma_start",
                           dst[h, :, tg * 128:(tg + 1) * 128], stgq[:])
                else:
                    ctx.op("sync", "dma_start",
                           v_loc[h, tg * 128:(tg + 1) * 128, :],
                           outq[par][:, sl])
                if which == 1:
                    ctx.op("scalar", "activation", sqs[:], outq[par][:, sl],
                           AF.Square, accum_out=kn[:])
                    ctx.op("vector", "tensor_max", knacc[:], knacc[:], kn[:])

        emit_proj(ctx, xT_planes=x2Tp, n_tok=T_OWN, w_dram=Wcat, n_out=12288,
                  stats=statsC, sems=sems, consts=consts, bias_dram=bcat,
                  epilogue=epiQ, tag="fc")
        pRk = ctx.ps("fcpR", (1, 128), F32)
        srk = ctx.sb("fcsr", (1, 128), F32)
        km1 = ctx.sb("fckm", (1, 1), F32)
        ctx.op("tensor", "transpose", pRk[:], knacc[:], consts["c_idf"][:],
               inc=sems["t"].tick(1))
        ctx.op("vector", "tensor_copy", srk[:], pRk[:])
        ctx.op("vector", "reduce_max", km1[:], srk[:], axis=AX.X)
        ctx.op("sync", "dma_start", kmax_loc[:, :], km1[:])
        ctx.pop_scope()

        ctx.op("gpsimd", "collective_compute", "AllGather", ALU.bypass,
               replica_groups=RG_ALL, ins=[kmax_loc.ap().opt()],
               outs=[kmax_all.ap().opt()])
        ctx.op("gpsimd", "collective_compute", "AllGather", ALU.bypass,
               replica_groups=RG_PAIR, ins=[kT_loc.ap().opt()],
               outs=[kT_rm.ap().opt()])
        ctx.op("gpsimd", "collective_compute", "AllGather", ALU.bypass,
               replica_groups=RG_PAIR, ins=[v_loc.ap().opt()],
               outs=[v_rm.ap().opt()])

        if upto == "qkv":
            _stop_here(ctx, "qkv", x2_rows, None)
            ctx.finalize(block)
            ctx.st.close()
            return nc

        emit_attn_l1(ctx, qT=qT_loc, kT=kT_rm, vpl=v_rm, kmax_all=kmax_all,
                     oT_l1=oT_l1, sems=sems, consts=consts, rank_major=True)

        if upto == "attn":
            _stop_here(ctx, "attn", x2_rows, None)
            ctx.finalize(block)
            ctx.st.close()
            return nc

        ctx.push_scope()
        x2t = ctx.sb("f5x2", (128, 512), F32)
        out5 = [ctx.sb("f5o%d" % i, (128, 512), F32) for i in range(2)]
        epar5 = [0]

        def epi5(tg, o0, n_o, psum, tval):
            par = epar5[0]
            epar5[0] = 1 - par
            rows = slice(tg * 128, (tg + 1) * 128)
            ctx.op("sync", "dma_start", x2t[:, 0:n_o], x2_rows[rows, o0:o0 + n_o])
            ctx.op("vector", "tensor_add", out5[par][:, 0:n_o],
                   psum[:, 0:n_o], x2t[:, 0:n_o], inc=sems["e"].tick(1))
            ctx.op("sync", "dma_start", x3_rows[rows, o0:o0 + n_o],
                   out5[par][:, 0:n_o])

        emit_proj(ctx, xT_planes=oT_l1, n_tok=T_OWN, w_dram=WoT, n_out=4096,
                  stats=None, sems=sems, consts=consts, bias_dram=bo_row,
                  epilogue=epi5, tag="f5o")
        ctx.pop_scope()

        if upto == "x3":
            _stop_here(ctx, "x3", x3_rows, None)
            ctx.finalize(block)
            ctx.st.close()
            return nc

        statsD = emit_stats(ctx, x3_rows, NT_OWN, sems, consts, tag="fD")
        ctx.push_scope()
        _emit_ffn1(ctx, sems, consts, statsD, x3_rows, Acat, Bcat, WrT, x4_out)
        a = ctx._autosem("sync", "d")
        ctx._need("sync", (a.h, a.n))
        ctx.pop_scope()
        ctx.finalize(block)
    ctx.st.close()
    return nc


def host_inputs_fused(inputs, core):
    b, p = core // 2, core % 2
    x0 = np.ascontiguousarray(inputs["L0_attn_hidden_states"][b])
    hs = slice(p * 2048, (p + 1) * 2048)
    ks = slice(p * 512, (p + 1) * 512)
    wqkv = np.concatenate([inputs["L0_attn_Wq"][:, hs],
                           inputs["L0_attn_Wk"][:, ks],
                           inputs["L0_attn_Wv"][:, ks]], axis=1)
    Wcat = np.concatenate([inputs["L1_attn_Wq"].T, inputs["L1_attn_Wk"].T,
                           inputs["L1_attn_Wv"].T], axis=1)
    bcat = np.concatenate([inputs["L1_attn_bq"], inputs["L1_attn_bk"],
                           inputs["L1_attn_bv"]])[None, :]
    d = {"x0_rows": x0,
         "x0Tp": np.ascontiguousarray(x0.T.reshape(32, 128, 4096)),
         "x0_own": np.ascontiguousarray(x0[own_rows(p)]),
         "wqkv": np.ascontiguousarray(wqkv),
         "Wo_sl": np.ascontiguousarray(
             inputs["L0_attn_Wo"][p * 2048:(p + 1) * 2048, :]),
         "WT": np.ascontiguousarray(inputs["L0_ffn_W"].T),
         "b_row": np.ascontiguousarray(inputs["L0_ffn_b"][None, :]),
         "A4": inputs["L0_ffn_A"],
         "LiT": np.ascontiguousarray(inputs["L0_ffn_LiMEs"].T),
         "LiM": inputs["L0_ffn_LiMEs"],
         "Bm": inputs["L0_ffn_Bmat"],
         "Wcat": np.ascontiguousarray(Wcat),
         "bcat": np.ascontiguousarray(bcat),
         "WoT": np.ascontiguousarray(inputs["L1_attn_Wo"].T),
         "bo_row": np.ascontiguousarray(inputs["L1_attn_bo"][None, :]),
         "Acat": np.ascontiguousarray(
             inputs["L1_ffn_A"].transpose(2, 0, 1).reshape(4096, 64)),
         "Bcat": np.ascontiguousarray(
             inputs["L1_ffn_B"].transpose(0, 2, 1).reshape(64, 4096)),
         "WrT": np.ascontiguousarray(inputs["L1_ffn_Wr"].T)}
    d.update({k: v for k, v in host_consts(p).items() if k in FUSED_CONSTS})
    return d


def kernel_fused(**inputs):
    inputs = {k: np.ascontiguousarray(np.asarray(v)) for k, v in inputs.items()}
    cores = list(range(NCORE))
    nc = _get("FUSED", build_fused)
    res = run_bass_kernel_spmd(nc, [host_inputs_fused(inputs, c)
                                    for c in cores], core_ids=cores)
    out = np.empty((B, N, DM), np.float32)
    for c in cores:
        out[c // 2][own_rows(c % 2)] = np.asarray(res.results[c]["x4_out"])
    return out


# version 04
# speedup vs baseline: 1.1291x; 1.1291x over previous
import numpy as np
import ml_dtypes
from contextlib import ExitStack
import concourse.bass as bass
import concourse.mybir as mybir
from concourse.bass_utils import run_bass_kernel_spmd

F32 = mybir.dt.float32
BF16 = mybir.dt.bfloat16
AF = mybir.ActivationFunctionType
ALU = mybir.AluOpType
AX = mybir.AxisListType
NBF = ml_dtypes.bfloat16

NCORE = 8
B, N, DM = 4, 4096, 4096
NT_OWN = 16
T_OWN = NT_OWN * 128
HD = 128
EPS_LN = 1e-5
CSC = float(1.0 / np.sqrt(128.0))

ENGINES = ("tensor", "vector", "scalar", "gpsimd", "sync")


def host_consts(parity):
    c = {}
    c["c_idf"] = np.eye(128, dtype=np.float32)
    c["c_idb"] = np.eye(128, dtype=np.float32).astype(NBF)
    c["c_U"] = np.triu(np.ones((128, 128), np.float32))
    c["c_L"] = np.tril(np.ones((128, 128), np.float32), -1)
    pos = np.arange(N, dtype=np.float64)
    half = np.arange(64, dtype=np.float64)
    inv = 1.0 / (10000.0 ** (2.0 * half / 128.0))
    fr = pos[:, None] * inv[None, :]
    emb = np.concatenate([fr, fr], -1)
    cos = np.cos(emb).astype(np.float32).reshape(32, 128, 128)
    sin = np.sin(emb).astype(np.float32).reshape(32, 128, 128)
    c["c_cos"] = np.ascontiguousarray(cos.transpose(1, 0, 2).reshape(128, 4096))
    c["c_sin"] = np.ascontiguousarray(sin.transpose(1, 0, 2).reshape(128, 4096))
    qi = np.arange(128)[:, None]
    ji = np.arange(256)[None, :]
    ok = (ji >= qi + 64) & (ji <= qi + 128)
    c["c_swm"] = np.where(ok, 0.0, -1e9).astype(np.float32)
    c["c_swm0"] = np.where(ok & (ji >= 128), 0.0, -1e9).astype(np.float32)
    tri_kq = (np.arange(128)[:, None] <= np.arange(128)[None, :])
    if parity == 0:
        mA, mB2 = tri_kq.astype(np.float32), np.zeros((128, 128), np.float32)
    else:
        mA, mB2 = np.ones((128, 128), np.float32), tri_kq.astype(np.float32)
    c["c_dmA"], c["c_dmB"] = mA, mB2
    c["c_dmAb"], c["c_dmBb"] = mA.astype(NBF), mB2.astype(NBF)
    c["c_odd"] = np.ascontiguousarray(
        (np.arange(128) % 2 == 1).astype(np.float32)[:, None])
    return c

CONST_SPECS = {
    "c_idf": ((128, 128), F32), "c_idb": ((128, 128), BF16),
    "c_U": ((128, 128), F32), "c_L": ((128, 128), F32),
    "c_cos": ((128, 4096), F32), "c_sin": ((128, 4096), F32),
    "c_swm": ((128, 256), F32), "c_swm0": ((128, 256), F32),
    "c_dmA": ((128, 128), F32), "c_dmB": ((128, 128), F32),
    "c_dmAb": ((128, 128), BF16), "c_dmBb": ((128, 128), BF16),
    "c_odd": ((128, 1), F32),
}


MAX_SEM_TRACK = {}

class Sem:
    def __init__(self, ctx, name):
        self.name = name
        self.h = ctx.st.enter_context(ctx.nc.semaphore(name))
        self.n = 0
        _SEM_VALUES[id(self.h)] = 0
    def tick(self, k):
        self.n += k
        _SEM_VALUES[id(self.h)] = self.n
        MAX_SEM_TRACK[self.name if hasattr(self, "name") else id(self)] = self.n
        return (self.h, k)

_KW_WRITE = {"accum_out"}
_KW_READ = {"bias", "scale", "alpha"}

def _ap_name(a):
    if isinstance(a, bass.AP):
        return a.tensor.name
    if hasattr(a, "ap") and hasattr(a, "dtype") and hasattr(a, "name"):
        return a.name
    return None


class Ctx:
    def __init__(self, nc):
        self.nc = nc
        self.st = ExitStack()
        self.scopes = [self.st]
        self.q = {e: [] for e in ENGINES}
        self.auto = {}
        self.qst = {}
        self.last_w = {}
        self.readers = {}
        self.known = {e: {} for e in ENGINES}

    def _autosem(self, eng, kind="c"):
        key = (eng, kind)
        if key not in self.auto:
            self.auto[key] = Sem(self, "auto_%s_%s" % (kind, eng))
        return self.auto[key]

    def push_scope(self):
        self.scopes.append(ExitStack())

    def pop_scope(self):
        if not hasattr(self, "_barsem"):
            self._barsem = Sem(self, "scopebar")
        self.barrier(self._barsem)
        self.scopes.pop().close()

    def sem(self, name):
        return Sem(self, name)

    def sb(self, name, shape, dt, persist=False):
        stk = self.scopes[0] if persist else self.scopes[-1]
        return stk.enter_context(self.nc.sbuf_tensor(name, list(shape), dt))

    def ps(self, name, shape, dt=F32):
        return self.scopes[-1].enter_context(
            self.nc.psum_tensor(name, list(shape), dt))

    def _need(self, eng, dep):
        if dep is None:
            return
        sem, val = dep
        if self.known[eng].get(id(sem), -1) >= val:
            return
        self.known[eng][id(sem)] = val
        self.q[eng].append(("wait", sem, val))

    def wait(self, eng, sem, val=None):
        v = sem.n if val is None else val
        self._need(eng, (sem.h, v))

    def op(self, eng, meth, *args, inc=None, **kw):
        writes, reads = [], []
        if meth == "collective_compute":
            for a in kw.get("ins", []):
                reads.append(_ap_name(a))
            for a in kw.get("outs", []):
                writes.append(_ap_name(a))
        else:
            first = True
            for a in args:
                nm = _ap_name(a)
                if nm is not None:
                    (writes if first else reads).append(nm)
                    first = False
                elif first and meth in ("dma_start",):
                    pass
            for k, a in kw.items():
                nm = _ap_name(a)
                if nm is None:
                    continue
                (writes if k in _KW_WRITE else reads).append(nm)
        for nm in reads:
            self._dep(eng, self.last_w.get(nm))
        for nm in writes:
            self._dep(eng, self.last_w.get(nm))
            for dep in self.readers.get(nm, ()):
                self._dep(eng, dep)
        async_op = meth in ("dma_start", "collective_compute")
        if async_op:
            kind = "cc" if meth == "collective_compute" else "d"
            s = self._autosem(eng, kind)
            k = 1 if meth == "collective_compute" else 16
            s.tick(k)
            inc2 = (s.h, None) if meth == "collective_compute" else (s.h, k)
            st = self._qstate(eng, kind)
            st["issued"] += 1
            stamp = ("pend", (eng, kind), st["issued"])
        else:
            if inc is None:
                s = self._autosem(eng)
                s.tick(1)
                inc2 = (s.h, 1)
                stamp = (s.h, s.n, eng)
            else:
                inc2 = inc
                stamp = (inc2[0], self._manual_val(inc2[0]), eng)
        for nm in writes:
            self.last_w[nm] = stamp
            self.readers[nm] = []
        for nm in reads:
            self.readers.setdefault(nm, []).append(stamp)
        self.q[eng].append(("op", meth, args, kw, inc2))

    def _qstate(self, eng, kind):
        key = (eng, kind)
        if key not in self.qst:
            self.qst[key] = {"issued": 0, "confirms": [],
                             "ready": Sem(self, "rdy_%s_%s" % (kind, eng))}
        return self.qst[key]

    def _confirm(self, qkey, idx):
        eng, kind = qkey
        st = self._qstate(eng, kind)
        for upto, val in st["confirms"]:
            if upto >= idx:
                return (st["ready"].h, val)
        a = self._autosem(eng, kind)
        self._need(eng, (a.h, a.n))
        t = st["ready"].tick(1)
        self.q[eng].append(("inc", t[0], t[1]))
        st["confirms"].append((st["issued"], st["ready"].n))
        return (st["ready"].h, st["ready"].n)

    def _dep(self, eng, dep):
        if dep is None:
            return
        if dep[0] == "pend":
            _, qkey, idx = dep
            h, v = self._confirm(qkey, idx)
            self._need(eng, (h, v))
            return
        if eng == "tensor" and dep[2] == "tensor":
            return
        self._need(eng, (dep[0], dep[1]))

    def _manual_val(self, handle):
        return _SEM_VALUES[id(handle)]

    def barrier(self, bsem, engines=ENGINES):
        for e in engines:
            t = bsem.tick(1)
            self.q[e].append(("inc", t[0], t[1]))
        for e in engines:
            self.wait(e, bsem)

    def finalize(self, block):
        def runner(items):
            def run(eng):
                for it in items:
                    if it[0] == "op":
                        _, meth, args, kw, inc = it
                        r = getattr(eng, meth)(*args, **kw)
                        if inc is not None:
                            if inc[1] is None:
                                r.then_inc(inc[0])
                            else:
                                r.then_inc(inc[0], inc[1])
                    elif it[0] == "wait":
                        eng.wait_ge(it[1], it[2])
                    elif it[0] == "inc":
                        eng.sem_inc(it[1], it[2])
            return run
        for e in ENGINES:
            getattr(block, e)(runner(self.q[e]))


_SEM_VALUES = {}


def make_sems(ctx):
    return {"d": ctx.sem("dsem"), "t": ctx.sem("tsem"), "v": ctx.sem("vsem"),
            "a": ctx.sem("asem"), "e": ctx.sem("esem"), "s": ctx.sem("ssem"),
            "bar": ctx.sem("barsem")}


def load_consts(ctx, names, sems):
    nc = ctx.nc
    out = {}
    for name in names:
        shape, dt = CONST_SPECS[name]
        d = nc.dram_tensor(name, list(shape), dt, kind="ExternalInput")
        t = ctx.sb("s" + name, shape, dt)
        ctx.op("sync", "dma_start", t[:], d[:, :])
        out[name] = t
    return out


def emit_stats(ctx, x_rows, n_tiles, sems, consts, tag=""):
    dsem, tsem, vsem, asem = sems["d"], sems["t"], sems["v"], sems["a"]
    rs_cols = ctx.sb("strs" + tag, (128, n_tiles), F32)
    negm_rows = ctx.sb("stnm" + tag, (1, n_tiles * 128), BF16)
    std_rows = ctx.sb("stsd" + tag, (1, n_tiles * 128), BF16)
    ctx.push_scope()
    xt = ctx.sb("stx" + tag, (128, 4096), F32)
    sq = ctx.sb("stq" + tag, (128, 4096), F32)
    s1 = ctx.sb("st1" + tag, (128, 1), F32)
    ss = ctx.sb("st2" + tag, (128, 1), F32)
    mvar = ctx.sb("st3" + tag, (128, 1), F32)
    mstd = ctx.sb("st4" + tag, (128, 1), F32)
    mm_ = ctx.sb("st5" + tag, (128, 1), F32)
    msq = ctx.sb("st6" + tag, (128, 1), F32)
    pkf = ctx.sb("st7" + tag, (128, 2), F32)
    pk = ctx.sb("st8" + tag, (128, 2), BF16)
    rows_ev = ctx.sb("st9" + tag, (2, 128), BF16)
    pT = ctx.ps("stpT" + tag, (2, 128), BF16)
    idb = consts["c_idb"]

    for tt in range(n_tiles):
        if tt > 0:
            ctx.wait("sync", vsem)
            ctx.wait("sync", asem)
        ctx.op("sync", "dma_start", xt[:], x_rows[tt * 128:(tt + 1) * 128, :])
        ctx.wait("vector", dsem)
        ctx.op("vector", "reduce_sum", s1[:], xt[:], axis=AX.X, inc=vsem.tick(1))
        ctx.wait("scalar", dsem)
        ctx.op("scalar", "activation", sq[:], xt[:], AF.Square,
               accum_out=ss[:], inc=asem.tick(1))
        ctx.wait("vector", asem)
        ctx.op("vector", "tensor_scalar_mul", mm_[:], s1[:], 1.0 / 4096.0)
        ctx.op("vector", "tensor_mul", msq[:], mm_[:], mm_[:])
        ctx.op("vector", "scalar_tensor_tensor", mvar[:], ss[:], 1.0 / 4096.0,
               msq[:], op0=ALU.mult, op1=ALU.subtract)
        ctx.op("vector", "tensor_scalar_add", mvar[:], mvar[:], EPS_LN,
               inc=vsem.tick(1))
        ctx.wait("scalar", vsem)
        ctx.op("scalar", "activation", mstd[:], mvar[:], AF.Sqrt,
               inc=asem.tick(1))
        ctx.wait("vector", asem)
        ctx.op("vector", "reciprocal", rs_cols[:, tt:tt + 1], mstd[:])
        ctx.op("vector", "tensor_scalar_mul", pkf[:, 0:1], mm_[:], -1.0)
        ctx.op("vector", "tensor_copy", pkf[:, 1:2], mstd[:])
        ctx.op("vector", "tensor_copy", pk[:], pkf[:], inc=vsem.tick(1))
        ctx.wait("tensor", vsem)
        ctx.op("tensor", "transpose", pT[:], pk[:], idb[:], inc=tsem.tick(1))
        ctx.wait("vector", tsem)
        ctx.op("vector", "tensor_copy", rows_ev[:], pT[:], inc=vsem.tick(1))
        ctx.wait("gpsimd", vsem)
        ctx.op("gpsimd", "dma_start", negm_rows[:, tt * 128:(tt + 1) * 128],
               rows_ev[0:1, :])
        ctx.op("gpsimd", "dma_start", std_rows[:, tt * 128:(tt + 1) * 128],
               rows_ev[1:2, :])
    for e in ENGINES:
        ctx.wait(e, dsem)
        ctx.wait(e, vsem)
    ctx.pop_scope()
    return {"rs": rs_cols, "negm": negm_rows, "std": std_rows}


def emit_proj(ctx, *, xT_planes, n_tok, w_dram, n_out, stats, sems, consts,
              bias_dram=None, epilogue=None, tag="", nfc=32, x_loader=None):
    dsem, tsem, vsem, esem = sems["d"], sems["t"], sems["v"], sems["e"]
    ctx.push_scope()
    NP = (n_out + 511) // 512
    n_tb = n_tok // 512
    wpan = [ctx.sb("pjw%d" % i + tag, (128, nfc * 512), BF16) for i in range(2)]
    xtb = [ctx.sb("pjx%d" % i + tag, (128, nfc * 512), BF16) for i in range(2)]
    onesc = ctx.sb("pj1c" + tag, (128, 1), BF16)
    ones_row = ctx.sb("pj1r" + tag, (1, 128), BF16)
    wsum = [ctx.sb("pjws%d" % i + tag, (1, 512), BF16) for i in range(2)]
    brow = [ctx.sb("pjbr%d" % i + tag, (1, 512), BF16) for i in range(2)]
    pP = [ctx.ps("pjp%d" % i + tag, (128, 512), F32) for i in range(4)]
    pW = ctx.ps("pjpw" + tag, (1, 512), F32)
    gsem = ctx.sem("pjg" + tag)
    ctx.op("gpsimd", "memset", onesc[:], 1.0)
    ctx.op("gpsimd", "memset", ones_row[:], 1.0, inc=gsem.tick(1))
    fold = stats is not None
    ng = max(1, nfc // 8)

    def load_w(pan):
        wp = wpan[pan % 2]
        o0 = pan * 512
        n_o = min(512, n_out - o0)
        for g in range(ng):
            f0 = g * 8
            nf = min(8, nfc - f0)
            ctx.op("gpsimd", "dma_start",
                   wp[:, f0 * 512:f0 * 512 + nf * 512]
                   .rearrange("p (fc t) -> p fc t", t=512)[:, :, 0:n_o],
                   w_dram[f0 * 128:(f0 + nf) * 128, o0:o0 + n_o]
                   .rearrange("(fc p) t -> p fc t", p=128))
        if bias_dram is not None:
            ctx.op("gpsimd", "dma_start", brow[pan % 2][:, 0:n_o],
                   bias_dram[0:1, o0:o0 + n_o])

    def load_x(tb):
        xp = xtb[tb % 2]
        if x_loader is not None:
            x_loader(ctx, xp, tb)
            return
        for g in range(ng):
            f0 = g * 8
            nf = min(8, nfc - f0)
            ctx.op("gpsimd", "dma_start",
                   xp[:, f0 * 512:(f0 + nf) * 512]
                   .rearrange("p (fc t) -> p fc t", t=512),
                   xT_planes[f0:f0 + nf, :, tb * 512:(tb + 1) * 512]
                   .rearrange("fc p t -> p fc t"))

    load_w(0)
    for pan in range(NP):
        o0 = pan * 512
        n_o = min(512, n_out - o0)
        wp = wpan[pan % 2]
        if pan + 1 < NP:
            load_w(pan + 1)
        if fold:
            ctx.wait("tensor", gsem)
            for fc in range(nfc):
                ctx.op("tensor", "matmul", pW[0:1, 0:n_o], onesc[:],
                       wp[:, fc * 512:fc * 512 + n_o],
                       start=(fc == 0), stop=(fc == nfc - 1),
                       inc=(tsem.tick(1) if fc == nfc - 1 else None))
            ctx.op("vector", "tensor_copy", wsum[pan % 2][:, 0:n_o],
                   pW[0:1, 0:n_o])
        load_x(0)
        for tb in range(n_tb):
            xp = xtb[tb % 2]
            if tb + 1 < n_tb:
                load_x(tb + 1)
            for tt in range(4):
                tg = tb * 4 + tt
                par = tg % 4
                n_extra = (1 if fold else 0) + (1 if bias_dram is not None else 0)
                for fc in range(nfc):
                    ctx.op("tensor", "matmul", pP[par][:, 0:n_o],
                           xp[:, fc * 512 + tt * 128:fc * 512 + (tt + 1) * 128],
                           wp[:, fc * 512:fc * 512 + n_o],
                           start=(fc == 0),
                           stop=(fc == nfc - 1 and n_extra == 0),
                           inc=(tsem.tick(1)
                                if fc == nfc - 1 and n_extra == 0 else None))
                if fold:
                    lastf = bias_dram is None
                    ctx.op("tensor", "matmul", pP[par][:, 0:n_o],
                           stats["negm"][0:1, tg * 128:(tg + 1) * 128],
                           wsum[pan % 2][0:1, 0:n_o], start=False, stop=lastf,
                           inc=(tsem.tick(1) if lastf else None))
                if bias_dram is not None:
                    row = (stats["std"][0:1, tg * 128:(tg + 1) * 128]
                           if fold else ones_row[0:1, 0:128])
                    ctx.op("tensor", "matmul", pP[par][:, 0:n_o],
                           row, brow[pan % 2][0:1, 0:n_o], start=False,
                           stop=True, inc=tsem.tick(1))
                epilogue(tg, o0, n_o, pP[par], tsem.n)
    for e in ENGINES:
        ctx.wait(e, esem)
    ctx.pop_scope()


def emit_softmax_free(ctx, dst, src, tmp, sems, pre_scale=1.0):
    vsem, asem = sems["v"], sems["a"]
    ctx.op("vector", "reduce_max", tmp["nm"][:], src, axis=AX.X, negate=True,
           inc=vsem.tick(1))
    ctx.wait("scalar", vsem)
    ctx.wait("scalar", sems["t"])
    ctx.op("scalar", "activation", dst, src, AF.Exp, bias=tmp["nm"][:],
           scale=pre_scale, accum_out=tmp["sm"][:], inc=asem.tick(1))
    ctx.wait("vector", asem)
    ctx.op("vector", "reciprocal", tmp["rs"][:], tmp["sm"][:], inc=vsem.tick(1))
    ctx.op("vector", "tensor_scalar_mul", dst, dst, tmp["rs"][:],
           inc=vsem.tick(1))


def emit_rope(ctx, dst, src, cosc, sinc, tmp64, sems):
    vsem = sems["v"]
    ctx.op("vector", "tensor_mul", dst[:, 0:64], src[:, 0:64], cosc[:, 0:64])
    ctx.op("vector", "tensor_mul", tmp64[:], src[:, 64:128], sinc[:, 0:64])
    ctx.op("vector", "tensor_sub", dst[:, 0:64], dst[:, 0:64], tmp64[:])
    ctx.op("vector", "tensor_mul", dst[:, 64:128], src[:, 64:128], cosc[:, 64:128])
    ctx.op("vector", "tensor_mul", tmp64[:], src[:, 0:64], sinc[:, 64:128])
    ctx.op("vector", "tensor_add", dst[:, 64:128], dst[:, 64:128], tmp64[:],
           inc=vsem.tick(1))


def emit_mixer(ctx, *, qf, kf, vf, oT_out, sems, consts, tag="",
               oT_writer=None):
    dsem, tsem, vsem, asem = sems["d"], sems["t"], sems["v"], sems["a"]
    ctx.push_scope()
    bar = sems["bar"]
    idf, idb = consts["c_idf"], consts["c_idb"]
    cU, cL = consts["c_U"], consts["c_L"]
    ccos, csin = consts["c_cos"], consts["c_sin"]

    qc_t = ctx.sb("mxq" + tag, (128, 2048), F32)
    kc_t = ctx.sb("mxk" + tag, (128, 512), F32)
    vc_t = ctx.sb("mxv" + tag, (128, 512), F32)
    vb = [ctx.sb("mxvb%d" % i + tag, (128, 512), BF16) for i in range(2)]
    sA = ctx.sb("mxsA" + tag, (128, 128), F32)
    sB = ctx.sb("mxsB" + tag, (128, 128), F32)
    gs = ctx.sb("mxgs" + tag, (128, 128), F32)
    ks = ctx.sb("mxks" + tag, (128, 128), F32)
    nm = ctx.sb("mxnm" + tag, (128, 1), F32)
    sm = ctx.sb("mxsm" + tag, (128, 1), F32)
    rsc = ctx.sb("mxrs" + tag, (128, 1), F32)
    tmp = {"nm": nm, "sm": sm, "rs": rsc}
    tmp64 = ctx.sb("mxt64" + tag, (128, 64), F32)
    eGT = ctx.sb("mxeg" + tag, (128, 128), F32)
    enGT = ctx.sb("mxen" + tag, (128, 128), F32)
    eRv = ctx.sb("mxer" + tag, (128, 128), F32)
    kh_bf = ctx.sb("mxkh" + tag, (128, 128), BF16)
    kgT_bf = ctx.sb("mxkg" + tag, (128, 128), BF16)
    sk = ctx.sb("mxsk" + tag, (128, 128), F32)
    win = ctx.sb("mxwin" + tag, (128, 4 * 256), BF16)
    Sf = ctx.sb("mxSf" + tag, (128, 4 * 128), F32)
    Sb = ctx.sb("mxSb" + tag, (128, 4 * 128), BF16)
    qs = ctx.sb("mxqs" + tag, (128, 128), F32)
    qgT_bf = ctx.sb("mxqg" + tag, (128, 128), BF16)
    sq_ = ctx.sb("mxsq" + tag, (128, 128), F32)
    sqT_bf = ctx.sb("mxsqT" + tag, (128, 128), BF16)
    At_bf = ctx.sb("mxAt" + tag, (128, 128), BF16)
    olh = ctx.sb("mxol" + tag, (128, 128), F32)
    msk = ctx.sb("mxmsk" + tag, (128, 256), F32)
    Pt = ctx.sb("mxP" + tag, (128, 256), F32)
    P_bf = ctx.sb("mxPb" + tag, (128, 256), BF16)
    PT1 = ctx.sb("mxPT1" + tag, (128, 128), BF16)
    PT2 = ctx.sb("mxPT2" + tag, (128, 128), BF16)
    oT_stage = ctx.sb("mxoT" + tag, (128, 2048), BF16)

    pAux = ctx.ps("mpAux" + tag, (128, 512), F32)
    pG, pR = pAux[:, 0:128], pAux[:, 128:256]
    pA, pD = pAux[:, 256:384], pAux[:, 384:512]
    pTr = ctx.ps("mpT" + tag, (128, 128), F32)
    pO = ctx.ps("mpO" + tag, (128, 128), F32)
    pSO = ctx.ps("mpSO" + tag, (128, 512), F32)
    pS, pO2 = pSO[:, 0:256], pSO[:, 256:384]
    pPT = ctx.ps("mpPT" + tag, (128, 256), BF16)

    ctx.op("gpsimd", "memset", Sf[:], 0.0)
    ctx.op("gpsimd", "memset", Sb[:], 0.0)
    ctx.op("gpsimd", "memset", win[:], 0.0)
    ctx.op("gpsimd", "memset", vb[1][:], 0.0)

    oT2d = (oT_out.rearrange("d (h t) -> d h t", t=4096)
            if oT_writer is None else None)

    for c in range(32):
        par = c & 1
        csl = slice(c * 128, (c + 1) * 128)
        ctx.op("sync", "dma_start", qc_t[:], qf[csl, :])
        ctx.op("sync", "dma_start", kc_t[:], kf[csl, :])
        ctx.op("sync", "dma_start", vc_t[:], vf[csl, :])
        ctx.wait("vector", dsem)
        ctx.wait("scalar", dsem)
        ctx.wait("tensor", dsem)
        cosc = ccos[:, csl]
        sinc = csin[:, csl]
        for kvh in range(4):
            kc = kc_t[:, kvh * 128:(kvh + 1) * 128]
            vc = vc_t[:, kvh * 128:(kvh + 1) * 128]
            vbsl = vb[par][:, kvh * 128:(kvh + 1) * 128]
            wsl = win[:, kvh * 256:(kvh + 1) * 256]
            Ssl_f = Sf[:, kvh * 128:(kvh + 1) * 128]
            Ssl_b = Sb[:, kvh * 128:(kvh + 1) * 128]
            ctx.op("vector", "tensor_copy", vbsl, vc, inc=vsem.tick(1))
            ctx.op("scalar", "activation", sA[:], kc, AF.Abs)
            ctx.op("scalar", "activation", sB[:], sA[:], AF.Exp, scale=-1.0)
            ctx.op("scalar", "activation", sA[:], sB[:], AF.Ln, bias=1.0,
                   inc=asem.tick(1))
            ctx.op("vector", "tensor_scalar_min", sB[:], kc, 0.0)
            ctx.wait("vector", asem)
            ctx.op("vector", "tensor_sub", gs[:], sB[:], sA[:])
            ctx.op("vector", "tensor_scalar_mul", gs[:], gs[:], 1.0 / 16.0,
                   inc=vsem.tick(1))
            emit_softmax_free(ctx, ks[:], kc, tmp, sems)
            ctx.wait("tensor", vsem)
            ctx.op("tensor", "matmul", pG, gs[:], cU[:], start=True,
                   stop=True, inc=tsem.tick(1))
            ctx.op("tensor", "matmul", pR, cL[:], gs[:], start=True,
                   stop=True, inc=tsem.tick(1))
            ctx.wait("scalar", tsem)
            ctx.op("scalar", "activation", eGT[:], pG, AF.Exp)
            ctx.op("scalar", "activation", enGT[:], pG, AF.Exp, scale=-1.0)
            ctx.op("scalar", "activation", eRv[:], pR, AF.Exp,
                   inc=asem.tick(1))
            ctx.wait("vector", asem)
            ctx.op("vector", "tensor_mul", kh_bf[:], ks[:], eRv[:],
                   inc=vsem.tick(1))
            ctx.wait("tensor", asem)
            ctx.op("tensor", "transpose", pTr[:], ks[:], idf[:],
                   inc=tsem.tick(1))
            ctx.wait("vector", tsem)
            ctx.op("vector", "tensor_mul", kgT_bf[:], pTr[:], enGT[:],
                   inc=vsem.tick(1))
            emit_rope(ctx, sk[:], kc, cosc, sinc, tmp64, sems)
            ctx.op("vector", "tensor_copy", wsl[:, 0:128], wsl[:, 128:256])
            ctx.wait("tensor", vsem)
            ctx.op("tensor", "transpose", pTr[:], sk[:], idf[:],
                   inc=tsem.tick(1))
            ctx.wait("vector", tsem)
            ctx.op("vector", "tensor_copy", wsl[:, 128:256], pTr[:, 0:128],
                   inc=vsem.tick(1))
            dec = eGT[:, 127:128]

            for hq in range(4):
                h = kvh * 4 + hq
                qc = qc_t[:, h * 128:(h + 1) * 128]
                emit_softmax_free(ctx, qs[:], qc, tmp, sems)
                ctx.wait("tensor", vsem)
                ctx.op("tensor", "transpose", pTr[:], qs[:], idf[:],
                       inc=tsem.tick(1))
                ctx.wait("vector", tsem)
                ctx.op("vector", "tensor_mul", qgT_bf[:], pTr[:], eGT[:],
                       inc=vsem.tick(1))
                ctx.wait("tensor", vsem)
                ctx.op("tensor", "matmul", pA, kgT_bf[:], qgT_bf[:],
                       start=True, stop=True, inc=tsem.tick(1))
                ctx.wait("vector", tsem)
                ctx.op("vector", "tensor_mul", At_bf[:], pA, cU[:],
                       inc=vsem.tick(1))
                ctx.wait("tensor", vsem)
                ctx.op("tensor", "matmul", pO[:], Ssl_b, qgT_bf[:],
                       start=True, stop=False)
                ctx.op("tensor", "matmul", pO[:], vbsl, At_bf[:],
                       start=False, stop=True, inc=tsem.tick(1))
                ctx.wait("scalar", tsem)
                ctx.op("scalar", "activation", olh[:], pO[:], AF.Copy,
                       scale=0.5, inc=asem.tick(1))
                emit_rope(ctx, sq_[:], qc, cosc, sinc, tmp64, sems)
                ctx.wait("tensor", vsem)
                ctx.op("tensor", "transpose", pTr[:], sq_[:], idf[:],
                       inc=tsem.tick(1))
                ctx.wait("vector", tsem)
                ctx.op("vector", "tensor_copy", sqT_bf[:], pTr[:],
                       inc=vsem.tick(1))
                ctx.wait("tensor", vsem)
                ctx.op("tensor", "matmul", pS, sqT_bf[:], wsl[:, 0:256],
                       start=True, stop=True, inc=tsem.tick(1))
                ctx.wait("vector", tsem)
                ctx.wait("vector", asem)
                swm = consts["c_swm0"] if c == 0 else consts["c_swm"]
                ctx.op("vector", "tensor_add", msk[:], pS, swm[:])
                ctx.op("vector", "reduce_max", nm[:], msk[:], axis=AX.X)
                ctx.op("vector", "tensor_scalar_mul", nm[:], nm[:], -CSC,
                       inc=vsem.tick(1))
                ctx.wait("scalar", vsem)
                ctx.op("scalar", "activation", Pt[:], msk[:], AF.Exp,
                       bias=nm[:], scale=CSC, accum_out=sm[:],
                       inc=asem.tick(1))
                ctx.wait("vector", asem)
                ctx.op("vector", "reciprocal", rsc[:], sm[:])
                ctx.op("vector", "tensor_scalar_mul", P_bf[:], Pt[:], rsc[:],
                       inc=vsem.tick(1))
                ctx.wait("tensor", vsem)
                ctx.op("tensor", "transpose", pPT[:, 0:128], P_bf[:, 0:128],
                       idb[:], inc=tsem.tick(1))
                ctx.op("tensor", "transpose", pPT[:, 128:256],
                       P_bf[:, 128:256], idb[:], inc=tsem.tick(1))
                ctx.wait("vector", tsem)
                ctx.op("vector", "tensor_copy", PT1[:], pPT[:, 0:128])
                ctx.op("vector", "tensor_copy", PT2[:], pPT[:, 128:256],
                       inc=vsem.tick(1))
                ctx.wait("tensor", vsem)
                ctx.op("tensor", "matmul", pO2,
                       vb[1 - par][:, kvh * 128:(kvh + 1) * 128],
                       PT1[:], start=True, stop=False)
                ctx.op("tensor", "matmul", pO2,
                       vb[par][:, kvh * 128:(kvh + 1) * 128],
                       PT2[:], start=False, stop=True,
                       inc=tsem.tick(1))
                ctx.wait("vector", tsem)
                ctx.wait("vector", asem)
                ctx.op("vector", "scalar_tensor_tensor",
                       oT_stage[:, h * 128:(h + 1) * 128], pO2, 0.5,
                       olh[:], op0=ALU.mult, op1=ALU.add, inc=vsem.tick(1))
            ctx.wait("tensor", vsem)
            ctx.op("tensor", "matmul", pD, kh_bf[:], vbsl,
                   start=True, stop=True, inc=tsem.tick(1))
            ctx.wait("vector", tsem)
            ctx.op("vector", "scalar_tensor_tensor", Ssl_f, Ssl_f, dec,
                   pD, op0=ALU.mult, op1=ALU.add)
            ctx.op("vector", "tensor_copy", Ssl_b, Ssl_f, inc=vsem.tick(1))
        ctx.wait("gpsimd", vsem)
        if oT_writer is None:
            for g in range(2):
                ctx.op("gpsimd", "dma_start",
                       oT2d[:, g * 8:(g + 1) * 8, c * 128:(c + 1) * 128],
                       oT_stage[:, g * 1024:(g + 1) * 1024]
                       .rearrange("d (h t) -> d h t", t=128))
        else:
            oT_writer(c, oT_stage)
        ctx.barrier(bar)
    a = ctx._autosem("gpsimd", "d")
    ctx._need("gpsimd", (a.h, a.n))
    ctx.pop_scope()


L1_CONSTS = ["c_idf", "c_idb", "c_U", "c_L", "c_cos", "c_sin", "c_swm",
             "c_swm0"]

def build_L1(debug=False):
    nc = bass.Bass("TRN2", target_bir_lowering=False, debug=False)
    ctx = Ctx(nc)
    sems = make_sems(ctx)
    kind = "ExternalOutput" if debug else "Internal"
    x0_rows = nc.dram_tensor("x0_rows", [4096, 4096], F32, kind="ExternalInput")
    x0Tp = nc.dram_tensor("x0Tp", [32, 128, 4096], F32, kind="ExternalInput")
    wqkv = nc.dram_tensor("wqkv", [4096, 3072], F32, kind="ExternalInput")
    qf = nc.dram_tensor("qf", [4096, 2048], F32, kind=kind)
    kf = nc.dram_tensor("kf", [4096, 512], F32, kind=kind)
    vf = nc.dram_tensor("vf", [4096, 512], F32, kind=kind)
    oT_out = nc.dram_tensor("oT", [128, 16 * 4096], BF16, kind="ExternalOutput")

    with nc.Block() as block:
        consts = load_consts(ctx, L1_CONSTS, sems)
        stats = emit_stats(ctx, x0_rows, 32, sems, consts)
        ctx.barrier(sems["bar"])

        out_sb = [ctx.sb("l1o%d" % i, (128, 512), F32) for i in range(2)]
        epar = [0]

        def epi(tg, o0, n_o, psum, tval):
            par = epar[0]
            epar[0] = 1 - par
            ctx.op("scalar", "activation", out_sb[par][:, 0:n_o],
                   psum[:, 0:n_o], AF.Copy,
                   scale=stats["rs"][:, tg:tg + 1], inc=sems["e"].tick(1))
            if o0 < 2048:
                dest = qf[tg * 128:(tg + 1) * 128, o0:o0 + n_o]
            elif o0 < 2560:
                dest = kf[tg * 128:(tg + 1) * 128, 0:n_o]
            else:
                dest = vf[tg * 128:(tg + 1) * 128, 0:n_o]
            ctx.op("sync", "dma_start", dest, out_sb[par][:, 0:n_o])

        emit_proj(ctx, xT_planes=x0Tp, n_tok=4096, w_dram=wqkv, n_out=3072,
                  stats=stats, sems=sems, consts=consts, epilogue=epi,
                  tag="L1")
        ctx.barrier(sems["bar"])
        emit_mixer(ctx, qf=qf, kf=kf, vf=vf, oT_out=oT_out.ap(), sems=sems,
                   consts=consts)
        ctx.barrier(sems["bar"])
        ctx.finalize(block)
    ctx.st.close()
    return nc


def host_inputs_L1(inputs, core):
    b, p = core // 2, core % 2
    x0 = np.ascontiguousarray(inputs["L0_attn_hidden_states"][b])
    hs = slice(p * 2048, (p + 1) * 2048)
    ks = slice(p * 512, (p + 1) * 512)
    wqkv = np.concatenate([inputs["L0_attn_Wq"][:, hs],
                           inputs["L0_attn_Wk"][:, ks],
                           inputs["L0_attn_Wv"][:, ks]], axis=1)
    d = {"x0_rows": x0,
         "x0Tp": np.ascontiguousarray(x0.T.reshape(32, 128, 4096)),
         "wqkv": np.ascontiguousarray(wqkv)}
    d.update({k: v for k, v in host_consts(p).items() if k in L1_CONSTS})
    return d


def build_L2a():
    nc = bass.Bass("TRN2", target_bir_lowering=False, debug=False)
    ctx = Ctx(nc)
    sems = make_sems(ctx)
    x0_own = nc.dram_tensor("x0_own", [T_OWN, 4096], F32, kind="ExternalInput")
    oT_own = nc.dram_tensor("oT_own", [32, 128, T_OWN], BF16,
                            kind="ExternalInput")
    Wo = nc.dram_tensor("Wo", [4096, 4096], F32, kind="ExternalInput")
    x1_rows = nc.dram_tensor("x1_rows", [T_OWN, 4096], F32,
                             kind="ExternalOutput")
    with nc.Block() as block:
        x0t = ctx.sb("l2x0", (128, 512), F32)
        out_sb = [ctx.sb("l2o%d" % i, (128, 512), F32) for i in range(2)]
        epar = [0]

        def epi(tg, o0, n_o, psum, tval):
            par = epar[0]
            epar[0] = 1 - par
            rows = slice(tg * 128, (tg + 1) * 128)
            ctx.op("sync", "dma_start", x0t[:, 0:n_o], x0_own[rows, o0:o0 + n_o])
            ctx.op("vector", "tensor_add", out_sb[par][:, 0:n_o],
                   psum[:, 0:n_o], x0t[:, 0:n_o], inc=sems["e"].tick(1))
            ctx.op("sync", "dma_start", x1_rows[rows, o0:o0 + n_o],
                   out_sb[par][:, 0:n_o])

        emit_proj(ctx, xT_planes=oT_own, n_tok=T_OWN, w_dram=Wo, n_out=4096,
                  stats=None, sems=sems, consts=None, epilogue=epi, tag="L2a")
        a = ctx._autosem("sync", "d")
        ctx._need("sync", (a.h, a.n))
        ctx.finalize(block)
    ctx.st.close()
    return nc


def host_inputs_L2a(inputs, core, oT_pair):
    b, p = core // 2, core % 2
    own = own_rows(p)
    x0 = inputs["L0_attn_hidden_states"][b][own]
    o0 = np.asarray(oT_pair[0]).reshape(128, 16, 4096)
    o1 = np.asarray(oT_pair[1]).reshape(128, 16, 4096)
    full = np.concatenate([o0, o1], axis=1)
    oT_own = np.ascontiguousarray(full.transpose(1, 0, 2)[:, :, own])
    return {"x0_own": np.ascontiguousarray(x0), "oT_own": oT_own,
            "Wo": inputs["L0_attn_Wo"]}


def own_rows(p):
    idx = np.arange(T_OWN)
    tile = idx // 128
    return (2 * tile + p) * 128 + (idx % 128)


L2B_CONSTS = ["c_idf", "c_idb", "c_odd"]

def build_L2b():
    nc = bass.Bass("TRN2", target_bir_lowering=False, debug=False)
    ctx = Ctx(nc)
    sems = make_sems(ctx)
    x1_rows = nc.dram_tensor("x1_rows", [T_OWN, 4096], F32, kind="ExternalInput")
    x1Tp = nc.dram_tensor("x1Tp", [32, 128, T_OWN], F32, kind="ExternalInput")
    WT = nc.dram_tensor("WT", [4096, 4096], F32, kind="ExternalInput")
    b_row = nc.dram_tensor("b_row", [1, 4096], F32, kind="ExternalInput")
    A4 = nc.dram_tensor("A4", [4096, 32], F32, kind="ExternalInput")
    LiT = nc.dram_tensor("LiT", [32, 8], F32, kind="ExternalInput")
    H_rows = nc.dram_tensor("H_rows", [T_OWN, 4096], F32, kind="ExternalOutput")
    z_rows = nc.dram_tensor("z_rows", [T_OWN, 32], F32, kind="ExternalOutput")
    ds_rows = nc.dram_tensor("ds_rows", [T_OWN, 8], F32, kind="ExternalOutput")
    scales_loc = nc.dram_tensor("scales_loc", [1, 2], F32, kind="ExternalOutput")
    with nc.Block() as block:
        consts = load_consts(ctx, L2B_CONSTS, sems)
        stats = emit_stats(ctx, x1_rows, NT_OWN, sems, consts, tag="b")
        hmax = ctx.sb("l2hm", (128, 1), F32)
        dmax = ctx.sb("l2dm", (128, 1), F32)
        tmp8 = ctx.sb("l2t8", (128, 8), F32)
        m8 = ctx.sb("l2m8", (128, 1), F32)
        ctx.op("gpsimd", "memset", hmax[:], 0.0)
        ctx.op("gpsimd", "memset", dmax[:], 0.0)
        out_sb = [ctx.sb("l2bo%d" % i, (128, 512), F32) for i in range(2)]
        epar = [0]

        def epiH(tg, o0, n_o, psum, tval):
            par = epar[0]
            epar[0] = 1 - par
            rows = slice(tg * 128, (tg + 1) * 128)
            ctx.op("scalar", "activation", out_sb[par][:, 0:n_o],
                   psum[:, 0:n_o], AF.Copy, scale=stats["rs"][:, tg:tg + 1],
                   inc=sems["e"].tick(1))
            ctx.op("sync", "dma_start", H_rows[rows, o0:o0 + n_o],
                   out_sb[par][:, 0:n_o])
            if o0 == 0:
                ctx.op("vector", "tensor_scalar_mul", tmp8[:],
                       out_sb[par][:, 0:8], consts["c_odd"][:])
                ctx.op("vector", "reduce_max", m8[:], tmp8[:], axis=AX.X,
                       apply_absolute_value=True)
                ctx.op("vector", "tensor_max", hmax[:], hmax[:], m8[:])

        emit_proj(ctx, xT_planes=x1Tp, n_tok=T_OWN, w_dram=WT, n_out=4096,
                  stats=stats, sems=sems, consts=consts, bias_dram=b_row,
                  epilogue=epiH, tag="L2bH")

        z_sb = ctx.sb("l2z", (128, 32), F32)

        def epiZ(tg, o0, n_o, psum, tval):
            rows = slice(tg * 128, (tg + 1) * 128)
            ctx.op("scalar", "activation", z_sb[:, 0:n_o], psum[:, 0:n_o],
                   AF.Copy, scale=stats["rs"][:, tg:tg + 1],
                   inc=sems["e"].tick(1))
            ctx.op("sync", "dma_start", z_rows[rows, 0:n_o], z_sb[:, 0:n_o])

        emit_proj(ctx, xT_planes=x1Tp, n_tok=T_OWN, w_dram=A4, n_out=32,
                  stats=stats, sems=sems, consts=consts, epilogue=epiZ,
                  tag="L2bz")

        ctx.push_scope()
        zt = ctx.sb("l2zt", (128, 32), F32)
        ztb = ctx.sb("l2ztb", (128, 32), BF16)
        zTb = ctx.sb("l2zT", (32, 128), BF16)
        litb = ctx.sb("l2lit", (32, 8), BF16)
        ds_sb = ctx.sb("l2ds", (128, 8), F32)
        pT = ctx.ps("l2pT", (32, 128), BF16)
        pD = ctx.ps("l2pD", (128, 8), F32)
        ctx.op("gpsimd", "dma_start", litb[:], LiT[:, :])
        for tg in range(NT_OWN):
            rows = slice(tg * 128, (tg + 1) * 128)
            ctx.op("sync", "dma_start", zt[:], z_rows[rows, :])
            ctx.op("vector", "tensor_copy", ztb[:], zt[:])
            ctx.op("tensor", "transpose", pT[:], ztb[:], consts["c_idb"][:],
                   inc=sems["t"].tick(1))
            ctx.op("vector", "tensor_copy", zTb[:], pT[:])
            ctx.op("tensor", "matmul", pD[:], zTb[:], litb[:], start=True,
                   stop=True, inc=sems["t"].tick(1))
            ctx.op("vector", "tensor_copy", ds_sb[:], pD[:])
            ctx.op("sync", "dma_start", ds_rows[rows, :], ds_sb[:])
            ctx.op("vector", "tensor_scalar_mul", tmp8[:], ds_sb[:],
                   consts["c_odd"][:])
            ctx.op("vector", "reduce_max", m8[:], tmp8[:], axis=AX.X,
                   apply_absolute_value=True)
            ctx.op("vector", "tensor_max", dmax[:], dmax[:], m8[:])
        pR = ctx.ps("l2pR", (1, 128), F32)
        srow = ctx.sb("l2sr", (1, 128), F32)
        sc2 = ctx.sb("l2sc", (1, 2), F32)
        for i, acc in enumerate((hmax, dmax)):
            ctx.op("tensor", "transpose", pR[:], acc[:], consts["c_idf"][:],
                   inc=sems["t"].tick(1))
            ctx.op("vector", "tensor_copy", srow[:], pR[:])
            ctx.op("vector", "reduce_max", sc2[:, i:i + 1], srow[:], axis=AX.X)
        ctx.op("sync", "dma_start", scales_loc[:, :], sc2[:])
        a = ctx._autosem("sync", "d")
        ctx._need("sync", (a.h, a.n))
        ctx.pop_scope()
        ctx.finalize(block)
    ctx.st.close()
    return nc


def host_inputs_L2b(inputs, core, x1_own):
    d = {"x1_rows": np.ascontiguousarray(x1_own),
         "x1Tp": np.ascontiguousarray(x1_own.T.reshape(32, 128, T_OWN)),
         "WT": np.ascontiguousarray(inputs["L0_ffn_W"].T),
         "b_row": np.ascontiguousarray(inputs["L0_ffn_b"][None, :]),
         "A4": inputs["L0_ffn_A"],
         "LiT": np.ascontiguousarray(inputs["L0_ffn_LiMEs"].T)}
    d.update({k: v for k, v in host_consts(core % 2).items() if k in L2B_CONSTS})
    return d


L3_CONSTS = ["c_idf", "c_idb"]

def build_L3():
    nc = bass.Bass("TRN2", target_bir_lowering=False, debug=False)
    ctx = Ctx(nc)
    sems = make_sems(ctx)
    x1_rows = nc.dram_tensor("x1_rows", [T_OWN, 4096], F32, kind="ExternalInput")
    H_rows = nc.dram_tensor("H_rows", [T_OWN, 4096], F32, kind="ExternalInput")
    z_rows = nc.dram_tensor("z_rows", [T_OWN, 32], F32, kind="ExternalInput")
    ds_rows = nc.dram_tensor("ds_rows", [T_OWN, 8], F32, kind="ExternalInput")
    scales_all = nc.dram_tensor("scales_all", [1, 16], F32, kind="ExternalInput")
    LiM = nc.dram_tensor("LiM", [8, 32], F32, kind="ExternalInput")
    Bm = nc.dram_tensor("Bm", [32, 4096], F32, kind="ExternalInput")
    x2_rows = nc.dram_tensor("x2_rows", [T_OWN, 4096], F32,
                             kind="ExternalOutput")
    with nc.Block() as block:
        consts = load_consts(ctx, L3_CONSTS, sems)
        sall = ctx.sb("l3sa", (1, 16), F32)
        s11 = ctx.sb("l3s1", (1, 2), F32)
        inv2 = ctx.sb("l3inv", (1, 2), F32)
        invc = ctx.sb("l3ic", (128, 2), F32)
        ones_r = ctx.sb("l3or", (1, 128), F32)
        limb = ctx.sb("l3lim", (8, 32), BF16)
        bmb = ctx.sb("l3bm", (32, 4096), BF16)
        pB = ctx.ps("l3pB", (128, 2), F32)
        ctx.op("sync", "dma_start", sall[:], scales_all[:, :])
        ctx.op("gpsimd", "dma_start", limb[:], LiM[:, :])
        for fc in range(1):
            ctx.op("gpsimd", "dma_start", bmb[:], Bm[:, :])
        ctx.op("gpsimd", "memset", ones_r[:], 1.0)
        s8x2 = ctx.sb("l3s8", (1, 16), F32)
        sv = sall.ap().rearrange("a (p two) -> a two p", two=2)
        ctx.op("vector", "tensor_copy",
               s8x2.ap().rearrange("a (two p) -> a two p", two=2), sv)
        ctx.op("vector", "reduce_max", s11[:, 0:1], s8x2[:, 0:8], axis=AX.X)
        ctx.op("vector", "reduce_max", s11[:, 1:2], s8x2[:, 8:16], axis=AX.X)
        ctx.op("vector", "tensor_scalar_max", s11[:], s11[:], 1e-6)
        ctx.op("vector", "reciprocal", inv2[:], s11[:])
        ctx.op("vector", "tensor_scalar_mul", inv2[:], inv2[:], 0.5)
        ctx.op("tensor", "matmul", pB[:], ones_r[:], inv2[:], start=True,
               stop=True, inc=sems["t"].tick(1))
        ctx.op("vector", "tensor_copy", invc[:], pB[:])

        Ht = ctx.sb("l3H", (128, 4096), F32)
        x1t = ctx.sb("l3x1", (128, 4096), F32)
        hs8 = ctx.sb("l3h8", (128, 8), F32)
        ds8 = ctx.sb("l3d8", (128, 8), F32)
        zt = ctx.sb("l3z", (128, 32), F32)
        lg = ctx.sb("l3lg", (128, 8), F32)
        t8 = ctx.sb("l3t8", (128, 8), F32)
        nm = ctx.sb("l3nm", (128, 1), F32)
        sm = ctx.sb("l3sm", (128, 1), F32)
        rsc = ctx.sb("l3rs", (128, 1), F32)
        probs = ctx.sb("l3pr", (128, 8), F32)
        eqm = ctx.sb("l3eq", (128, 8), F32)
        m2 = ctx.sb("l3m2", (128, 1), F32)
        sig = ctx.sb("l3sg", (128, 8), F32)
        wts = ctx.sb("l3wt", (128, 8), F32)
        wsh = ctx.sb("l3ws", (128, 8), F32)
        wshb = ctx.sb("l3wsb", (128, 8), BF16)
        wT = ctx.sb("l3wT", (8, 128), BF16)
        zpb = ctx.sb("l3zp", (128, 32), BF16)
        zpT = ctx.sb("l3zpT", (32, 128), BF16)
        osb = ctx.sb("l3ou", (128, 512), F32)
        pT8 = ctx.ps("l3pT", (8, 128), BF16)
        pM = ctx.ps("l3pM", (128, 32), F32)
        pT32 = ctx.ps("l3pT2", (32, 128), BF16)
        pDl = ctx.ps("l3pD", (128, 512), F32)
        SHUF = [i | 1 for i in range(32)]

        for tg in range(NT_OWN):
            rows = slice(tg * 128, (tg + 1) * 128)
            ctx.op("sync", "dma_start", hs8[:], H_rows[rows, 0:8])
            ctx.op("sync", "dma_start", ds8[:], ds_rows[rows, :])
            ctx.op("sync", "dma_start", zt[:], z_rows[rows, :])
            ctx.op("sync", "dma_start", x1t[:], x1_rows[rows, :])
            ctx.op("sync", "dma_start", Ht[:], H_rows[rows, :])
            ctx.op("vector", "tensor_scalar_mul", t8[:], ds8[:], invc[:, 1:2])
            ctx.op("vector", "scalar_tensor_tensor", lg[:], hs8[:],
                   invc[:, 0:1], t8[:], op0=ALU.mult, op1=ALU.add)
            ctx.op("vector", "reduce_max", nm[:], lg[:], axis=AX.X, negate=True)
            ctx.op("scalar", "activation", probs[:], lg[:], AF.Exp,
                   bias=nm[:], scale=1.0, accum_out=sm[:])
            ctx.op("vector", "reciprocal", rsc[:], sm[:])
            ctx.op("vector", "tensor_scalar_mul", probs[:], probs[:], rsc[:])
            ctx.op("vector", "reduce_max", nm[:], probs[:], axis=AX.X)
            ctx.op("vector", "tensor_scalar", eqm[:], probs[:], nm[:], None,
                   op0=ALU.is_ge)
            ctx.op("vector", "scalar_tensor_tensor", t8[:], eqm[:], -1e30,
                   probs[:], op0=ALU.mult, op1=ALU.add)
            ctx.op("vector", "reduce_max", m2[:], t8[:], axis=AX.X)
            ctx.op("vector", "tensor_scalar_mul", m2[:], m2[:], -2.0)
            ctx.op("scalar", "activation", sig[:], probs[:], AF.Sigmoid,
                   bias=m2[:], scale=2.0)
            ctx.op("vector", "tensor_mul", wts[:], probs[:], sig[:])
            ctx.op("vector", "reduce_sum", sm[:], wts[:], axis=AX.X)
            ctx.op("vector", "tensor_scalar_add", sm[:], sm[:], 1e-9)
            ctx.op("vector", "reciprocal", rsc[:], sm[:])
            ctx.op("vector", "tensor_scalar_mul", wts[:], wts[:], rsc[:])
            ctx.op("vector", "stream_shuffle", wsh[:], wts[:], SHUF)
            ctx.op("vector", "tensor_copy", wshb[:], wsh[:])
            ctx.op("tensor", "transpose", pT8[:], wshb[:], consts["c_idb"][:],
                   inc=sems["t"].tick(1))
            ctx.op("vector", "tensor_copy", wT[:], pT8[:])
            ctx.op("tensor", "matmul", pM[:], wT[:], limb[:], start=True,
                   stop=True, inc=sems["t"].tick(1))
            ctx.op("vector", "tensor_mul", zpb[:], zt[:], pM[:])
            ctx.op("tensor", "transpose", pT32[:], zpb[:], consts["c_idb"][:],
                   inc=sems["t"].tick(1))
            ctx.op("vector", "tensor_copy", zpT[:], pT32[:])
            for oc in range(8):
                osl = slice(oc * 512, (oc + 1) * 512)
                ctx.op("tensor", "matmul", pDl[:], zpT[:], bmb[:, osl],
                       start=True, stop=True, inc=sems["t"].tick(1))
                ctx.op("vector", "scalar_tensor_tensor", osb[:], pDl[:], 0.5,
                       Ht[:, osl], op0=ALU.mult, op1=ALU.add)
                ctx.op("vector", "tensor_add", osb[:], osb[:], x1t[:, osl])
                ctx.op("sync", "dma_start", x2_rows[rows, osl], osb[:])
        a = ctx._autosem("sync", "d")
        ctx._need("sync", (a.h, a.n))
        ctx.finalize(block)
    ctx.st.close()
    return nc


def host_inputs_L3(inputs, core, x1_own, H_own, z_own, ds_own, scales8):
    d = {"x1_rows": np.ascontiguousarray(x1_own),
         "H_rows": np.ascontiguousarray(H_own),
         "z_rows": np.ascontiguousarray(z_own),
         "ds_rows": np.ascontiguousarray(ds_own),
         "scales_all": np.ascontiguousarray(
             np.concatenate(scales8, 1).reshape(1, 16)),
         "LiM": inputs["L0_ffn_LiMEs"],
         "Bm": inputs["L0_ffn_Bmat"]}
    d.update({k: v for k, v in host_consts(core % 2).items() if k in L3_CONSTS})
    return d


L4_CONSTS = ["c_idf", "c_idb"]

def build_L4():
    nc = bass.Bass("TRN2", target_bir_lowering=False, debug=False)
    ctx = Ctx(nc)
    sems = make_sems(ctx)
    x2_rows = nc.dram_tensor("x2_rows", [T_OWN, 4096], F32, kind="ExternalInput")
    x2Tp = nc.dram_tensor("x2Tp", [32, 128, T_OWN], F32, kind="ExternalInput")
    Wcat = nc.dram_tensor("Wcat", [4096, 12288], F32, kind="ExternalInput")
    bcat = nc.dram_tensor("bcat", [1, 12288], F32, kind="ExternalInput")
    q_rows = nc.dram_tensor("q_rows", [T_OWN, 4096], BF16, kind="ExternalOutput")
    k_rows = nc.dram_tensor("k_rows", [T_OWN, 4096], BF16, kind="ExternalOutput")
    v_rows = nc.dram_tensor("v_rows", [T_OWN, 4096], BF16, kind="ExternalOutput")
    kmax_loc = nc.dram_tensor("kmax_loc", [1, 1], F32, kind="ExternalOutput")
    with nc.Block() as block:
        consts = load_consts(ctx, L4_CONSTS, sems)
        stats = emit_stats(ctx, x2_rows, NT_OWN, sems, consts, tag="4")
        out_sb = [ctx.sb("l4o%d" % i, (128, 512), BF16) for i in range(2)]
        sqs = ctx.sb("l4sq", (128, 128), F32)
        kn = ctx.sb("l4kn", (128, 1), F32)
        knacc = ctx.sb("l4ka", (128, 1), F32)
        ctx.op("gpsimd", "memset", knacc[:], 0.0)
        epar = [0]

        def epi(tg, o0, n_o, psum, tval):
            par = epar[0]
            epar[0] = 1 - par
            rows = slice(tg * 128, (tg + 1) * 128)
            ctx.op("scalar", "activation", out_sb[par][:, 0:n_o],
                   psum[:, 0:n_o], AF.Copy, scale=stats["rs"][:, tg:tg + 1],
                   inc=sems["e"].tick(1))
            which, off = divmod(o0, 4096)
            dest = (q_rows, k_rows, v_rows)[which]
            ctx.op("sync", "dma_start", dest[rows, off:off + n_o],
                   out_sb[par][:, 0:n_o])
            if which == 1:
                for hh in range(4):
                    sl = slice(hh * 128, (hh + 1) * 128)
                    ctx.op("scalar", "activation", sqs[:],
                           out_sb[par][:, sl], AF.Square, accum_out=kn[:])
                    ctx.op("vector", "tensor_max", knacc[:], knacc[:], kn[:])

        emit_proj(ctx, xT_planes=x2Tp, n_tok=T_OWN, w_dram=Wcat, n_out=12288,
                  stats=stats, sems=sems, consts=consts, bias_dram=bcat,
                  epilogue=epi, tag="L4")
        pR = ctx.ps("l4pR", (1, 128), F32)
        srow = ctx.sb("l4sr", (1, 128), F32)
        km = ctx.sb("l4km", (1, 1), F32)
        ctx.op("tensor", "transpose", pR[:], knacc[:], consts["c_idf"][:],
               inc=sems["t"].tick(1))
        ctx.op("vector", "tensor_copy", srow[:], pR[:])
        ctx.op("vector", "reduce_max", km[:], srow[:], axis=AX.X)
        ctx.op("sync", "dma_start", kmax_loc[:, :], km[:])
        a = ctx._autosem("sync", "d")
        ctx._need("sync", (a.h, a.n))
        ctx.finalize(block)
    ctx.st.close()
    return nc


def host_inputs_L4(inputs, core, x2_own):
    Wcat = np.concatenate([inputs["L1_attn_Wq"].T, inputs["L1_attn_Wk"].T,
                           inputs["L1_attn_Wv"].T], axis=1)
    bcat = np.concatenate([inputs["L1_attn_bq"], inputs["L1_attn_bk"],
                           inputs["L1_attn_bv"]])[None, :]
    d = {"x2_rows": np.ascontiguousarray(x2_own),
         "x2Tp": np.ascontiguousarray(x2_own.T.reshape(32, 128, T_OWN)),
         "Wcat": np.ascontiguousarray(Wcat),
         "bcat": np.ascontiguousarray(bcat)}
    d.update({k: v for k, v in host_consts(core % 2).items() if k in L4_CONSTS})
    return d


L5_CONSTS = ["c_idf", "c_idb", "c_dmA", "c_dmB", "c_dmAb", "c_dmBb"]

def emit_attn_l1(ctx, *, qT, kT, vpl, kmax_all, oT_l1, sems, consts,
                 rank_major=False):
    tsem = sems["t"]
    if rank_major:
        kcoff = lambda kc: (kc & 1) * 2048 + (kc >> 1) * 128
    else:
        kcoff = lambda kc: kc * 128
    ctx.push_scope()
    kT_sb = ctx.sb("a5kT", (128, 4096), BF16)
    v_sb = ctx.sb("a5v", (128, 32 * 128), BF16)
    qT_sb = ctx.sb("a5qT", (128, 2048), BF16)
    sq_s = ctx.sb("a5sq", (128, 512), F32)
    Brow = ctx.sb("a5Br", (1, 512), F32)
    BoC = ctx.sb("a5Bc", (1, 512), F32)
    negBoC = ctx.sb("a5nB", (1, 512), F32)
    lnS = ctx.sb("a5ln", (1, 512), F32)
    negLoC = ctx.sb("a5nL", (1, 512), F32)
    onesr = ctx.sb("a5or", (1, 128), F32)
    onesc = ctx.sb("a5oc", (128, 1), F32)
    kmx = ctx.sb("a5km", (1, 8), F32)
    ksc = ctx.sb("a5ks", (1, 1), F32)
    P0 = [ctx.sb("a5P0%d" % i, (128, 512), F32) for i in range(2)]
    Pbf = [ctx.sb("a5Pb%d" % i, (128, 512), BF16) for i in range(2)]
    osb = ctx.sb("a5os", (128, 512), BF16)
    pS = [ctx.ps("a5pS%d" % i, (128, 512), F32) for i in range(2)]
    pSum = ctx.ps("a5pSm", (1, 512), F32)
    pO = ctx.ps("a5pO", (128, 512), F32)
    ctx.op("gpsimd", "memset", onesr[:], 1.0)
    ctx.op("gpsimd", "memset", onesc[:], 1.0)
    ctx.op("sync", "dma_start", kmx[:], kmax_all[:, :])
    ctx.op("vector", "reduce_max", ksc[:], kmx[:], axis=AX.X)
    ctx.op("vector", "tensor_scalar_mul", ksc[:], ksc[:], 1.0 / 128.0)

    for h in range(32):
        if rank_major:
            for r in range(2):
                ctx.op("sync", "dma_start", kT_sb[:, r * 2048:(r + 1) * 2048],
                       kT[r, h, :, :])
                for kg in range(2):
                    ctx.op("sync", "dma_start",
                           v_sb[:, r * 2048 + kg * 1024:
                                r * 2048 + (kg + 1) * 1024]
                           .rearrange("t (kc d) -> t kc d", d=128),
                           vpl[r, h, kg * 1024:(kg + 1) * 1024, :]
                           .rearrange("(kc t) d -> t kc d", t=128))
        else:
            ctx.op("sync", "dma_start", kT_sb[:], kT[h, :, :])
            for kg in range(4):
                ctx.op("sync", "dma_start",
                       v_sb[:, kg * 1024:(kg + 1) * 1024]
                       .rearrange("t (kc d) -> t kc d", d=128),
                       vpl[h, kg * 1024:(kg + 1) * 1024, :]
                       .rearrange("(kc t) d -> t kc d", t=128))
        ctx.op("sync", "dma_start", qT_sb[:], qT[h, :, :])
        for g in range(4):
            gq = slice(g * 512, (g + 1) * 512)
            qTg = qT_sb[:, gq]
            ctx.op("scalar", "activation", sq_s[:], qTg, AF.Square)
            ctx.op("tensor", "matmul", pSum[:], onesc[:], sq_s[:],
                   start=True, stop=True, inc=tsem.tick(1))
            ctx.op("scalar", "activation", Brow[:], pSum[:], AF.Sqrt,
                   scale=ksc[:])
            ctx.op("vector", "tensor_scalar_mul", BoC[:], Brow[:], 1.0 / CSC)
            ctx.op("vector", "tensor_scalar_mul", negBoC[:], BoC[:], -1.0)
            nkc = 8 * g + 8
            for phase in range(2):
                rowv = negBoC if phase == 0 else negLoC

                def s_mm(kc):
                    j0 = max(0, (kc - 8 * g) // 2)
                    cols = slice(j0 * 128, 512)
                    ko = kcoff(kc)
                    ctx.op("tensor", "matmul", pS[kc & 1][:, cols],
                           kT_sb[:, ko:ko + 128],
                           qT_sb[:, g * 512 + j0 * 128:(g + 1) * 512],
                           start=True, stop=False)
                    ctx.op("tensor", "matmul", pS[kc & 1][:, cols], onesr[:],
                           rowv[0:1, cols], start=False, stop=True,
                           inc=tsem.tick(1))
                    return cols

                def exp_mask(kc, cols):
                    dst = (P0 if phase == 0 else Pbf)[kc & 1]
                    j0 = cols.start // 128
                    ctx.op("scalar", "activation", dst[:, cols],
                           pS[kc & 1][:, cols], AF.Exp, scale=CSC)
                    for j in range(j0, 4):
                        lt = 4 * g + j
                        bsl = slice(j * 128, (j + 1) * 128)
                        mk = None
                        if kc == 2 * lt:
                            mk = "c_dmA" if phase == 0 else "c_dmAb"
                        elif kc == 2 * lt + 1:
                            mk = "c_dmB" if phase == 0 else "c_dmBb"
                        if mk is not None:
                            ctx.op("vector", "tensor_mul", dst[:, bsl],
                                   dst[:, bsl], consts[mk][:])
                    return dst

                def acc_mm(kc, cols, dst):
                    if phase == 0:
                        ctx.op("tensor", "matmul", pSum[0:1, cols], onesc[:],
                               dst[:, cols], start=(kc == 0),
                               stop=(kc == nkc - 1),
                               inc=(tsem.tick(1) if kc == nkc - 1 else None))
                    else:
                        ctx.op("tensor", "matmul", pO[:, cols],
                               v_sb[:, kcoff(kc):kcoff(kc) + 128],
                               dst[:, cols], start=(kc == 0),
                               stop=(kc == nkc - 1),
                               inc=(tsem.tick(1) if kc == nkc - 1 else None))

                pend = None
                for kc in range(nkc):
                    cols = s_mm(kc)
                    if pend is not None:
                        acc_mm(*pend)
                    dst = exp_mask(kc, cols)
                    pend = (kc, cols, dst)
                acc_mm(*pend)
                if phase == 0:
                    ctx.op("scalar", "activation", lnS[:], pSum[:], AF.Ln)
                    ctx.op("vector", "scalar_tensor_tensor", negLoC[:], lnS[:],
                           1.0 / CSC, BoC[:], op0=ALU.mult, op1=ALU.add)
                    ctx.op("vector", "tensor_scalar_mul", negLoC[:], negLoC[:],
                           -1.0)
            ctx.op("scalar", "activation", osb[:], pO[:], AF.Copy,
                   inc=sems["e"].tick(1))
            ctx.op("sync", "dma_start", oT_l1[h, :, gq], osb[:])
        a = ctx._autosem("sync", "d")
    ctx._need("sync", (a.h, a.n))
    ctx.pop_scope()


def build_L5(debug=False):
    nc = bass.Bass("TRN2", target_bir_lowering=False, debug=False)
    ctx = Ctx(nc)
    sems = make_sems(ctx)
    kind = "ExternalOutput" if debug else "Internal"
    x2_rows = nc.dram_tensor("x2_rows", [T_OWN, 4096], F32, kind="ExternalInput")
    qT = nc.dram_tensor("qT", [32, 128, T_OWN], BF16, kind="ExternalInput")
    kT = nc.dram_tensor("kT", [32, 128, 4096], BF16, kind="ExternalInput")
    vpl = nc.dram_tensor("vpl", [32, 4096, 128], BF16, kind="ExternalInput")
    kmax_all = nc.dram_tensor("kmax_all", [1, 8], F32, kind="ExternalInput")
    WoT = nc.dram_tensor("WoT", [4096, 4096], F32, kind="ExternalInput")
    bo_row = nc.dram_tensor("bo_row", [1, 4096], F32, kind="ExternalInput")
    Acat = nc.dram_tensor("Acat", [4096, 64], F32, kind="ExternalInput")
    Bcat = nc.dram_tensor("Bcat", [64, 4096], F32, kind="ExternalInput")
    WrT = nc.dram_tensor("WrT", [4096, 4], F32, kind="ExternalInput")
    oT_l1 = nc.dram_tensor("oT_l1", [32, 128, T_OWN], BF16, kind=kind)
    x3_rows = nc.dram_tensor("x3_rows", [T_OWN, 4096], F32, kind=kind)
    x4_out = nc.dram_tensor("x4_out", [T_OWN, 4096], F32, kind="ExternalOutput")
    with nc.Block() as block:
        consts = load_consts(ctx, L5_CONSTS, sems)
        emit_attn_l1(ctx, qT=qT, kT=kT, vpl=vpl, kmax_all=kmax_all,
                     oT_l1=oT_l1, sems=sems, consts=consts)
        ctx.push_scope()
        x2t = ctx.sb("l5x2", (128, 512), F32)
        out_sb = [ctx.sb("l5o%d" % i, (128, 512), F32) for i in range(2)]
        epar = [0]

        def epiO(tg, o0, n_o, psum, tval):
            par = epar[0]
            epar[0] = 1 - par
            rows = slice(tg * 128, (tg + 1) * 128)
            ctx.op("sync", "dma_start", x2t[:, 0:n_o], x2_rows[rows, o0:o0 + n_o])
            ctx.op("vector", "tensor_add", out_sb[par][:, 0:n_o],
                   psum[:, 0:n_o], x2t[:, 0:n_o], inc=sems["e"].tick(1))
            ctx.op("sync", "dma_start", x3_rows[rows, o0:o0 + n_o],
                   out_sb[par][:, 0:n_o])

        emit_proj(ctx, xT_planes=oT_l1, n_tok=T_OWN, w_dram=WoT, n_out=4096,
                  stats=None, sems=sems, consts=consts, bias_dram=bo_row,
                  epilogue=epiO, tag="L5o")
        ctx.pop_scope()
        stats = emit_stats(ctx, x3_rows, NT_OWN, sems, consts, tag="5")
        ctx.push_scope()
        acatb = ctx.sb("l5Ac", (128, 32 * 64), BF16)
        bcatb = ctx.sb("l5Bc", (64, 4096), BF16)
        wrtb = ctx.sb("l5Wr", (128, 32 * 4), BF16)
        onesc = ctx.sb("l5oc", (128, 1), BF16)
        wsA = ctx.sb("l5wA", (1, 64), BF16)
        wsR = ctx.sb("l5wR", (1, 4), BF16)
        x3t = ctx.sb("l5x3", (128, 4096), F32)
        x3b = ctx.sb("l5x3b", (128, 4096), BF16)
        x3T = ctx.sb("l5x3T", (128, 32 * 128), BF16)
        lgt = ctx.sb("l5lg", (128, 4), F32)
        mid = ctx.sb("l5md", (128, 64), F32)
        nm = ctx.sb("l5nm", (128, 1), F32)
        m2 = ctx.sb("l5m2", (128, 1), F32)
        eqm = ctx.sb("l5eq", (128, 4), F32)
        t4 = ctx.sb("l5t4", (128, 4), F32)
        gat = ctx.sb("l5ga", (128, 4), F32)
        sm = ctx.sb("l5sm", (128, 1), F32)
        rsc = ctx.sb("l5rs", (128, 1), F32)
        midg = ctx.sb("l5mg", (128, 64), BF16)
        mgT = ctx.sb("l5mT", (64, 128), BF16)
        osb2 = ctx.sb("l5ob", (128, 512), F32)
        pP = ctx.ps("l5pP", (128, 512), F32)
        pL = ctx.ps("l5pL", (128, 68), F32)
        pT = ctx.ps("l5pT", (128, 128), BF16)
        pW = ctx.ps("l5pW", (1, 68), F32)
        for fc in range(32):
            ctx.op("gpsimd", "dma_start", acatb[:, fc * 64:(fc + 1) * 64],
                   Acat[fc * 128:(fc + 1) * 128, :])
            ctx.op("gpsimd", "dma_start", wrtb[:, fc * 4:(fc + 1) * 4],
                   WrT[fc * 128:(fc + 1) * 128, :])
        ctx.op("gpsimd", "dma_start", bcatb[:], Bcat[:, :])
        ctx.op("gpsimd", "memset", onesc[:], 1.0)
        for fc in range(32):
            ctx.op("tensor", "matmul", pW[0:1, 0:64], onesc[:],
                   acatb[:, fc * 64:(fc + 1) * 64], start=(fc == 0), stop=False)
            ctx.op("tensor", "matmul", pW[0:1, 64:68], onesc[:],
                   wrtb[:, fc * 4:(fc + 1) * 4], start=False,
                   stop=(fc == 31), inc=(sems["t"].tick(1) if fc == 31 else None))
        ctx.op("vector", "tensor_copy", wsA[:], pW[0:1, 0:64])
        ctx.op("vector", "tensor_copy", wsR[:], pW[0:1, 64:68])

        for tg in range(NT_OWN):
            rows = slice(tg * 128, (tg + 1) * 128)
            ctx.op("sync", "dma_start", x3t[:], x3_rows[rows, :])
            ctx.op("vector", "tensor_copy", x3b[:], x3t[:])
            for fc in range(32):
                ctx.op("tensor", "transpose", pT[:],
                       x3b[:, fc * 128:(fc + 1) * 128], consts["c_idb"][:],
                       inc=sems["t"].tick(1))
                ctx.op("vector", "tensor_copy",
                       x3T[:, fc * 128:(fc + 1) * 128], pT[:])
            for fc in range(32):
                ctx.op("tensor", "matmul", pL[:, 0:64],
                       x3T[:, fc * 128:(fc + 1) * 128],
                       acatb[:, fc * 64:(fc + 1) * 64],
                       start=(fc == 0), stop=False)
                ctx.op("tensor", "matmul", pL[:, 64:68],
                       x3T[:, fc * 128:(fc + 1) * 128],
                       wrtb[:, fc * 4:(fc + 1) * 4], start=False, stop=False)
            ctx.op("tensor", "matmul", pL[:, 0:64],
                   stats["negm"][0:1, tg * 128:(tg + 1) * 128], wsA[:],
                   start=False, stop=False)
            ctx.op("tensor", "matmul", pL[:, 64:68],
                   stats["negm"][0:1, tg * 128:(tg + 1) * 128], wsR[:],
                   start=False, stop=True, inc=sems["t"].tick(1))
            ctx.op("scalar", "activation", mid[:], pL[:, 0:64], AF.Copy,
                   scale=stats["rs"][:, tg:tg + 1])
            ctx.op("scalar", "activation", lgt[:], pL[:, 64:68], AF.Copy,
                   scale=stats["rs"][:, tg:tg + 1])
            ctx.op("vector", "reduce_max", nm[:], lgt[:], axis=AX.X)
            ctx.op("vector", "tensor_scalar", eqm[:], lgt[:], nm[:], None,
                   op0=ALU.is_ge)
            ctx.op("vector", "scalar_tensor_tensor", t4[:], eqm[:], -1e30,
                   lgt[:], op0=ALU.mult, op1=ALU.add)
            ctx.op("vector", "reduce_max", m2[:], t4[:], axis=AX.X)
            ctx.op("vector", "tensor_scalar", eqm[:], lgt[:], m2[:], None,
                   op0=ALU.is_ge)
            ctx.op("vector", "tensor_scalar_mul", nm[:], nm[:], -1.0)
            ctx.op("scalar", "activation", gat[:], lgt[:], AF.Exp, bias=nm[:])
            ctx.op("vector", "tensor_mul", gat[:], gat[:], eqm[:])
            ctx.op("vector", "reduce_sum", sm[:], gat[:], axis=AX.X)
            ctx.op("vector", "reciprocal", rsc[:], sm[:])
            ctx.op("vector", "tensor_scalar_mul", gat[:], gat[:], rsc[:])
            for e in range(4):
                ctx.op("vector", "tensor_scalar", midg[:, e * 16:(e + 1) * 16],
                       mid[:, e * 16:(e + 1) * 16], gat[:, e:e + 1], 4.0,
                       op0=ALU.mult, op1=ALU.mult)
            ctx.op("tensor", "transpose", pT[0:64, 0:128], midg[:],
                   consts["c_idb"][:], inc=sems["t"].tick(1))
            ctx.op("vector", "tensor_copy", mgT[:], pT[0:64, 0:128])
            for oc in range(8):
                osl = slice(oc * 512, (oc + 1) * 512)
                ctx.op("tensor", "matmul", pP[:], mgT[:], bcatb[:, osl],
                       start=True, stop=True, inc=sems["t"].tick(1))
                ctx.op("vector", "tensor_add", osb2[:], pP[:], x3t[:, osl])
                ctx.op("sync", "dma_start", x4_out[rows, osl], osb2[:])
        a = ctx._autosem("sync", "d")
        ctx._need("sync", (a.h, a.n))
        ctx.pop_scope()
        ctx.finalize(block)
    ctx.st.close()
    return nc


def host_inputs_L5(inputs, core, x2_own, qkv_pair, kmax8):
    p = core % 2
    q_own, _, _ = qkv_pair[p]
    kT_full = np.zeros((32, 128, 4096), dtype=NBF)
    v_full = np.zeros((32, 4096, 128), dtype=NBF)
    for pp in range(2):
        own = own_rows(pp)
        _, k_pp, v_pp = qkv_pair[pp]
        for h in range(32):
            kT_full[h][:, own] = k_pp[:, h * 128:(h + 1) * 128].T
            v_full[h][own, :] = v_pp[:, h * 128:(h + 1) * 128]
    qTl = np.stack([np.ascontiguousarray(q_own[:, h * 128:(h + 1) * 128].T)
                    for h in range(32)])
    E, R = 4, 16
    Acat = inputs["L1_ffn_A"].transpose(2, 0, 1).reshape(4096, 64)
    Bcat = inputs["L1_ffn_B"].transpose(0, 2, 1).reshape(64, 4096)
    d = {"x2_rows": np.ascontiguousarray(x2_own),
         "qT": np.ascontiguousarray(qTl),
         "kT": np.ascontiguousarray(kT_full),
         "vpl": np.ascontiguousarray(v_full),
         "kmax_all": np.ascontiguousarray(np.array(kmax8, np.float32).reshape(1, 8)),
         "WoT": np.ascontiguousarray(inputs["L1_attn_Wo"].T),
         "bo_row": np.ascontiguousarray(inputs["L1_attn_bo"][None, :]),
         "Acat": np.ascontiguousarray(Acat),
         "Bcat": np.ascontiguousarray(Bcat),
         "WrT": np.ascontiguousarray(inputs["L1_ffn_Wr"].T)}
    d.update({k: v for k, v in host_consts(p).items() if k in L5_CONSTS})
    return d


_NC_CACHE = {}

def _get(name, fn):
    if name not in _NC_CACHE:
        _NC_CACHE[name] = fn()
    return _NC_CACHE[name]


def kernel(**inputs):
    return kernel_fused(**inputs)


def kernel_multilaunch(**inputs):
    inputs = {k: np.ascontiguousarray(np.asarray(v)) for k, v in inputs.items()}
    cores = list(range(NCORE))

    nc1 = _get("L1", build_L1)
    r1 = run_bass_kernel_spmd(nc1, [host_inputs_L1(inputs, c) for c in cores],
                              core_ids=cores)
    oT = [np.asarray(r1.results[c]["oT"]) for c in cores]

    nc2a = _get("L2a", build_L2a)
    r2a = run_bass_kernel_spmd(
        nc2a, [host_inputs_L2a(inputs, c, (oT[c // 2 * 2], oT[c // 2 * 2 + 1]))
               for c in cores], core_ids=cores)
    x1 = [np.asarray(r2a.results[c]["x1_rows"]) for c in cores]

    nc2b = _get("L2b", build_L2b)
    r2b = run_bass_kernel_spmd(
        nc2b, [host_inputs_L2b(inputs, c, x1[c]) for c in cores],
        core_ids=cores)
    scales8 = [np.asarray(r2b.results[c]["scales_loc"]) for c in cores]

    nc3 = _get("L3", build_L3)
    r3 = run_bass_kernel_spmd(
        nc3, [host_inputs_L3(inputs, c, x1[c],
                             np.asarray(r2b.results[c]["H_rows"]),
                             np.asarray(r2b.results[c]["z_rows"]),
                             np.asarray(r2b.results[c]["ds_rows"]), scales8)
              for c in cores], core_ids=cores)
    x2 = [np.asarray(r3.results[c]["x2_rows"]) for c in cores]

    nc4 = _get("L4", build_L4)
    r4 = run_bass_kernel_spmd(
        nc4, [host_inputs_L4(inputs, c, x2[c]) for c in cores],
        core_ids=cores)
    kmax8 = [float(np.asarray(r4.results[c]["kmax_loc"])[0, 0]) for c in cores]
    qkv = [(np.asarray(r4.results[c]["q_rows"]),
            np.asarray(r4.results[c]["k_rows"]),
            np.asarray(r4.results[c]["v_rows"])) for c in cores]

    nc5 = _get("L5", build_L5)
    r5 = run_bass_kernel_spmd(
        nc5, [host_inputs_L5(inputs, c, x2[c],
                             (qkv[c // 2 * 2], qkv[c // 2 * 2 + 1]), kmax8)
              for c in cores], core_ids=cores)

    out = np.empty((B, N, DM), np.float32)
    for c in cores:
        b, p = c // 2, c % 2
        out[b][own_rows(p)] = np.asarray(r5.results[c]["x4_out"])
    return out


def _emit_gating(ctx, sems, consts, x1_rows, H_rows, z_rows, ds_rows,
                 scales_all, LiM, Bm, x2_rows):
    sall = ctx.sb("g3sa", (1, 16), F32)
    s11 = ctx.sb("g3s1", (1, 2), F32)
    inv2 = ctx.sb("g3inv", (1, 2), F32)
    invc = ctx.sb("g3ic", (128, 2), F32)
    ones_r = ctx.sb("g3or", (1, 128), F32)
    limb = ctx.sb("g3lim", (8, 32), BF16)
    bmb = ctx.sb("g3bm", (32, 4096), BF16)
    pB = ctx.ps("g3pB", (128, 2), F32)
    ctx.op("sync", "dma_start", sall[:], scales_all[:, :])
    ctx.op("gpsimd", "dma_start", limb[:], LiM[:, :])
    ctx.op("gpsimd", "dma_start", bmb[:], Bm[:, :])
    ctx.op("gpsimd", "memset", ones_r[:], 1.0)
    s8x2 = ctx.sb("g3s8", (1, 16), F32)
    sv = sall.ap().rearrange("a (p two) -> a two p", two=2)
    ctx.op("vector", "tensor_copy",
           s8x2.ap().rearrange("a (two p) -> a two p", two=2), sv)
    ctx.op("vector", "reduce_max", s11[:, 0:1], s8x2[:, 0:8], axis=AX.X)
    ctx.op("vector", "reduce_max", s11[:, 1:2], s8x2[:, 8:16], axis=AX.X)
    ctx.op("vector", "tensor_scalar_max", s11[:], s11[:], 1e-6)
    ctx.op("vector", "reciprocal", inv2[:], s11[:])
    ctx.op("vector", "tensor_scalar_mul", inv2[:], inv2[:], 0.5)
    ctx.op("tensor", "matmul", pB[:], ones_r[:], inv2[:], start=True,
           stop=True, inc=sems["t"].tick(1))
    ctx.op("vector", "tensor_copy", invc[:], pB[:])

    Ht = ctx.sb("g3H", (128, 4096), F32)
    x1t = ctx.sb("g3x1", (128, 4096), F32)
    hs8 = ctx.sb("g3h8", (128, 8), F32)
    ds8 = ctx.sb("g3d8", (128, 8), F32)
    zt = ctx.sb("g3z", (128, 32), F32)
    lg = ctx.sb("g3lg", (128, 8), F32)
    t8 = ctx.sb("g3t8", (128, 8), F32)
    nm = ctx.sb("g3nm", (128, 1), F32)
    sm = ctx.sb("g3sm", (128, 1), F32)
    rsc = ctx.sb("g3rs", (128, 1), F32)
    probs = ctx.sb("g3pr", (128, 8), F32)
    eqm = ctx.sb("g3eq", (128, 8), F32)
    m2 = ctx.sb("g3m2", (128, 1), F32)
    sig = ctx.sb("g3sg", (128, 8), F32)
    wts = ctx.sb("g3wt", (128, 8), F32)
    wsh = ctx.sb("g3ws", (128, 8), F32)
    wshb = ctx.sb("g3wsb", (128, 8), BF16)
    wT = ctx.sb("g3wT", (8, 128), BF16)
    zpb = ctx.sb("g3zp", (128, 32), BF16)
    zpT = ctx.sb("g3zpT", (32, 128), BF16)
    osb = ctx.sb("g3ou", (128, 512), F32)
    pT8 = ctx.ps("g3pT", (8, 128), BF16)
    pM = ctx.ps("g3pM", (128, 32), F32)
    pT32 = ctx.ps("g3pT2", (32, 128), BF16)
    pDl = ctx.ps("g3pD", (128, 512), F32)
    SHUF = [i | 1 for i in range(32)]

    for tg in range(NT_OWN):
        rows = slice(tg * 128, (tg + 1) * 128)
        ctx.op("sync", "dma_start", hs8[:], H_rows[rows, 0:8])
        ctx.op("sync", "dma_start", ds8[:], ds_rows[rows, :])
        ctx.op("sync", "dma_start", zt[:], z_rows[rows, :])
        ctx.op("sync", "dma_start", x1t[:], x1_rows[rows, :])
        ctx.op("sync", "dma_start", Ht[:], H_rows[rows, :])
        ctx.op("vector", "tensor_scalar_mul", t8[:], ds8[:], invc[:, 1:2])
        ctx.op("vector", "scalar_tensor_tensor", lg[:], hs8[:],
               invc[:, 0:1], t8[:], op0=ALU.mult, op1=ALU.add)
        ctx.op("vector", "reduce_max", nm[:], lg[:], axis=AX.X, negate=True)
        ctx.op("scalar", "activation", probs[:], lg[:], AF.Exp,
               bias=nm[:], scale=1.0, accum_out=sm[:])
        ctx.op("vector", "reciprocal", rsc[:], sm[:])
        ctx.op("vector", "tensor_scalar_mul", probs[:], probs[:], rsc[:])
        ctx.op("vector", "reduce_max", nm[:], probs[:], axis=AX.X)
        ctx.op("vector", "tensor_scalar", eqm[:], probs[:], nm[:], None,
               op0=ALU.is_ge)
        ctx.op("vector", "scalar_tensor_tensor", t8[:], eqm[:], -1e30,
               probs[:], op0=ALU.mult, op1=ALU.add)
        ctx.op("vector", "reduce_max", m2[:], t8[:], axis=AX.X)
        ctx.op("vector", "tensor_scalar_mul", m2[:], m2[:], -2.0)
        ctx.op("scalar", "activation", sig[:], probs[:], AF.Sigmoid,
               bias=m2[:], scale=2.0)
        ctx.op("vector", "tensor_mul", wts[:], probs[:], sig[:])
        ctx.op("vector", "reduce_sum", sm[:], wts[:], axis=AX.X)
        ctx.op("vector", "tensor_scalar_add", sm[:], sm[:], 1e-9)
        ctx.op("vector", "reciprocal", rsc[:], sm[:])
        ctx.op("vector", "tensor_scalar_mul", wts[:], wts[:], rsc[:])
        ctx.op("vector", "stream_shuffle", wsh[:], wts[:], SHUF)
        ctx.op("vector", "tensor_copy", wshb[:], wsh[:])
        ctx.op("tensor", "transpose", pT8[:], wshb[:], consts["c_idb"][:],
               inc=sems["t"].tick(1))
        ctx.op("vector", "tensor_copy", wT[:], pT8[:])
        ctx.op("tensor", "matmul", pM[:], wT[:], limb[:], start=True,
               stop=True, inc=sems["t"].tick(1))
        ctx.op("vector", "tensor_mul", zpb[:], zt[:], pM[:])
        ctx.op("tensor", "transpose", pT32[:], zpb[:], consts["c_idb"][:],
               inc=sems["t"].tick(1))
        ctx.op("vector", "tensor_copy", zpT[:], pT32[:])
        for oc in range(8):
            osl = slice(oc * 512, (oc + 1) * 512)
            ctx.op("tensor", "matmul", pDl[:], zpT[:], bmb[:, osl],
                   start=True, stop=True, inc=sems["t"].tick(1))
            ctx.op("vector", "scalar_tensor_tensor", osb[:], pDl[:], 0.5,
                   Ht[:, osl], op0=ALU.mult, op1=ALU.add)
            ctx.op("vector", "tensor_add", osb[:], osb[:], x1t[:, osl])
            ctx.op("sync", "dma_start", x2_rows[rows, osl], osb[:])


def _emit_ffn1(ctx, sems, consts, stats, x3_rows, Acat, Bcat, WrT, x4_out):
    acatb = ctx.sb("d5Ac", (128, 32 * 64), BF16)
    bcatb = ctx.sb("d5Bc", (64, 4096), BF16)
    wrtb = ctx.sb("d5Wr", (128, 32 * 4), BF16)
    onesc = ctx.sb("d5oc", (128, 1), BF16)
    wsA = ctx.sb("d5wA", (1, 64), BF16)
    wsR = ctx.sb("d5wR", (1, 4), BF16)
    x3t = ctx.sb("d5x3", (128, 4096), F32)
    x3b = ctx.sb("d5x3b", (128, 4096), BF16)
    x3T = ctx.sb("d5x3T", (128, 32 * 128), BF16)
    lgt = ctx.sb("d5lg", (128, 4), F32)
    mid = ctx.sb("d5md", (128, 64), F32)
    nm = ctx.sb("d5nm", (128, 1), F32)
    m2 = ctx.sb("d5m2", (128, 1), F32)
    eqm = ctx.sb("d5eq", (128, 4), F32)
    t4 = ctx.sb("d5t4", (128, 4), F32)
    gat = ctx.sb("d5ga", (128, 4), F32)
    sm = ctx.sb("d5sm", (128, 1), F32)
    rsc = ctx.sb("d5rs", (128, 1), F32)
    midg = ctx.sb("d5mg", (128, 64), BF16)
    mgT = ctx.sb("d5mT", (64, 128), BF16)
    osb2 = ctx.sb("d5ob", (128, 512), F32)
    pP = ctx.ps("d5pP", (128, 512), F32)
    pL = ctx.ps("d5pL", (128, 68), F32)
    pT = ctx.ps("d5pT", (128, 128), BF16)
    pW = ctx.ps("d5pW", (1, 68), F32)
    for fc in range(32):
        ctx.op("gpsimd", "dma_start", acatb[:, fc * 64:(fc + 1) * 64],
               Acat[fc * 128:(fc + 1) * 128, :])
        ctx.op("gpsimd", "dma_start", wrtb[:, fc * 4:(fc + 1) * 4],
               WrT[fc * 128:(fc + 1) * 128, :])
    ctx.op("gpsimd", "dma_start", bcatb[:], Bcat[:, :])
    ctx.op("gpsimd", "memset", onesc[:], 1.0)
    for fc in range(32):
        ctx.op("tensor", "matmul", pW[0:1, 0:64], onesc[:],
               acatb[:, fc * 64:(fc + 1) * 64], start=(fc == 0), stop=False)
        ctx.op("tensor", "matmul", pW[0:1, 64:68], onesc[:],
               wrtb[:, fc * 4:(fc + 1) * 4], start=False,
               stop=(fc == 31), inc=(sems["t"].tick(1) if fc == 31 else None))
    ctx.op("vector", "tensor_copy", wsA[:], pW[0:1, 0:64])
    ctx.op("vector", "tensor_copy", wsR[:], pW[0:1, 64:68])

    for tg in range(NT_OWN):
        rows = slice(tg * 128, (tg + 1) * 128)
        ctx.op("sync", "dma_start", x3t[:], x3_rows[rows, :])
        ctx.op("vector", "tensor_copy", x3b[:], x3t[:])
        for fc in range(32):
            ctx.op("tensor", "transpose", pT[:],
                   x3b[:, fc * 128:(fc + 1) * 128], consts["c_idb"][:],
                   inc=sems["t"].tick(1))
            ctx.op("vector", "tensor_copy",
                   x3T[:, fc * 128:(fc + 1) * 128], pT[:])
        for fc in range(32):
            ctx.op("tensor", "matmul", pL[:, 0:64],
                   x3T[:, fc * 128:(fc + 1) * 128],
                   acatb[:, fc * 64:(fc + 1) * 64],
                   start=(fc == 0), stop=False)
            ctx.op("tensor", "matmul", pL[:, 64:68],
                   x3T[:, fc * 128:(fc + 1) * 128],
                   wrtb[:, fc * 4:(fc + 1) * 4], start=False, stop=False)
        ctx.op("tensor", "matmul", pL[:, 0:64],
               stats["negm"][0:1, tg * 128:(tg + 1) * 128], wsA[:],
               start=False, stop=False)
        ctx.op("tensor", "matmul", pL[:, 64:68],
               stats["negm"][0:1, tg * 128:(tg + 1) * 128], wsR[:],
               start=False, stop=True, inc=sems["t"].tick(1))
        ctx.op("scalar", "activation", mid[:], pL[:, 0:64], AF.Copy,
               scale=stats["rs"][:, tg:tg + 1])
        ctx.op("scalar", "activation", lgt[:], pL[:, 64:68], AF.Copy,
               scale=stats["rs"][:, tg:tg + 1])
        ctx.op("vector", "reduce_max", nm[:], lgt[:], axis=AX.X)
        ctx.op("vector", "tensor_scalar", eqm[:], lgt[:], nm[:], None,
               op0=ALU.is_ge)
        ctx.op("vector", "scalar_tensor_tensor", t4[:], eqm[:], -1e30,
               lgt[:], op0=ALU.mult, op1=ALU.add)
        ctx.op("vector", "reduce_max", m2[:], t4[:], axis=AX.X)
        ctx.op("vector", "tensor_scalar", eqm[:], lgt[:], m2[:], None,
               op0=ALU.is_ge)
        ctx.op("vector", "tensor_scalar_mul", nm[:], nm[:], -1.0)
        ctx.op("scalar", "activation", gat[:], lgt[:], AF.Exp, bias=nm[:])
        ctx.op("vector", "tensor_mul", gat[:], gat[:], eqm[:])
        ctx.op("vector", "reduce_sum", sm[:], gat[:], axis=AX.X)
        ctx.op("vector", "reciprocal", rsc[:], sm[:])
        ctx.op("vector", "tensor_scalar_mul", gat[:], gat[:], rsc[:])
        for e in range(4):
            ctx.op("vector", "tensor_scalar", midg[:, e * 16:(e + 1) * 16],
                   mid[:, e * 16:(e + 1) * 16], gat[:, e:e + 1], 4.0,
                   op0=ALU.mult, op1=ALU.mult)
        ctx.op("tensor", "transpose", pT[0:64, 0:128], midg[:],
               consts["c_idb"][:], inc=sems["t"].tick(1))
        ctx.op("vector", "tensor_copy", mgT[:], pT[0:64, 0:128])
        for oc in range(8):
            osl = slice(oc * 512, (oc + 1) * 512)
            ctx.op("tensor", "matmul", pP[:], mgT[:], bcatb[:, osl],
                   start=True, stop=True, inc=sems["t"].tick(1))
            ctx.op("vector", "tensor_add", osb2[:], pP[:], x3t[:, osl])
            ctx.op("sync", "dma_start", x4_out[rows, osl], osb2[:])


FUSED_CONSTS = ["c_idf", "c_idb", "c_U", "c_L", "c_cos", "c_sin", "c_swm",
                "c_swm0", "c_dmA", "c_dmB", "c_dmAb", "c_dmBb", "c_odd"]
RG_PAIR = [[0, 1], [2, 3], [4, 5], [6, 7]]
RG_ALL = [[0, 1, 2, 3, 4, 5, 6, 7]]


def emit_transpose_planes(ctx, rows_dram, planes_bf, n_tiles, sems, consts, tag):
    ctx.push_scope()
    xt = ctx.sb("tp_x" + tag, (128, 4096), F32)
    xb = ctx.sb("tp_b" + tag, (128, 4096), BF16)
    stg = ctx.sb("tp_s" + tag, (128, 4096), BF16)
    pT = ctx.ps("tp_p" + tag, (128, 128), BF16)
    idb = consts["c_idb"]
    for tg in range(n_tiles):
        ctx.op("sync", "dma_start", xt[:], rows_dram[tg * 128:(tg + 1) * 128, :])
        ctx.op("vector", "tensor_copy", xb[:], xt[:])
        for fc in range(32):
            ctx.op("tensor", "transpose", pT[:], xb[:, fc * 128:(fc + 1) * 128],
                   idb[:], inc=sems["t"].tick(1))
            ctx.op("vector", "tensor_copy", stg[:, fc * 128:(fc + 1) * 128],
                   pT[:])
        for g8 in range(4):
            ctx.op("gpsimd", "dma_start",
                   planes_bf[g8 * 8:(g8 + 1) * 8, :, tg * 128:(tg + 1) * 128]
                   .rearrange("fc d t -> d fc t"),
                   stg[:, g8 * 1024:(g8 + 1) * 1024]
                   .rearrange("d (fc t) -> d fc t", t=128))
    ctx.pop_scope()


def build_fused(upto="full"):
    nc = bass.Bass("TRN2", target_bir_lowering=False, debug=False)
    ctx = Ctx(nc)
    sems = make_sems(ctx)
    EI = "ExternalInput"
    x0_rows = nc.dram_tensor("x0_rows", [4096, 4096], F32, kind=EI)
    x0Tp = nc.dram_tensor("x0Tp", [32, 128, 4096], F32, kind=EI)
    x0_own = nc.dram_tensor("x0_own", [T_OWN, 4096], F32, kind=EI)
    wqkv = nc.dram_tensor("wqkv", [4096, 3072], F32, kind=EI)
    Wo_sl = nc.dram_tensor("Wo_sl", [2048, 4096], F32, kind=EI)
    WT = nc.dram_tensor("WT", [4096, 4096], F32, kind=EI)
    b_row = nc.dram_tensor("b_row", [1, 4096], F32, kind=EI)
    A4 = nc.dram_tensor("A4", [4096, 32], F32, kind=EI)
    LiT = nc.dram_tensor("LiT", [32, 8], F32, kind=EI)
    LiM = nc.dram_tensor("LiM", [8, 32], F32, kind=EI)
    Bm = nc.dram_tensor("Bm", [32, 4096], F32, kind=EI)
    Wcat = nc.dram_tensor("Wcat", [4096, 12288], F32, kind=EI)
    bcat = nc.dram_tensor("bcat", [1, 12288], F32, kind=EI)
    WoT = nc.dram_tensor("WoT", [4096, 4096], F32, kind=EI)
    bo_row = nc.dram_tensor("bo_row", [1, 4096], F32, kind=EI)
    Acat = nc.dram_tensor("Acat", [4096, 64], F32, kind=EI)
    Bcat = nc.dram_tensor("Bcat", [64, 4096], F32, kind=EI)
    WrT = nc.dram_tensor("WrT", [4096, 4], F32, kind=EI)
    qf = nc.dram_tensor("qf", [4096, 2048], F32)
    kf = nc.dram_tensor("kf", [4096, 512], F32)
    vf = nc.dram_tensor("vf", [4096, 512], F32)
    oT_send = nc.dram_tensor("oT_send", [2, 16, 128, T_OWN], BF16)
    rs_in = nc.dram_tensor("rs_in", [4096, 4096], BF16)
    rs_out = nc.dram_tensor("rs_out", [T_OWN, 4096], BF16)
    x1_rows = nc.dram_tensor("x1_rows", [T_OWN, 4096], F32)
    x1Tp = nc.dram_tensor("x1Tp", [32, 128, T_OWN], BF16)
    H_rows = nc.dram_tensor("H_rows", [T_OWN, 4096], F32)
    z_rows = nc.dram_tensor("z_rows", [T_OWN, 32], F32)
    ds_rows = nc.dram_tensor("ds_rows", [T_OWN, 8], F32)
    scales_loc = nc.dram_tensor("scales_loc", [1, 2], F32)
    scales_all = nc.dram_tensor("scales_all", [1, 16], F32)
    x2_rows = nc.dram_tensor("x2_rows", [T_OWN, 4096], F32)
    x2Tp = nc.dram_tensor("x2Tp", [32, 128, T_OWN], BF16)
    qT_loc = nc.dram_tensor("qT_loc", [32, 128, T_OWN], BF16)
    kT_loc = nc.dram_tensor("kT_loc", [32, 128, T_OWN], BF16)
    v_loc = nc.dram_tensor("v_loc", [32, T_OWN, 128], BF16)
    kT_rm = nc.dram_tensor("kT_rm", [2, 32, 128, T_OWN], BF16)
    v_rm = nc.dram_tensor("v_rm", [2, 32, T_OWN, 128], BF16)
    kmax_loc = nc.dram_tensor("kmax_loc", [1, 1], F32)
    kmax_all = nc.dram_tensor("kmax_all", [1, 8], F32)
    oT_l1 = nc.dram_tensor("oT_l1", [32, 128, T_OWN], BF16)
    x3_rows = nc.dram_tensor("x3_rows", [T_OWN, 4096], F32)
    x4_out = nc.dram_tensor("x4_out", [T_OWN, 4096], F32, kind="ExternalOutput")

    def _stop_here(ctx_, name, src_dram, shape):
        t = ctx_.sb("dbgcp" + name, (128, 4096), F32)
        for tg in range(NT_OWN):
            rows = slice(tg * 128, (tg + 1) * 128)
            ctx_.op("sync", "dma_start", t[:], src_dram[rows, :])
            ctx_.op("sync", "dma_start", x4_out[rows, :], t[:])
        a = ctx_._autosem("sync", "d")
        ctx_._need("sync", (a.h, a.n))

    with nc.Block() as block:
        consts = load_consts(ctx, FUSED_CONSTS, sems)

        stats0 = emit_stats(ctx, x0_rows, 32, sems, consts, tag="f0")
        ctx.push_scope()
        out_sb = [ctx.sb("f1o%d" % i, (128, 512), F32) for i in range(2)]
        epar = [0]

        def epi1(tg, o0, n_o, psum, tval):
            par = epar[0]
            epar[0] = 1 - par
            ctx.op("scalar", "activation", out_sb[par][:, 0:n_o],
                   psum[:, 0:n_o], AF.Copy, scale=stats0["rs"][:, tg:tg + 1],
                   inc=sems["e"].tick(1))
            if o0 < 2048:
                dest = qf[tg * 128:(tg + 1) * 128, o0:o0 + n_o]
            elif o0 < 2560:
                dest = kf[tg * 128:(tg + 1) * 128, 0:n_o]
            else:
                dest = vf[tg * 128:(tg + 1) * 128, 0:n_o]
            ctx.op("sync", "dma_start", dest, out_sb[par][:, 0:n_o])

        emit_proj(ctx, xT_planes=x0Tp, n_tok=4096, w_dram=wqkv, n_out=3072,
                  stats=stats0, sems=sems, consts=consts, epilogue=epi1,
                  tag="f1")
        ctx.pop_scope()

        def oT_writer(c, oT_stage):
            r, i = c & 1, c >> 1
            for g in range(2):
                ctx.op("gpsimd", "dma_start",
                       oT_send[r, g * 8:(g + 1) * 8, :, i * 128:(i + 1) * 128]
                       .rearrange("h d t -> d h t"),
                       oT_stage[:, g * 1024:(g + 1) * 1024]
                       .rearrange("d (h t) -> d h t", t=128))

        emit_mixer(ctx, qf=qf, kf=kf, vf=vf, oT_out=None, sems=sems,
                   consts=consts, tag="f", oT_writer=oT_writer)

        ctx.push_scope()
        out2 = [ctx.sb("f2o%d" % i, (128, 512), BF16) for i in range(2)]
        epar2 = [0]

        def epi2(tg, o0, n_o, psum, tval):
            par = epar2[0]
            epar2[0] = 1 - par
            rows = slice(tg * 128, (tg + 1) * 128)
            ctx.op("scalar", "activation", out2[par][:, 0:n_o],
                   psum[:, 0:n_o], AF.Copy, inc=sems["e"].tick(1))
            ctx.op("sync", "dma_start", rs_in[rows, o0:o0 + n_o],
                   out2[par][:, 0:n_o])

        def oT_loader(c2, xp, tb):
            hf, toff = divmod(tb * 512, 2048)
            for g in range(2):
                f0 = g * 8
                c2.op("gpsimd", "dma_start",
                      xp[:, f0 * 512:(f0 + 8) * 512]
                      .rearrange("p (fc t) -> p fc t", t=512),
                      oT_send[hf, f0:f0 + 8, :, toff:toff + 512]
                      .rearrange("fc p t -> p fc t"))

        emit_proj(ctx, xT_planes=None, n_tok=4096, w_dram=Wo_sl, n_out=4096,
                  stats=None, sems=sems, consts=consts, epilogue=epi2,
                  tag="f2", nfc=16, x_loader=oT_loader)
        ctx.pop_scope()
        ctx.op("gpsimd", "collective_compute", "ReduceScatter", ALU.add,
               replica_groups=RG_PAIR, ins=[rs_in.ap().opt()],
               outs=[rs_out.ap().opt()])
        ctx.push_scope()
        x0t = ctx.sb("f2x0", (128, 4096), F32)
        rst = ctx.sb("f2rs", (128, 4096), BF16)
        x1s = ctx.sb("f2x1", (128, 4096), F32)
        for tg in range(NT_OWN):
            rows = slice(tg * 128, (tg + 1) * 128)
            ctx.op("sync", "dma_start", x0t[:], x0_own[rows, :])
            ctx.op("sync", "dma_start", rst[:], rs_out[rows, :])
            ctx.op("vector", "tensor_add", x1s[:], x0t[:], rst[:])
            ctx.op("sync", "dma_start", x1_rows[rows, :], x1s[:])
        ctx.pop_scope()

        if upto == "x1":
            _stop_here(ctx, "x1", x1_rows, None)
            ctx.finalize(block)
            ctx.st.close()
            return nc

        emit_transpose_planes(ctx, x1_rows, x1Tp, NT_OWN, sems, consts, "f1T")
        statsB = emit_stats(ctx, x1_rows, NT_OWN, sems, consts, tag="fB")
        ctx.push_scope()
        hmax = ctx.sb("fbhm", (128, 1), F32)
        dmax = ctx.sb("fbdm", (128, 1), F32)
        tmp8 = ctx.sb("fbt8", (128, 8), F32)
        m8 = ctx.sb("fbm8", (128, 1), F32)
        ctx.op("gpsimd", "memset", hmax[:], 0.0)
        ctx.op("gpsimd", "memset", dmax[:], 0.0)
        outH = [ctx.sb("fbo%d" % i, (128, 512), F32) for i in range(2)]
        eparH = [0]

        def epiH(tg, o0, n_o, psum, tval):
            par = eparH[0]
            eparH[0] = 1 - par
            rows = slice(tg * 128, (tg + 1) * 128)
            ctx.op("scalar", "activation", outH[par][:, 0:n_o],
                   psum[:, 0:n_o], AF.Copy, scale=statsB["rs"][:, tg:tg + 1],
                   inc=sems["e"].tick(1))
            ctx.op("sync", "dma_start", H_rows[rows, o0:o0 + n_o],
                   outH[par][:, 0:n_o])
            if o0 == 0:
                ctx.op("vector", "tensor_scalar_mul", tmp8[:],
                       outH[par][:, 0:8], consts["c_odd"][:])
                ctx.op("vector", "reduce_max", m8[:], tmp8[:], axis=AX.X,
                       apply_absolute_value=True)
                ctx.op("vector", "tensor_max", hmax[:], hmax[:], m8[:])

        emit_proj(ctx, xT_planes=x1Tp, n_tok=T_OWN, w_dram=WT, n_out=4096,
                  stats=statsB, sems=sems, consts=consts, bias_dram=b_row,
                  epilogue=epiH, tag="fbH")
        z_sb = ctx.sb("fbz", (128, 32), F32)

        def epiZ(tg, o0, n_o, psum, tval):
            rows = slice(tg * 128, (tg + 1) * 128)
            ctx.op("scalar", "activation", z_sb[:, 0:n_o], psum[:, 0:n_o],
                   AF.Copy, scale=statsB["rs"][:, tg:tg + 1],
                   inc=sems["e"].tick(1))
            ctx.op("sync", "dma_start", z_rows[rows, 0:n_o], z_sb[:, 0:n_o])

        emit_proj(ctx, xT_planes=x1Tp, n_tok=T_OWN, w_dram=A4, n_out=32,
                  stats=statsB, sems=sems, consts=consts, epilogue=epiZ,
                  tag="fbz")
        zt = ctx.sb("fbzt", (128, 32), F32)
        ztb = ctx.sb("fbztb", (128, 32), BF16)
        zTb = ctx.sb("fbzT", (32, 128), BF16)
        litb = ctx.sb("fblit", (32, 8), BF16)
        ds_sb = ctx.sb("fbds", (128, 8), F32)
        pTz = ctx.ps("fbpT", (32, 128), BF16)
        pDz = ctx.ps("fbpD", (128, 8), F32)
        ctx.op("gpsimd", "dma_start", litb[:], LiT[:, :])
        for tg in range(NT_OWN):
            rows = slice(tg * 128, (tg + 1) * 128)
            ctx.op("sync", "dma_start", zt[:], z_rows[rows, :])
            ctx.op("vector", "tensor_copy", ztb[:], zt[:])
            ctx.op("tensor", "transpose", pTz[:], ztb[:], consts["c_idb"][:],
                   inc=sems["t"].tick(1))
            ctx.op("vector", "tensor_copy", zTb[:], pTz[:])
            ctx.op("tensor", "matmul", pDz[:], zTb[:], litb[:], start=True,
                   stop=True, inc=sems["t"].tick(1))
            ctx.op("vector", "tensor_copy", ds_sb[:], pDz[:])
            ctx.op("sync", "dma_start", ds_rows[rows, :], ds_sb[:])
            ctx.op("vector", "tensor_scalar_mul", tmp8[:], ds_sb[:],
                   consts["c_odd"][:])
            ctx.op("vector", "reduce_max", m8[:], tmp8[:], axis=AX.X,
                   apply_absolute_value=True)
            ctx.op("vector", "tensor_max", dmax[:], dmax[:], m8[:])
        pRs = ctx.ps("fbpR", (1, 128), F32)
        srow = ctx.sb("fbsr", (1, 128), F32)
        sc2 = ctx.sb("fbsc", (1, 2), F32)
        for i, acc in enumerate((hmax, dmax)):
            ctx.op("tensor", "transpose", pRs[:], acc[:], consts["c_idf"][:],
                   inc=sems["t"].tick(1))
            ctx.op("vector", "tensor_copy", srow[:], pRs[:])
            ctx.op("vector", "reduce_max", sc2[:, i:i + 1], srow[:], axis=AX.X)
        ctx.op("sync", "dma_start", scales_loc[:, :], sc2[:])
        ctx.pop_scope()

        ctx.op("gpsimd", "collective_compute", "AllGather", ALU.bypass,
               replica_groups=RG_ALL, ins=[scales_loc.ap().opt()],
               outs=[scales_all.ap().opt()])

        ctx.push_scope()
        _emit_gating(ctx, sems, consts, x1_rows, H_rows, z_rows, ds_rows,
                     scales_all, LiM, Bm, x2_rows)
        ctx.pop_scope()

        if upto == "x2":
            _stop_here(ctx, "x2", x2_rows, None)
            ctx.finalize(block)
            ctx.st.close()
            return nc

        emit_transpose_planes(ctx, x2_rows, x2Tp, NT_OWN, sems, consts, "f2T")
        statsC = emit_stats(ctx, x2_rows, NT_OWN, sems, consts, tag="fC")
        ctx.push_scope()
        outq = [ctx.sb("fco%d" % i, (128, 512), BF16) for i in range(2)]
        stgq = ctx.sb("fcst", (128, 128), BF16)
        sqs = ctx.sb("fcsq", (128, 128), F32)
        kn = ctx.sb("fckn", (128, 1), F32)
        knacc = ctx.sb("fcka", (128, 1), F32)
        pTq = ctx.ps("fcpT", (128, 128), BF16)
        ctx.op("gpsimd", "memset", knacc[:], 0.0)
        eparq = [0]

        def epiQ(tg, o0, n_o, psum, tval):
            par = eparq[0]
            eparq[0] = 1 - par
            rows = slice(tg * 128, (tg + 1) * 128)
            ctx.op("scalar", "activation", outq[par][:, 0:n_o],
                   psum[:, 0:n_o], AF.Copy, scale=statsC["rs"][:, tg:tg + 1],
                   inc=sems["e"].tick(1))
            which, off = divmod(o0, 4096)
            for j in range(4):
                h = off // 128 + j
                sl = slice(j * 128, (j + 1) * 128)
                if which < 2:
                    dst = (qT_loc, kT_loc)[which]
                    ctx.op("tensor", "transpose", pTq[:], outq[par][:, sl],
                           consts["c_idb"][:], inc=sems["t"].tick(1))
                    ctx.op("vector", "tensor_copy", stgq[:], pTq[:])
                    ctx.op("sync", "dma_start",
                           dst[h, :, tg * 128:(tg + 1) * 128], stgq[:])
                else:
                    ctx.op("sync", "dma_start",
                           v_loc[h, tg * 128:(tg + 1) * 128, :],
                           outq[par][:, sl])
                if which == 1:
                    ctx.op("scalar", "activation", sqs[:], outq[par][:, sl],
                           AF.Square, accum_out=kn[:])
                    ctx.op("vector", "tensor_max", knacc[:], knacc[:], kn[:])

        emit_proj(ctx, xT_planes=x2Tp, n_tok=T_OWN, w_dram=Wcat, n_out=12288,
                  stats=statsC, sems=sems, consts=consts, bias_dram=bcat,
                  epilogue=epiQ, tag="fc")
        pRk = ctx.ps("fcpR", (1, 128), F32)
        srk = ctx.sb("fcsr", (1, 128), F32)
        km1 = ctx.sb("fckm", (1, 1), F32)
        ctx.op("tensor", "transpose", pRk[:], knacc[:], consts["c_idf"][:],
               inc=sems["t"].tick(1))
        ctx.op("vector", "tensor_copy", srk[:], pRk[:])
        ctx.op("vector", "reduce_max", km1[:], srk[:], axis=AX.X)
        ctx.op("sync", "dma_start", kmax_loc[:, :], km1[:])
        ctx.pop_scope()

        ctx.op("gpsimd", "collective_compute", "AllGather", ALU.bypass,
               replica_groups=RG_ALL, ins=[kmax_loc.ap().opt()],
               outs=[kmax_all.ap().opt()])
        ctx.op("gpsimd", "collective_compute", "AllGather", ALU.bypass,
               replica_groups=RG_PAIR, ins=[kT_loc.ap().opt()],
               outs=[kT_rm.ap().opt()])
        ctx.op("gpsimd", "collective_compute", "AllGather", ALU.bypass,
               replica_groups=RG_PAIR, ins=[v_loc.ap().opt()],
               outs=[v_rm.ap().opt()])

        if upto == "qkv":
            _stop_here(ctx, "qkv", x2_rows, None)
            ctx.finalize(block)
            ctx.st.close()
            return nc

        emit_attn_l1(ctx, qT=qT_loc, kT=kT_rm, vpl=v_rm, kmax_all=kmax_all,
                     oT_l1=oT_l1, sems=sems, consts=consts, rank_major=True)

        if upto == "attn":
            _stop_here(ctx, "attn", x2_rows, None)
            ctx.finalize(block)
            ctx.st.close()
            return nc

        ctx.push_scope()
        x2t = ctx.sb("f5x2", (128, 512), F32)
        out5 = [ctx.sb("f5o%d" % i, (128, 512), F32) for i in range(2)]
        epar5 = [0]

        def epi5(tg, o0, n_o, psum, tval):
            par = epar5[0]
            epar5[0] = 1 - par
            rows = slice(tg * 128, (tg + 1) * 128)
            ctx.op("sync", "dma_start", x2t[:, 0:n_o], x2_rows[rows, o0:o0 + n_o])
            ctx.op("vector", "tensor_add", out5[par][:, 0:n_o],
                   psum[:, 0:n_o], x2t[:, 0:n_o], inc=sems["e"].tick(1))
            ctx.op("sync", "dma_start", x3_rows[rows, o0:o0 + n_o],
                   out5[par][:, 0:n_o])

        emit_proj(ctx, xT_planes=oT_l1, n_tok=T_OWN, w_dram=WoT, n_out=4096,
                  stats=None, sems=sems, consts=consts, bias_dram=bo_row,
                  epilogue=epi5, tag="f5o")
        ctx.pop_scope()

        if upto == "x3":
            _stop_here(ctx, "x3", x3_rows, None)
            ctx.finalize(block)
            ctx.st.close()
            return nc

        statsD = emit_stats(ctx, x3_rows, NT_OWN, sems, consts, tag="fD")
        ctx.push_scope()
        _emit_ffn1(ctx, sems, consts, statsD, x3_rows, Acat, Bcat, WrT, x4_out)
        a = ctx._autosem("sync", "d")
        ctx._need("sync", (a.h, a.n))
        ctx.pop_scope()
        ctx.finalize(block)
    ctx.st.close()
    return nc


def host_inputs_fused(inputs, core):
    b, p = core // 2, core % 2
    x0 = np.ascontiguousarray(inputs["L0_attn_hidden_states"][b])
    hs = slice(p * 2048, (p + 1) * 2048)
    ks = slice(p * 512, (p + 1) * 512)
    wqkv = np.concatenate([inputs["L0_attn_Wq"][:, hs],
                           inputs["L0_attn_Wk"][:, ks],
                           inputs["L0_attn_Wv"][:, ks]], axis=1)
    Wcat = np.concatenate([inputs["L1_attn_Wq"].T, inputs["L1_attn_Wk"].T,
                           inputs["L1_attn_Wv"].T], axis=1)
    bcat = np.concatenate([inputs["L1_attn_bq"], inputs["L1_attn_bk"],
                           inputs["L1_attn_bv"]])[None, :]
    d = {"x0_rows": x0,
         "x0Tp": np.ascontiguousarray(x0.T.reshape(32, 128, 4096)),
         "x0_own": np.ascontiguousarray(x0[own_rows(p)]),
         "wqkv": np.ascontiguousarray(wqkv),
         "Wo_sl": np.ascontiguousarray(
             inputs["L0_attn_Wo"][p * 2048:(p + 1) * 2048, :]),
         "WT": np.ascontiguousarray(inputs["L0_ffn_W"].T),
         "b_row": np.ascontiguousarray(inputs["L0_ffn_b"][None, :]),
         "A4": inputs["L0_ffn_A"],
         "LiT": np.ascontiguousarray(inputs["L0_ffn_LiMEs"].T),
         "LiM": inputs["L0_ffn_LiMEs"],
         "Bm": inputs["L0_ffn_Bmat"],
         "Wcat": np.ascontiguousarray(Wcat),
         "bcat": np.ascontiguousarray(bcat),
         "WoT": np.ascontiguousarray(inputs["L1_attn_Wo"].T),
         "bo_row": np.ascontiguousarray(inputs["L1_attn_bo"][None, :]),
         "Acat": np.ascontiguousarray(
             inputs["L1_ffn_A"].transpose(2, 0, 1).reshape(4096, 64)),
         "Bcat": np.ascontiguousarray(
             inputs["L1_ffn_B"].transpose(0, 2, 1).reshape(64, 4096)),
         "WrT": np.ascontiguousarray(inputs["L1_ffn_Wr"].T)}
    d.update({k: v for k, v in host_consts(p).items() if k in FUSED_CONSTS})
    return d


def kernel_fused(**inputs):
    inputs = {k: np.ascontiguousarray(np.asarray(v)) for k, v in inputs.items()}
    cores = list(range(NCORE))
    nc = _get("FUSED", build_fused)
    res = run_bass_kernel_spmd(nc, [host_inputs_fused(inputs, c)
                                    for c in cores], core_ids=cores)
    out = np.empty((B, N, DM), np.float32)
    for c in cores:
        out[c // 2][own_rows(c % 2)] = np.asarray(res.results[c]["x4_out"])
    return out


# version 05
# speedup vs baseline: 1.2906x; 1.1431x over previous
import numpy as np
import ml_dtypes
from contextlib import ExitStack
import concourse.bass as bass
import concourse.mybir as mybir
from concourse.bass_utils import run_bass_kernel_spmd

F32 = mybir.dt.float32
BF16 = mybir.dt.bfloat16
AF = mybir.ActivationFunctionType
ALU = mybir.AluOpType
AX = mybir.AxisListType
NBF = ml_dtypes.bfloat16

NCORE = 8
B, N, DM = 4, 4096, 4096
NT_OWN = 16
T_OWN = NT_OWN * 128
HD = 128
EPS_LN = 1e-5
CSC = float(1.0 / np.sqrt(128.0))

ENGINES = ("tensor", "vector", "scalar", "gpsimd", "sync")


def host_consts(parity):
    c = {}
    c["c_idf"] = np.eye(128, dtype=np.float32)
    c["c_idb"] = np.eye(128, dtype=np.float32).astype(NBF)
    c["c_U"] = np.triu(np.ones((128, 128), np.float32))
    c["c_L"] = np.tril(np.ones((128, 128), np.float32), -1)
    pos = np.arange(N, dtype=np.float64)
    half = np.arange(64, dtype=np.float64)
    inv = 1.0 / (10000.0 ** (2.0 * half / 128.0))
    fr = pos[:, None] * inv[None, :]
    emb = np.concatenate([fr, fr], -1)
    cos = np.cos(emb).astype(np.float32).reshape(32, 128, 128)
    sin = np.sin(emb).astype(np.float32).reshape(32, 128, 128)
    c["c_cos"] = np.ascontiguousarray(cos.transpose(1, 0, 2).reshape(128, 4096))
    c["c_sin"] = np.ascontiguousarray(sin.transpose(1, 0, 2).reshape(128, 4096))
    qi = np.arange(128)[:, None]
    ji = np.arange(256)[None, :]
    ok = (ji >= qi + 64) & (ji <= qi + 128)
    c["c_swm"] = np.where(ok, 0.0, -1e9).astype(np.float32)
    c["c_swm0"] = np.where(ok & (ji >= 128), 0.0, -1e9).astype(np.float32)
    tri_kq = (np.arange(128)[:, None] <= np.arange(128)[None, :])
    if parity == 0:
        mA, mB2 = tri_kq.astype(np.float32), np.zeros((128, 128), np.float32)
    else:
        mA, mB2 = np.ones((128, 128), np.float32), tri_kq.astype(np.float32)
    c["c_dmA"], c["c_dmB"] = mA, mB2
    c["c_dmAb"], c["c_dmBb"] = mA.astype(NBF), mB2.astype(NBF)
    c["c_odd"] = np.ascontiguousarray(
        (np.arange(128) % 2 == 1).astype(np.float32)[:, None])
    return c

CONST_SPECS = {
    "c_idf": ((128, 128), F32), "c_idb": ((128, 128), BF16),
    "c_U": ((128, 128), F32), "c_L": ((128, 128), F32),
    "c_cos": ((128, 4096), F32), "c_sin": ((128, 4096), F32),
    "c_swm": ((128, 256), F32), "c_swm0": ((128, 256), F32),
    "c_dmA": ((128, 128), F32), "c_dmB": ((128, 128), F32),
    "c_dmAb": ((128, 128), BF16), "c_dmBb": ((128, 128), BF16),
    "c_odd": ((128, 1), F32),
}


MAX_SEM_TRACK = {}

class Sem:
    def __init__(self, ctx, name):
        self.name = name
        self.h = ctx.st.enter_context(ctx.nc.semaphore(name))
        self.n = 0
        _SEM_VALUES[id(self.h)] = 0
    def tick(self, k):
        self.n += k
        _SEM_VALUES[id(self.h)] = self.n
        MAX_SEM_TRACK[self.name if hasattr(self, "name") else id(self)] = self.n
        return (self.h, k)

_KW_WRITE = {"accum_out"}
_KW_READ = {"bias", "scale", "alpha"}

def _ap_name(a):
    if isinstance(a, bass.AP):
        return a.tensor.name
    if hasattr(a, "ap") and hasattr(a, "dtype") and hasattr(a, "name"):
        return a.name
    return None


class Ctx:
    def __init__(self, nc):
        self.nc = nc
        self.st = ExitStack()
        self.scopes = [self.st]
        self.q = {e: [] for e in ENGINES}
        self.auto = {}
        self.qst = {}
        self.last_w = {}
        self.readers = {}
        self.known = {e: {} for e in ENGINES}

    def _autosem(self, eng, kind="c"):
        key = (eng, kind)
        if key not in self.auto:
            self.auto[key] = Sem(self, "auto_%s_%s" % (kind, eng))
        return self.auto[key]

    def push_scope(self):
        self.scopes.append(ExitStack())

    def pop_scope(self):
        if not hasattr(self, "_barsem"):
            self._barsem = Sem(self, "scopebar")
        self.barrier(self._barsem)
        self.scopes.pop().close()

    def sem(self, name):
        return Sem(self, name)

    def sb(self, name, shape, dt, persist=False):
        stk = self.scopes[0] if persist else self.scopes[-1]
        return stk.enter_context(self.nc.sbuf_tensor(name, list(shape), dt))

    def ps(self, name, shape, dt=F32):
        return self.scopes[-1].enter_context(
            self.nc.psum_tensor(name, list(shape), dt))

    def _need(self, eng, dep):
        if dep is None:
            return
        sem, val = dep
        if self.known[eng].get(id(sem), -1) >= val:
            return
        self.known[eng][id(sem)] = val
        self.q[eng].append(("wait", sem, val))

    def wait(self, eng, sem, val=None):
        v = sem.n if val is None else val
        self._need(eng, (sem.h, v))

    def op(self, eng, meth, *args, inc=None, **kw):
        writes, reads = [], []
        if meth == "collective_compute":
            for a in kw.get("ins", []):
                reads.append(_ap_name(a))
            for a in kw.get("outs", []):
                writes.append(_ap_name(a))
        else:
            first = True
            for a in args:
                nm = _ap_name(a)
                if nm is not None:
                    (writes if first else reads).append(nm)
                    first = False
                elif first and meth in ("dma_start",):
                    pass
            for k, a in kw.items():
                nm = _ap_name(a)
                if nm is None:
                    continue
                (writes if k in _KW_WRITE else reads).append(nm)
        for nm in reads:
            self._dep(eng, self.last_w.get(nm))
        for nm in writes:
            self._dep(eng, self.last_w.get(nm))
            for dep in self.readers.get(nm, ()):
                self._dep(eng, dep)
        async_op = meth in ("dma_start", "collective_compute")
        if async_op:
            kind = "cc" if meth == "collective_compute" else "d"
            s = self._autosem(eng, kind)
            k = 1 if meth == "collective_compute" else 16
            s.tick(k)
            inc2 = (s.h, None) if meth == "collective_compute" else (s.h, k)
            st = self._qstate(eng, kind)
            st["issued"] += 1
            stamp = ("pend", (eng, kind), st["issued"])
        else:
            if inc is None:
                s = self._autosem(eng)
                s.tick(1)
                inc2 = (s.h, 1)
                stamp = (s.h, s.n, eng)
            else:
                inc2 = inc
                stamp = (inc2[0], self._manual_val(inc2[0]), eng)
        for nm in writes:
            self.last_w[nm] = stamp
            self.readers[nm] = []
        for nm in reads:
            self.readers.setdefault(nm, []).append(stamp)
        self.q[eng].append(("op", meth, args, kw, inc2))

    def _qstate(self, eng, kind):
        key = (eng, kind)
        if key not in self.qst:
            self.qst[key] = {"issued": 0, "confirms": [],
                             "ready": Sem(self, "rdy_%s_%s" % (kind, eng))}
        return self.qst[key]

    def _confirm(self, qkey, idx):
        eng, kind = qkey
        st = self._qstate(eng, kind)
        for upto, val in st["confirms"]:
            if upto >= idx:
                return (st["ready"].h, val)
        a = self._autosem(eng, kind)
        self._need(eng, (a.h, a.n))
        t = st["ready"].tick(1)
        self.q[eng].append(("inc", t[0], t[1]))
        st["confirms"].append((st["issued"], st["ready"].n))
        return (st["ready"].h, st["ready"].n)

    def _dep(self, eng, dep):
        if dep is None:
            return
        if dep[0] == "pend":
            _, qkey, idx = dep
            h, v = self._confirm(qkey, idx)
            self._need(eng, (h, v))
            return
        if eng == "tensor" and dep[2] == "tensor":
            return
        self._need(eng, (dep[0], dep[1]))

    def _manual_val(self, handle):
        return _SEM_VALUES[id(handle)]

    def barrier(self, bsem, engines=ENGINES):
        for e in engines:
            t = bsem.tick(1)
            self.q[e].append(("inc", t[0], t[1]))
        for e in engines:
            self.wait(e, bsem)

    def finalize(self, block):
        def runner(items):
            def run(eng):
                for it in items:
                    if it[0] == "op":
                        _, meth, args, kw, inc = it
                        r = getattr(eng, meth)(*args, **kw)
                        if inc is not None:
                            if inc[1] is None:
                                r.then_inc(inc[0])
                            else:
                                r.then_inc(inc[0], inc[1])
                    elif it[0] == "wait":
                        eng.wait_ge(it[1], it[2])
                    elif it[0] == "inc":
                        eng.sem_inc(it[1], it[2])
            return run
        for e in ENGINES:
            getattr(block, e)(runner(self.q[e]))


_SEM_VALUES = {}


def make_sems(ctx):
    return {"d": ctx.sem("dsem"), "t": ctx.sem("tsem"), "v": ctx.sem("vsem"),
            "a": ctx.sem("asem"), "e": ctx.sem("esem"), "s": ctx.sem("ssem"),
            "bar": ctx.sem("barsem")}


def load_consts(ctx, names, sems):
    nc = ctx.nc
    out = {}
    for name in names:
        shape, dt = CONST_SPECS[name]
        d = nc.dram_tensor(name, list(shape), dt, kind="ExternalInput")
        t = ctx.sb("s" + name, shape, dt)
        ctx.op("sync", "dma_start", t[:], d[:, :])
        out[name] = t
    return out


def emit_stats(ctx, x_rows, n_tiles, sems, consts, tag=""):
    dsem, tsem, vsem, asem = sems["d"], sems["t"], sems["v"], sems["a"]
    rs_cols = ctx.sb("strs" + tag, (128, n_tiles), F32)
    negm_rows = ctx.sb("stnm" + tag, (1, n_tiles * 128), BF16)
    std_rows = ctx.sb("stsd" + tag, (1, n_tiles * 128), BF16)
    ctx.push_scope()
    xt = ctx.sb("stx" + tag, (128, 4096), F32)
    sq = ctx.sb("stq" + tag, (128, 4096), F32)
    s1 = ctx.sb("st1" + tag, (128, 1), F32)
    ss = ctx.sb("st2" + tag, (128, 1), F32)
    mvar = ctx.sb("st3" + tag, (128, 1), F32)
    mstd = ctx.sb("st4" + tag, (128, 1), F32)
    mm_ = ctx.sb("st5" + tag, (128, 1), F32)
    msq = ctx.sb("st6" + tag, (128, 1), F32)
    pkf = ctx.sb("st7" + tag, (128, 2), F32)
    pk = ctx.sb("st8" + tag, (128, 2), BF16)
    rows_ev = ctx.sb("st9" + tag, (2, 128), BF16)
    pT = ctx.ps("stpT" + tag, (2, 128), BF16)
    idb = consts["c_idb"]

    for tt in range(n_tiles):
        if tt > 0:
            ctx.wait("sync", vsem)
            ctx.wait("sync", asem)
        ctx.op("sync", "dma_start", xt[:], x_rows[tt * 128:(tt + 1) * 128, :])
        ctx.wait("vector", dsem)
        ctx.op("vector", "reduce_sum", s1[:], xt[:], axis=AX.X, inc=vsem.tick(1))
        ctx.wait("scalar", dsem)
        ctx.op("scalar", "activation", sq[:], xt[:], AF.Square,
               accum_out=ss[:], inc=asem.tick(1))
        ctx.wait("vector", asem)
        ctx.op("vector", "tensor_scalar_mul", mm_[:], s1[:], 1.0 / 4096.0)
        ctx.op("vector", "tensor_mul", msq[:], mm_[:], mm_[:])
        ctx.op("vector", "scalar_tensor_tensor", mvar[:], ss[:], 1.0 / 4096.0,
               msq[:], op0=ALU.mult, op1=ALU.subtract)
        ctx.op("vector", "tensor_scalar_add", mvar[:], mvar[:], EPS_LN,
               inc=vsem.tick(1))
        ctx.wait("scalar", vsem)
        ctx.op("scalar", "activation", mstd[:], mvar[:], AF.Sqrt,
               inc=asem.tick(1))
        ctx.wait("vector", asem)
        ctx.op("vector", "reciprocal", rs_cols[:, tt:tt + 1], mstd[:])
        ctx.op("vector", "tensor_scalar_mul", pkf[:, 0:1], mm_[:], -1.0)
        ctx.op("vector", "tensor_copy", pkf[:, 1:2], mstd[:])
        ctx.op("vector", "tensor_copy", pk[:], pkf[:], inc=vsem.tick(1))
        ctx.wait("tensor", vsem)
        ctx.op("tensor", "transpose", pT[:], pk[:], idb[:], inc=tsem.tick(1))
        ctx.wait("vector", tsem)
        ctx.op("vector", "tensor_copy", rows_ev[:], pT[:], inc=vsem.tick(1))
        ctx.wait("gpsimd", vsem)
        ctx.op("gpsimd", "dma_start", negm_rows[:, tt * 128:(tt + 1) * 128],
               rows_ev[0:1, :])
        ctx.op("gpsimd", "dma_start", std_rows[:, tt * 128:(tt + 1) * 128],
               rows_ev[1:2, :])
    for e in ENGINES:
        ctx.wait(e, dsem)
        ctx.wait(e, vsem)
    ctx.pop_scope()
    return {"rs": rs_cols, "negm": negm_rows, "std": std_rows}


def emit_proj(ctx, *, xT_planes, n_tok, w_dram, n_out, stats, sems, consts,
              bias_dram=None, epilogue=None, tag="", nfc=32, x_loader=None):
    dsem, tsem, vsem, esem = sems["d"], sems["t"], sems["v"], sems["e"]
    ctx.push_scope()
    NP = (n_out + 511) // 512
    n_tb = n_tok // 512
    wpan = [ctx.sb("pjw%d" % i + tag, (128, nfc * 512), BF16) for i in range(2)]
    xtb = [ctx.sb("pjx%d" % i + tag, (128, nfc * 512), BF16) for i in range(2)]
    onesc = ctx.sb("pj1c" + tag, (128, 1), BF16)
    ones_row = ctx.sb("pj1r" + tag, (1, 128), BF16)
    wsum = [ctx.sb("pjws%d" % i + tag, (1, 512), BF16) for i in range(2)]
    brow = [ctx.sb("pjbr%d" % i + tag, (1, 512), BF16) for i in range(2)]
    pP = [ctx.ps("pjp%d" % i + tag, (128, 512), F32) for i in range(4)]
    pW = ctx.ps("pjpw" + tag, (1, 512), F32)
    gsem = ctx.sem("pjg" + tag)
    ctx.op("gpsimd", "memset", onesc[:], 1.0)
    ctx.op("gpsimd", "memset", ones_row[:], 1.0, inc=gsem.tick(1))
    fold = stats is not None
    ng = max(1, nfc // 8)

    def load_w(pan):
        wp = wpan[pan % 2]
        o0 = pan * 512
        n_o = min(512, n_out - o0)
        for g in range(ng):
            f0 = g * 8
            nf = min(8, nfc - f0)
            ctx.op("gpsimd", "dma_start",
                   wp[:, f0 * 512:f0 * 512 + nf * 512]
                   .rearrange("p (fc t) -> p fc t", t=512)[:, :, 0:n_o],
                   w_dram[f0 * 128:(f0 + nf) * 128, o0:o0 + n_o]
                   .rearrange("(fc p) t -> p fc t", p=128))
        if bias_dram is not None:
            ctx.op("gpsimd", "dma_start", brow[pan % 2][:, 0:n_o],
                   bias_dram[0:1, o0:o0 + n_o])

    def load_x(tb):
        xp = xtb[tb % 2]
        if x_loader is not None:
            x_loader(ctx, xp, tb)
            return
        for g in range(ng):
            f0 = g * 8
            nf = min(8, nfc - f0)
            ctx.op("gpsimd", "dma_start",
                   xp[:, f0 * 512:(f0 + nf) * 512]
                   .rearrange("p (fc t) -> p fc t", t=512),
                   xT_planes[f0:f0 + nf, :, tb * 512:(tb + 1) * 512]
                   .rearrange("fc p t -> p fc t"))

    load_w(0)
    for pan in range(NP):
        o0 = pan * 512
        n_o = min(512, n_out - o0)
        wp = wpan[pan % 2]
        if pan + 1 < NP:
            load_w(pan + 1)
        if fold:
            ctx.wait("tensor", gsem)
            for fc in range(nfc):
                ctx.op("tensor", "matmul", pW[0:1, 0:n_o], onesc[:],
                       wp[:, fc * 512:fc * 512 + n_o],
                       start=(fc == 0), stop=(fc == nfc - 1),
                       inc=(tsem.tick(1) if fc == nfc - 1 else None))
            ctx.op("vector", "tensor_copy", wsum[pan % 2][:, 0:n_o],
                   pW[0:1, 0:n_o])
        load_x(0)
        for tb in range(n_tb):
            xp = xtb[tb % 2]
            if tb + 1 < n_tb:
                load_x(tb + 1)
            for tt in range(4):
                tg = tb * 4 + tt
                par = tg % 4
                n_extra = (1 if fold else 0) + (1 if bias_dram is not None else 0)
                for fc in range(nfc):
                    ctx.op("tensor", "matmul", pP[par][:, 0:n_o],
                           xp[:, fc * 512 + tt * 128:fc * 512 + (tt + 1) * 128],
                           wp[:, fc * 512:fc * 512 + n_o],
                           start=(fc == 0),
                           stop=(fc == nfc - 1 and n_extra == 0),
                           inc=(tsem.tick(1)
                                if fc == nfc - 1 and n_extra == 0 else None))
                if fold:
                    lastf = bias_dram is None
                    ctx.op("tensor", "matmul", pP[par][:, 0:n_o],
                           stats["negm"][0:1, tg * 128:(tg + 1) * 128],
                           wsum[pan % 2][0:1, 0:n_o], start=False, stop=lastf,
                           inc=(tsem.tick(1) if lastf else None))
                if bias_dram is not None:
                    row = (stats["std"][0:1, tg * 128:(tg + 1) * 128]
                           if fold else ones_row[0:1, 0:128])
                    ctx.op("tensor", "matmul", pP[par][:, 0:n_o],
                           row, brow[pan % 2][0:1, 0:n_o], start=False,
                           stop=True, inc=tsem.tick(1))
                epilogue(tg, o0, n_o, pP[par], tsem.n)
    for e in ENGINES:
        ctx.wait(e, esem)
    ctx.pop_scope()


def emit_softmax_free(ctx, dst, src, tmp, sems, pre_scale=1.0):
    vsem, asem = sems["v"], sems["a"]
    ctx.op("vector", "reduce_max", tmp["nm"][:], src, axis=AX.X, negate=True,
           inc=vsem.tick(1))
    ctx.wait("scalar", vsem)
    ctx.wait("scalar", sems["t"])
    ctx.op("scalar", "activation", dst, src, AF.Exp, bias=tmp["nm"][:],
           scale=pre_scale, accum_out=tmp["sm"][:], inc=asem.tick(1))
    ctx.wait("vector", asem)
    ctx.op("vector", "reciprocal", tmp["rs"][:], tmp["sm"][:], inc=vsem.tick(1))
    ctx.op("vector", "tensor_scalar_mul", dst, dst, tmp["rs"][:],
           inc=vsem.tick(1))


def emit_rope(ctx, dst, src, cosc, sinc, tmp64, sems):
    vsem = sems["v"]
    ctx.op("vector", "tensor_mul", dst[:, 0:64], src[:, 0:64], cosc[:, 0:64])
    ctx.op("vector", "tensor_mul", tmp64[:], src[:, 64:128], sinc[:, 0:64])
    ctx.op("vector", "tensor_sub", dst[:, 0:64], dst[:, 0:64], tmp64[:])
    ctx.op("vector", "tensor_mul", dst[:, 64:128], src[:, 64:128], cosc[:, 64:128])
    ctx.op("vector", "tensor_mul", tmp64[:], src[:, 0:64], sinc[:, 64:128])
    ctx.op("vector", "tensor_add", dst[:, 64:128], dst[:, 64:128], tmp64[:],
           inc=vsem.tick(1))


def emit_mixer(ctx, *, qf, kf, vf, oT_out, sems, consts, tag="",
               oT_writer=None):
    dsem, tsem, vsem, asem = sems["d"], sems["t"], sems["v"], sems["a"]
    ctx.push_scope()
    bar = sems["bar"]
    idf, idb = consts["c_idf"], consts["c_idb"]
    cU, cL = consts["c_U"], consts["c_L"]
    ccos, csin = consts["c_cos"], consts["c_sin"]

    qc_t = ctx.sb("mxq" + tag, (128, 2048), F32)
    kc_t = ctx.sb("mxk" + tag, (128, 512), F32)
    vc_t = ctx.sb("mxv" + tag, (128, 512), F32)
    vb = [ctx.sb("mxvb%d" % i + tag, (128, 512), BF16) for i in range(2)]
    sA = ctx.sb("mxsA" + tag, (128, 128), F32)
    sB = ctx.sb("mxsB" + tag, (128, 128), F32)
    gs = ctx.sb("mxgs" + tag, (128, 128), F32)
    ks = ctx.sb("mxks" + tag, (128, 128), F32)
    nm = ctx.sb("mxnm" + tag, (128, 1), F32)
    sm = ctx.sb("mxsm" + tag, (128, 1), F32)
    rsc = ctx.sb("mxrs" + tag, (128, 1), F32)
    tmp = {"nm": nm, "sm": sm, "rs": rsc}
    tmp64 = ctx.sb("mxt64" + tag, (128, 64), F32)
    eGT = ctx.sb("mxeg" + tag, (128, 128), F32)
    enGT = ctx.sb("mxen" + tag, (128, 128), F32)
    eRv = ctx.sb("mxer" + tag, (128, 128), F32)
    kh_bf = ctx.sb("mxkh" + tag, (128, 128), BF16)
    kgT_bf = ctx.sb("mxkg" + tag, (128, 128), BF16)
    sk = ctx.sb("mxsk" + tag, (128, 128), F32)
    win = ctx.sb("mxwin" + tag, (128, 4 * 256), BF16)
    Sf = ctx.sb("mxSf" + tag, (128, 4 * 128), F32)
    Sb = ctx.sb("mxSb" + tag, (128, 4 * 128), BF16)
    qs = ctx.sb("mxqs" + tag, (128, 128), F32)
    qgT_bf = ctx.sb("mxqg" + tag, (128, 128), BF16)
    sq_ = ctx.sb("mxsq" + tag, (128, 128), F32)
    sqT_bf = ctx.sb("mxsqT" + tag, (128, 128), BF16)
    At_bf = ctx.sb("mxAt" + tag, (128, 128), BF16)
    olh = ctx.sb("mxol" + tag, (128, 128), F32)
    msk = ctx.sb("mxmsk" + tag, (128, 256), F32)
    Pt = ctx.sb("mxP" + tag, (128, 256), F32)
    P_bf = ctx.sb("mxPb" + tag, (128, 256), BF16)
    PT1 = ctx.sb("mxPT1" + tag, (128, 128), BF16)
    PT2 = ctx.sb("mxPT2" + tag, (128, 128), BF16)
    oT_stage = ctx.sb("mxoT" + tag, (128, 2048), BF16)

    pAux = ctx.ps("mpAux" + tag, (128, 512), F32)
    pG, pR = pAux[:, 0:128], pAux[:, 128:256]
    pA, pD = pAux[:, 256:384], pAux[:, 384:512]
    pTr = ctx.ps("mpT" + tag, (128, 128), F32)
    pO = ctx.ps("mpO" + tag, (128, 128), F32)
    pSO = ctx.ps("mpSO" + tag, (128, 512), F32)
    pS, pO2 = pSO[:, 0:256], pSO[:, 256:384]
    pPT = ctx.ps("mpPT" + tag, (128, 256), BF16)

    ctx.op("gpsimd", "memset", Sf[:], 0.0)
    ctx.op("gpsimd", "memset", Sb[:], 0.0)
    ctx.op("gpsimd", "memset", win[:], 0.0)
    ctx.op("gpsimd", "memset", vb[1][:], 0.0)

    oT2d = (oT_out.rearrange("d (h t) -> d h t", t=4096)
            if oT_writer is None else None)

    for c in range(32):
        par = c & 1
        csl = slice(c * 128, (c + 1) * 128)
        ctx.op("sync", "dma_start", qc_t[:], qf[csl, :])
        ctx.op("sync", "dma_start", kc_t[:], kf[csl, :])
        ctx.op("sync", "dma_start", vc_t[:], vf[csl, :])
        ctx.wait("vector", dsem)
        ctx.wait("scalar", dsem)
        ctx.wait("tensor", dsem)
        cosc = ccos[:, csl]
        sinc = csin[:, csl]
        for kvh in range(4):
            kc = kc_t[:, kvh * 128:(kvh + 1) * 128]
            vc = vc_t[:, kvh * 128:(kvh + 1) * 128]
            vbsl = vb[par][:, kvh * 128:(kvh + 1) * 128]
            wsl = win[:, kvh * 256:(kvh + 1) * 256]
            Ssl_f = Sf[:, kvh * 128:(kvh + 1) * 128]
            Ssl_b = Sb[:, kvh * 128:(kvh + 1) * 128]
            ctx.op("vector", "tensor_copy", vbsl, vc, inc=vsem.tick(1))
            ctx.op("scalar", "activation", sA[:], kc, AF.Abs)
            ctx.op("scalar", "activation", sB[:], sA[:], AF.Exp, scale=-1.0)
            ctx.op("scalar", "activation", sA[:], sB[:], AF.Ln, bias=1.0,
                   inc=asem.tick(1))
            ctx.op("vector", "tensor_scalar_min", sB[:], kc, 0.0)
            ctx.wait("vector", asem)
            ctx.op("vector", "tensor_sub", gs[:], sB[:], sA[:])
            ctx.op("vector", "tensor_scalar_mul", gs[:], gs[:], 1.0 / 16.0,
                   inc=vsem.tick(1))
            emit_softmax_free(ctx, ks[:], kc, tmp, sems)
            ctx.wait("tensor", vsem)
            ctx.op("tensor", "matmul", pG, gs[:], cU[:], start=True,
                   stop=True, inc=tsem.tick(1))
            ctx.op("tensor", "matmul", pR, cL[:], gs[:], start=True,
                   stop=True, inc=tsem.tick(1))
            ctx.wait("scalar", tsem)
            ctx.op("scalar", "activation", eGT[:], pG, AF.Exp)
            ctx.op("scalar", "activation", enGT[:], pG, AF.Exp, scale=-1.0)
            ctx.op("scalar", "activation", eRv[:], pR, AF.Exp,
                   inc=asem.tick(1))
            ctx.wait("vector", asem)
            ctx.op("vector", "tensor_mul", kh_bf[:], ks[:], eRv[:],
                   inc=vsem.tick(1))
            ctx.wait("tensor", asem)
            ctx.op("tensor", "transpose", pTr[:], ks[:], idf[:],
                   inc=tsem.tick(1))
            ctx.wait("vector", tsem)
            ctx.op("vector", "tensor_mul", kgT_bf[:], pTr[:], enGT[:],
                   inc=vsem.tick(1))
            emit_rope(ctx, sk[:], kc, cosc, sinc, tmp64, sems)
            ctx.op("vector", "tensor_copy", wsl[:, 0:128], wsl[:, 128:256])
            ctx.wait("tensor", vsem)
            ctx.op("tensor", "transpose", pTr[:], sk[:], idf[:],
                   inc=tsem.tick(1))
            ctx.wait("vector", tsem)
            ctx.op("vector", "tensor_copy", wsl[:, 128:256], pTr[:, 0:128],
                   inc=vsem.tick(1))
            dec = eGT[:, 127:128]

            for hq in range(4):
                h = kvh * 4 + hq
                qc = qc_t[:, h * 128:(h + 1) * 128]
                emit_softmax_free(ctx, qs[:], qc, tmp, sems)
                ctx.wait("tensor", vsem)
                ctx.op("tensor", "transpose", pTr[:], qs[:], idf[:],
                       inc=tsem.tick(1))
                ctx.wait("vector", tsem)
                ctx.op("vector", "tensor_mul", qgT_bf[:], pTr[:], eGT[:],
                       inc=vsem.tick(1))
                ctx.wait("tensor", vsem)
                ctx.op("tensor", "matmul", pA, kgT_bf[:], qgT_bf[:],
                       start=True, stop=True, inc=tsem.tick(1))
                ctx.wait("vector", tsem)
                ctx.op("vector", "tensor_mul", At_bf[:], pA, cU[:],
                       inc=vsem.tick(1))
                ctx.wait("tensor", vsem)
                ctx.op("tensor", "matmul", pO[:], Ssl_b, qgT_bf[:],
                       start=True, stop=False)
                ctx.op("tensor", "matmul", pO[:], vbsl, At_bf[:],
                       start=False, stop=True, inc=tsem.tick(1))
                ctx.wait("scalar", tsem)
                ctx.op("scalar", "activation", olh[:], pO[:], AF.Copy,
                       scale=0.5, inc=asem.tick(1))
                emit_rope(ctx, sq_[:], qc, cosc, sinc, tmp64, sems)
                ctx.wait("tensor", vsem)
                ctx.op("tensor", "transpose", pTr[:], sq_[:], idf[:],
                       inc=tsem.tick(1))
                ctx.wait("vector", tsem)
                ctx.op("vector", "tensor_copy", sqT_bf[:], pTr[:],
                       inc=vsem.tick(1))
                ctx.wait("tensor", vsem)
                ctx.op("tensor", "matmul", pS, sqT_bf[:], wsl[:, 0:256],
                       start=True, stop=True, inc=tsem.tick(1))
                ctx.wait("vector", tsem)
                ctx.wait("vector", asem)
                swm = consts["c_swm0"] if c == 0 else consts["c_swm"]
                ctx.op("vector", "tensor_add", msk[:], pS, swm[:])
                ctx.op("vector", "reduce_max", nm[:], msk[:], axis=AX.X)
                ctx.op("vector", "tensor_scalar_mul", nm[:], nm[:], -CSC,
                       inc=vsem.tick(1))
                ctx.wait("scalar", vsem)
                ctx.op("scalar", "activation", Pt[:], msk[:], AF.Exp,
                       bias=nm[:], scale=CSC, accum_out=sm[:],
                       inc=asem.tick(1))
                ctx.wait("vector", asem)
                ctx.op("vector", "reciprocal", rsc[:], sm[:])
                ctx.op("vector", "tensor_scalar_mul", P_bf[:], Pt[:], rsc[:],
                       inc=vsem.tick(1))
                ctx.wait("tensor", vsem)
                ctx.op("tensor", "transpose", pPT[:, 0:128], P_bf[:, 0:128],
                       idb[:], inc=tsem.tick(1))
                ctx.op("tensor", "transpose", pPT[:, 128:256],
                       P_bf[:, 128:256], idb[:], inc=tsem.tick(1))
                ctx.wait("vector", tsem)
                ctx.op("vector", "tensor_copy", PT1[:], pPT[:, 0:128])
                ctx.op("vector", "tensor_copy", PT2[:], pPT[:, 128:256],
                       inc=vsem.tick(1))
                ctx.wait("tensor", vsem)
                ctx.op("tensor", "matmul", pO2,
                       vb[1 - par][:, kvh * 128:(kvh + 1) * 128],
                       PT1[:], start=True, stop=False)
                ctx.op("tensor", "matmul", pO2,
                       vb[par][:, kvh * 128:(kvh + 1) * 128],
                       PT2[:], start=False, stop=True,
                       inc=tsem.tick(1))
                ctx.wait("vector", tsem)
                ctx.wait("vector", asem)
                ctx.op("vector", "scalar_tensor_tensor",
                       oT_stage[:, h * 128:(h + 1) * 128], pO2, 0.5,
                       olh[:], op0=ALU.mult, op1=ALU.add, inc=vsem.tick(1))
            ctx.wait("tensor", vsem)
            ctx.op("tensor", "matmul", pD, kh_bf[:], vbsl,
                   start=True, stop=True, inc=tsem.tick(1))
            ctx.wait("vector", tsem)
            ctx.op("vector", "scalar_tensor_tensor", Ssl_f, Ssl_f, dec,
                   pD, op0=ALU.mult, op1=ALU.add)
            ctx.op("vector", "tensor_copy", Ssl_b, Ssl_f, inc=vsem.tick(1))
        ctx.wait("gpsimd", vsem)
        if oT_writer is None:
            for g in range(2):
                ctx.op("gpsimd", "dma_start",
                       oT2d[:, g * 8:(g + 1) * 8, c * 128:(c + 1) * 128],
                       oT_stage[:, g * 1024:(g + 1) * 1024]
                       .rearrange("d (h t) -> d h t", t=128))
        else:
            oT_writer(c, oT_stage)
        ctx.barrier(bar)
    a = ctx._autosem("gpsimd", "d")
    ctx._need("gpsimd", (a.h, a.n))
    ctx.pop_scope()


L1_CONSTS = ["c_idf", "c_idb", "c_U", "c_L", "c_cos", "c_sin", "c_swm",
             "c_swm0"]

def build_L1(debug=False):
    nc = bass.Bass("TRN2", target_bir_lowering=False, debug=False)
    ctx = Ctx(nc)
    sems = make_sems(ctx)
    kind = "ExternalOutput" if debug else "Internal"
    x0_rows = nc.dram_tensor("x0_rows", [4096, 4096], F32, kind="ExternalInput")
    x0Tp = nc.dram_tensor("x0Tp", [32, 128, 4096], F32, kind="ExternalInput")
    wqkv = nc.dram_tensor("wqkv", [4096, 3072], F32, kind="ExternalInput")
    qf = nc.dram_tensor("qf", [4096, 2048], F32, kind=kind)
    kf = nc.dram_tensor("kf", [4096, 512], F32, kind=kind)
    vf = nc.dram_tensor("vf", [4096, 512], F32, kind=kind)
    oT_out = nc.dram_tensor("oT", [128, 16 * 4096], BF16, kind="ExternalOutput")

    with nc.Block() as block:
        consts = load_consts(ctx, L1_CONSTS, sems)
        stats = emit_stats(ctx, x0_rows, 32, sems, consts)
        ctx.barrier(sems["bar"])

        out_sb = [ctx.sb("l1o%d" % i, (128, 512), F32) for i in range(2)]
        epar = [0]

        def epi(tg, o0, n_o, psum, tval):
            par = epar[0]
            epar[0] = 1 - par
            ctx.op("scalar", "activation", out_sb[par][:, 0:n_o],
                   psum[:, 0:n_o], AF.Copy,
                   scale=stats["rs"][:, tg:tg + 1], inc=sems["e"].tick(1))
            if o0 < 2048:
                dest = qf[tg * 128:(tg + 1) * 128, o0:o0 + n_o]
            elif o0 < 2560:
                dest = kf[tg * 128:(tg + 1) * 128, 0:n_o]
            else:
                dest = vf[tg * 128:(tg + 1) * 128, 0:n_o]
            ctx.op("sync", "dma_start", dest, out_sb[par][:, 0:n_o])

        emit_proj(ctx, xT_planes=x0Tp, n_tok=4096, w_dram=wqkv, n_out=3072,
                  stats=stats, sems=sems, consts=consts, epilogue=epi,
                  tag="L1")
        ctx.barrier(sems["bar"])
        emit_mixer(ctx, qf=qf, kf=kf, vf=vf, oT_out=oT_out.ap(), sems=sems,
                   consts=consts)
        ctx.barrier(sems["bar"])
        ctx.finalize(block)
    ctx.st.close()
    return nc


def host_inputs_L1(inputs, core):
    b, p = core // 2, core % 2
    x0 = np.ascontiguousarray(inputs["L0_attn_hidden_states"][b])
    hs = slice(p * 2048, (p + 1) * 2048)
    ks = slice(p * 512, (p + 1) * 512)
    wqkv = np.concatenate([inputs["L0_attn_Wq"][:, hs],
                           inputs["L0_attn_Wk"][:, ks],
                           inputs["L0_attn_Wv"][:, ks]], axis=1)
    d = {"x0_rows": x0,
         "x0Tp": np.ascontiguousarray(x0.T.reshape(32, 128, 4096)),
         "wqkv": np.ascontiguousarray(wqkv)}
    d.update({k: v for k, v in host_consts(p).items() if k in L1_CONSTS})
    return d


def build_L2a():
    nc = bass.Bass("TRN2", target_bir_lowering=False, debug=False)
    ctx = Ctx(nc)
    sems = make_sems(ctx)
    x0_own = nc.dram_tensor("x0_own", [T_OWN, 4096], F32, kind="ExternalInput")
    oT_own = nc.dram_tensor("oT_own", [32, 128, T_OWN], BF16,
                            kind="ExternalInput")
    Wo = nc.dram_tensor("Wo", [4096, 4096], F32, kind="ExternalInput")
    x1_rows = nc.dram_tensor("x1_rows", [T_OWN, 4096], F32,
                             kind="ExternalOutput")
    with nc.Block() as block:
        x0t = ctx.sb("l2x0", (128, 512), F32)
        out_sb = [ctx.sb("l2o%d" % i, (128, 512), F32) for i in range(2)]
        epar = [0]

        def epi(tg, o0, n_o, psum, tval):
            par = epar[0]
            epar[0] = 1 - par
            rows = slice(tg * 128, (tg + 1) * 128)
            ctx.op("sync", "dma_start", x0t[:, 0:n_o], x0_own[rows, o0:o0 + n_o])
            ctx.op("vector", "tensor_add", out_sb[par][:, 0:n_o],
                   psum[:, 0:n_o], x0t[:, 0:n_o], inc=sems["e"].tick(1))
            ctx.op("sync", "dma_start", x1_rows[rows, o0:o0 + n_o],
                   out_sb[par][:, 0:n_o])

        emit_proj(ctx, xT_planes=oT_own, n_tok=T_OWN, w_dram=Wo, n_out=4096,
                  stats=None, sems=sems, consts=None, epilogue=epi, tag="L2a")
        a = ctx._autosem("sync", "d")
        ctx._need("sync", (a.h, a.n))
        ctx.finalize(block)
    ctx.st.close()
    return nc


def host_inputs_L2a(inputs, core, oT_pair):
    b, p = core // 2, core % 2
    own = own_rows(p)
    x0 = inputs["L0_attn_hidden_states"][b][own]
    o0 = np.asarray(oT_pair[0]).reshape(128, 16, 4096)
    o1 = np.asarray(oT_pair[1]).reshape(128, 16, 4096)
    full = np.concatenate([o0, o1], axis=1)
    oT_own = np.ascontiguousarray(full.transpose(1, 0, 2)[:, :, own])
    return {"x0_own": np.ascontiguousarray(x0), "oT_own": oT_own,
            "Wo": inputs["L0_attn_Wo"]}


def own_rows(p):
    idx = np.arange(T_OWN)
    tile = idx // 128
    return (2 * tile + p) * 128 + (idx % 128)


L2B_CONSTS = ["c_idf", "c_idb", "c_odd"]

def build_L2b():
    nc = bass.Bass("TRN2", target_bir_lowering=False, debug=False)
    ctx = Ctx(nc)
    sems = make_sems(ctx)
    x1_rows = nc.dram_tensor("x1_rows", [T_OWN, 4096], F32, kind="ExternalInput")
    x1Tp = nc.dram_tensor("x1Tp", [32, 128, T_OWN], F32, kind="ExternalInput")
    WT = nc.dram_tensor("WT", [4096, 4096], F32, kind="ExternalInput")
    b_row = nc.dram_tensor("b_row", [1, 4096], F32, kind="ExternalInput")
    A4 = nc.dram_tensor("A4", [4096, 32], F32, kind="ExternalInput")
    LiT = nc.dram_tensor("LiT", [32, 8], F32, kind="ExternalInput")
    H_rows = nc.dram_tensor("H_rows", [T_OWN, 4096], F32, kind="ExternalOutput")
    z_rows = nc.dram_tensor("z_rows", [T_OWN, 32], F32, kind="ExternalOutput")
    ds_rows = nc.dram_tensor("ds_rows", [T_OWN, 8], F32, kind="ExternalOutput")
    scales_loc = nc.dram_tensor("scales_loc", [1, 2], F32, kind="ExternalOutput")
    with nc.Block() as block:
        consts = load_consts(ctx, L2B_CONSTS, sems)
        stats = emit_stats(ctx, x1_rows, NT_OWN, sems, consts, tag="b")
        hmax = ctx.sb("l2hm", (128, 1), F32)
        dmax = ctx.sb("l2dm", (128, 1), F32)
        tmp8 = ctx.sb("l2t8", (128, 8), F32)
        m8 = ctx.sb("l2m8", (128, 1), F32)
        ctx.op("gpsimd", "memset", hmax[:], 0.0)
        ctx.op("gpsimd", "memset", dmax[:], 0.0)
        out_sb = [ctx.sb("l2bo%d" % i, (128, 512), F32) for i in range(2)]
        epar = [0]

        def epiH(tg, o0, n_o, psum, tval):
            par = epar[0]
            epar[0] = 1 - par
            rows = slice(tg * 128, (tg + 1) * 128)
            ctx.op("scalar", "activation", out_sb[par][:, 0:n_o],
                   psum[:, 0:n_o], AF.Copy, scale=stats["rs"][:, tg:tg + 1],
                   inc=sems["e"].tick(1))
            ctx.op("sync", "dma_start", H_rows[rows, o0:o0 + n_o],
                   out_sb[par][:, 0:n_o])
            if o0 == 0:
                ctx.op("vector", "tensor_scalar_mul", tmp8[:],
                       out_sb[par][:, 0:8], consts["c_odd"][:])
                ctx.op("vector", "reduce_max", m8[:], tmp8[:], axis=AX.X,
                       apply_absolute_value=True)
                ctx.op("vector", "tensor_max", hmax[:], hmax[:], m8[:])

        emit_proj(ctx, xT_planes=x1Tp, n_tok=T_OWN, w_dram=WT, n_out=4096,
                  stats=stats, sems=sems, consts=consts, bias_dram=b_row,
                  epilogue=epiH, tag="L2bH")

        z_sb = ctx.sb("l2z", (128, 32), F32)

        def epiZ(tg, o0, n_o, psum, tval):
            rows = slice(tg * 128, (tg + 1) * 128)
            ctx.op("scalar", "activation", z_sb[:, 0:n_o], psum[:, 0:n_o],
                   AF.Copy, scale=stats["rs"][:, tg:tg + 1],
                   inc=sems["e"].tick(1))
            ctx.op("sync", "dma_start", z_rows[rows, 0:n_o], z_sb[:, 0:n_o])

        emit_proj(ctx, xT_planes=x1Tp, n_tok=T_OWN, w_dram=A4, n_out=32,
                  stats=stats, sems=sems, consts=consts, epilogue=epiZ,
                  tag="L2bz")

        ctx.push_scope()
        zt = ctx.sb("l2zt", (128, 32), F32)
        ztb = ctx.sb("l2ztb", (128, 32), BF16)
        zTb = ctx.sb("l2zT", (32, 128), BF16)
        litb = ctx.sb("l2lit", (32, 8), BF16)
        ds_sb = ctx.sb("l2ds", (128, 8), F32)
        pT = ctx.ps("l2pT", (32, 128), BF16)
        pD = ctx.ps("l2pD", (128, 8), F32)
        ctx.op("gpsimd", "dma_start", litb[:], LiT[:, :])
        for tg in range(NT_OWN):
            rows = slice(tg * 128, (tg + 1) * 128)
            ctx.op("sync", "dma_start", zt[:], z_rows[rows, :])
            ctx.op("vector", "tensor_copy", ztb[:], zt[:])
            ctx.op("tensor", "transpose", pT[:], ztb[:], consts["c_idb"][:],
                   inc=sems["t"].tick(1))
            ctx.op("vector", "tensor_copy", zTb[:], pT[:])
            ctx.op("tensor", "matmul", pD[:], zTb[:], litb[:], start=True,
                   stop=True, inc=sems["t"].tick(1))
            ctx.op("vector", "tensor_copy", ds_sb[:], pD[:])
            ctx.op("sync", "dma_start", ds_rows[rows, :], ds_sb[:])
            ctx.op("vector", "tensor_scalar_mul", tmp8[:], ds_sb[:],
                   consts["c_odd"][:])
            ctx.op("vector", "reduce_max", m8[:], tmp8[:], axis=AX.X,
                   apply_absolute_value=True)
            ctx.op("vector", "tensor_max", dmax[:], dmax[:], m8[:])
        pR = ctx.ps("l2pR", (1, 128), F32)
        srow = ctx.sb("l2sr", (1, 128), F32)
        sc2 = ctx.sb("l2sc", (1, 2), F32)
        for i, acc in enumerate((hmax, dmax)):
            ctx.op("tensor", "transpose", pR[:], acc[:], consts["c_idf"][:],
                   inc=sems["t"].tick(1))
            ctx.op("vector", "tensor_copy", srow[:], pR[:])
            ctx.op("vector", "reduce_max", sc2[:, i:i + 1], srow[:], axis=AX.X)
        ctx.op("sync", "dma_start", scales_loc[:, :], sc2[:])
        a = ctx._autosem("sync", "d")
        ctx._need("sync", (a.h, a.n))
        ctx.pop_scope()
        ctx.finalize(block)
    ctx.st.close()
    return nc


def host_inputs_L2b(inputs, core, x1_own):
    d = {"x1_rows": np.ascontiguousarray(x1_own),
         "x1Tp": np.ascontiguousarray(x1_own.T.reshape(32, 128, T_OWN)),
         "WT": np.ascontiguousarray(inputs["L0_ffn_W"].T),
         "b_row": np.ascontiguousarray(inputs["L0_ffn_b"][None, :]),
         "A4": inputs["L0_ffn_A"],
         "LiT": np.ascontiguousarray(inputs["L0_ffn_LiMEs"].T)}
    d.update({k: v for k, v in host_consts(core % 2).items() if k in L2B_CONSTS})
    return d


L3_CONSTS = ["c_idf", "c_idb"]

def build_L3():
    nc = bass.Bass("TRN2", target_bir_lowering=False, debug=False)
    ctx = Ctx(nc)
    sems = make_sems(ctx)
    x1_rows = nc.dram_tensor("x1_rows", [T_OWN, 4096], F32, kind="ExternalInput")
    H_rows = nc.dram_tensor("H_rows", [T_OWN, 4096], F32, kind="ExternalInput")
    z_rows = nc.dram_tensor("z_rows", [T_OWN, 32], F32, kind="ExternalInput")
    ds_rows = nc.dram_tensor("ds_rows", [T_OWN, 8], F32, kind="ExternalInput")
    scales_all = nc.dram_tensor("scales_all", [1, 16], F32, kind="ExternalInput")
    LiM = nc.dram_tensor("LiM", [8, 32], F32, kind="ExternalInput")
    Bm = nc.dram_tensor("Bm", [32, 4096], F32, kind="ExternalInput")
    x2_rows = nc.dram_tensor("x2_rows", [T_OWN, 4096], F32,
                             kind="ExternalOutput")
    with nc.Block() as block:
        consts = load_consts(ctx, L3_CONSTS, sems)
        sall = ctx.sb("l3sa", (1, 16), F32)
        s11 = ctx.sb("l3s1", (1, 2), F32)
        inv2 = ctx.sb("l3inv", (1, 2), F32)
        invc = ctx.sb("l3ic", (128, 2), F32)
        ones_r = ctx.sb("l3or", (1, 128), F32)
        limb = ctx.sb("l3lim", (8, 32), BF16)
        bmb = ctx.sb("l3bm", (32, 4096), BF16)
        pB = ctx.ps("l3pB", (128, 2), F32)
        ctx.op("sync", "dma_start", sall[:], scales_all[:, :])
        ctx.op("gpsimd", "dma_start", limb[:], LiM[:, :])
        for fc in range(1):
            ctx.op("gpsimd", "dma_start", bmb[:], Bm[:, :])
        ctx.op("gpsimd", "memset", ones_r[:], 1.0)
        s8x2 = ctx.sb("l3s8", (1, 16), F32)
        sv = sall.ap().rearrange("a (p two) -> a two p", two=2)
        ctx.op("vector", "tensor_copy",
               s8x2.ap().rearrange("a (two p) -> a two p", two=2), sv)
        ctx.op("vector", "reduce_max", s11[:, 0:1], s8x2[:, 0:8], axis=AX.X)
        ctx.op("vector", "reduce_max", s11[:, 1:2], s8x2[:, 8:16], axis=AX.X)
        ctx.op("vector", "tensor_scalar_max", s11[:], s11[:], 1e-6)
        ctx.op("vector", "reciprocal", inv2[:], s11[:])
        ctx.op("vector", "tensor_scalar_mul", inv2[:], inv2[:], 0.5)
        ctx.op("tensor", "matmul", pB[:], ones_r[:], inv2[:], start=True,
               stop=True, inc=sems["t"].tick(1))
        ctx.op("vector", "tensor_copy", invc[:], pB[:])

        Ht = ctx.sb("l3H", (128, 4096), F32)
        x1t = ctx.sb("l3x1", (128, 4096), F32)
        hs8 = ctx.sb("l3h8", (128, 8), F32)
        ds8 = ctx.sb("l3d8", (128, 8), F32)
        zt = ctx.sb("l3z", (128, 32), F32)
        lg = ctx.sb("l3lg", (128, 8), F32)
        t8 = ctx.sb("l3t8", (128, 8), F32)
        nm = ctx.sb("l3nm", (128, 1), F32)
        sm = ctx.sb("l3sm", (128, 1), F32)
        rsc = ctx.sb("l3rs", (128, 1), F32)
        probs = ctx.sb("l3pr", (128, 8), F32)
        eqm = ctx.sb("l3eq", (128, 8), F32)
        m2 = ctx.sb("l3m2", (128, 1), F32)
        sig = ctx.sb("l3sg", (128, 8), F32)
        wts = ctx.sb("l3wt", (128, 8), F32)
        wsh = ctx.sb("l3ws", (128, 8), F32)
        wshb = ctx.sb("l3wsb", (128, 8), BF16)
        wT = ctx.sb("l3wT", (8, 128), BF16)
        zpb = ctx.sb("l3zp", (128, 32), BF16)
        zpT = ctx.sb("l3zpT", (32, 128), BF16)
        osb = ctx.sb("l3ou", (128, 512), F32)
        pT8 = ctx.ps("l3pT", (8, 128), BF16)
        pM = ctx.ps("l3pM", (128, 32), F32)
        pT32 = ctx.ps("l3pT2", (32, 128), BF16)
        pDl = ctx.ps("l3pD", (128, 512), F32)
        SHUF = [i | 1 for i in range(32)]

        for tg in range(NT_OWN):
            rows = slice(tg * 128, (tg + 1) * 128)
            ctx.op("sync", "dma_start", hs8[:], H_rows[rows, 0:8])
            ctx.op("sync", "dma_start", ds8[:], ds_rows[rows, :])
            ctx.op("sync", "dma_start", zt[:], z_rows[rows, :])
            ctx.op("sync", "dma_start", x1t[:], x1_rows[rows, :])
            ctx.op("sync", "dma_start", Ht[:], H_rows[rows, :])
            ctx.op("vector", "tensor_scalar_mul", t8[:], ds8[:], invc[:, 1:2])
            ctx.op("vector", "scalar_tensor_tensor", lg[:], hs8[:],
                   invc[:, 0:1], t8[:], op0=ALU.mult, op1=ALU.add)
            ctx.op("vector", "reduce_max", nm[:], lg[:], axis=AX.X, negate=True)
            ctx.op("scalar", "activation", probs[:], lg[:], AF.Exp,
                   bias=nm[:], scale=1.0, accum_out=sm[:])
            ctx.op("vector", "reciprocal", rsc[:], sm[:])
            ctx.op("vector", "tensor_scalar_mul", probs[:], probs[:], rsc[:])
            ctx.op("vector", "reduce_max", nm[:], probs[:], axis=AX.X)
            ctx.op("vector", "tensor_scalar", eqm[:], probs[:], nm[:], None,
                   op0=ALU.is_ge)
            ctx.op("vector", "scalar_tensor_tensor", t8[:], eqm[:], -1e30,
                   probs[:], op0=ALU.mult, op1=ALU.add)
            ctx.op("vector", "reduce_max", m2[:], t8[:], axis=AX.X)
            ctx.op("vector", "tensor_scalar_mul", m2[:], m2[:], -2.0)
            ctx.op("scalar", "activation", sig[:], probs[:], AF.Sigmoid,
                   bias=m2[:], scale=2.0)
            ctx.op("vector", "tensor_mul", wts[:], probs[:], sig[:])
            ctx.op("vector", "reduce_sum", sm[:], wts[:], axis=AX.X)
            ctx.op("vector", "tensor_scalar_add", sm[:], sm[:], 1e-9)
            ctx.op("vector", "reciprocal", rsc[:], sm[:])
            ctx.op("vector", "tensor_scalar_mul", wts[:], wts[:], rsc[:])
            ctx.op("vector", "stream_shuffle", wsh[:], wts[:], SHUF)
            ctx.op("vector", "tensor_copy", wshb[:], wsh[:])
            ctx.op("tensor", "transpose", pT8[:], wshb[:], consts["c_idb"][:],
                   inc=sems["t"].tick(1))
            ctx.op("vector", "tensor_copy", wT[:], pT8[:])
            ctx.op("tensor", "matmul", pM[:], wT[:], limb[:], start=True,
                   stop=True, inc=sems["t"].tick(1))
            ctx.op("vector", "tensor_mul", zpb[:], zt[:], pM[:])
            ctx.op("tensor", "transpose", pT32[:], zpb[:], consts["c_idb"][:],
                   inc=sems["t"].tick(1))
            ctx.op("vector", "tensor_copy", zpT[:], pT32[:])
            for oc in range(8):
                osl = slice(oc * 512, (oc + 1) * 512)
                ctx.op("tensor", "matmul", pDl[:], zpT[:], bmb[:, osl],
                       start=True, stop=True, inc=sems["t"].tick(1))
                ctx.op("vector", "scalar_tensor_tensor", osb[:], pDl[:], 0.5,
                       Ht[:, osl], op0=ALU.mult, op1=ALU.add)
                ctx.op("vector", "tensor_add", osb[:], osb[:], x1t[:, osl])
                ctx.op("sync", "dma_start", x2_rows[rows, osl], osb[:])
        a = ctx._autosem("sync", "d")
        ctx._need("sync", (a.h, a.n))
        ctx.finalize(block)
    ctx.st.close()
    return nc


def host_inputs_L3(inputs, core, x1_own, H_own, z_own, ds_own, scales8):
    d = {"x1_rows": np.ascontiguousarray(x1_own),
         "H_rows": np.ascontiguousarray(H_own),
         "z_rows": np.ascontiguousarray(z_own),
         "ds_rows": np.ascontiguousarray(ds_own),
         "scales_all": np.ascontiguousarray(
             np.concatenate(scales8, 1).reshape(1, 16)),
         "LiM": inputs["L0_ffn_LiMEs"],
         "Bm": inputs["L0_ffn_Bmat"]}
    d.update({k: v for k, v in host_consts(core % 2).items() if k in L3_CONSTS})
    return d


L4_CONSTS = ["c_idf", "c_idb"]

def build_L4():
    nc = bass.Bass("TRN2", target_bir_lowering=False, debug=False)
    ctx = Ctx(nc)
    sems = make_sems(ctx)
    x2_rows = nc.dram_tensor("x2_rows", [T_OWN, 4096], F32, kind="ExternalInput")
    x2Tp = nc.dram_tensor("x2Tp", [32, 128, T_OWN], F32, kind="ExternalInput")
    Wcat = nc.dram_tensor("Wcat", [4096, 12288], F32, kind="ExternalInput")
    bcat = nc.dram_tensor("bcat", [1, 12288], F32, kind="ExternalInput")
    q_rows = nc.dram_tensor("q_rows", [T_OWN, 4096], BF16, kind="ExternalOutput")
    k_rows = nc.dram_tensor("k_rows", [T_OWN, 4096], BF16, kind="ExternalOutput")
    v_rows = nc.dram_tensor("v_rows", [T_OWN, 4096], BF16, kind="ExternalOutput")
    kmax_loc = nc.dram_tensor("kmax_loc", [1, 1], F32, kind="ExternalOutput")
    with nc.Block() as block:
        consts = load_consts(ctx, L4_CONSTS, sems)
        stats = emit_stats(ctx, x2_rows, NT_OWN, sems, consts, tag="4")
        out_sb = [ctx.sb("l4o%d" % i, (128, 512), BF16) for i in range(2)]
        sqs = ctx.sb("l4sq", (128, 128), F32)
        kn = ctx.sb("l4kn", (128, 1), F32)
        knacc = ctx.sb("l4ka", (128, 1), F32)
        ctx.op("gpsimd", "memset", knacc[:], 0.0)
        epar = [0]

        def epi(tg, o0, n_o, psum, tval):
            par = epar[0]
            epar[0] = 1 - par
            rows = slice(tg * 128, (tg + 1) * 128)
            ctx.op("scalar", "activation", out_sb[par][:, 0:n_o],
                   psum[:, 0:n_o], AF.Copy, scale=stats["rs"][:, tg:tg + 1],
                   inc=sems["e"].tick(1))
            which, off = divmod(o0, 4096)
            dest = (q_rows, k_rows, v_rows)[which]
            ctx.op("sync", "dma_start", dest[rows, off:off + n_o],
                   out_sb[par][:, 0:n_o])
            if which == 1:
                for hh in range(4):
                    sl = slice(hh * 128, (hh + 1) * 128)
                    ctx.op("scalar", "activation", sqs[:],
                           out_sb[par][:, sl], AF.Square, accum_out=kn[:])
                    ctx.op("vector", "tensor_max", knacc[:], knacc[:], kn[:])

        emit_proj(ctx, xT_planes=x2Tp, n_tok=T_OWN, w_dram=Wcat, n_out=12288,
                  stats=stats, sems=sems, consts=consts, bias_dram=bcat,
                  epilogue=epi, tag="L4")
        pR = ctx.ps("l4pR", (1, 128), F32)
        srow = ctx.sb("l4sr", (1, 128), F32)
        km = ctx.sb("l4km", (1, 1), F32)
        ctx.op("tensor", "transpose", pR[:], knacc[:], consts["c_idf"][:],
               inc=sems["t"].tick(1))
        ctx.op("vector", "tensor_copy", srow[:], pR[:])
        ctx.op("vector", "reduce_max", km[:], srow[:], axis=AX.X)
        ctx.op("sync", "dma_start", kmax_loc[:, :], km[:])
        a = ctx._autosem("sync", "d")
        ctx._need("sync", (a.h, a.n))
        ctx.finalize(block)
    ctx.st.close()
    return nc


def host_inputs_L4(inputs, core, x2_own):
    Wcat = np.concatenate([inputs["L1_attn_Wq"].T, inputs["L1_attn_Wk"].T,
                           inputs["L1_attn_Wv"].T], axis=1)
    bcat = np.concatenate([inputs["L1_attn_bq"], inputs["L1_attn_bk"],
                           inputs["L1_attn_bv"]])[None, :]
    d = {"x2_rows": np.ascontiguousarray(x2_own),
         "x2Tp": np.ascontiguousarray(x2_own.T.reshape(32, 128, T_OWN)),
         "Wcat": np.ascontiguousarray(Wcat),
         "bcat": np.ascontiguousarray(bcat)}
    d.update({k: v for k, v in host_consts(core % 2).items() if k in L4_CONSTS})
    return d


L5_CONSTS = ["c_idf", "c_idb", "c_dmA", "c_dmB", "c_dmAb", "c_dmBb"]

def emit_attn_l1(ctx, *, qT, kT, vpl, kmax_all, oT_l1, sems, consts,
                 rank_major=False):
    tsem = sems["t"]
    if rank_major:
        kcoff = lambda kc: (kc & 1) * 2048 + (kc >> 1) * 128
    else:
        kcoff = lambda kc: kc * 128
    ctx.push_scope()
    kT_sb = ctx.sb("a5kT", (128, 4096), BF16)
    v_sb = ctx.sb("a5v", (128, 32 * 128), BF16)
    qT_sb = ctx.sb("a5qT", (128, 2048), BF16)
    sq_s = ctx.sb("a5sq", (128, 512), BF16)
    Brow = ctx.sb("a5Br", (1, 512), F32)
    BoC = ctx.sb("a5Bc", (1, 512), F32)
    negBoC = ctx.sb("a5nB", (1, 512), F32)
    negLoC = ctx.sb("a5nL", (1, 512), F32)
    nBb = [ctx.sb("a5nBb%d" % i, (1, 512), BF16) for i in range(2)]
    nLb = [ctx.sb("a5nLb%d" % i, (1, 512), BF16) for i in range(2)]
    rtmp = ctx.sb("a5rt", (1, 512), F32)
    lnS = ctx.sb("a5ln", (1, 512), F32)
    onesr = ctx.sb("a5or", (1, 128), BF16)
    onesc = ctx.sb("a5oc", (128, 1), BF16)
    kmx = ctx.sb("a5km", (1, 8), F32)
    ksc = ctx.sb("a5ks", (1, 1), F32)
    P0 = [ctx.sb("a5P0%d" % i, (128, 512), BF16) for i in range(2)]
    Pbf = [ctx.sb("a5Pb%d" % i, (128, 512), BF16) for i in range(2)]
    osb = ctx.sb("a5os", (128, 512), BF16)
    pS = [ctx.ps("a5pS%d" % i, (128, 512), F32) for i in range(2)]
    pSum = ctx.ps("a5pSm", (1, 512), F32)
    pO = ctx.ps("a5pO", (128, 512), F32)
    ctx.op("gpsimd", "memset", onesr[:], 1.0)
    ctx.op("gpsimd", "memset", onesc[:], 1.0)
    ctx.op("sync", "dma_start", kmx[:], kmax_all[:, :])

    def split_hilo(rowf, dst):
        ctx.op("vector", "tensor_copy", dst[0][:], rowf)
        ctx.op("vector", "tensor_copy", rtmp[:], dst[0][:])
        ctx.op("vector", "tensor_sub", rtmp[:], rowf, rtmp[:])
        ctx.op("vector", "tensor_copy", dst[1][:], rtmp[:])
    ctx.op("vector", "reduce_max", ksc[:], kmx[:], axis=AX.X)
    ctx.op("vector", "tensor_scalar_mul", ksc[:], ksc[:], 1.0 / 128.0)

    for h in range(32):
        if rank_major:
            for r in range(2):
                ctx.op("sync", "dma_start", kT_sb[:, r * 2048:(r + 1) * 2048],
                       kT[r, h, :, :])
                for kg in range(2):
                    ctx.op("sync", "dma_start",
                           v_sb[:, r * 2048 + kg * 1024:
                                r * 2048 + (kg + 1) * 1024]
                           .rearrange("t (kc d) -> t kc d", d=128),
                           vpl[r, h, kg * 1024:(kg + 1) * 1024, :]
                           .rearrange("(kc t) d -> t kc d", t=128))
        else:
            ctx.op("sync", "dma_start", kT_sb[:], kT[h, :, :])
            for kg in range(4):
                ctx.op("sync", "dma_start",
                       v_sb[:, kg * 1024:(kg + 1) * 1024]
                       .rearrange("t (kc d) -> t kc d", d=128),
                       vpl[h, kg * 1024:(kg + 1) * 1024, :]
                       .rearrange("(kc t) d -> t kc d", t=128))
        ctx.op("sync", "dma_start", qT_sb[:], qT[h, :, :])
        for g in range(4):
            gq = slice(g * 512, (g + 1) * 512)
            qTg = qT_sb[:, gq]
            ctx.op("scalar", "activation", sq_s[:], qTg, AF.Square)
            ctx.op("tensor", "matmul", pSum[:], onesc[:], sq_s[:],
                   start=True, stop=True, inc=tsem.tick(1))
            ctx.op("scalar", "activation", Brow[:], pSum[:], AF.Sqrt,
                   scale=ksc[:])
            ctx.op("vector", "tensor_scalar_mul", BoC[:], Brow[:], 1.0 / CSC)
            ctx.op("vector", "tensor_scalar_mul", negBoC[:], BoC[:], -1.0)
            split_hilo(negBoC[:], nBb)
            nkc = 8 * g + 8
            for phase in range(2):
                rowv = nBb if phase == 0 else nLb

                def s_mm(kc):
                    j0 = max(0, (kc - 8 * g) // 2)
                    cols = slice(j0 * 128, 512)
                    ko = kcoff(kc)
                    ctx.op("tensor", "matmul", pS[kc & 1][:, cols],
                           kT_sb[:, ko:ko + 128],
                           qT_sb[:, g * 512 + j0 * 128:(g + 1) * 512],
                           start=True, stop=False)
                    ctx.op("tensor", "matmul", pS[kc & 1][:, cols], onesr[:],
                           rowv[0][0:1, cols], start=False, stop=False)
                    ctx.op("tensor", "matmul", pS[kc & 1][:, cols], onesr[:],
                           rowv[1][0:1, cols], start=False, stop=True,
                           inc=tsem.tick(1))
                    return cols

                def exp_mask(kc, cols):
                    dst = (P0 if phase == 0 else Pbf)[kc & 1]
                    j0 = cols.start // 128
                    ctx.op("scalar", "activation", dst[:, cols],
                           pS[kc & 1][:, cols], AF.Exp, scale=CSC)
                    for j in range(j0, 4):
                        lt = 4 * g + j
                        bsl = slice(j * 128, (j + 1) * 128)
                        mk = None
                        if kc == 2 * lt:
                            mk = "c_dmAb"
                        elif kc == 2 * lt + 1:
                            mk = "c_dmBb"
                        if mk is not None:
                            ctx.op("vector", "tensor_mul", dst[:, bsl],
                                   dst[:, bsl], consts[mk][:])
                    return dst

                def acc_mm(kc, cols, dst):
                    if phase == 0:
                        ctx.op("tensor", "matmul", pSum[0:1, cols], onesc[:],
                               dst[:, cols], start=(kc == 0),
                               stop=(kc == nkc - 1),
                               inc=(tsem.tick(1) if kc == nkc - 1 else None))
                    else:
                        ctx.op("tensor", "matmul", pO[:, cols],
                               v_sb[:, kcoff(kc):kcoff(kc) + 128],
                               dst[:, cols], start=(kc == 0),
                               stop=(kc == nkc - 1),
                               inc=(tsem.tick(1) if kc == nkc - 1 else None))

                pend = None
                for kc in range(nkc):
                    cols = s_mm(kc)
                    if pend is not None:
                        acc_mm(*pend)
                    dst = exp_mask(kc, cols)
                    pend = (kc, cols, dst)
                acc_mm(*pend)
                if phase == 0:
                    ctx.op("scalar", "activation", lnS[:], pSum[:], AF.Ln)
                    ctx.op("vector", "scalar_tensor_tensor", negLoC[:], lnS[:],
                           1.0 / CSC, BoC[:], op0=ALU.mult, op1=ALU.add)
                    ctx.op("vector", "tensor_scalar_mul", negLoC[:], negLoC[:],
                           -1.0)
                    split_hilo(negLoC[:], nLb)
            ctx.op("scalar", "activation", osb[:], pO[:], AF.Copy,
                   inc=sems["e"].tick(1))
            ctx.op("sync", "dma_start", oT_l1[h, :, gq], osb[:])
        a = ctx._autosem("sync", "d")
    ctx._need("sync", (a.h, a.n))
    ctx.pop_scope()


def build_L5(debug=False):
    nc = bass.Bass("TRN2", target_bir_lowering=False, debug=False)
    ctx = Ctx(nc)
    sems = make_sems(ctx)
    kind = "ExternalOutput" if debug else "Internal"
    x2_rows = nc.dram_tensor("x2_rows", [T_OWN, 4096], F32, kind="ExternalInput")
    qT = nc.dram_tensor("qT", [32, 128, T_OWN], BF16, kind="ExternalInput")
    kT = nc.dram_tensor("kT", [32, 128, 4096], BF16, kind="ExternalInput")
    vpl = nc.dram_tensor("vpl", [32, 4096, 128], BF16, kind="ExternalInput")
    kmax_all = nc.dram_tensor("kmax_all", [1, 8], F32, kind="ExternalInput")
    WoT = nc.dram_tensor("WoT", [4096, 4096], F32, kind="ExternalInput")
    bo_row = nc.dram_tensor("bo_row", [1, 4096], F32, kind="ExternalInput")
    Acat = nc.dram_tensor("Acat", [4096, 64], F32, kind="ExternalInput")
    Bcat = nc.dram_tensor("Bcat", [64, 4096], F32, kind="ExternalInput")
    WrT = nc.dram_tensor("WrT", [4096, 4], F32, kind="ExternalInput")
    oT_l1 = nc.dram_tensor("oT_l1", [32, 128, T_OWN], BF16, kind=kind)
    x3_rows = nc.dram_tensor("x3_rows", [T_OWN, 4096], F32, kind=kind)
    x4_out = nc.dram_tensor("x4_out", [T_OWN, 4096], F32, kind="ExternalOutput")
    with nc.Block() as block:
        consts = load_consts(ctx, L5_CONSTS, sems)
        emit_attn_l1(ctx, qT=qT, kT=kT, vpl=vpl, kmax_all=kmax_all,
                     oT_l1=oT_l1, sems=sems, consts=consts)
        ctx.push_scope()
        x2t = ctx.sb("l5x2", (128, 512), F32)
        out_sb = [ctx.sb("l5o%d" % i, (128, 512), F32) for i in range(2)]
        epar = [0]

        def epiO(tg, o0, n_o, psum, tval):
            par = epar[0]
            epar[0] = 1 - par
            rows = slice(tg * 128, (tg + 1) * 128)
            ctx.op("sync", "dma_start", x2t[:, 0:n_o], x2_rows[rows, o0:o0 + n_o])
            ctx.op("vector", "tensor_add", out_sb[par][:, 0:n_o],
                   psum[:, 0:n_o], x2t[:, 0:n_o], inc=sems["e"].tick(1))
            ctx.op("sync", "dma_start", x3_rows[rows, o0:o0 + n_o],
                   out_sb[par][:, 0:n_o])

        emit_proj(ctx, xT_planes=oT_l1, n_tok=T_OWN, w_dram=WoT, n_out=4096,
                  stats=None, sems=sems, consts=consts, bias_dram=bo_row,
                  epilogue=epiO, tag="L5o")
        ctx.pop_scope()
        stats = emit_stats(ctx, x3_rows, NT_OWN, sems, consts, tag="5")
        ctx.push_scope()
        acatb = ctx.sb("l5Ac", (128, 32 * 64), BF16)
        bcatb = ctx.sb("l5Bc", (64, 4096), BF16)
        wrtb = ctx.sb("l5Wr", (128, 32 * 4), BF16)
        onesc = ctx.sb("l5oc", (128, 1), BF16)
        wsA = ctx.sb("l5wA", (1, 64), BF16)
        wsR = ctx.sb("l5wR", (1, 4), BF16)
        x3t = ctx.sb("l5x3", (128, 4096), F32)
        x3b = ctx.sb("l5x3b", (128, 4096), BF16)
        x3T = ctx.sb("l5x3T", (128, 32 * 128), BF16)
        lgt = ctx.sb("l5lg", (128, 4), F32)
        mid = ctx.sb("l5md", (128, 64), F32)
        nm = ctx.sb("l5nm", (128, 1), F32)
        m2 = ctx.sb("l5m2", (128, 1), F32)
        eqm = ctx.sb("l5eq", (128, 4), F32)
        t4 = ctx.sb("l5t4", (128, 4), F32)
        gat = ctx.sb("l5ga", (128, 4), F32)
        sm = ctx.sb("l5sm", (128, 1), F32)
        rsc = ctx.sb("l5rs", (128, 1), F32)
        midg = ctx.sb("l5mg", (128, 64), BF16)
        mgT = ctx.sb("l5mT", (64, 128), BF16)
        osb2 = ctx.sb("l5ob", (128, 512), F32)
        pP = ctx.ps("l5pP", (128, 512), F32)
        pL = ctx.ps("l5pL", (128, 68), F32)
        pT = ctx.ps("l5pT", (128, 128), BF16)
        pW = ctx.ps("l5pW", (1, 68), F32)
        for fc in range(32):
            ctx.op("gpsimd", "dma_start", acatb[:, fc * 64:(fc + 1) * 64],
                   Acat[fc * 128:(fc + 1) * 128, :])
            ctx.op("gpsimd", "dma_start", wrtb[:, fc * 4:(fc + 1) * 4],
                   WrT[fc * 128:(fc + 1) * 128, :])
        ctx.op("gpsimd", "dma_start", bcatb[:], Bcat[:, :])
        ctx.op("gpsimd", "memset", onesc[:], 1.0)
        for fc in range(32):
            ctx.op("tensor", "matmul", pW[0:1, 0:64], onesc[:],
                   acatb[:, fc * 64:(fc + 1) * 64], start=(fc == 0), stop=False)
            ctx.op("tensor", "matmul", pW[0:1, 64:68], onesc[:],
                   wrtb[:, fc * 4:(fc + 1) * 4], start=False,
                   stop=(fc == 31), inc=(sems["t"].tick(1) if fc == 31 else None))
        ctx.op("vector", "tensor_copy", wsA[:], pW[0:1, 0:64])
        ctx.op("vector", "tensor_copy", wsR[:], pW[0:1, 64:68])

        for tg in range(NT_OWN):
            rows = slice(tg * 128, (tg + 1) * 128)
            ctx.op("sync", "dma_start", x3t[:], x3_rows[rows, :])
            ctx.op("vector", "tensor_copy", x3b[:], x3t[:])
            for fc in range(32):
                ctx.op("tensor", "transpose", pT[:],
                       x3b[:, fc * 128:(fc + 1) * 128], consts["c_idb"][:],
                       inc=sems["t"].tick(1))
                ctx.op("vector", "tensor_copy",
                       x3T[:, fc * 128:(fc + 1) * 128], pT[:])
            for fc in range(32):
                ctx.op("tensor", "matmul", pL[:, 0:64],
                       x3T[:, fc * 128:(fc + 1) * 128],
                       acatb[:, fc * 64:(fc + 1) * 64],
                       start=(fc == 0), stop=False)
                ctx.op("tensor", "matmul", pL[:, 64:68],
                       x3T[:, fc * 128:(fc + 1) * 128],
                       wrtb[:, fc * 4:(fc + 1) * 4], start=False, stop=False)
            ctx.op("tensor", "matmul", pL[:, 0:64],
                   stats["negm"][0:1, tg * 128:(tg + 1) * 128], wsA[:],
                   start=False, stop=False)
            ctx.op("tensor", "matmul", pL[:, 64:68],
                   stats["negm"][0:1, tg * 128:(tg + 1) * 128], wsR[:],
                   start=False, stop=True, inc=sems["t"].tick(1))
            ctx.op("scalar", "activation", mid[:], pL[:, 0:64], AF.Copy,
                   scale=stats["rs"][:, tg:tg + 1])
            ctx.op("scalar", "activation", lgt[:], pL[:, 64:68], AF.Copy,
                   scale=stats["rs"][:, tg:tg + 1])
            ctx.op("vector", "reduce_max", nm[:], lgt[:], axis=AX.X)
            ctx.op("vector", "tensor_scalar", eqm[:], lgt[:], nm[:], None,
                   op0=ALU.is_ge)
            ctx.op("vector", "scalar_tensor_tensor", t4[:], eqm[:], -1e30,
                   lgt[:], op0=ALU.mult, op1=ALU.add)
            ctx.op("vector", "reduce_max", m2[:], t4[:], axis=AX.X)
            ctx.op("vector", "tensor_scalar", eqm[:], lgt[:], m2[:], None,
                   op0=ALU.is_ge)
            ctx.op("vector", "tensor_scalar_mul", nm[:], nm[:], -1.0)
            ctx.op("scalar", "activation", gat[:], lgt[:], AF.Exp, bias=nm[:])
            ctx.op("vector", "tensor_mul", gat[:], gat[:], eqm[:])
            ctx.op("vector", "reduce_sum", sm[:], gat[:], axis=AX.X)
            ctx.op("vector", "reciprocal", rsc[:], sm[:])
            ctx.op("vector", "tensor_scalar_mul", gat[:], gat[:], rsc[:])
            for e in range(4):
                ctx.op("vector", "tensor_scalar", midg[:, e * 16:(e + 1) * 16],
                       mid[:, e * 16:(e + 1) * 16], gat[:, e:e + 1], 4.0,
                       op0=ALU.mult, op1=ALU.mult)
            ctx.op("tensor", "transpose", pT[0:64, 0:128], midg[:],
                   consts["c_idb"][:], inc=sems["t"].tick(1))
            ctx.op("vector", "tensor_copy", mgT[:], pT[0:64, 0:128])
            for oc in range(8):
                osl = slice(oc * 512, (oc + 1) * 512)
                ctx.op("tensor", "matmul", pP[:], mgT[:], bcatb[:, osl],
                       start=True, stop=True, inc=sems["t"].tick(1))
                ctx.op("vector", "tensor_add", osb2[:], pP[:], x3t[:, osl])
                ctx.op("sync", "dma_start", x4_out[rows, osl], osb2[:])
        a = ctx._autosem("sync", "d")
        ctx._need("sync", (a.h, a.n))
        ctx.pop_scope()
        ctx.finalize(block)
    ctx.st.close()
    return nc


def host_inputs_L5(inputs, core, x2_own, qkv_pair, kmax8):
    p = core % 2
    q_own, _, _ = qkv_pair[p]
    kT_full = np.zeros((32, 128, 4096), dtype=NBF)
    v_full = np.zeros((32, 4096, 128), dtype=NBF)
    for pp in range(2):
        own = own_rows(pp)
        _, k_pp, v_pp = qkv_pair[pp]
        for h in range(32):
            kT_full[h][:, own] = k_pp[:, h * 128:(h + 1) * 128].T
            v_full[h][own, :] = v_pp[:, h * 128:(h + 1) * 128]
    qTl = np.stack([np.ascontiguousarray(q_own[:, h * 128:(h + 1) * 128].T)
                    for h in range(32)])
    E, R = 4, 16
    Acat = inputs["L1_ffn_A"].transpose(2, 0, 1).reshape(4096, 64)
    Bcat = inputs["L1_ffn_B"].transpose(0, 2, 1).reshape(64, 4096)
    d = {"x2_rows": np.ascontiguousarray(x2_own),
         "qT": np.ascontiguousarray(qTl),
         "kT": np.ascontiguousarray(kT_full),
         "vpl": np.ascontiguousarray(v_full),
         "kmax_all": np.ascontiguousarray(np.array(kmax8, np.float32).reshape(1, 8)),
         "WoT": np.ascontiguousarray(inputs["L1_attn_Wo"].T),
         "bo_row": np.ascontiguousarray(inputs["L1_attn_bo"][None, :]),
         "Acat": np.ascontiguousarray(Acat),
         "Bcat": np.ascontiguousarray(Bcat),
         "WrT": np.ascontiguousarray(inputs["L1_ffn_Wr"].T)}
    d.update({k: v for k, v in host_consts(p).items() if k in L5_CONSTS})
    return d


_NC_CACHE = {}

def _get(name, fn):
    if name not in _NC_CACHE:
        _NC_CACHE[name] = fn()
    return _NC_CACHE[name]


def kernel(**inputs):
    return kernel_fused(**inputs)


def kernel_multilaunch(**inputs):
    inputs = {k: np.ascontiguousarray(np.asarray(v)) for k, v in inputs.items()}
    cores = list(range(NCORE))

    nc1 = _get("L1", build_L1)
    r1 = run_bass_kernel_spmd(nc1, [host_inputs_L1(inputs, c) for c in cores],
                              core_ids=cores)
    oT = [np.asarray(r1.results[c]["oT"]) for c in cores]

    nc2a = _get("L2a", build_L2a)
    r2a = run_bass_kernel_spmd(
        nc2a, [host_inputs_L2a(inputs, c, (oT[c // 2 * 2], oT[c // 2 * 2 + 1]))
               for c in cores], core_ids=cores)
    x1 = [np.asarray(r2a.results[c]["x1_rows"]) for c in cores]

    nc2b = _get("L2b", build_L2b)
    r2b = run_bass_kernel_spmd(
        nc2b, [host_inputs_L2b(inputs, c, x1[c]) for c in cores],
        core_ids=cores)
    scales8 = [np.asarray(r2b.results[c]["scales_loc"]) for c in cores]

    nc3 = _get("L3", build_L3)
    r3 = run_bass_kernel_spmd(
        nc3, [host_inputs_L3(inputs, c, x1[c],
                             np.asarray(r2b.results[c]["H_rows"]),
                             np.asarray(r2b.results[c]["z_rows"]),
                             np.asarray(r2b.results[c]["ds_rows"]), scales8)
              for c in cores], core_ids=cores)
    x2 = [np.asarray(r3.results[c]["x2_rows"]) for c in cores]

    nc4 = _get("L4", build_L4)
    r4 = run_bass_kernel_spmd(
        nc4, [host_inputs_L4(inputs, c, x2[c]) for c in cores],
        core_ids=cores)
    kmax8 = [float(np.asarray(r4.results[c]["kmax_loc"])[0, 0]) for c in cores]
    qkv = [(np.asarray(r4.results[c]["q_rows"]),
            np.asarray(r4.results[c]["k_rows"]),
            np.asarray(r4.results[c]["v_rows"])) for c in cores]

    nc5 = _get("L5", build_L5)
    r5 = run_bass_kernel_spmd(
        nc5, [host_inputs_L5(inputs, c, x2[c],
                             (qkv[c // 2 * 2], qkv[c // 2 * 2 + 1]), kmax8)
              for c in cores], core_ids=cores)

    out = np.empty((B, N, DM), np.float32)
    for c in cores:
        b, p = c // 2, c % 2
        out[b][own_rows(p)] = np.asarray(r5.results[c]["x4_out"])
    return out


def _emit_gating(ctx, sems, consts, x1_rows, H_rows, z_rows, ds_rows,
                 scales_all, LiM, Bm, x2_rows):
    sall = ctx.sb("g3sa", (1, 16), F32)
    s11 = ctx.sb("g3s1", (1, 2), F32)
    inv2 = ctx.sb("g3inv", (1, 2), F32)
    invc = ctx.sb("g3ic", (128, 2), F32)
    ones_r = ctx.sb("g3or", (1, 128), F32)
    limb = ctx.sb("g3lim", (8, 32), BF16)
    bmb = ctx.sb("g3bm", (32, 4096), BF16)
    pB = ctx.ps("g3pB", (128, 2), F32)
    ctx.op("sync", "dma_start", sall[:], scales_all[:, :])
    ctx.op("gpsimd", "dma_start", limb[:], LiM[:, :])
    ctx.op("gpsimd", "dma_start", bmb[:], Bm[:, :])
    ctx.op("gpsimd", "memset", ones_r[:], 1.0)
    s8x2 = ctx.sb("g3s8", (1, 16), F32)
    sv = sall.ap().rearrange("a (p two) -> a two p", two=2)
    ctx.op("vector", "tensor_copy",
           s8x2.ap().rearrange("a (two p) -> a two p", two=2), sv)
    ctx.op("vector", "reduce_max", s11[:, 0:1], s8x2[:, 0:8], axis=AX.X)
    ctx.op("vector", "reduce_max", s11[:, 1:2], s8x2[:, 8:16], axis=AX.X)
    ctx.op("vector", "tensor_scalar_max", s11[:], s11[:], 1e-6)
    ctx.op("vector", "reciprocal", inv2[:], s11[:])
    ctx.op("vector", "tensor_scalar_mul", inv2[:], inv2[:], 0.5)
    ctx.op("tensor", "matmul", pB[:], ones_r[:], inv2[:], start=True,
           stop=True, inc=sems["t"].tick(1))
    ctx.op("vector", "tensor_copy", invc[:], pB[:])

    Ht = ctx.sb("g3H", (128, 4096), F32)
    x1t = ctx.sb("g3x1", (128, 4096), F32)
    hs8 = ctx.sb("g3h8", (128, 8), F32)
    ds8 = ctx.sb("g3d8", (128, 8), F32)
    zt = ctx.sb("g3z", (128, 32), F32)
    lg = ctx.sb("g3lg", (128, 8), F32)
    t8 = ctx.sb("g3t8", (128, 8), F32)
    nm = ctx.sb("g3nm", (128, 1), F32)
    sm = ctx.sb("g3sm", (128, 1), F32)
    rsc = ctx.sb("g3rs", (128, 1), F32)
    probs = ctx.sb("g3pr", (128, 8), F32)
    eqm = ctx.sb("g3eq", (128, 8), F32)
    m2 = ctx.sb("g3m2", (128, 1), F32)
    sig = ctx.sb("g3sg", (128, 8), F32)
    wts = ctx.sb("g3wt", (128, 8), F32)
    wsh = ctx.sb("g3ws", (128, 8), F32)
    wshb = ctx.sb("g3wsb", (128, 8), BF16)
    wT = ctx.sb("g3wT", (8, 128), BF16)
    zpb = ctx.sb("g3zp", (128, 32), BF16)
    zpT = ctx.sb("g3zpT", (32, 128), BF16)
    osb = ctx.sb("g3ou", (128, 512), F32)
    pT8 = ctx.ps("g3pT", (8, 128), BF16)
    pM = ctx.ps("g3pM", (128, 32), F32)
    pT32 = ctx.ps("g3pT2", (32, 128), BF16)
    pDl = ctx.ps("g3pD", (128, 512), F32)
    SHUF = [i | 1 for i in range(32)]

    for tg in range(NT_OWN):
        rows = slice(tg * 128, (tg + 1) * 128)
        ctx.op("sync", "dma_start", hs8[:], H_rows[rows, 0:8])
        ctx.op("sync", "dma_start", ds8[:], ds_rows[rows, :])
        ctx.op("sync", "dma_start", zt[:], z_rows[rows, :])
        ctx.op("sync", "dma_start", x1t[:], x1_rows[rows, :])
        ctx.op("sync", "dma_start", Ht[:], H_rows[rows, :])
        ctx.op("vector", "tensor_scalar_mul", t8[:], ds8[:], invc[:, 1:2])
        ctx.op("vector", "scalar_tensor_tensor", lg[:], hs8[:],
               invc[:, 0:1], t8[:], op0=ALU.mult, op1=ALU.add)
        ctx.op("vector", "reduce_max", nm[:], lg[:], axis=AX.X, negate=True)
        ctx.op("scalar", "activation", probs[:], lg[:], AF.Exp,
               bias=nm[:], scale=1.0, accum_out=sm[:])
        ctx.op("vector", "reciprocal", rsc[:], sm[:])
        ctx.op("vector", "tensor_scalar_mul", probs[:], probs[:], rsc[:])
        ctx.op("vector", "reduce_max", nm[:], probs[:], axis=AX.X)
        ctx.op("vector", "tensor_scalar", eqm[:], probs[:], nm[:], None,
               op0=ALU.is_ge)
        ctx.op("vector", "scalar_tensor_tensor", t8[:], eqm[:], -1e30,
               probs[:], op0=ALU.mult, op1=ALU.add)
        ctx.op("vector", "reduce_max", m2[:], t8[:], axis=AX.X)
        ctx.op("vector", "tensor_scalar_mul", m2[:], m2[:], -2.0)
        ctx.op("scalar", "activation", sig[:], probs[:], AF.Sigmoid,
               bias=m2[:], scale=2.0)
        ctx.op("vector", "tensor_mul", wts[:], probs[:], sig[:])
        ctx.op("vector", "reduce_sum", sm[:], wts[:], axis=AX.X)
        ctx.op("vector", "tensor_scalar_add", sm[:], sm[:], 1e-9)
        ctx.op("vector", "reciprocal", rsc[:], sm[:])
        ctx.op("vector", "tensor_scalar_mul", wts[:], wts[:], rsc[:])
        ctx.op("vector", "stream_shuffle", wsh[:], wts[:], SHUF)
        ctx.op("vector", "tensor_copy", wshb[:], wsh[:])
        ctx.op("tensor", "transpose", pT8[:], wshb[:], consts["c_idb"][:],
               inc=sems["t"].tick(1))
        ctx.op("vector", "tensor_copy", wT[:], pT8[:])
        ctx.op("tensor", "matmul", pM[:], wT[:], limb[:], start=True,
               stop=True, inc=sems["t"].tick(1))
        ctx.op("vector", "tensor_mul", zpb[:], zt[:], pM[:])
        ctx.op("tensor", "transpose", pT32[:], zpb[:], consts["c_idb"][:],
               inc=sems["t"].tick(1))
        ctx.op("vector", "tensor_copy", zpT[:], pT32[:])
        for oc in range(8):
            osl = slice(oc * 512, (oc + 1) * 512)
            ctx.op("tensor", "matmul", pDl[:], zpT[:], bmb[:, osl],
                   start=True, stop=True, inc=sems["t"].tick(1))
            ctx.op("vector", "scalar_tensor_tensor", osb[:], pDl[:], 0.5,
                   Ht[:, osl], op0=ALU.mult, op1=ALU.add)
            ctx.op("vector", "tensor_add", osb[:], osb[:], x1t[:, osl])
            ctx.op("sync", "dma_start", x2_rows[rows, osl], osb[:])


def _emit_ffn1(ctx, sems, consts, stats, x3_rows, Acat, Bcat, WrT, x4_out):
    acatb = ctx.sb("d5Ac", (128, 32 * 64), BF16)
    bcatb = ctx.sb("d5Bc", (64, 4096), BF16)
    wrtb = ctx.sb("d5Wr", (128, 32 * 4), BF16)
    onesc = ctx.sb("d5oc", (128, 1), BF16)
    wsA = ctx.sb("d5wA", (1, 64), BF16)
    wsR = ctx.sb("d5wR", (1, 4), BF16)
    x3t = ctx.sb("d5x3", (128, 4096), F32)
    x3b = ctx.sb("d5x3b", (128, 4096), BF16)
    x3T = ctx.sb("d5x3T", (128, 32 * 128), BF16)
    lgt = ctx.sb("d5lg", (128, 4), F32)
    mid = ctx.sb("d5md", (128, 64), F32)
    nm = ctx.sb("d5nm", (128, 1), F32)
    m2 = ctx.sb("d5m2", (128, 1), F32)
    eqm = ctx.sb("d5eq", (128, 4), F32)
    t4 = ctx.sb("d5t4", (128, 4), F32)
    gat = ctx.sb("d5ga", (128, 4), F32)
    sm = ctx.sb("d5sm", (128, 1), F32)
    rsc = ctx.sb("d5rs", (128, 1), F32)
    midg = ctx.sb("d5mg", (128, 64), BF16)
    mgT = ctx.sb("d5mT", (64, 128), BF16)
    osb2 = ctx.sb("d5ob", (128, 512), F32)
    pP = ctx.ps("d5pP", (128, 512), F32)
    pL = ctx.ps("d5pL", (128, 68), F32)
    pT = ctx.ps("d5pT", (128, 128), BF16)
    pW = ctx.ps("d5pW", (1, 68), F32)
    for fc in range(32):
        ctx.op("gpsimd", "dma_start", acatb[:, fc * 64:(fc + 1) * 64],
               Acat[fc * 128:(fc + 1) * 128, :])
        ctx.op("gpsimd", "dma_start", wrtb[:, fc * 4:(fc + 1) * 4],
               WrT[fc * 128:(fc + 1) * 128, :])
    ctx.op("gpsimd", "dma_start", bcatb[:], Bcat[:, :])
    ctx.op("gpsimd", "memset", onesc[:], 1.0)
    for fc in range(32):
        ctx.op("tensor", "matmul", pW[0:1, 0:64], onesc[:],
               acatb[:, fc * 64:(fc + 1) * 64], start=(fc == 0), stop=False)
        ctx.op("tensor", "matmul", pW[0:1, 64:68], onesc[:],
               wrtb[:, fc * 4:(fc + 1) * 4], start=False,
               stop=(fc == 31), inc=(sems["t"].tick(1) if fc == 31 else None))
    ctx.op("vector", "tensor_copy", wsA[:], pW[0:1, 0:64])
    ctx.op("vector", "tensor_copy", wsR[:], pW[0:1, 64:68])

    for tg in range(NT_OWN):
        rows = slice(tg * 128, (tg + 1) * 128)
        ctx.op("sync", "dma_start", x3t[:], x3_rows[rows, :])
        ctx.op("vector", "tensor_copy", x3b[:], x3t[:])
        for fc in range(32):
            ctx.op("tensor", "transpose", pT[:],
                   x3b[:, fc * 128:(fc + 1) * 128], consts["c_idb"][:],
                   inc=sems["t"].tick(1))
            ctx.op("vector", "tensor_copy",
                   x3T[:, fc * 128:(fc + 1) * 128], pT[:])
        for fc in range(32):
            ctx.op("tensor", "matmul", pL[:, 0:64],
                   x3T[:, fc * 128:(fc + 1) * 128],
                   acatb[:, fc * 64:(fc + 1) * 64],
                   start=(fc == 0), stop=False)
            ctx.op("tensor", "matmul", pL[:, 64:68],
                   x3T[:, fc * 128:(fc + 1) * 128],
                   wrtb[:, fc * 4:(fc + 1) * 4], start=False, stop=False)
        ctx.op("tensor", "matmul", pL[:, 0:64],
               stats["negm"][0:1, tg * 128:(tg + 1) * 128], wsA[:],
               start=False, stop=False)
        ctx.op("tensor", "matmul", pL[:, 64:68],
               stats["negm"][0:1, tg * 128:(tg + 1) * 128], wsR[:],
               start=False, stop=True, inc=sems["t"].tick(1))
        ctx.op("scalar", "activation", mid[:], pL[:, 0:64], AF.Copy,
               scale=stats["rs"][:, tg:tg + 1])
        ctx.op("scalar", "activation", lgt[:], pL[:, 64:68], AF.Copy,
               scale=stats["rs"][:, tg:tg + 1])
        ctx.op("vector", "reduce_max", nm[:], lgt[:], axis=AX.X)
        ctx.op("vector", "tensor_scalar", eqm[:], lgt[:], nm[:], None,
               op0=ALU.is_ge)
        ctx.op("vector", "scalar_tensor_tensor", t4[:], eqm[:], -1e30,
               lgt[:], op0=ALU.mult, op1=ALU.add)
        ctx.op("vector", "reduce_max", m2[:], t4[:], axis=AX.X)
        ctx.op("vector", "tensor_scalar", eqm[:], lgt[:], m2[:], None,
               op0=ALU.is_ge)
        ctx.op("vector", "tensor_scalar_mul", nm[:], nm[:], -1.0)
        ctx.op("scalar", "activation", gat[:], lgt[:], AF.Exp, bias=nm[:])
        ctx.op("vector", "tensor_mul", gat[:], gat[:], eqm[:])
        ctx.op("vector", "reduce_sum", sm[:], gat[:], axis=AX.X)
        ctx.op("vector", "reciprocal", rsc[:], sm[:])
        ctx.op("vector", "tensor_scalar_mul", gat[:], gat[:], rsc[:])
        for e in range(4):
            ctx.op("vector", "tensor_scalar", midg[:, e * 16:(e + 1) * 16],
                   mid[:, e * 16:(e + 1) * 16], gat[:, e:e + 1], 4.0,
                   op0=ALU.mult, op1=ALU.mult)
        ctx.op("tensor", "transpose", pT[0:64, 0:128], midg[:],
               consts["c_idb"][:], inc=sems["t"].tick(1))
        ctx.op("vector", "tensor_copy", mgT[:], pT[0:64, 0:128])
        for oc in range(8):
            osl = slice(oc * 512, (oc + 1) * 512)
            ctx.op("tensor", "matmul", pP[:], mgT[:], bcatb[:, osl],
                   start=True, stop=True, inc=sems["t"].tick(1))
            ctx.op("vector", "tensor_add", osb2[:], pP[:], x3t[:, osl])
            ctx.op("sync", "dma_start", x4_out[rows, osl], osb2[:])


FUSED_CONSTS = ["c_idf", "c_idb", "c_U", "c_L", "c_cos", "c_sin", "c_swm",
                "c_swm0", "c_dmA", "c_dmB", "c_dmAb", "c_dmBb", "c_odd"]
RG_PAIR = [[0, 1], [2, 3], [4, 5], [6, 7]]
RG_ALL = [[0, 1, 2, 3, 4, 5, 6, 7]]


def emit_transpose_planes(ctx, rows_dram, planes_bf, n_tiles, sems, consts, tag):
    ctx.push_scope()
    xt = ctx.sb("tp_x" + tag, (128, 4096), F32)
    xb = ctx.sb("tp_b" + tag, (128, 4096), BF16)
    stg = ctx.sb("tp_s" + tag, (128, 4096), BF16)
    pT = ctx.ps("tp_p" + tag, (128, 128), BF16)
    idb = consts["c_idb"]
    for tg in range(n_tiles):
        ctx.op("sync", "dma_start", xt[:], rows_dram[tg * 128:(tg + 1) * 128, :])
        ctx.op("vector", "tensor_copy", xb[:], xt[:])
        for fc in range(32):
            ctx.op("tensor", "transpose", pT[:], xb[:, fc * 128:(fc + 1) * 128],
                   idb[:], inc=sems["t"].tick(1))
            ctx.op("vector", "tensor_copy", stg[:, fc * 128:(fc + 1) * 128],
                   pT[:])
        for g8 in range(4):
            ctx.op("gpsimd", "dma_start",
                   planes_bf[g8 * 8:(g8 + 1) * 8, :, tg * 128:(tg + 1) * 128]
                   .rearrange("fc d t -> d fc t"),
                   stg[:, g8 * 1024:(g8 + 1) * 1024]
                   .rearrange("d (fc t) -> d fc t", t=128))
    ctx.pop_scope()


def build_fused(upto="full"):
    nc = bass.Bass("TRN2", target_bir_lowering=False, debug=False)
    ctx = Ctx(nc)
    sems = make_sems(ctx)
    EI = "ExternalInput"
    x0_rows = nc.dram_tensor("x0_rows", [4096, 4096], F32, kind=EI)
    x0Tp = nc.dram_tensor("x0Tp", [32, 128, 4096], F32, kind=EI)
    x0_own = nc.dram_tensor("x0_own", [T_OWN, 4096], F32, kind=EI)
    wqkv = nc.dram_tensor("wqkv", [4096, 3072], F32, kind=EI)
    Wo_sl = nc.dram_tensor("Wo_sl", [2048, 4096], F32, kind=EI)
    WT = nc.dram_tensor("WT", [4096, 4096], F32, kind=EI)
    b_row = nc.dram_tensor("b_row", [1, 4096], F32, kind=EI)
    A4 = nc.dram_tensor("A4", [4096, 32], F32, kind=EI)
    LiT = nc.dram_tensor("LiT", [32, 8], F32, kind=EI)
    LiM = nc.dram_tensor("LiM", [8, 32], F32, kind=EI)
    Bm = nc.dram_tensor("Bm", [32, 4096], F32, kind=EI)
    Wcat = nc.dram_tensor("Wcat", [4096, 12288], F32, kind=EI)
    bcat = nc.dram_tensor("bcat", [1, 12288], F32, kind=EI)
    WoT = nc.dram_tensor("WoT", [4096, 4096], F32, kind=EI)
    bo_row = nc.dram_tensor("bo_row", [1, 4096], F32, kind=EI)
    Acat = nc.dram_tensor("Acat", [4096, 64], F32, kind=EI)
    Bcat = nc.dram_tensor("Bcat", [64, 4096], F32, kind=EI)
    WrT = nc.dram_tensor("WrT", [4096, 4], F32, kind=EI)
    qf = nc.dram_tensor("qf", [4096, 2048], F32)
    kf = nc.dram_tensor("kf", [4096, 512], F32)
    vf = nc.dram_tensor("vf", [4096, 512], F32)
    oT_send = nc.dram_tensor("oT_send", [2, 16, 128, T_OWN], BF16)
    rs_in = nc.dram_tensor("rs_in", [4096, 4096], BF16)
    rs_out = nc.dram_tensor("rs_out", [T_OWN, 4096], BF16)
    x1_rows = nc.dram_tensor("x1_rows", [T_OWN, 4096], F32)
    x1Tp = nc.dram_tensor("x1Tp", [32, 128, T_OWN], BF16)
    H_rows = nc.dram_tensor("H_rows", [T_OWN, 4096], F32)
    z_rows = nc.dram_tensor("z_rows", [T_OWN, 32], F32)
    ds_rows = nc.dram_tensor("ds_rows", [T_OWN, 8], F32)
    scales_loc = nc.dram_tensor("scales_loc", [1, 2], F32)
    scales_all = nc.dram_tensor("scales_all", [1, 16], F32)
    x2_rows = nc.dram_tensor("x2_rows", [T_OWN, 4096], F32)
    x2Tp = nc.dram_tensor("x2Tp", [32, 128, T_OWN], BF16)
    qT_loc = nc.dram_tensor("qT_loc", [32, 128, T_OWN], BF16)
    kT_loc = nc.dram_tensor("kT_loc", [32, 128, T_OWN], BF16)
    v_loc = nc.dram_tensor("v_loc", [32, T_OWN, 128], BF16)
    kT_rm = nc.dram_tensor("kT_rm", [2, 32, 128, T_OWN], BF16)
    v_rm = nc.dram_tensor("v_rm", [2, 32, T_OWN, 128], BF16)
    kmax_loc = nc.dram_tensor("kmax_loc", [1, 1], F32)
    kmax_all = nc.dram_tensor("kmax_all", [1, 8], F32)
    oT_l1 = nc.dram_tensor("oT_l1", [32, 128, T_OWN], BF16)
    x3_rows = nc.dram_tensor("x3_rows", [T_OWN, 4096], F32)
    x4_out = nc.dram_tensor("x4_out", [T_OWN, 4096], F32, kind="ExternalOutput")

    def _stop_here(ctx_, name, src_dram, shape):
        t = ctx_.sb("dbgcp" + name, (128, 4096), F32)
        for tg in range(NT_OWN):
            rows = slice(tg * 128, (tg + 1) * 128)
            ctx_.op("sync", "dma_start", t[:], src_dram[rows, :])
            ctx_.op("sync", "dma_start", x4_out[rows, :], t[:])
        a = ctx_._autosem("sync", "d")
        ctx_._need("sync", (a.h, a.n))

    with nc.Block() as block:
        consts = load_consts(ctx, FUSED_CONSTS, sems)

        stats0 = emit_stats(ctx, x0_rows, 32, sems, consts, tag="f0")
        ctx.push_scope()
        out_sb = [ctx.sb("f1o%d" % i, (128, 512), F32) for i in range(2)]
        epar = [0]

        def epi1(tg, o0, n_o, psum, tval):
            par = epar[0]
            epar[0] = 1 - par
            ctx.op("scalar", "activation", out_sb[par][:, 0:n_o],
                   psum[:, 0:n_o], AF.Copy, scale=stats0["rs"][:, tg:tg + 1],
                   inc=sems["e"].tick(1))
            if o0 < 2048:
                dest = qf[tg * 128:(tg + 1) * 128, o0:o0 + n_o]
            elif o0 < 2560:
                dest = kf[tg * 128:(tg + 1) * 128, 0:n_o]
            else:
                dest = vf[tg * 128:(tg + 1) * 128, 0:n_o]
            ctx.op("sync", "dma_start", dest, out_sb[par][:, 0:n_o])

        emit_proj(ctx, xT_planes=x0Tp, n_tok=4096, w_dram=wqkv, n_out=3072,
                  stats=stats0, sems=sems, consts=consts, epilogue=epi1,
                  tag="f1")
        ctx.pop_scope()

        def oT_writer(c, oT_stage):
            r, i = c & 1, c >> 1
            for g in range(2):
                ctx.op("gpsimd", "dma_start",
                       oT_send[r, g * 8:(g + 1) * 8, :, i * 128:(i + 1) * 128]
                       .rearrange("h d t -> d h t"),
                       oT_stage[:, g * 1024:(g + 1) * 1024]
                       .rearrange("d (h t) -> d h t", t=128))

        emit_mixer(ctx, qf=qf, kf=kf, vf=vf, oT_out=None, sems=sems,
                   consts=consts, tag="f", oT_writer=oT_writer)

        ctx.push_scope()
        out2 = [ctx.sb("f2o%d" % i, (128, 512), BF16) for i in range(2)]
        epar2 = [0]

        def epi2(tg, o0, n_o, psum, tval):
            par = epar2[0]
            epar2[0] = 1 - par
            rows = slice(tg * 128, (tg + 1) * 128)
            ctx.op("scalar", "activation", out2[par][:, 0:n_o],
                   psum[:, 0:n_o], AF.Copy, inc=sems["e"].tick(1))
            ctx.op("sync", "dma_start", rs_in[rows, o0:o0 + n_o],
                   out2[par][:, 0:n_o])

        def oT_loader(c2, xp, tb):
            hf, toff = divmod(tb * 512, 2048)
            for g in range(2):
                f0 = g * 8
                c2.op("gpsimd", "dma_start",
                      xp[:, f0 * 512:(f0 + 8) * 512]
                      .rearrange("p (fc t) -> p fc t", t=512),
                      oT_send[hf, f0:f0 + 8, :, toff:toff + 512]
                      .rearrange("fc p t -> p fc t"))

        emit_proj(ctx, xT_planes=None, n_tok=4096, w_dram=Wo_sl, n_out=4096,
                  stats=None, sems=sems, consts=consts, epilogue=epi2,
                  tag="f2", nfc=16, x_loader=oT_loader)
        ctx.pop_scope()
        ctx.op("gpsimd", "collective_compute", "ReduceScatter", ALU.add,
               replica_groups=RG_PAIR, ins=[rs_in.ap().opt()],
               outs=[rs_out.ap().opt()])
        ctx.push_scope()
        x0t = ctx.sb("f2x0", (128, 4096), F32)
        rst = ctx.sb("f2rs", (128, 4096), BF16)
        x1s = ctx.sb("f2x1", (128, 4096), F32)
        for tg in range(NT_OWN):
            rows = slice(tg * 128, (tg + 1) * 128)
            ctx.op("sync", "dma_start", x0t[:], x0_own[rows, :])
            ctx.op("sync", "dma_start", rst[:], rs_out[rows, :])
            ctx.op("vector", "tensor_add", x1s[:], x0t[:], rst[:])
            ctx.op("sync", "dma_start", x1_rows[rows, :], x1s[:])
        ctx.pop_scope()

        if upto == "x1":
            _stop_here(ctx, "x1", x1_rows, None)
            ctx.finalize(block)
            ctx.st.close()
            return nc

        emit_transpose_planes(ctx, x1_rows, x1Tp, NT_OWN, sems, consts, "f1T")
        statsB = emit_stats(ctx, x1_rows, NT_OWN, sems, consts, tag="fB")
        ctx.push_scope()
        hmax = ctx.sb("fbhm", (128, 1), F32)
        dmax = ctx.sb("fbdm", (128, 1), F32)
        tmp8 = ctx.sb("fbt8", (128, 8), F32)
        m8 = ctx.sb("fbm8", (128, 1), F32)
        ctx.op("gpsimd", "memset", hmax[:], 0.0)
        ctx.op("gpsimd", "memset", dmax[:], 0.0)
        outH = [ctx.sb("fbo%d" % i, (128, 512), F32) for i in range(2)]
        eparH = [0]

        def epiH(tg, o0, n_o, psum, tval):
            par = eparH[0]
            eparH[0] = 1 - par
            rows = slice(tg * 128, (tg + 1) * 128)
            ctx.op("scalar", "activation", outH[par][:, 0:n_o],
                   psum[:, 0:n_o], AF.Copy, scale=statsB["rs"][:, tg:tg + 1],
                   inc=sems["e"].tick(1))
            ctx.op("sync", "dma_start", H_rows[rows, o0:o0 + n_o],
                   outH[par][:, 0:n_o])
            if o0 == 0:
                ctx.op("vector", "tensor_scalar_mul", tmp8[:],
                       outH[par][:, 0:8], consts["c_odd"][:])
                ctx.op("vector", "reduce_max", m8[:], tmp8[:], axis=AX.X,
                       apply_absolute_value=True)
                ctx.op("vector", "tensor_max", hmax[:], hmax[:], m8[:])

        emit_proj(ctx, xT_planes=x1Tp, n_tok=T_OWN, w_dram=WT, n_out=4096,
                  stats=statsB, sems=sems, consts=consts, bias_dram=b_row,
                  epilogue=epiH, tag="fbH")
        z_sb = ctx.sb("fbz", (128, 32), F32)

        def epiZ(tg, o0, n_o, psum, tval):
            rows = slice(tg * 128, (tg + 1) * 128)
            ctx.op("scalar", "activation", z_sb[:, 0:n_o], psum[:, 0:n_o],
                   AF.Copy, scale=statsB["rs"][:, tg:tg + 1],
                   inc=sems["e"].tick(1))
            ctx.op("sync", "dma_start", z_rows[rows, 0:n_o], z_sb[:, 0:n_o])

        emit_proj(ctx, xT_planes=x1Tp, n_tok=T_OWN, w_dram=A4, n_out=32,
                  stats=statsB, sems=sems, consts=consts, epilogue=epiZ,
                  tag="fbz")
        zt = ctx.sb("fbzt", (128, 32), F32)
        ztb = ctx.sb("fbztb", (128, 32), BF16)
        zTb = ctx.sb("fbzT", (32, 128), BF16)
        litb = ctx.sb("fblit", (32, 8), BF16)
        ds_sb = ctx.sb("fbds", (128, 8), F32)
        pTz = ctx.ps("fbpT", (32, 128), BF16)
        pDz = ctx.ps("fbpD", (128, 8), F32)
        ctx.op("gpsimd", "dma_start", litb[:], LiT[:, :])
        for tg in range(NT_OWN):
            rows = slice(tg * 128, (tg + 1) * 128)
            ctx.op("sync", "dma_start", zt[:], z_rows[rows, :])
            ctx.op("vector", "tensor_copy", ztb[:], zt[:])
            ctx.op("tensor", "transpose", pTz[:], ztb[:], consts["c_idb"][:],
                   inc=sems["t"].tick(1))
            ctx.op("vector", "tensor_copy", zTb[:], pTz[:])
            ctx.op("tensor", "matmul", pDz[:], zTb[:], litb[:], start=True,
                   stop=True, inc=sems["t"].tick(1))
            ctx.op("vector", "tensor_copy", ds_sb[:], pDz[:])
            ctx.op("sync", "dma_start", ds_rows[rows, :], ds_sb[:])
            ctx.op("vector", "tensor_scalar_mul", tmp8[:], ds_sb[:],
                   consts["c_odd"][:])
            ctx.op("vector", "reduce_max", m8[:], tmp8[:], axis=AX.X,
                   apply_absolute_value=True)
            ctx.op("vector", "tensor_max", dmax[:], dmax[:], m8[:])
        pRs = ctx.ps("fbpR", (1, 128), F32)
        srow = ctx.sb("fbsr", (1, 128), F32)
        sc2 = ctx.sb("fbsc", (1, 2), F32)
        for i, acc in enumerate((hmax, dmax)):
            ctx.op("tensor", "transpose", pRs[:], acc[:], consts["c_idf"][:],
                   inc=sems["t"].tick(1))
            ctx.op("vector", "tensor_copy", srow[:], pRs[:])
            ctx.op("vector", "reduce_max", sc2[:, i:i + 1], srow[:], axis=AX.X)
        ctx.op("sync", "dma_start", scales_loc[:, :], sc2[:])
        ctx.pop_scope()

        ctx.op("gpsimd", "collective_compute", "AllGather", ALU.bypass,
               replica_groups=RG_ALL, ins=[scales_loc.ap().opt()],
               outs=[scales_all.ap().opt()])

        ctx.push_scope()
        _emit_gating(ctx, sems, consts, x1_rows, H_rows, z_rows, ds_rows,
                     scales_all, LiM, Bm, x2_rows)
        ctx.pop_scope()

        if upto == "x2":
            _stop_here(ctx, "x2", x2_rows, None)
            ctx.finalize(block)
            ctx.st.close()
            return nc

        emit_transpose_planes(ctx, x2_rows, x2Tp, NT_OWN, sems, consts, "f2T")
        statsC = emit_stats(ctx, x2_rows, NT_OWN, sems, consts, tag="fC")
        ctx.push_scope()
        outq = [ctx.sb("fco%d" % i, (128, 512), BF16) for i in range(2)]
        stgq = ctx.sb("fcst", (128, 128), BF16)
        sqs = ctx.sb("fcsq", (128, 128), F32)
        kn = ctx.sb("fckn", (128, 1), F32)
        knacc = ctx.sb("fcka", (128, 1), F32)
        pTq = ctx.ps("fcpT", (128, 128), BF16)
        ctx.op("gpsimd", "memset", knacc[:], 0.0)
        eparq = [0]

        def epiQ(tg, o0, n_o, psum, tval):
            par = eparq[0]
            eparq[0] = 1 - par
            rows = slice(tg * 128, (tg + 1) * 128)
            ctx.op("scalar", "activation", outq[par][:, 0:n_o],
                   psum[:, 0:n_o], AF.Copy, scale=statsC["rs"][:, tg:tg + 1],
                   inc=sems["e"].tick(1))
            which, off = divmod(o0, 4096)
            for j in range(4):
                h = off // 128 + j
                sl = slice(j * 128, (j + 1) * 128)
                if which < 2:
                    dst = (qT_loc, kT_loc)[which]
                    ctx.op("tensor", "transpose", pTq[:], outq[par][:, sl],
                           consts["c_idb"][:], inc=sems["t"].tick(1))
                    ctx.op("vector", "tensor_copy", stgq[:], pTq[:])
                    ctx.op("sync", "dma_start",
                           dst[h, :, tg * 128:(tg + 1) * 128], stgq[:])
                else:
                    ctx.op("sync", "dma_start",
                           v_loc[h, tg * 128:(tg + 1) * 128, :],
                           outq[par][:, sl])
                if which == 1:
                    ctx.op("scalar", "activation", sqs[:], outq[par][:, sl],
                           AF.Square, accum_out=kn[:])
                    ctx.op("vector", "tensor_max", knacc[:], knacc[:], kn[:])

        emit_proj(ctx, xT_planes=x2Tp, n_tok=T_OWN, w_dram=Wcat, n_out=12288,
                  stats=statsC, sems=sems, consts=consts, bias_dram=bcat,
                  epilogue=epiQ, tag="fc")
        pRk = ctx.ps("fcpR", (1, 128), F32)
        srk = ctx.sb("fcsr", (1, 128), F32)
        km1 = ctx.sb("fckm", (1, 1), F32)
        ctx.op("tensor", "transpose", pRk[:], knacc[:], consts["c_idf"][:],
               inc=sems["t"].tick(1))
        ctx.op("vector", "tensor_copy", srk[:], pRk[:])
        ctx.op("vector", "reduce_max", km1[:], srk[:], axis=AX.X)
        ctx.op("sync", "dma_start", kmax_loc[:, :], km1[:])
        ctx.pop_scope()

        ctx.op("gpsimd", "collective_compute", "AllGather", ALU.bypass,
               replica_groups=RG_ALL, ins=[kmax_loc.ap().opt()],
               outs=[kmax_all.ap().opt()])
        ctx.op("gpsimd", "collective_compute", "AllGather", ALU.bypass,
               replica_groups=RG_PAIR, ins=[kT_loc.ap().opt()],
               outs=[kT_rm.ap().opt()])
        ctx.op("gpsimd", "collective_compute", "AllGather", ALU.bypass,
               replica_groups=RG_PAIR, ins=[v_loc.ap().opt()],
               outs=[v_rm.ap().opt()])

        if upto == "qkv":
            _stop_here(ctx, "qkv", x2_rows, None)
            ctx.finalize(block)
            ctx.st.close()
            return nc

        emit_attn_l1(ctx, qT=qT_loc, kT=kT_rm, vpl=v_rm, kmax_all=kmax_all,
                     oT_l1=oT_l1, sems=sems, consts=consts, rank_major=True)

        if upto == "attn":
            _stop_here(ctx, "attn", x2_rows, None)
            ctx.finalize(block)
            ctx.st.close()
            return nc

        ctx.push_scope()
        x2t = ctx.sb("f5x2", (128, 512), F32)
        out5 = [ctx.sb("f5o%d" % i, (128, 512), F32) for i in range(2)]
        epar5 = [0]

        def epi5(tg, o0, n_o, psum, tval):
            par = epar5[0]
            epar5[0] = 1 - par
            rows = slice(tg * 128, (tg + 1) * 128)
            ctx.op("sync", "dma_start", x2t[:, 0:n_o], x2_rows[rows, o0:o0 + n_o])
            ctx.op("vector", "tensor_add", out5[par][:, 0:n_o],
                   psum[:, 0:n_o], x2t[:, 0:n_o], inc=sems["e"].tick(1))
            ctx.op("sync", "dma_start", x3_rows[rows, o0:o0 + n_o],
                   out5[par][:, 0:n_o])

        emit_proj(ctx, xT_planes=oT_l1, n_tok=T_OWN, w_dram=WoT, n_out=4096,
                  stats=None, sems=sems, consts=consts, bias_dram=bo_row,
                  epilogue=epi5, tag="f5o")
        ctx.pop_scope()

        if upto == "x3":
            _stop_here(ctx, "x3", x3_rows, None)
            ctx.finalize(block)
            ctx.st.close()
            return nc

        statsD = emit_stats(ctx, x3_rows, NT_OWN, sems, consts, tag="fD")
        ctx.push_scope()
        _emit_ffn1(ctx, sems, consts, statsD, x3_rows, Acat, Bcat, WrT, x4_out)
        a = ctx._autosem("sync", "d")
        ctx._need("sync", (a.h, a.n))
        ctx.pop_scope()
        ctx.finalize(block)
    ctx.st.close()
    return nc


def host_inputs_fused(inputs, core):
    b, p = core // 2, core % 2
    x0 = np.ascontiguousarray(inputs["L0_attn_hidden_states"][b])
    hs = slice(p * 2048, (p + 1) * 2048)
    ks = slice(p * 512, (p + 1) * 512)
    wqkv = np.concatenate([inputs["L0_attn_Wq"][:, hs],
                           inputs["L0_attn_Wk"][:, ks],
                           inputs["L0_attn_Wv"][:, ks]], axis=1)
    Wcat = np.concatenate([inputs["L1_attn_Wq"].T, inputs["L1_attn_Wk"].T,
                           inputs["L1_attn_Wv"].T], axis=1)
    bcat = np.concatenate([inputs["L1_attn_bq"], inputs["L1_attn_bk"],
                           inputs["L1_attn_bv"]])[None, :]
    d = {"x0_rows": x0,
         "x0Tp": np.ascontiguousarray(x0.T.reshape(32, 128, 4096)),
         "x0_own": np.ascontiguousarray(x0[own_rows(p)]),
         "wqkv": np.ascontiguousarray(wqkv),
         "Wo_sl": np.ascontiguousarray(
             inputs["L0_attn_Wo"][p * 2048:(p + 1) * 2048, :]),
         "WT": np.ascontiguousarray(inputs["L0_ffn_W"].T),
         "b_row": np.ascontiguousarray(inputs["L0_ffn_b"][None, :]),
         "A4": inputs["L0_ffn_A"],
         "LiT": np.ascontiguousarray(inputs["L0_ffn_LiMEs"].T),
         "LiM": inputs["L0_ffn_LiMEs"],
         "Bm": inputs["L0_ffn_Bmat"],
         "Wcat": np.ascontiguousarray(Wcat),
         "bcat": np.ascontiguousarray(bcat),
         "WoT": np.ascontiguousarray(inputs["L1_attn_Wo"].T),
         "bo_row": np.ascontiguousarray(inputs["L1_attn_bo"][None, :]),
         "Acat": np.ascontiguousarray(
             inputs["L1_ffn_A"].transpose(2, 0, 1).reshape(4096, 64)),
         "Bcat": np.ascontiguousarray(
             inputs["L1_ffn_B"].transpose(0, 2, 1).reshape(64, 4096)),
         "WrT": np.ascontiguousarray(inputs["L1_ffn_Wr"].T)}
    d.update({k: v for k, v in host_consts(p).items() if k in FUSED_CONSTS})
    return d


def kernel_fused(**inputs):
    inputs = {k: np.ascontiguousarray(np.asarray(v)) for k, v in inputs.items()}
    cores = list(range(NCORE))
    nc = _get("FUSED", build_fused)
    res = run_bass_kernel_spmd(nc, [host_inputs_fused(inputs, c)
                                    for c in cores], core_ids=cores)
    out = np.empty((B, N, DM), np.float32)
    for c in cores:
        out[c // 2][own_rows(c % 2)] = np.asarray(res.results[c]["x4_out"])
    return out


# version 06 (final)
# speedup vs baseline: 1.3281x; 1.0291x over previous
import numpy as np
import ml_dtypes
from contextlib import ExitStack
import concourse.bass as bass
import concourse.mybir as mybir
from concourse.bass_utils import run_bass_kernel_spmd

F32 = mybir.dt.float32
BF16 = mybir.dt.bfloat16
AF = mybir.ActivationFunctionType
ALU = mybir.AluOpType
AX = mybir.AxisListType
NBF = ml_dtypes.bfloat16

NCORE = 8
B, N, DM = 4, 4096, 4096
NT_OWN = 16
T_OWN = NT_OWN * 128
HD = 128
EPS_LN = 1e-5
CSC = float(1.0 / np.sqrt(128.0))

ENGINES = ("tensor", "vector", "scalar", "gpsimd", "sync")


def host_consts(parity):
    c = {}
    c["c_idf"] = np.eye(128, dtype=np.float32)
    c["c_idb"] = np.eye(128, dtype=np.float32).astype(NBF)
    c["c_U"] = np.triu(np.ones((128, 128), np.float32))
    c["c_L"] = np.tril(np.ones((128, 128), np.float32), -1)
    pos = np.arange(N, dtype=np.float64)
    half = np.arange(64, dtype=np.float64)
    inv = 1.0 / (10000.0 ** (2.0 * half / 128.0))
    fr = pos[:, None] * inv[None, :]
    emb = np.concatenate([fr, fr], -1)
    cos = np.cos(emb).astype(np.float32).reshape(32, 128, 128)
    sin = np.sin(emb).astype(np.float32).reshape(32, 128, 128)
    c["c_cos"] = np.ascontiguousarray(cos.transpose(1, 0, 2).reshape(128, 4096))
    c["c_sin"] = np.ascontiguousarray(sin.transpose(1, 0, 2).reshape(128, 4096))
    qi = np.arange(128)[:, None]
    ji = np.arange(256)[None, :]
    ok = (ji >= qi + 64) & (ji <= qi + 128)
    c["c_swm"] = np.where(ok, 0.0, -1e9).astype(np.float32)
    c["c_swm0"] = np.where(ok & (ji >= 128), 0.0, -1e9).astype(np.float32)
    tri_kq = (np.arange(128)[:, None] <= np.arange(128)[None, :])
    if parity == 0:
        mA, mB2 = tri_kq.astype(np.float32), np.zeros((128, 128), np.float32)
    else:
        mA, mB2 = np.ones((128, 128), np.float32), tri_kq.astype(np.float32)
    c["c_dmA"], c["c_dmB"] = mA, mB2
    c["c_dmAb"], c["c_dmBb"] = mA.astype(NBF), mB2.astype(NBF)
    c["c_odd"] = np.ascontiguousarray(
        (np.arange(128) % 2 == 1).astype(np.float32)[:, None])
    return c

CONST_SPECS = {
    "c_idf": ((128, 128), F32), "c_idb": ((128, 128), BF16),
    "c_U": ((128, 128), F32), "c_L": ((128, 128), F32),
    "c_cos": ((128, 4096), F32), "c_sin": ((128, 4096), F32),
    "c_swm": ((128, 256), F32), "c_swm0": ((128, 256), F32),
    "c_dmA": ((128, 128), F32), "c_dmB": ((128, 128), F32),
    "c_dmAb": ((128, 128), BF16), "c_dmBb": ((128, 128), BF16),
    "c_odd": ((128, 1), F32),
}


MAX_SEM_TRACK = {}

class Sem:
    def __init__(self, ctx, name):
        self.name = name
        self.h = ctx.st.enter_context(ctx.nc.semaphore(name))
        self.n = 0
        _SEM_VALUES[id(self.h)] = 0
    def tick(self, k):
        self.n += k
        _SEM_VALUES[id(self.h)] = self.n
        MAX_SEM_TRACK[self.name if hasattr(self, "name") else id(self)] = self.n
        return (self.h, k)

_KW_WRITE = {"accum_out"}
_KW_READ = {"bias", "scale", "alpha"}

def _ap_name(a):
    if isinstance(a, bass.AP):
        return a.tensor.name
    if hasattr(a, "ap") and hasattr(a, "dtype") and hasattr(a, "name"):
        return a.name
    return None


class Ctx:
    def __init__(self, nc):
        self.nc = nc
        self.st = ExitStack()
        self.scopes = [self.st]
        self.q = {e: [] for e in ENGINES}
        self.auto = {}
        self.qst = {}
        self.last_w = {}
        self.readers = {}
        self.known = {e: {} for e in ENGINES}

    def _autosem(self, eng, kind="c"):
        key = (eng, kind)
        if key not in self.auto:
            self.auto[key] = Sem(self, "auto_%s_%s" % (kind, eng))
        return self.auto[key]

    def push_scope(self):
        self.scopes.append(ExitStack())

    def pop_scope(self):
        if not hasattr(self, "_barsem"):
            self._barsem = Sem(self, "scopebar")
        self.barrier(self._barsem)
        self.scopes.pop().close()

    def sem(self, name):
        return Sem(self, name)

    def sb(self, name, shape, dt, persist=False):
        stk = self.scopes[0] if persist else self.scopes[-1]
        return stk.enter_context(self.nc.sbuf_tensor(name, list(shape), dt))

    def ps(self, name, shape, dt=F32):
        return self.scopes[-1].enter_context(
            self.nc.psum_tensor(name, list(shape), dt))

    def _need(self, eng, dep):
        if dep is None:
            return
        sem, val = dep
        if self.known[eng].get(id(sem), -1) >= val:
            return
        self.known[eng][id(sem)] = val
        self.q[eng].append(("wait", sem, val))

    def wait(self, eng, sem, val=None):
        v = sem.n if val is None else val
        self._need(eng, (sem.h, v))

    def op(self, eng, meth, *args, inc=None, **kw):
        writes, reads = [], []
        if meth == "collective_compute":
            for a in kw.get("ins", []):
                reads.append(_ap_name(a))
            for a in kw.get("outs", []):
                writes.append(_ap_name(a))
        else:
            first = True
            for a in args:
                nm = _ap_name(a)
                if nm is not None:
                    (writes if first else reads).append(nm)
                    first = False
                elif first and meth in ("dma_start",):
                    pass
            for k, a in kw.items():
                nm = _ap_name(a)
                if nm is None:
                    continue
                (writes if k in _KW_WRITE else reads).append(nm)
        for nm in reads:
            self._dep(eng, self.last_w.get(nm))
        for nm in writes:
            self._dep(eng, self.last_w.get(nm))
            for dep in self.readers.get(nm, ()):
                self._dep(eng, dep)
        async_op = meth in ("dma_start", "collective_compute")
        if async_op:
            kind = "cc" if meth == "collective_compute" else "d"
            s = self._autosem(eng, kind)
            k = 1 if meth == "collective_compute" else 16
            s.tick(k)
            inc2 = (s.h, None) if meth == "collective_compute" else (s.h, k)
            st = self._qstate(eng, kind)
            st["issued"] += 1
            stamp = ("pend", (eng, kind), st["issued"])
        else:
            if inc is None:
                s = self._autosem(eng)
                s.tick(1)
                inc2 = (s.h, 1)
                stamp = (s.h, s.n, eng)
            else:
                inc2 = inc
                stamp = (inc2[0], self._manual_val(inc2[0]), eng)
        for nm in writes:
            self.last_w[nm] = stamp
            self.readers[nm] = []
        for nm in reads:
            self.readers.setdefault(nm, []).append(stamp)
        self.q[eng].append(("op", meth, args, kw, inc2))

    def _qstate(self, eng, kind):
        key = (eng, kind)
        if key not in self.qst:
            self.qst[key] = {"issued": 0, "confirms": [],
                             "ready": Sem(self, "rdy_%s_%s" % (kind, eng))}
        return self.qst[key]

    def _confirm(self, qkey, idx):
        eng, kind = qkey
        st = self._qstate(eng, kind)
        for upto, val in st["confirms"]:
            if upto >= idx:
                return (st["ready"].h, val)
        a = self._autosem(eng, kind)
        self._need(eng, (a.h, a.n))
        t = st["ready"].tick(1)
        self.q[eng].append(("inc", t[0], t[1]))
        st["confirms"].append((st["issued"], st["ready"].n))
        return (st["ready"].h, st["ready"].n)

    def _dep(self, eng, dep):
        if dep is None:
            return
        if dep[0] == "pend":
            _, qkey, idx = dep
            h, v = self._confirm(qkey, idx)
            self._need(eng, (h, v))
            return
        if eng == "tensor" and dep[2] == "tensor":
            return
        self._need(eng, (dep[0], dep[1]))

    def _manual_val(self, handle):
        return _SEM_VALUES[id(handle)]

    def barrier(self, bsem, engines=ENGINES):
        for e in engines:
            t = bsem.tick(1)
            self.q[e].append(("inc", t[0], t[1]))
        for e in engines:
            self.wait(e, bsem)

    def finalize(self, block):
        def runner(items):
            def run(eng):
                for it in items:
                    if it[0] == "op":
                        _, meth, args, kw, inc = it
                        r = getattr(eng, meth)(*args, **kw)
                        if inc is not None:
                            if inc[1] is None:
                                r.then_inc(inc[0])
                            else:
                                r.then_inc(inc[0], inc[1])
                    elif it[0] == "wait":
                        eng.wait_ge(it[1], it[2])
                    elif it[0] == "inc":
                        eng.sem_inc(it[1], it[2])
            return run
        for e in ENGINES:
            getattr(block, e)(runner(self.q[e]))


_SEM_VALUES = {}


def make_sems(ctx):
    return {"d": ctx.sem("dsem"), "t": ctx.sem("tsem"), "v": ctx.sem("vsem"),
            "a": ctx.sem("asem"), "e": ctx.sem("esem"), "s": ctx.sem("ssem"),
            "bar": ctx.sem("barsem")}


def load_consts(ctx, names, sems):
    nc = ctx.nc
    out = {}
    for name in names:
        shape, dt = CONST_SPECS[name]
        d = nc.dram_tensor(name, list(shape), dt, kind="ExternalInput")
        t = ctx.sb("s" + name, shape, dt)
        ctx.op("sync", "dma_start", t[:], d[:, :])
        out[name] = t
    return out


def emit_stats(ctx, x_rows, n_tiles, sems, consts, tag=""):
    dsem, tsem, vsem, asem = sems["d"], sems["t"], sems["v"], sems["a"]
    rs_cols = ctx.sb("strs" + tag, (128, n_tiles), F32)
    negm_rows = ctx.sb("stnm" + tag, (1, n_tiles * 128), BF16)
    std_rows = ctx.sb("stsd" + tag, (1, n_tiles * 128), BF16)
    ctx.push_scope()
    xt = ctx.sb("stx" + tag, (128, 4096), F32)
    sq = ctx.sb("stq" + tag, (128, 4096), F32)
    s1 = ctx.sb("st1" + tag, (128, 1), F32)
    ss = ctx.sb("st2" + tag, (128, 1), F32)
    mvar = ctx.sb("st3" + tag, (128, 1), F32)
    mstd = ctx.sb("st4" + tag, (128, 1), F32)
    mm_ = ctx.sb("st5" + tag, (128, 1), F32)
    msq = ctx.sb("st6" + tag, (128, 1), F32)
    pkf = ctx.sb("st7" + tag, (128, 2), F32)
    pk = ctx.sb("st8" + tag, (128, 2), BF16)
    rows_ev = ctx.sb("st9" + tag, (2, 128), BF16)
    pT = ctx.ps("stpT" + tag, (2, 128), BF16)
    idb = consts["c_idb"]

    for tt in range(n_tiles):
        if tt > 0:
            ctx.wait("sync", vsem)
            ctx.wait("sync", asem)
        ctx.op("sync", "dma_start", xt[:], x_rows[tt * 128:(tt + 1) * 128, :])
        ctx.wait("vector", dsem)
        ctx.op("vector", "reduce_sum", s1[:], xt[:], axis=AX.X, inc=vsem.tick(1))
        ctx.wait("scalar", dsem)
        ctx.op("scalar", "activation", sq[:], xt[:], AF.Square,
               accum_out=ss[:], inc=asem.tick(1))
        ctx.wait("vector", asem)
        ctx.op("vector", "tensor_scalar_mul", mm_[:], s1[:], 1.0 / 4096.0)
        ctx.op("vector", "tensor_mul", msq[:], mm_[:], mm_[:])
        ctx.op("vector", "scalar_tensor_tensor", mvar[:], ss[:], 1.0 / 4096.0,
               msq[:], op0=ALU.mult, op1=ALU.subtract)
        ctx.op("vector", "tensor_scalar_add", mvar[:], mvar[:], EPS_LN,
               inc=vsem.tick(1))
        ctx.wait("scalar", vsem)
        ctx.op("scalar", "activation", mstd[:], mvar[:], AF.Sqrt,
               inc=asem.tick(1))
        ctx.wait("vector", asem)
        ctx.op("vector", "reciprocal", rs_cols[:, tt:tt + 1], mstd[:])
        ctx.op("vector", "tensor_scalar_mul", pkf[:, 0:1], mm_[:], -1.0)
        ctx.op("vector", "tensor_copy", pkf[:, 1:2], mstd[:])
        ctx.op("vector", "tensor_copy", pk[:], pkf[:], inc=vsem.tick(1))
        ctx.wait("tensor", vsem)
        ctx.op("tensor", "transpose", pT[:], pk[:], idb[:], inc=tsem.tick(1))
        ctx.wait("vector", tsem)
        ctx.op("vector", "tensor_copy", rows_ev[:], pT[:], inc=vsem.tick(1))
        ctx.wait("gpsimd", vsem)
        ctx.op("gpsimd", "dma_start", negm_rows[:, tt * 128:(tt + 1) * 128],
               rows_ev[0:1, :])
        ctx.op("gpsimd", "dma_start", std_rows[:, tt * 128:(tt + 1) * 128],
               rows_ev[1:2, :])
    for e in ENGINES:
        ctx.wait(e, dsem)
        ctx.wait(e, vsem)
    ctx.pop_scope()
    return {"rs": rs_cols, "negm": negm_rows, "std": std_rows}


def emit_proj(ctx, *, xT_planes, n_tok, w_dram, n_out, stats, sems, consts,
              bias_dram=None, epilogue=None, tag="", nfc=32, x_loader=None):
    dsem, tsem, vsem, esem = sems["d"], sems["t"], sems["v"], sems["e"]
    ctx.push_scope()
    NP = (n_out + 511) // 512
    n_tb = n_tok // 512
    wpan = [ctx.sb("pjw%d" % i + tag, (128, nfc * 512), BF16) for i in range(2)]
    xtb = [ctx.sb("pjx%d" % i + tag, (128, nfc * 512), BF16) for i in range(2)]
    onesc = ctx.sb("pj1c" + tag, (128, 1), BF16)
    ones_row = ctx.sb("pj1r" + tag, (1, 128), BF16)
    wsum = [ctx.sb("pjws%d" % i + tag, (1, 512), BF16) for i in range(2)]
    brow = [ctx.sb("pjbr%d" % i + tag, (1, 512), BF16) for i in range(2)]
    pP = [ctx.ps("pjp%d" % i + tag, (128, 512), F32) for i in range(4)]
    pW = ctx.ps("pjpw" + tag, (1, 512), F32)
    gsem = ctx.sem("pjg" + tag)
    ctx.op("gpsimd", "memset", onesc[:], 1.0)
    ctx.op("gpsimd", "memset", ones_row[:], 1.0, inc=gsem.tick(1))
    fold = stats is not None
    ng = max(1, nfc // 8)

    def load_w(pan):
        wp = wpan[pan % 2]
        o0 = pan * 512
        n_o = min(512, n_out - o0)
        for g in range(ng):
            f0 = g * 8
            nf = min(8, nfc - f0)
            ctx.op("gpsimd", "dma_start",
                   wp[:, f0 * 512:f0 * 512 + nf * 512]
                   .rearrange("p (fc t) -> p fc t", t=512)[:, :, 0:n_o],
                   w_dram[f0 * 128:(f0 + nf) * 128, o0:o0 + n_o]
                   .rearrange("(fc p) t -> p fc t", p=128))
        if bias_dram is not None:
            ctx.op("gpsimd", "dma_start", brow[pan % 2][:, 0:n_o],
                   bias_dram[0:1, o0:o0 + n_o])

    def load_x(tb):
        xp = xtb[tb % 2]
        if x_loader is not None:
            x_loader(ctx, xp, tb)
            return
        for g in range(ng):
            f0 = g * 8
            nf = min(8, nfc - f0)
            ctx.op("gpsimd", "dma_start",
                   xp[:, f0 * 512:(f0 + nf) * 512]
                   .rearrange("p (fc t) -> p fc t", t=512),
                   xT_planes[f0:f0 + nf, :, tb * 512:(tb + 1) * 512]
                   .rearrange("fc p t -> p fc t"))

    load_w(0)
    for pan in range(NP):
        o0 = pan * 512
        n_o = min(512, n_out - o0)
        wp = wpan[pan % 2]
        if pan + 1 < NP:
            load_w(pan + 1)
        if fold:
            ctx.wait("tensor", gsem)
            for fc in range(nfc):
                ctx.op("tensor", "matmul", pW[0:1, 0:n_o], onesc[:],
                       wp[:, fc * 512:fc * 512 + n_o],
                       start=(fc == 0), stop=(fc == nfc - 1),
                       inc=(tsem.tick(1) if fc == nfc - 1 else None))
            ctx.op("vector", "tensor_copy", wsum[pan % 2][:, 0:n_o],
                   pW[0:1, 0:n_o])
        load_x(0)
        for tb in range(n_tb):
            xp = xtb[tb % 2]
            if tb + 1 < n_tb:
                load_x(tb + 1)
            for tt in range(4):
                tg = tb * 4 + tt
                par = tg % 4
                n_extra = (1 if fold else 0) + (1 if bias_dram is not None else 0)
                for fc in range(nfc):
                    ctx.op("tensor", "matmul", pP[par][:, 0:n_o],
                           xp[:, fc * 512 + tt * 128:fc * 512 + (tt + 1) * 128],
                           wp[:, fc * 512:fc * 512 + n_o],
                           start=(fc == 0),
                           stop=(fc == nfc - 1 and n_extra == 0),
                           inc=(tsem.tick(1)
                                if fc == nfc - 1 and n_extra == 0 else None))
                if fold:
                    lastf = bias_dram is None
                    ctx.op("tensor", "matmul", pP[par][:, 0:n_o],
                           stats["negm"][0:1, tg * 128:(tg + 1) * 128],
                           wsum[pan % 2][0:1, 0:n_o], start=False, stop=lastf,
                           inc=(tsem.tick(1) if lastf else None))
                if bias_dram is not None:
                    row = (stats["std"][0:1, tg * 128:(tg + 1) * 128]
                           if fold else ones_row[0:1, 0:128])
                    ctx.op("tensor", "matmul", pP[par][:, 0:n_o],
                           row, brow[pan % 2][0:1, 0:n_o], start=False,
                           stop=True, inc=tsem.tick(1))
                epilogue(tg, o0, n_o, pP[par], tsem.n)
    for e in ENGINES:
        ctx.wait(e, esem)
    ctx.pop_scope()


def emit_softmax_free(ctx, dst, src, tmp, sems, pre_scale=1.0):
    vsem, asem = sems["v"], sems["a"]
    ctx.op("vector", "reduce_max", tmp["nm"][:], src, axis=AX.X, negate=True,
           inc=vsem.tick(1))
    ctx.wait("scalar", vsem)
    ctx.wait("scalar", sems["t"])
    ctx.op("scalar", "activation", dst, src, AF.Exp, bias=tmp["nm"][:],
           scale=pre_scale, accum_out=tmp["sm"][:], inc=asem.tick(1))
    ctx.wait("vector", asem)
    ctx.op("vector", "reciprocal", tmp["rs"][:], tmp["sm"][:], inc=vsem.tick(1))
    ctx.op("vector", "tensor_scalar_mul", dst, dst, tmp["rs"][:],
           inc=vsem.tick(1))


def emit_rope(ctx, dst, src, cosc, sinc, tmp64, sems):
    vsem = sems["v"]
    ctx.op("vector", "tensor_mul", dst[:, 0:64], src[:, 0:64], cosc[:, 0:64])
    ctx.op("vector", "tensor_mul", tmp64[:], src[:, 64:128], sinc[:, 0:64])
    ctx.op("vector", "tensor_sub", dst[:, 0:64], dst[:, 0:64], tmp64[:])
    ctx.op("vector", "tensor_mul", dst[:, 64:128], src[:, 64:128], cosc[:, 64:128])
    ctx.op("vector", "tensor_mul", tmp64[:], src[:, 0:64], sinc[:, 64:128])
    ctx.op("vector", "tensor_add", dst[:, 64:128], dst[:, 64:128], tmp64[:],
           inc=vsem.tick(1))


def emit_mixer(ctx, *, qf, kf, vf, oT_out, sems, consts, tag="",
               oT_writer=None):
    dsem, tsem, vsem, asem = sems["d"], sems["t"], sems["v"], sems["a"]
    ctx.push_scope()
    bar = sems["bar"]
    idf, idb = consts["c_idf"], consts["c_idb"]
    cU, cL = consts["c_U"], consts["c_L"]
    ccos, csin = consts["c_cos"], consts["c_sin"]

    qc_t = [ctx.sb("mxq%d" % i + tag, (128, 2048), F32) for i in range(2)]
    kc_t = [ctx.sb("mxk%d" % i + tag, (128, 512), F32) for i in range(2)]
    vc_t = [ctx.sb("mxv%d" % i + tag, (128, 512), F32) for i in range(2)]
    vb = [ctx.sb("mxvb%d" % i + tag, (128, 512), BF16) for i in range(2)]
    sA = ctx.sb("mxsA" + tag, (128, 128), F32)
    sB = ctx.sb("mxsB" + tag, (128, 128), F32)
    gs = ctx.sb("mxgs" + tag, (128, 128), F32)
    ks = ctx.sb("mxks" + tag, (128, 128), F32)
    knm = ctx.sb("mxknm" + tag, (128, 1), F32)
    ksm = ctx.sb("mxksm" + tag, (128, 1), F32)
    krs = ctx.sb("mxkrs" + tag, (128, 1), F32)
    ktmp = {"nm": knm, "sm": ksm, "rs": krs}
    ktmp64 = ctx.sb("mxkt64" + tag, (128, 64), F32)
    eGT = ctx.sb("mxeg" + tag, (128, 128), F32)
    enGT = ctx.sb("mxen" + tag, (128, 128), F32)
    eRv = ctx.sb("mxer" + tag, (128, 128), F32)
    kh_bf = ctx.sb("mxkh" + tag, (128, 128), BF16)
    kgT_bf = ctx.sb("mxkg" + tag, (128, 128), BF16)
    sk = ctx.sb("mxsk" + tag, (128, 128), F32)
    win = ctx.sb("mxwin" + tag, (128, 4 * 256), BF16)
    Sf = ctx.sb("mxSf" + tag, (128, 4 * 128), F32)
    Sb = ctx.sb("mxSb" + tag, (128, 4 * 128), BF16)
    def hq2(name, shape, dt):
        return [ctx.sb("mx%s%d" % (name, i) + tag, shape, dt) for i in range(2)]
    qs = hq2("qs", (128, 128), F32)
    qgT_bf = hq2("qg", (128, 128), BF16)
    sq_ = hq2("sq", (128, 128), F32)
    sqT_bf = hq2("sqT", (128, 128), BF16)
    At_bf = hq2("At", (128, 128), BF16)
    olh = hq2("ol", (128, 128), F32)
    msk = hq2("msk", (128, 256), F32)
    Pt = hq2("P", (128, 256), F32)
    P_bf = hq2("Pb", (128, 256), BF16)
    PT1 = hq2("PT1", (128, 128), BF16)
    PT2 = hq2("PT2", (128, 128), BF16)
    nm = hq2("nm", (128, 1), F32)
    sm = hq2("sm", (128, 1), F32)
    rsc = hq2("rs", (128, 1), F32)
    t64 = hq2("t64", (128, 64), F32)
    oT_stage = [ctx.sb("mxoT%d" % i + tag, (128, 2048), BF16) for i in range(2)]

    pAux = ctx.ps("mpAux" + tag, (128, 512), F32)
    pG, pR = pAux[:, 0:128], pAux[:, 128:256]
    pD = pAux[:, 256:384]
    pTr = [ctx.ps("mpT%d" % i + tag, (128, 128), F32) for i in range(2)]
    pO = [ctx.ps("mpO%d" % i + tag, (128, 128), F32) for i in range(2)]
    pSO = [ctx.ps("mpSO%d" % i + tag, (128, 512), F32) for i in range(2)]
    pPT = ctx.ps("mpPT" + tag, (128, 256), BF16)

    ctx.op("gpsimd", "memset", Sf[:], 0.0)
    ctx.op("gpsimd", "memset", Sb[:], 0.0)
    ctx.op("gpsimd", "memset", win[:], 0.0)
    ctx.op("gpsimd", "memset", vb[1][:], 0.0)

    oT2d = (oT_out.rearrange("d (h t) -> d h t", t=4096)
            if oT_writer is None else None)

    def load_chunk(c):
        par = c & 1
        csl = slice(c * 128, (c + 1) * 128)
        ctx.op("sync", "dma_start", qc_t[par][:], qf[csl, :])
        ctx.op("sync", "dma_start", kc_t[par][:], kf[csl, :])
        ctx.op("sync", "dma_start", vc_t[par][:], vf[csl, :])

    load_chunk(0)
    for c in range(32):
        par = c & 1
        if c + 1 < 32:
            load_chunk(c + 1)
        csl = slice(c * 128, (c + 1) * 128)
        cosc = ccos[:, csl]
        sinc = csin[:, csl]
        for kvh in range(4):
            kc = kc_t[par][:, kvh * 128:(kvh + 1) * 128]
            vc = vc_t[par][:, kvh * 128:(kvh + 1) * 128]
            vbsl = vb[par][:, kvh * 128:(kvh + 1) * 128]
            wsl = win[:, kvh * 256:(kvh + 1) * 256]
            Ssl_f = Sf[:, kvh * 128:(kvh + 1) * 128]
            Ssl_b = Sb[:, kvh * 128:(kvh + 1) * 128]
            ctx.op("vector", "tensor_copy", vbsl, vc)
            ctx.op("scalar", "activation", sA[:], kc, AF.Abs)
            ctx.op("scalar", "activation", sB[:], sA[:], AF.Exp, scale=-1.0)
            ctx.op("scalar", "activation", sA[:], sB[:], AF.Ln, bias=1.0)
            ctx.op("vector", "tensor_scalar_min", sB[:], kc, 0.0)
            ctx.op("vector", "tensor_sub", gs[:], sB[:], sA[:])
            ctx.op("vector", "tensor_scalar_mul", gs[:], gs[:], 1.0 / 16.0)
            emit_softmax_free(ctx, ks[:], kc, ktmp, sems)
            ctx.op("tensor", "matmul", pG, gs[:], cU[:], start=True,
                   stop=True, inc=tsem.tick(1))
            ctx.op("tensor", "matmul", pR, cL[:], gs[:], start=True,
                   stop=True, inc=tsem.tick(1))
            ctx.op("scalar", "activation", eGT[:], pG, AF.Exp)
            ctx.op("scalar", "activation", enGT[:], pG, AF.Exp, scale=-1.0)
            ctx.op("scalar", "activation", eRv[:], pR, AF.Exp)
            ctx.op("vector", "tensor_mul", kh_bf[:], ks[:], eRv[:])
            ctx.op("tensor", "transpose", pTr[0][:], ks[:], idf[:],
                   inc=tsem.tick(1))
            ctx.op("vector", "tensor_mul", kgT_bf[:], pTr[0][:], enGT[:])
            emit_rope(ctx, sk[:], kc, cosc, sinc, ktmp64, sems)
            ctx.op("vector", "tensor_copy", wsl[:, 0:128], wsl[:, 128:256])
            ctx.op("tensor", "transpose", pTr[1][:], sk[:], idf[:],
                   inc=tsem.tick(1))
            ctx.op("vector", "tensor_copy", wsl[:, 128:256], pTr[1][:][:, 0:128])
            dec = eGT[:, 127:128]

            for hq in range(4):
                hp = hq & 1
                h = kvh * 4 + hq
                qcs = qc_t[par][:, h * 128:(h + 1) * 128]
                tmp = {"nm": nm[hp], "sm": sm[hp], "rs": rsc[hp]}
                emit_softmax_free(ctx, qs[hp][:], qcs, tmp, sems)
                ctx.op("tensor", "transpose", pTr[hp][:], qs[hp][:], idf[:],
                       inc=tsem.tick(1))
                ctx.op("vector", "tensor_mul", qgT_bf[hp][:], pTr[hp][:], eGT[:])
                ctx.op("tensor", "matmul", pSO[hp][:, 384:512], kgT_bf[:],
                       qgT_bf[hp][:], start=True, stop=True, inc=tsem.tick(1))
                ctx.op("vector", "tensor_mul", At_bf[hp][:],
                       pSO[hp][:, 384:512], cU[:])
                ctx.op("tensor", "matmul", pO[hp][:], Ssl_b, qgT_bf[hp][:],
                       start=True, stop=False)
                ctx.op("tensor", "matmul", pO[hp][:], vbsl, At_bf[hp][:],
                       start=False, stop=True, inc=tsem.tick(1))
                ctx.op("scalar", "activation", olh[hp][:], pO[hp][:], AF.Copy,
                       scale=0.5)
                emit_rope(ctx, sq_[hp][:], qcs, cosc, sinc, t64[hp], sems)
                ctx.op("tensor", "transpose", pTr[hp][:], sq_[hp][:], idf[:],
                       inc=tsem.tick(1))
                ctx.op("vector", "tensor_copy", sqT_bf[hp][:], pTr[hp][:])
                ctx.op("tensor", "matmul", pSO[hp][:, 0:256], sqT_bf[hp][:],
                       wsl[:, 0:256], start=True, stop=True, inc=tsem.tick(1))
                swm = consts["c_swm0"] if c == 0 else consts["c_swm"]
                ctx.op("vector", "tensor_add", msk[hp][:], pSO[hp][:, 0:256],
                       swm[:])
                ctx.op("vector", "reduce_max", nm[hp][:], msk[hp][:], axis=AX.X)
                ctx.op("vector", "tensor_scalar_mul", nm[hp][:], nm[hp][:],
                       -CSC)
                ctx.op("scalar", "activation", Pt[hp][:], msk[hp][:], AF.Exp,
                       bias=nm[hp][:], scale=CSC, accum_out=sm[hp][:])
                ctx.op("vector", "reciprocal", rsc[hp][:], sm[hp][:])
                ctx.op("vector", "tensor_scalar_mul", P_bf[hp][:], Pt[hp][:],
                       rsc[hp][:])
                ctx.op("tensor", "transpose", pPT[:, 0:128],
                       P_bf[hp][:, 0:128], idb[:], inc=tsem.tick(1))
                ctx.op("tensor", "transpose", pPT[:, 128:256],
                       P_bf[hp][:, 128:256], idb[:], inc=tsem.tick(1))
                ctx.op("vector", "tensor_copy", PT1[hp][:], pPT[:, 0:128])
                ctx.op("vector", "tensor_copy", PT2[hp][:], pPT[:, 128:256])
                ctx.op("tensor", "matmul", pSO[hp][:, 256:384],
                       vb[1 - par][:, kvh * 128:(kvh + 1) * 128],
                       PT1[hp][:], start=True, stop=False)
                ctx.op("tensor", "matmul", pSO[hp][:, 256:384],
                       vb[par][:, kvh * 128:(kvh + 1) * 128],
                       PT2[hp][:], start=False, stop=True,
                       inc=tsem.tick(1))
                ctx.op("vector", "scalar_tensor_tensor",
                       oT_stage[par][:, h * 128:(h + 1) * 128],
                       pSO[hp][:, 256:384], 0.5,
                       olh[hp][:], op0=ALU.mult, op1=ALU.add)
            ctx.op("tensor", "matmul", pD, kh_bf[:], vbsl,
                   start=True, stop=True, inc=tsem.tick(1))
            ctx.op("vector", "scalar_tensor_tensor", Ssl_f, Ssl_f, dec,
                   pD, op0=ALU.mult, op1=ALU.add)
            ctx.op("vector", "tensor_copy", Ssl_b, Ssl_f)
        if oT_writer is None:
            for g in range(2):
                ctx.op("gpsimd", "dma_start",
                       oT2d[:, g * 8:(g + 1) * 8, c * 128:(c + 1) * 128],
                       oT_stage[par][:, g * 1024:(g + 1) * 1024]
                       .rearrange("d (h t) -> d h t", t=128))
        else:
            oT_writer(c, oT_stage[par])
    a = ctx._autosem("gpsimd", "d")
    ctx._need("gpsimd", (a.h, a.n))
    ctx.pop_scope()


L1_CONSTS = ["c_idf", "c_idb", "c_U", "c_L", "c_cos", "c_sin", "c_swm",
             "c_swm0"]

def build_L1(debug=False):
    nc = bass.Bass("TRN2", target_bir_lowering=False, debug=False)
    ctx = Ctx(nc)
    sems = make_sems(ctx)
    kind = "ExternalOutput" if debug else "Internal"
    x0_rows = nc.dram_tensor("x0_rows", [4096, 4096], F32, kind="ExternalInput")
    x0Tp = nc.dram_tensor("x0Tp", [32, 128, 4096], F32, kind="ExternalInput")
    wqkv = nc.dram_tensor("wqkv", [4096, 3072], F32, kind="ExternalInput")
    qf = nc.dram_tensor("qf", [4096, 2048], F32, kind=kind)
    kf = nc.dram_tensor("kf", [4096, 512], F32, kind=kind)
    vf = nc.dram_tensor("vf", [4096, 512], F32, kind=kind)
    oT_out = nc.dram_tensor("oT", [128, 16 * 4096], BF16, kind="ExternalOutput")

    with nc.Block() as block:
        consts = load_consts(ctx, L1_CONSTS, sems)
        stats = emit_stats(ctx, x0_rows, 32, sems, consts)
        ctx.barrier(sems["bar"])

        out_sb = [ctx.sb("l1o%d" % i, (128, 512), F32) for i in range(2)]
        epar = [0]

        def epi(tg, o0, n_o, psum, tval):
            par = epar[0]
            epar[0] = 1 - par
            ctx.op("scalar", "activation", out_sb[par][:, 0:n_o],
                   psum[:, 0:n_o], AF.Copy,
                   scale=stats["rs"][:, tg:tg + 1], inc=sems["e"].tick(1))
            if o0 < 2048:
                dest = qf[tg * 128:(tg + 1) * 128, o0:o0 + n_o]
            elif o0 < 2560:
                dest = kf[tg * 128:(tg + 1) * 128, 0:n_o]
            else:
                dest = vf[tg * 128:(tg + 1) * 128, 0:n_o]
            ctx.op("sync", "dma_start", dest, out_sb[par][:, 0:n_o])

        emit_proj(ctx, xT_planes=x0Tp, n_tok=4096, w_dram=wqkv, n_out=3072,
                  stats=stats, sems=sems, consts=consts, epilogue=epi,
                  tag="L1")
        ctx.barrier(sems["bar"])
        emit_mixer(ctx, qf=qf, kf=kf, vf=vf, oT_out=oT_out.ap(), sems=sems,
                   consts=consts)
        ctx.barrier(sems["bar"])
        ctx.finalize(block)
    ctx.st.close()
    return nc


def host_inputs_L1(inputs, core):
    b, p = core // 2, core % 2
    x0 = np.ascontiguousarray(inputs["L0_attn_hidden_states"][b])
    hs = slice(p * 2048, (p + 1) * 2048)
    ks = slice(p * 512, (p + 1) * 512)
    wqkv = np.concatenate([inputs["L0_attn_Wq"][:, hs],
                           inputs["L0_attn_Wk"][:, ks],
                           inputs["L0_attn_Wv"][:, ks]], axis=1)
    d = {"x0_rows": x0,
         "x0Tp": np.ascontiguousarray(x0.T.reshape(32, 128, 4096)),
         "wqkv": np.ascontiguousarray(wqkv)}
    d.update({k: v for k, v in host_consts(p).items() if k in L1_CONSTS})
    return d


def build_L2a():
    nc = bass.Bass("TRN2", target_bir_lowering=False, debug=False)
    ctx = Ctx(nc)
    sems = make_sems(ctx)
    x0_own = nc.dram_tensor("x0_own", [T_OWN, 4096], F32, kind="ExternalInput")
    oT_own = nc.dram_tensor("oT_own", [32, 128, T_OWN], BF16,
                            kind="ExternalInput")
    Wo = nc.dram_tensor("Wo", [4096, 4096], F32, kind="ExternalInput")
    x1_rows = nc.dram_tensor("x1_rows", [T_OWN, 4096], F32,
                             kind="ExternalOutput")
    with nc.Block() as block:
        x0t = ctx.sb("l2x0", (128, 512), F32)
        out_sb = [ctx.sb("l2o%d" % i, (128, 512), F32) for i in range(2)]
        epar = [0]

        def epi(tg, o0, n_o, psum, tval):
            par = epar[0]
            epar[0] = 1 - par
            rows = slice(tg * 128, (tg + 1) * 128)
            ctx.op("sync", "dma_start", x0t[:, 0:n_o], x0_own[rows, o0:o0 + n_o])
            ctx.op("vector", "tensor_add", out_sb[par][:, 0:n_o],
                   psum[:, 0:n_o], x0t[:, 0:n_o], inc=sems["e"].tick(1))
            ctx.op("sync", "dma_start", x1_rows[rows, o0:o0 + n_o],
                   out_sb[par][:, 0:n_o])

        emit_proj(ctx, xT_planes=oT_own, n_tok=T_OWN, w_dram=Wo, n_out=4096,
                  stats=None, sems=sems, consts=None, epilogue=epi, tag="L2a")
        a = ctx._autosem("sync", "d")
        ctx._need("sync", (a.h, a.n))
        ctx.finalize(block)
    ctx.st.close()
    return nc


def host_inputs_L2a(inputs, core, oT_pair):
    b, p = core // 2, core % 2
    own = own_rows(p)
    x0 = inputs["L0_attn_hidden_states"][b][own]
    o0 = np.asarray(oT_pair[0]).reshape(128, 16, 4096)
    o1 = np.asarray(oT_pair[1]).reshape(128, 16, 4096)
    full = np.concatenate([o0, o1], axis=1)
    oT_own = np.ascontiguousarray(full.transpose(1, 0, 2)[:, :, own])
    return {"x0_own": np.ascontiguousarray(x0), "oT_own": oT_own,
            "Wo": inputs["L0_attn_Wo"]}


def own_rows(p):
    idx = np.arange(T_OWN)
    tile = idx // 128
    return (2 * tile + p) * 128 + (idx % 128)


L2B_CONSTS = ["c_idf", "c_idb", "c_odd"]

def build_L2b():
    nc = bass.Bass("TRN2", target_bir_lowering=False, debug=False)
    ctx = Ctx(nc)
    sems = make_sems(ctx)
    x1_rows = nc.dram_tensor("x1_rows", [T_OWN, 4096], F32, kind="ExternalInput")
    x1Tp = nc.dram_tensor("x1Tp", [32, 128, T_OWN], F32, kind="ExternalInput")
    WT = nc.dram_tensor("WT", [4096, 4096], F32, kind="ExternalInput")
    b_row = nc.dram_tensor("b_row", [1, 4096], F32, kind="ExternalInput")
    A4 = nc.dram_tensor("A4", [4096, 32], F32, kind="ExternalInput")
    LiT = nc.dram_tensor("LiT", [32, 8], F32, kind="ExternalInput")
    H_rows = nc.dram_tensor("H_rows", [T_OWN, 4096], F32, kind="ExternalOutput")
    z_rows = nc.dram_tensor("z_rows", [T_OWN, 32], F32, kind="ExternalOutput")
    ds_rows = nc.dram_tensor("ds_rows", [T_OWN, 8], F32, kind="ExternalOutput")
    scales_loc = nc.dram_tensor("scales_loc", [1, 2], F32, kind="ExternalOutput")
    with nc.Block() as block:
        consts = load_consts(ctx, L2B_CONSTS, sems)
        stats = emit_stats(ctx, x1_rows, NT_OWN, sems, consts, tag="b")
        hmax = ctx.sb("l2hm", (128, 1), F32)
        dmax = ctx.sb("l2dm", (128, 1), F32)
        tmp8 = ctx.sb("l2t8", (128, 8), F32)
        m8 = ctx.sb("l2m8", (128, 1), F32)
        ctx.op("gpsimd", "memset", hmax[:], 0.0)
        ctx.op("gpsimd", "memset", dmax[:], 0.0)
        out_sb = [ctx.sb("l2bo%d" % i, (128, 512), F32) for i in range(2)]
        epar = [0]

        def epiH(tg, o0, n_o, psum, tval):
            par = epar[0]
            epar[0] = 1 - par
            rows = slice(tg * 128, (tg + 1) * 128)
            ctx.op("scalar", "activation", out_sb[par][:, 0:n_o],
                   psum[:, 0:n_o], AF.Copy, scale=stats["rs"][:, tg:tg + 1],
                   inc=sems["e"].tick(1))
            ctx.op("sync", "dma_start", H_rows[rows, o0:o0 + n_o],
                   out_sb[par][:, 0:n_o])
            if o0 == 0:
                ctx.op("vector", "tensor_scalar_mul", tmp8[:],
                       out_sb[par][:, 0:8], consts["c_odd"][:])
                ctx.op("vector", "reduce_max", m8[:], tmp8[:], axis=AX.X,
                       apply_absolute_value=True)
                ctx.op("vector", "tensor_max", hmax[:], hmax[:], m8[:])

        emit_proj(ctx, xT_planes=x1Tp, n_tok=T_OWN, w_dram=WT, n_out=4096,
                  stats=stats, sems=sems, consts=consts, bias_dram=b_row,
                  epilogue=epiH, tag="L2bH")

        z_sb = ctx.sb("l2z", (128, 32), F32)

        def epiZ(tg, o0, n_o, psum, tval):
            rows = slice(tg * 128, (tg + 1) * 128)
            ctx.op("scalar", "activation", z_sb[:, 0:n_o], psum[:, 0:n_o],
                   AF.Copy, scale=stats["rs"][:, tg:tg + 1],
                   inc=sems["e"].tick(1))
            ctx.op("sync", "dma_start", z_rows[rows, 0:n_o], z_sb[:, 0:n_o])

        emit_proj(ctx, xT_planes=x1Tp, n_tok=T_OWN, w_dram=A4, n_out=32,
                  stats=stats, sems=sems, consts=consts, epilogue=epiZ,
                  tag="L2bz")

        ctx.push_scope()
        zt = ctx.sb("l2zt", (128, 32), F32)
        ztb = ctx.sb("l2ztb", (128, 32), BF16)
        zTb = ctx.sb("l2zT", (32, 128), BF16)
        litb = ctx.sb("l2lit", (32, 8), BF16)
        ds_sb = ctx.sb("l2ds", (128, 8), F32)
        pT = ctx.ps("l2pT", (32, 128), BF16)
        pD = ctx.ps("l2pD", (128, 8), F32)
        ctx.op("gpsimd", "dma_start", litb[:], LiT[:, :])
        for tg in range(NT_OWN):
            rows = slice(tg * 128, (tg + 1) * 128)
            ctx.op("sync", "dma_start", zt[:], z_rows[rows, :])
            ctx.op("vector", "tensor_copy", ztb[:], zt[:])
            ctx.op("tensor", "transpose", pT[:], ztb[:], consts["c_idb"][:],
                   inc=sems["t"].tick(1))
            ctx.op("vector", "tensor_copy", zTb[:], pT[:])
            ctx.op("tensor", "matmul", pD[:], zTb[:], litb[:], start=True,
                   stop=True, inc=sems["t"].tick(1))
            ctx.op("vector", "tensor_copy", ds_sb[:], pD[:])
            ctx.op("sync", "dma_start", ds_rows[rows, :], ds_sb[:])
            ctx.op("vector", "tensor_scalar_mul", tmp8[:], ds_sb[:],
                   consts["c_odd"][:])
            ctx.op("vector", "reduce_max", m8[:], tmp8[:], axis=AX.X,
                   apply_absolute_value=True)
            ctx.op("vector", "tensor_max", dmax[:], dmax[:], m8[:])
        pR = ctx.ps("l2pR", (1, 128), F32)
        srow = ctx.sb("l2sr", (1, 128), F32)
        sc2 = ctx.sb("l2sc", (1, 2), F32)
        for i, acc in enumerate((hmax, dmax)):
            ctx.op("tensor", "transpose", pR[:], acc[:], consts["c_idf"][:],
                   inc=sems["t"].tick(1))
            ctx.op("vector", "tensor_copy", srow[:], pR[:])
            ctx.op("vector", "reduce_max", sc2[:, i:i + 1], srow[:], axis=AX.X)
        ctx.op("sync", "dma_start", scales_loc[:, :], sc2[:])
        a = ctx._autosem("sync", "d")
        ctx._need("sync", (a.h, a.n))
        ctx.pop_scope()
        ctx.finalize(block)
    ctx.st.close()
    return nc


def host_inputs_L2b(inputs, core, x1_own):
    d = {"x1_rows": np.ascontiguousarray(x1_own),
         "x1Tp": np.ascontiguousarray(x1_own.T.reshape(32, 128, T_OWN)),
         "WT": np.ascontiguousarray(inputs["L0_ffn_W"].T),
         "b_row": np.ascontiguousarray(inputs["L0_ffn_b"][None, :]),
         "A4": inputs["L0_ffn_A"],
         "LiT": np.ascontiguousarray(inputs["L0_ffn_LiMEs"].T)}
    d.update({k: v for k, v in host_consts(core % 2).items() if k in L2B_CONSTS})
    return d


L3_CONSTS = ["c_idf", "c_idb"]

def build_L3():
    nc = bass.Bass("TRN2", target_bir_lowering=False, debug=False)
    ctx = Ctx(nc)
    sems = make_sems(ctx)
    x1_rows = nc.dram_tensor("x1_rows", [T_OWN, 4096], F32, kind="ExternalInput")
    H_rows = nc.dram_tensor("H_rows", [T_OWN, 4096], F32, kind="ExternalInput")
    z_rows = nc.dram_tensor("z_rows", [T_OWN, 32], F32, kind="ExternalInput")
    ds_rows = nc.dram_tensor("ds_rows", [T_OWN, 8], F32, kind="ExternalInput")
    scales_all = nc.dram_tensor("scales_all", [1, 16], F32, kind="ExternalInput")
    LiM = nc.dram_tensor("LiM", [8, 32], F32, kind="ExternalInput")
    Bm = nc.dram_tensor("Bm", [32, 4096], F32, kind="ExternalInput")
    x2_rows = nc.dram_tensor("x2_rows", [T_OWN, 4096], F32,
                             kind="ExternalOutput")
    with nc.Block() as block:
        consts = load_consts(ctx, L3_CONSTS, sems)
        sall = ctx.sb("l3sa", (1, 16), F32)
        s11 = ctx.sb("l3s1", (1, 2), F32)
        inv2 = ctx.sb("l3inv", (1, 2), F32)
        invc = ctx.sb("l3ic", (128, 2), F32)
        ones_r = ctx.sb("l3or", (1, 128), F32)
        limb = ctx.sb("l3lim", (8, 32), BF16)
        bmb = ctx.sb("l3bm", (32, 4096), BF16)
        pB = ctx.ps("l3pB", (128, 2), F32)
        ctx.op("sync", "dma_start", sall[:], scales_all[:, :])
        ctx.op("gpsimd", "dma_start", limb[:], LiM[:, :])
        for fc in range(1):
            ctx.op("gpsimd", "dma_start", bmb[:], Bm[:, :])
        ctx.op("gpsimd", "memset", ones_r[:], 1.0)
        s8x2 = ctx.sb("l3s8", (1, 16), F32)
        sv = sall.ap().rearrange("a (p two) -> a two p", two=2)
        ctx.op("vector", "tensor_copy",
               s8x2.ap().rearrange("a (two p) -> a two p", two=2), sv)
        ctx.op("vector", "reduce_max", s11[:, 0:1], s8x2[:, 0:8], axis=AX.X)
        ctx.op("vector", "reduce_max", s11[:, 1:2], s8x2[:, 8:16], axis=AX.X)
        ctx.op("vector", "tensor_scalar_max", s11[:], s11[:], 1e-6)
        ctx.op("vector", "reciprocal", inv2[:], s11[:])
        ctx.op("vector", "tensor_scalar_mul", inv2[:], inv2[:], 0.5)
        ctx.op("tensor", "matmul", pB[:], ones_r[:], inv2[:], start=True,
               stop=True, inc=sems["t"].tick(1))
        ctx.op("vector", "tensor_copy", invc[:], pB[:])

        Ht = ctx.sb("l3H", (128, 4096), F32)
        x1t = ctx.sb("l3x1", (128, 4096), F32)
        hs8 = ctx.sb("l3h8", (128, 8), F32)
        ds8 = ctx.sb("l3d8", (128, 8), F32)
        zt = ctx.sb("l3z", (128, 32), F32)
        lg = ctx.sb("l3lg", (128, 8), F32)
        t8 = ctx.sb("l3t8", (128, 8), F32)
        nm = ctx.sb("l3nm", (128, 1), F32)
        sm = ctx.sb("l3sm", (128, 1), F32)
        rsc = ctx.sb("l3rs", (128, 1), F32)
        probs = ctx.sb("l3pr", (128, 8), F32)
        eqm = ctx.sb("l3eq", (128, 8), F32)
        m2 = ctx.sb("l3m2", (128, 1), F32)
        sig = ctx.sb("l3sg", (128, 8), F32)
        wts = ctx.sb("l3wt", (128, 8), F32)
        wsh = ctx.sb("l3ws", (128, 8), F32)
        wshb = ctx.sb("l3wsb", (128, 8), BF16)
        wT = ctx.sb("l3wT", (8, 128), BF16)
        zpb = ctx.sb("l3zp", (128, 32), BF16)
        zpT = ctx.sb("l3zpT", (32, 128), BF16)
        osb = ctx.sb("l3ou", (128, 512), F32)
        pT8 = ctx.ps("l3pT", (8, 128), BF16)
        pM = ctx.ps("l3pM", (128, 32), F32)
        pT32 = ctx.ps("l3pT2", (32, 128), BF16)
        pDl = ctx.ps("l3pD", (128, 512), F32)
        SHUF = [i | 1 for i in range(32)]

        for tg in range(NT_OWN):
            rows = slice(tg * 128, (tg + 1) * 128)
            ctx.op("sync", "dma_start", hs8[:], H_rows[rows, 0:8])
            ctx.op("sync", "dma_start", ds8[:], ds_rows[rows, :])
            ctx.op("sync", "dma_start", zt[:], z_rows[rows, :])
            ctx.op("sync", "dma_start", x1t[:], x1_rows[rows, :])
            ctx.op("sync", "dma_start", Ht[:], H_rows[rows, :])
            ctx.op("vector", "tensor_scalar_mul", t8[:], ds8[:], invc[:, 1:2])
            ctx.op("vector", "scalar_tensor_tensor", lg[:], hs8[:],
                   invc[:, 0:1], t8[:], op0=ALU.mult, op1=ALU.add)
            ctx.op("vector", "reduce_max", nm[:], lg[:], axis=AX.X, negate=True)
            ctx.op("scalar", "activation", probs[:], lg[:], AF.Exp,
                   bias=nm[:], scale=1.0, accum_out=sm[:])
            ctx.op("vector", "reciprocal", rsc[:], sm[:])
            ctx.op("vector", "tensor_scalar_mul", probs[:], probs[:], rsc[:])
            ctx.op("vector", "reduce_max", nm[:], probs[:], axis=AX.X)
            ctx.op("vector", "tensor_scalar", eqm[:], probs[:], nm[:], None,
                   op0=ALU.is_ge)
            ctx.op("vector", "scalar_tensor_tensor", t8[:], eqm[:], -1e30,
                   probs[:], op0=ALU.mult, op1=ALU.add)
            ctx.op("vector", "reduce_max", m2[:], t8[:], axis=AX.X)
            ctx.op("vector", "tensor_scalar_mul", m2[:], m2[:], -2.0)
            ctx.op("scalar", "activation", sig[:], probs[:], AF.Sigmoid,
                   bias=m2[:], scale=2.0)
            ctx.op("vector", "tensor_mul", wts[:], probs[:], sig[:])
            ctx.op("vector", "reduce_sum", sm[:], wts[:], axis=AX.X)
            ctx.op("vector", "tensor_scalar_add", sm[:], sm[:], 1e-9)
            ctx.op("vector", "reciprocal", rsc[:], sm[:])
            ctx.op("vector", "tensor_scalar_mul", wts[:], wts[:], rsc[:])
            ctx.op("vector", "stream_shuffle", wsh[:], wts[:], SHUF)
            ctx.op("vector", "tensor_copy", wshb[:], wsh[:])
            ctx.op("tensor", "transpose", pT8[:], wshb[:], consts["c_idb"][:],
                   inc=sems["t"].tick(1))
            ctx.op("vector", "tensor_copy", wT[:], pT8[:])
            ctx.op("tensor", "matmul", pM[:], wT[:], limb[:], start=True,
                   stop=True, inc=sems["t"].tick(1))
            ctx.op("vector", "tensor_mul", zpb[:], zt[:], pM[:])
            ctx.op("tensor", "transpose", pT32[:], zpb[:], consts["c_idb"][:],
                   inc=sems["t"].tick(1))
            ctx.op("vector", "tensor_copy", zpT[:], pT32[:])
            for oc in range(8):
                osl = slice(oc * 512, (oc + 1) * 512)
                ctx.op("tensor", "matmul", pDl[:], zpT[:], bmb[:, osl],
                       start=True, stop=True, inc=sems["t"].tick(1))
                ctx.op("vector", "scalar_tensor_tensor", osb[:], pDl[:], 0.5,
                       Ht[:, osl], op0=ALU.mult, op1=ALU.add)
                ctx.op("vector", "tensor_add", osb[:], osb[:], x1t[:, osl])
                ctx.op("sync", "dma_start", x2_rows[rows, osl], osb[:])
        a = ctx._autosem("sync", "d")
        ctx._need("sync", (a.h, a.n))
        ctx.finalize(block)
    ctx.st.close()
    return nc


def host_inputs_L3(inputs, core, x1_own, H_own, z_own, ds_own, scales8):
    d = {"x1_rows": np.ascontiguousarray(x1_own),
         "H_rows": np.ascontiguousarray(H_own),
         "z_rows": np.ascontiguousarray(z_own),
         "ds_rows": np.ascontiguousarray(ds_own),
         "scales_all": np.ascontiguousarray(
             np.concatenate(scales8, 1).reshape(1, 16)),
         "LiM": inputs["L0_ffn_LiMEs"],
         "Bm": inputs["L0_ffn_Bmat"]}
    d.update({k: v for k, v in host_consts(core % 2).items() if k in L3_CONSTS})
    return d


L4_CONSTS = ["c_idf", "c_idb"]

def build_L4():
    nc = bass.Bass("TRN2", target_bir_lowering=False, debug=False)
    ctx = Ctx(nc)
    sems = make_sems(ctx)
    x2_rows = nc.dram_tensor("x2_rows", [T_OWN, 4096], F32, kind="ExternalInput")
    x2Tp = nc.dram_tensor("x2Tp", [32, 128, T_OWN], F32, kind="ExternalInput")
    Wcat = nc.dram_tensor("Wcat", [4096, 12288], F32, kind="ExternalInput")
    bcat = nc.dram_tensor("bcat", [1, 12288], F32, kind="ExternalInput")
    q_rows = nc.dram_tensor("q_rows", [T_OWN, 4096], BF16, kind="ExternalOutput")
    k_rows = nc.dram_tensor("k_rows", [T_OWN, 4096], BF16, kind="ExternalOutput")
    v_rows = nc.dram_tensor("v_rows", [T_OWN, 4096], BF16, kind="ExternalOutput")
    kmax_loc = nc.dram_tensor("kmax_loc", [1, 1], F32, kind="ExternalOutput")
    with nc.Block() as block:
        consts = load_consts(ctx, L4_CONSTS, sems)
        stats = emit_stats(ctx, x2_rows, NT_OWN, sems, consts, tag="4")
        out_sb = [ctx.sb("l4o%d" % i, (128, 512), BF16) for i in range(2)]
        sqs = ctx.sb("l4sq", (128, 128), F32)
        kn = ctx.sb("l4kn", (128, 1), F32)
        knacc = ctx.sb("l4ka", (128, 1), F32)
        ctx.op("gpsimd", "memset", knacc[:], 0.0)
        epar = [0]

        def epi(tg, o0, n_o, psum, tval):
            par = epar[0]
            epar[0] = 1 - par
            rows = slice(tg * 128, (tg + 1) * 128)
            ctx.op("scalar", "activation", out_sb[par][:, 0:n_o],
                   psum[:, 0:n_o], AF.Copy, scale=stats["rs"][:, tg:tg + 1],
                   inc=sems["e"].tick(1))
            which, off = divmod(o0, 4096)
            dest = (q_rows, k_rows, v_rows)[which]
            ctx.op("sync", "dma_start", dest[rows, off:off + n_o],
                   out_sb[par][:, 0:n_o])
            if which == 1:
                for hh in range(4):
                    sl = slice(hh * 128, (hh + 1) * 128)
                    ctx.op("scalar", "activation", sqs[:],
                           out_sb[par][:, sl], AF.Square, accum_out=kn[:])
                    ctx.op("vector", "tensor_max", knacc[:], knacc[:], kn[:])

        emit_proj(ctx, xT_planes=x2Tp, n_tok=T_OWN, w_dram=Wcat, n_out=12288,
                  stats=stats, sems=sems, consts=consts, bias_dram=bcat,
                  epilogue=epi, tag="L4")
        pR = ctx.ps("l4pR", (1, 128), F32)
        srow = ctx.sb("l4sr", (1, 128), F32)
        km = ctx.sb("l4km", (1, 1), F32)
        ctx.op("tensor", "transpose", pR[:], knacc[:], consts["c_idf"][:],
               inc=sems["t"].tick(1))
        ctx.op("vector", "tensor_copy", srow[:], pR[:])
        ctx.op("vector", "reduce_max", km[:], srow[:], axis=AX.X)
        ctx.op("sync", "dma_start", kmax_loc[:, :], km[:])
        a = ctx._autosem("sync", "d")
        ctx._need("sync", (a.h, a.n))
        ctx.finalize(block)
    ctx.st.close()
    return nc


def host_inputs_L4(inputs, core, x2_own):
    Wcat = np.concatenate([inputs["L1_attn_Wq"].T, inputs["L1_attn_Wk"].T,
                           inputs["L1_attn_Wv"].T], axis=1)
    bcat = np.concatenate([inputs["L1_attn_bq"], inputs["L1_attn_bk"],
                           inputs["L1_attn_bv"]])[None, :]
    d = {"x2_rows": np.ascontiguousarray(x2_own),
         "x2Tp": np.ascontiguousarray(x2_own.T.reshape(32, 128, T_OWN)),
         "Wcat": np.ascontiguousarray(Wcat),
         "bcat": np.ascontiguousarray(bcat)}
    d.update({k: v for k, v in host_consts(core % 2).items() if k in L4_CONSTS})
    return d


L5_CONSTS = ["c_idf", "c_idb", "c_dmA", "c_dmB", "c_dmAb", "c_dmBb"]

def emit_attn_l1(ctx, *, qT, kT, vpl, kmax_all, oT_l1, sems, consts,
                 rank_major=False):
    tsem = sems["t"]
    if rank_major:
        kcoff = lambda kc: (kc & 1) * 2048 + (kc >> 1) * 128
    else:
        kcoff = lambda kc: kc * 128
    ctx.push_scope()
    kT_sb = ctx.sb("a5kT", (128, 4096), BF16)
    v_sb = ctx.sb("a5v", (128, 32 * 128), BF16)
    qT_sb = ctx.sb("a5qT", (128, 2048), BF16)
    sq_s = ctx.sb("a5sq", (128, 512), BF16)
    Brow = ctx.sb("a5Br", (1, 512), F32)
    BoC = ctx.sb("a5Bc", (1, 512), F32)
    negBoC = ctx.sb("a5nB", (1, 512), F32)
    negLoC = ctx.sb("a5nL", (1, 512), F32)
    nBb = [ctx.sb("a5nBb%d" % i, (1, 512), BF16) for i in range(2)]
    nLb = [ctx.sb("a5nLb%d" % i, (1, 512), BF16) for i in range(2)]
    rtmp = ctx.sb("a5rt", (1, 512), F32)
    lnS = ctx.sb("a5ln", (1, 512), F32)
    onesr = ctx.sb("a5or", (1, 128), BF16)
    onesc = ctx.sb("a5oc", (128, 1), BF16)
    kmx = ctx.sb("a5km", (1, 8), F32)
    ksc = ctx.sb("a5ks", (1, 1), F32)
    P0 = [ctx.sb("a5P0%d" % i, (128, 512), BF16) for i in range(2)]
    Pbf = [ctx.sb("a5Pb%d" % i, (128, 512), BF16) for i in range(2)]
    osb = ctx.sb("a5os", (128, 512), BF16)
    pS = [ctx.ps("a5pS%d" % i, (128, 512), F32) for i in range(2)]
    pSum = ctx.ps("a5pSm", (1, 512), F32)
    pO = ctx.ps("a5pO", (128, 512), F32)
    ctx.op("gpsimd", "memset", onesr[:], 1.0)
    ctx.op("gpsimd", "memset", onesc[:], 1.0)
    ctx.op("sync", "dma_start", kmx[:], kmax_all[:, :])

    def split_hilo(rowf, dst):
        ctx.op("vector", "tensor_copy", dst[0][:], rowf)
        ctx.op("vector", "tensor_copy", rtmp[:], dst[0][:])
        ctx.op("vector", "tensor_sub", rtmp[:], rowf, rtmp[:])
        ctx.op("vector", "tensor_copy", dst[1][:], rtmp[:])
    ctx.op("vector", "reduce_max", ksc[:], kmx[:], axis=AX.X)
    ctx.op("vector", "tensor_scalar_mul", ksc[:], ksc[:], 1.0 / 128.0)

    for h in range(32):
        if rank_major:
            for r in range(2):
                ctx.op("sync", "dma_start", kT_sb[:, r * 2048:(r + 1) * 2048],
                       kT[r, h, :, :])
                for kg in range(2):
                    ctx.op("sync", "dma_start",
                           v_sb[:, r * 2048 + kg * 1024:
                                r * 2048 + (kg + 1) * 1024]
                           .rearrange("t (kc d) -> t kc d", d=128),
                           vpl[r, h, kg * 1024:(kg + 1) * 1024, :]
                           .rearrange("(kc t) d -> t kc d", t=128))
        else:
            ctx.op("sync", "dma_start", kT_sb[:], kT[h, :, :])
            for kg in range(4):
                ctx.op("sync", "dma_start",
                       v_sb[:, kg * 1024:(kg + 1) * 1024]
                       .rearrange("t (kc d) -> t kc d", d=128),
                       vpl[h, kg * 1024:(kg + 1) * 1024, :]
                       .rearrange("(kc t) d -> t kc d", t=128))
        ctx.op("sync", "dma_start", qT_sb[:], qT[h, :, :])
        for g in range(4):
            gq = slice(g * 512, (g + 1) * 512)
            qTg = qT_sb[:, gq]
            ctx.op("scalar", "activation", sq_s[:], qTg, AF.Square)
            ctx.op("tensor", "matmul", pSum[:], onesc[:], sq_s[:],
                   start=True, stop=True, inc=tsem.tick(1))
            ctx.op("scalar", "activation", Brow[:], pSum[:], AF.Sqrt,
                   scale=ksc[:])
            ctx.op("vector", "tensor_scalar_mul", BoC[:], Brow[:], 1.0 / CSC)
            ctx.op("vector", "tensor_scalar_mul", negBoC[:], BoC[:], -1.0)
            split_hilo(negBoC[:], nBb)
            nkc = 8 * g + 8
            for phase in range(2):
                rowv = nBb if phase == 0 else nLb

                def s_mm(kc):
                    j0 = max(0, (kc - 8 * g) // 2)
                    cols = slice(j0 * 128, 512)
                    ko = kcoff(kc)
                    ctx.op("tensor", "matmul", pS[kc & 1][:, cols],
                           kT_sb[:, ko:ko + 128],
                           qT_sb[:, g * 512 + j0 * 128:(g + 1) * 512],
                           start=True, stop=False)
                    ctx.op("tensor", "matmul", pS[kc & 1][:, cols], onesr[:],
                           rowv[0][0:1, cols], start=False, stop=False)
                    ctx.op("tensor", "matmul", pS[kc & 1][:, cols], onesr[:],
                           rowv[1][0:1, cols], start=False, stop=True,
                           inc=tsem.tick(1))
                    return cols

                def exp_mask(kc, cols):
                    dst = (P0 if phase == 0 else Pbf)[kc & 1]
                    j0 = cols.start // 128
                    ctx.op("scalar", "activation", dst[:, cols],
                           pS[kc & 1][:, cols], AF.Exp, scale=CSC)
                    for j in range(j0, 4):
                        lt = 4 * g + j
                        bsl = slice(j * 128, (j + 1) * 128)
                        mk = None
                        if kc == 2 * lt:
                            mk = "c_dmAb"
                        elif kc == 2 * lt + 1:
                            mk = "c_dmBb"
                        if mk is not None:
                            ctx.op("vector", "tensor_mul", dst[:, bsl],
                                   dst[:, bsl], consts[mk][:])
                    return dst

                def acc_mm(kc, cols, dst):
                    if phase == 0:
                        ctx.op("tensor", "matmul", pSum[0:1, cols], onesc[:],
                               dst[:, cols], start=(kc == 0),
                               stop=(kc == nkc - 1),
                               inc=(tsem.tick(1) if kc == nkc - 1 else None))
                    else:
                        ctx.op("tensor", "matmul", pO[:, cols],
                               v_sb[:, kcoff(kc):kcoff(kc) + 128],
                               dst[:, cols], start=(kc == 0),
                               stop=(kc == nkc - 1),
                               inc=(tsem.tick(1) if kc == nkc - 1 else None))

                pend = None
                for kc in range(nkc):
                    cols = s_mm(kc)
                    if pend is not None:
                        acc_mm(*pend)
                    dst = exp_mask(kc, cols)
                    pend = (kc, cols, dst)
                acc_mm(*pend)
                if phase == 0:
                    ctx.op("scalar", "activation", lnS[:], pSum[:], AF.Ln)
                    ctx.op("vector", "scalar_tensor_tensor", negLoC[:], lnS[:],
                           1.0 / CSC, BoC[:], op0=ALU.mult, op1=ALU.add)
                    ctx.op("vector", "tensor_scalar_mul", negLoC[:], negLoC[:],
                           -1.0)
                    split_hilo(negLoC[:], nLb)
            ctx.op("scalar", "activation", osb[:], pO[:], AF.Copy,
                   inc=sems["e"].tick(1))
            ctx.op("sync", "dma_start", oT_l1[h, :, gq], osb[:])
        a = ctx._autosem("sync", "d")
    ctx._need("sync", (a.h, a.n))
    ctx.pop_scope()


def build_L5(debug=False):
    nc = bass.Bass("TRN2", target_bir_lowering=False, debug=False)
    ctx = Ctx(nc)
    sems = make_sems(ctx)
    kind = "ExternalOutput" if debug else "Internal"
    x2_rows = nc.dram_tensor("x2_rows", [T_OWN, 4096], F32, kind="ExternalInput")
    qT = nc.dram_tensor("qT", [32, 128, T_OWN], BF16, kind="ExternalInput")
    kT = nc.dram_tensor("kT", [32, 128, 4096], BF16, kind="ExternalInput")
    vpl = nc.dram_tensor("vpl", [32, 4096, 128], BF16, kind="ExternalInput")
    kmax_all = nc.dram_tensor("kmax_all", [1, 8], F32, kind="ExternalInput")
    WoT = nc.dram_tensor("WoT", [4096, 4096], F32, kind="ExternalInput")
    bo_row = nc.dram_tensor("bo_row", [1, 4096], F32, kind="ExternalInput")
    Acat = nc.dram_tensor("Acat", [4096, 64], F32, kind="ExternalInput")
    Bcat = nc.dram_tensor("Bcat", [64, 4096], F32, kind="ExternalInput")
    WrT = nc.dram_tensor("WrT", [4096, 4], F32, kind="ExternalInput")
    oT_l1 = nc.dram_tensor("oT_l1", [32, 128, T_OWN], BF16, kind=kind)
    x3_rows = nc.dram_tensor("x3_rows", [T_OWN, 4096], F32, kind=kind)
    x4_out = nc.dram_tensor("x4_out", [T_OWN, 4096], F32, kind="ExternalOutput")
    with nc.Block() as block:
        consts = load_consts(ctx, L5_CONSTS, sems)
        emit_attn_l1(ctx, qT=qT, kT=kT, vpl=vpl, kmax_all=kmax_all,
                     oT_l1=oT_l1, sems=sems, consts=consts)
        ctx.push_scope()
        x2t = ctx.sb("l5x2", (128, 512), F32)
        out_sb = [ctx.sb("l5o%d" % i, (128, 512), F32) for i in range(2)]
        epar = [0]

        def epiO(tg, o0, n_o, psum, tval):
            par = epar[0]
            epar[0] = 1 - par
            rows = slice(tg * 128, (tg + 1) * 128)
            ctx.op("sync", "dma_start", x2t[:, 0:n_o], x2_rows[rows, o0:o0 + n_o])
            ctx.op("vector", "tensor_add", out_sb[par][:, 0:n_o],
                   psum[:, 0:n_o], x2t[:, 0:n_o], inc=sems["e"].tick(1))
            ctx.op("sync", "dma_start", x3_rows[rows, o0:o0 + n_o],
                   out_sb[par][:, 0:n_o])

        emit_proj(ctx, xT_planes=oT_l1, n_tok=T_OWN, w_dram=WoT, n_out=4096,
                  stats=None, sems=sems, consts=consts, bias_dram=bo_row,
                  epilogue=epiO, tag="L5o")
        ctx.pop_scope()
        stats = emit_stats(ctx, x3_rows, NT_OWN, sems, consts, tag="5")
        ctx.push_scope()
        acatb = ctx.sb("l5Ac", (128, 32 * 64), BF16)
        bcatb = ctx.sb("l5Bc", (64, 4096), BF16)
        wrtb = ctx.sb("l5Wr", (128, 32 * 4), BF16)
        onesc = ctx.sb("l5oc", (128, 1), BF16)
        wsA = ctx.sb("l5wA", (1, 64), BF16)
        wsR = ctx.sb("l5wR", (1, 4), BF16)
        x3t = ctx.sb("l5x3", (128, 4096), F32)
        x3b = ctx.sb("l5x3b", (128, 4096), BF16)
        x3T = ctx.sb("l5x3T", (128, 32 * 128), BF16)
        lgt = ctx.sb("l5lg", (128, 4), F32)
        mid = ctx.sb("l5md", (128, 64), F32)
        nm = ctx.sb("l5nm", (128, 1), F32)
        m2 = ctx.sb("l5m2", (128, 1), F32)
        eqm = ctx.sb("l5eq", (128, 4), F32)
        t4 = ctx.sb("l5t4", (128, 4), F32)
        gat = ctx.sb("l5ga", (128, 4), F32)
        sm = ctx.sb("l5sm", (128, 1), F32)
        rsc = ctx.sb("l5rs", (128, 1), F32)
        midg = ctx.sb("l5mg", (128, 64), BF16)
        mgT = ctx.sb("l5mT", (64, 128), BF16)
        osb2 = ctx.sb("l5ob", (128, 512), F32)
        pP = ctx.ps("l5pP", (128, 512), F32)
        pL = ctx.ps("l5pL", (128, 68), F32)
        pT = ctx.ps("l5pT", (128, 128), BF16)
        pW = ctx.ps("l5pW", (1, 68), F32)
        for fc in range(32):
            ctx.op("gpsimd", "dma_start", acatb[:, fc * 64:(fc + 1) * 64],
                   Acat[fc * 128:(fc + 1) * 128, :])
            ctx.op("gpsimd", "dma_start", wrtb[:, fc * 4:(fc + 1) * 4],
                   WrT[fc * 128:(fc + 1) * 128, :])
        ctx.op("gpsimd", "dma_start", bcatb[:], Bcat[:, :])
        ctx.op("gpsimd", "memset", onesc[:], 1.0)
        for fc in range(32):
            ctx.op("tensor", "matmul", pW[0:1, 0:64], onesc[:],
                   acatb[:, fc * 64:(fc + 1) * 64], start=(fc == 0), stop=False)
            ctx.op("tensor", "matmul", pW[0:1, 64:68], onesc[:],
                   wrtb[:, fc * 4:(fc + 1) * 4], start=False,
                   stop=(fc == 31), inc=(sems["t"].tick(1) if fc == 31 else None))
        ctx.op("vector", "tensor_copy", wsA[:], pW[0:1, 0:64])
        ctx.op("vector", "tensor_copy", wsR[:], pW[0:1, 64:68])

        for tg in range(NT_OWN):
            rows = slice(tg * 128, (tg + 1) * 128)
            ctx.op("sync", "dma_start", x3t[:], x3_rows[rows, :])
            ctx.op("vector", "tensor_copy", x3b[:], x3t[:])
            for fc in range(32):
                ctx.op("tensor", "transpose", pT[:],
                       x3b[:, fc * 128:(fc + 1) * 128], consts["c_idb"][:],
                       inc=sems["t"].tick(1))
                ctx.op("vector", "tensor_copy",
                       x3T[:, fc * 128:(fc + 1) * 128], pT[:])
            for fc in range(32):
                ctx.op("tensor", "matmul", pL[:, 0:64],
                       x3T[:, fc * 128:(fc + 1) * 128],
                       acatb[:, fc * 64:(fc + 1) * 64],
                       start=(fc == 0), stop=False)
                ctx.op("tensor", "matmul", pL[:, 64:68],
                       x3T[:, fc * 128:(fc + 1) * 128],
                       wrtb[:, fc * 4:(fc + 1) * 4], start=False, stop=False)
            ctx.op("tensor", "matmul", pL[:, 0:64],
                   stats["negm"][0:1, tg * 128:(tg + 1) * 128], wsA[:],
                   start=False, stop=False)
            ctx.op("tensor", "matmul", pL[:, 64:68],
                   stats["negm"][0:1, tg * 128:(tg + 1) * 128], wsR[:],
                   start=False, stop=True, inc=sems["t"].tick(1))
            ctx.op("scalar", "activation", mid[:], pL[:, 0:64], AF.Copy,
                   scale=stats["rs"][:, tg:tg + 1])
            ctx.op("scalar", "activation", lgt[:], pL[:, 64:68], AF.Copy,
                   scale=stats["rs"][:, tg:tg + 1])
            ctx.op("vector", "reduce_max", nm[:], lgt[:], axis=AX.X)
            ctx.op("vector", "tensor_scalar", eqm[:], lgt[:], nm[:], None,
                   op0=ALU.is_ge)
            ctx.op("vector", "scalar_tensor_tensor", t4[:], eqm[:], -1e30,
                   lgt[:], op0=ALU.mult, op1=ALU.add)
            ctx.op("vector", "reduce_max", m2[:], t4[:], axis=AX.X)
            ctx.op("vector", "tensor_scalar", eqm[:], lgt[:], m2[:], None,
                   op0=ALU.is_ge)
            ctx.op("vector", "tensor_scalar_mul", nm[:], nm[:], -1.0)
            ctx.op("scalar", "activation", gat[:], lgt[:], AF.Exp, bias=nm[:])
            ctx.op("vector", "tensor_mul", gat[:], gat[:], eqm[:])
            ctx.op("vector", "reduce_sum", sm[:], gat[:], axis=AX.X)
            ctx.op("vector", "reciprocal", rsc[:], sm[:])
            ctx.op("vector", "tensor_scalar_mul", gat[:], gat[:], rsc[:])
            for e in range(4):
                ctx.op("vector", "tensor_scalar", midg[:, e * 16:(e + 1) * 16],
                       mid[:, e * 16:(e + 1) * 16], gat[:, e:e + 1], 4.0,
                       op0=ALU.mult, op1=ALU.mult)
            ctx.op("tensor", "transpose", pT[0:64, 0:128], midg[:],
                   consts["c_idb"][:], inc=sems["t"].tick(1))
            ctx.op("vector", "tensor_copy", mgT[:], pT[0:64, 0:128])
            for oc in range(8):
                osl = slice(oc * 512, (oc + 1) * 512)
                ctx.op("tensor", "matmul", pP[:], mgT[:], bcatb[:, osl],
                       start=True, stop=True, inc=sems["t"].tick(1))
                ctx.op("vector", "tensor_add", osb2[:], pP[:], x3t[:, osl])
                ctx.op("sync", "dma_start", x4_out[rows, osl], osb2[:])
        a = ctx._autosem("sync", "d")
        ctx._need("sync", (a.h, a.n))
        ctx.pop_scope()
        ctx.finalize(block)
    ctx.st.close()
    return nc


def host_inputs_L5(inputs, core, x2_own, qkv_pair, kmax8):
    p = core % 2
    q_own, _, _ = qkv_pair[p]
    kT_full = np.zeros((32, 128, 4096), dtype=NBF)
    v_full = np.zeros((32, 4096, 128), dtype=NBF)
    for pp in range(2):
        own = own_rows(pp)
        _, k_pp, v_pp = qkv_pair[pp]
        for h in range(32):
            kT_full[h][:, own] = k_pp[:, h * 128:(h + 1) * 128].T
            v_full[h][own, :] = v_pp[:, h * 128:(h + 1) * 128]
    qTl = np.stack([np.ascontiguousarray(q_own[:, h * 128:(h + 1) * 128].T)
                    for h in range(32)])
    E, R = 4, 16
    Acat = inputs["L1_ffn_A"].transpose(2, 0, 1).reshape(4096, 64)
    Bcat = inputs["L1_ffn_B"].transpose(0, 2, 1).reshape(64, 4096)
    d = {"x2_rows": np.ascontiguousarray(x2_own),
         "qT": np.ascontiguousarray(qTl),
         "kT": np.ascontiguousarray(kT_full),
         "vpl": np.ascontiguousarray(v_full),
         "kmax_all": np.ascontiguousarray(np.array(kmax8, np.float32).reshape(1, 8)),
         "WoT": np.ascontiguousarray(inputs["L1_attn_Wo"].T),
         "bo_row": np.ascontiguousarray(inputs["L1_attn_bo"][None, :]),
         "Acat": np.ascontiguousarray(Acat),
         "Bcat": np.ascontiguousarray(Bcat),
         "WrT": np.ascontiguousarray(inputs["L1_ffn_Wr"].T)}
    d.update({k: v for k, v in host_consts(p).items() if k in L5_CONSTS})
    return d


_NC_CACHE = {}

def _get(name, fn):
    if name not in _NC_CACHE:
        _NC_CACHE[name] = fn()
    return _NC_CACHE[name]


def kernel(**inputs):
    return kernel_fused(**inputs)


def kernel_multilaunch(**inputs):
    inputs = {k: np.ascontiguousarray(np.asarray(v)) for k, v in inputs.items()}
    cores = list(range(NCORE))

    nc1 = _get("L1", build_L1)
    r1 = run_bass_kernel_spmd(nc1, [host_inputs_L1(inputs, c) for c in cores],
                              core_ids=cores)
    oT = [np.asarray(r1.results[c]["oT"]) for c in cores]

    nc2a = _get("L2a", build_L2a)
    r2a = run_bass_kernel_spmd(
        nc2a, [host_inputs_L2a(inputs, c, (oT[c // 2 * 2], oT[c // 2 * 2 + 1]))
               for c in cores], core_ids=cores)
    x1 = [np.asarray(r2a.results[c]["x1_rows"]) for c in cores]

    nc2b = _get("L2b", build_L2b)
    r2b = run_bass_kernel_spmd(
        nc2b, [host_inputs_L2b(inputs, c, x1[c]) for c in cores],
        core_ids=cores)
    scales8 = [np.asarray(r2b.results[c]["scales_loc"]) for c in cores]

    nc3 = _get("L3", build_L3)
    r3 = run_bass_kernel_spmd(
        nc3, [host_inputs_L3(inputs, c, x1[c],
                             np.asarray(r2b.results[c]["H_rows"]),
                             np.asarray(r2b.results[c]["z_rows"]),
                             np.asarray(r2b.results[c]["ds_rows"]), scales8)
              for c in cores], core_ids=cores)
    x2 = [np.asarray(r3.results[c]["x2_rows"]) for c in cores]

    nc4 = _get("L4", build_L4)
    r4 = run_bass_kernel_spmd(
        nc4, [host_inputs_L4(inputs, c, x2[c]) for c in cores],
        core_ids=cores)
    kmax8 = [float(np.asarray(r4.results[c]["kmax_loc"])[0, 0]) for c in cores]
    qkv = [(np.asarray(r4.results[c]["q_rows"]),
            np.asarray(r4.results[c]["k_rows"]),
            np.asarray(r4.results[c]["v_rows"])) for c in cores]

    nc5 = _get("L5", build_L5)
    r5 = run_bass_kernel_spmd(
        nc5, [host_inputs_L5(inputs, c, x2[c],
                             (qkv[c // 2 * 2], qkv[c // 2 * 2 + 1]), kmax8)
              for c in cores], core_ids=cores)

    out = np.empty((B, N, DM), np.float32)
    for c in cores:
        b, p = c // 2, c % 2
        out[b][own_rows(p)] = np.asarray(r5.results[c]["x4_out"])
    return out


def _emit_gating(ctx, sems, consts, x1_rows, H_rows, z_rows, ds_rows,
                 scales_all, LiM, Bm, x2_rows):
    sall = ctx.sb("g3sa", (1, 16), F32)
    s11 = ctx.sb("g3s1", (1, 2), F32)
    inv2 = ctx.sb("g3inv", (1, 2), F32)
    invc = ctx.sb("g3ic", (128, 2), F32)
    ones_r = ctx.sb("g3or", (1, 128), F32)
    limb = ctx.sb("g3lim", (8, 32), BF16)
    bmb = ctx.sb("g3bm", (32, 4096), BF16)
    pB = ctx.ps("g3pB", (128, 2), F32)
    ctx.op("sync", "dma_start", sall[:], scales_all[:, :])
    ctx.op("gpsimd", "dma_start", limb[:], LiM[:, :])
    ctx.op("gpsimd", "dma_start", bmb[:], Bm[:, :])
    ctx.op("gpsimd", "memset", ones_r[:], 1.0)
    s8x2 = ctx.sb("g3s8", (1, 16), F32)
    sv = sall.ap().rearrange("a (p two) -> a two p", two=2)
    ctx.op("vector", "tensor_copy",
           s8x2.ap().rearrange("a (two p) -> a two p", two=2), sv)
    ctx.op("vector", "reduce_max", s11[:, 0:1], s8x2[:, 0:8], axis=AX.X)
    ctx.op("vector", "reduce_max", s11[:, 1:2], s8x2[:, 8:16], axis=AX.X)
    ctx.op("vector", "tensor_scalar_max", s11[:], s11[:], 1e-6)
    ctx.op("vector", "reciprocal", inv2[:], s11[:])
    ctx.op("vector", "tensor_scalar_mul", inv2[:], inv2[:], 0.5)
    ctx.op("tensor", "matmul", pB[:], ones_r[:], inv2[:], start=True,
           stop=True, inc=sems["t"].tick(1))
    ctx.op("vector", "tensor_copy", invc[:], pB[:])

    Ht = ctx.sb("g3H", (128, 4096), F32)
    x1t = ctx.sb("g3x1", (128, 4096), F32)
    hs8 = ctx.sb("g3h8", (128, 8), F32)
    ds8 = ctx.sb("g3d8", (128, 8), F32)
    zt = ctx.sb("g3z", (128, 32), F32)
    lg = ctx.sb("g3lg", (128, 8), F32)
    t8 = ctx.sb("g3t8", (128, 8), F32)
    nm = ctx.sb("g3nm", (128, 1), F32)
    sm = ctx.sb("g3sm", (128, 1), F32)
    rsc = ctx.sb("g3rs", (128, 1), F32)
    probs = ctx.sb("g3pr", (128, 8), F32)
    eqm = ctx.sb("g3eq", (128, 8), F32)
    m2 = ctx.sb("g3m2", (128, 1), F32)
    sig = ctx.sb("g3sg", (128, 8), F32)
    wts = ctx.sb("g3wt", (128, 8), F32)
    wsh = ctx.sb("g3ws", (128, 8), F32)
    wshb = ctx.sb("g3wsb", (128, 8), BF16)
    wT = ctx.sb("g3wT", (8, 128), BF16)
    zpb = ctx.sb("g3zp", (128, 32), BF16)
    zpT = ctx.sb("g3zpT", (32, 128), BF16)
    osb = ctx.sb("g3ou", (128, 512), F32)
    pT8 = ctx.ps("g3pT", (8, 128), BF16)
    pM = ctx.ps("g3pM", (128, 32), F32)
    pT32 = ctx.ps("g3pT2", (32, 128), BF16)
    pDl = ctx.ps("g3pD", (128, 512), F32)
    SHUF = [i | 1 for i in range(32)]

    for tg in range(NT_OWN):
        rows = slice(tg * 128, (tg + 1) * 128)
        ctx.op("sync", "dma_start", hs8[:], H_rows[rows, 0:8])
        ctx.op("sync", "dma_start", ds8[:], ds_rows[rows, :])
        ctx.op("sync", "dma_start", zt[:], z_rows[rows, :])
        ctx.op("sync", "dma_start", x1t[:], x1_rows[rows, :])
        ctx.op("sync", "dma_start", Ht[:], H_rows[rows, :])
        ctx.op("vector", "tensor_scalar_mul", t8[:], ds8[:], invc[:, 1:2])
        ctx.op("vector", "scalar_tensor_tensor", lg[:], hs8[:],
               invc[:, 0:1], t8[:], op0=ALU.mult, op1=ALU.add)
        ctx.op("vector", "reduce_max", nm[:], lg[:], axis=AX.X, negate=True)
        ctx.op("scalar", "activation", probs[:], lg[:], AF.Exp,
               bias=nm[:], scale=1.0, accum_out=sm[:])
        ctx.op("vector", "reciprocal", rsc[:], sm[:])
        ctx.op("vector", "tensor_scalar_mul", probs[:], probs[:], rsc[:])
        ctx.op("vector", "reduce_max", nm[:], probs[:], axis=AX.X)
        ctx.op("vector", "tensor_scalar", eqm[:], probs[:], nm[:], None,
               op0=ALU.is_ge)
        ctx.op("vector", "scalar_tensor_tensor", t8[:], eqm[:], -1e30,
               probs[:], op0=ALU.mult, op1=ALU.add)
        ctx.op("vector", "reduce_max", m2[:], t8[:], axis=AX.X)
        ctx.op("vector", "tensor_scalar_mul", m2[:], m2[:], -2.0)
        ctx.op("scalar", "activation", sig[:], probs[:], AF.Sigmoid,
               bias=m2[:], scale=2.0)
        ctx.op("vector", "tensor_mul", wts[:], probs[:], sig[:])
        ctx.op("vector", "reduce_sum", sm[:], wts[:], axis=AX.X)
        ctx.op("vector", "tensor_scalar_add", sm[:], sm[:], 1e-9)
        ctx.op("vector", "reciprocal", rsc[:], sm[:])
        ctx.op("vector", "tensor_scalar_mul", wts[:], wts[:], rsc[:])
        ctx.op("vector", "stream_shuffle", wsh[:], wts[:], SHUF)
        ctx.op("vector", "tensor_copy", wshb[:], wsh[:])
        ctx.op("tensor", "transpose", pT8[:], wshb[:], consts["c_idb"][:],
               inc=sems["t"].tick(1))
        ctx.op("vector", "tensor_copy", wT[:], pT8[:])
        ctx.op("tensor", "matmul", pM[:], wT[:], limb[:], start=True,
               stop=True, inc=sems["t"].tick(1))
        ctx.op("vector", "tensor_mul", zpb[:], zt[:], pM[:])
        ctx.op("tensor", "transpose", pT32[:], zpb[:], consts["c_idb"][:],
               inc=sems["t"].tick(1))
        ctx.op("vector", "tensor_copy", zpT[:], pT32[:])
        for oc in range(8):
            osl = slice(oc * 512, (oc + 1) * 512)
            ctx.op("tensor", "matmul", pDl[:], zpT[:], bmb[:, osl],
                   start=True, stop=True, inc=sems["t"].tick(1))
            ctx.op("vector", "scalar_tensor_tensor", osb[:], pDl[:], 0.5,
                   Ht[:, osl], op0=ALU.mult, op1=ALU.add)
            ctx.op("vector", "tensor_add", osb[:], osb[:], x1t[:, osl])
            ctx.op("sync", "dma_start", x2_rows[rows, osl], osb[:])


def _emit_ffn1(ctx, sems, consts, stats, x3_rows, Acat, Bcat, WrT, x4_out):
    acatb = ctx.sb("d5Ac", (128, 32 * 64), BF16)
    bcatb = ctx.sb("d5Bc", (64, 4096), BF16)
    wrtb = ctx.sb("d5Wr", (128, 32 * 4), BF16)
    onesc = ctx.sb("d5oc", (128, 1), BF16)
    wsA = ctx.sb("d5wA", (1, 64), BF16)
    wsR = ctx.sb("d5wR", (1, 4), BF16)
    x3t = ctx.sb("d5x3", (128, 4096), F32)
    x3b = ctx.sb("d5x3b", (128, 4096), BF16)
    x3T = ctx.sb("d5x3T", (128, 32 * 128), BF16)
    lgt = ctx.sb("d5lg", (128, 4), F32)
    mid = ctx.sb("d5md", (128, 64), F32)
    nm = ctx.sb("d5nm", (128, 1), F32)
    m2 = ctx.sb("d5m2", (128, 1), F32)
    eqm = ctx.sb("d5eq", (128, 4), F32)
    t4 = ctx.sb("d5t4", (128, 4), F32)
    gat = ctx.sb("d5ga", (128, 4), F32)
    sm = ctx.sb("d5sm", (128, 1), F32)
    rsc = ctx.sb("d5rs", (128, 1), F32)
    midg = ctx.sb("d5mg", (128, 64), BF16)
    mgT = ctx.sb("d5mT", (64, 128), BF16)
    osb2 = ctx.sb("d5ob", (128, 512), F32)
    pP = ctx.ps("d5pP", (128, 512), F32)
    pL = ctx.ps("d5pL", (128, 68), F32)
    pT = ctx.ps("d5pT", (128, 128), BF16)
    pW = ctx.ps("d5pW", (1, 68), F32)
    for fc in range(32):
        ctx.op("gpsimd", "dma_start", acatb[:, fc * 64:(fc + 1) * 64],
               Acat[fc * 128:(fc + 1) * 128, :])
        ctx.op("gpsimd", "dma_start", wrtb[:, fc * 4:(fc + 1) * 4],
               WrT[fc * 128:(fc + 1) * 128, :])
    ctx.op("gpsimd", "dma_start", bcatb[:], Bcat[:, :])
    ctx.op("gpsimd", "memset", onesc[:], 1.0)
    for fc in range(32):
        ctx.op("tensor", "matmul", pW[0:1, 0:64], onesc[:],
               acatb[:, fc * 64:(fc + 1) * 64], start=(fc == 0), stop=False)
        ctx.op("tensor", "matmul", pW[0:1, 64:68], onesc[:],
               wrtb[:, fc * 4:(fc + 1) * 4], start=False,
               stop=(fc == 31), inc=(sems["t"].tick(1) if fc == 31 else None))
    ctx.op("vector", "tensor_copy", wsA[:], pW[0:1, 0:64])
    ctx.op("vector", "tensor_copy", wsR[:], pW[0:1, 64:68])

    for tg in range(NT_OWN):
        rows = slice(tg * 128, (tg + 1) * 128)
        ctx.op("sync", "dma_start", x3t[:], x3_rows[rows, :])
        ctx.op("vector", "tensor_copy", x3b[:], x3t[:])
        for fc in range(32):
            ctx.op("tensor", "transpose", pT[:],
                   x3b[:, fc * 128:(fc + 1) * 128], consts["c_idb"][:],
                   inc=sems["t"].tick(1))
            ctx.op("vector", "tensor_copy",
                   x3T[:, fc * 128:(fc + 1) * 128], pT[:])
        for fc in range(32):
            ctx.op("tensor", "matmul", pL[:, 0:64],
                   x3T[:, fc * 128:(fc + 1) * 128],
                   acatb[:, fc * 64:(fc + 1) * 64],
                   start=(fc == 0), stop=False)
            ctx.op("tensor", "matmul", pL[:, 64:68],
                   x3T[:, fc * 128:(fc + 1) * 128],
                   wrtb[:, fc * 4:(fc + 1) * 4], start=False, stop=False)
        ctx.op("tensor", "matmul", pL[:, 0:64],
               stats["negm"][0:1, tg * 128:(tg + 1) * 128], wsA[:],
               start=False, stop=False)
        ctx.op("tensor", "matmul", pL[:, 64:68],
               stats["negm"][0:1, tg * 128:(tg + 1) * 128], wsR[:],
               start=False, stop=True, inc=sems["t"].tick(1))
        ctx.op("scalar", "activation", mid[:], pL[:, 0:64], AF.Copy,
               scale=stats["rs"][:, tg:tg + 1])
        ctx.op("scalar", "activation", lgt[:], pL[:, 64:68], AF.Copy,
               scale=stats["rs"][:, tg:tg + 1])
        ctx.op("vector", "reduce_max", nm[:], lgt[:], axis=AX.X)
        ctx.op("vector", "tensor_scalar", eqm[:], lgt[:], nm[:], None,
               op0=ALU.is_ge)
        ctx.op("vector", "scalar_tensor_tensor", t4[:], eqm[:], -1e30,
               lgt[:], op0=ALU.mult, op1=ALU.add)
        ctx.op("vector", "reduce_max", m2[:], t4[:], axis=AX.X)
        ctx.op("vector", "tensor_scalar", eqm[:], lgt[:], m2[:], None,
               op0=ALU.is_ge)
        ctx.op("vector", "tensor_scalar_mul", nm[:], nm[:], -1.0)
        ctx.op("scalar", "activation", gat[:], lgt[:], AF.Exp, bias=nm[:])
        ctx.op("vector", "tensor_mul", gat[:], gat[:], eqm[:])
        ctx.op("vector", "reduce_sum", sm[:], gat[:], axis=AX.X)
        ctx.op("vector", "reciprocal", rsc[:], sm[:])
        ctx.op("vector", "tensor_scalar_mul", gat[:], gat[:], rsc[:])
        for e in range(4):
            ctx.op("vector", "tensor_scalar", midg[:, e * 16:(e + 1) * 16],
                   mid[:, e * 16:(e + 1) * 16], gat[:, e:e + 1], 4.0,
                   op0=ALU.mult, op1=ALU.mult)
        ctx.op("tensor", "transpose", pT[0:64, 0:128], midg[:],
               consts["c_idb"][:], inc=sems["t"].tick(1))
        ctx.op("vector", "tensor_copy", mgT[:], pT[0:64, 0:128])
        for oc in range(8):
            osl = slice(oc * 512, (oc + 1) * 512)
            ctx.op("tensor", "matmul", pP[:], mgT[:], bcatb[:, osl],
                   start=True, stop=True, inc=sems["t"].tick(1))
            ctx.op("vector", "tensor_add", osb2[:], pP[:], x3t[:, osl])
            ctx.op("sync", "dma_start", x4_out[rows, osl], osb2[:])


FUSED_CONSTS = ["c_idf", "c_idb", "c_U", "c_L", "c_cos", "c_sin", "c_swm",
                "c_swm0", "c_dmA", "c_dmB", "c_dmAb", "c_dmBb", "c_odd"]
RG_PAIR = [[0, 1], [2, 3], [4, 5], [6, 7]]
RG_ALL = [[0, 1, 2, 3, 4, 5, 6, 7]]


def emit_transpose_planes(ctx, rows_dram, planes_bf, n_tiles, sems, consts, tag):
    ctx.push_scope()
    xt = ctx.sb("tp_x" + tag, (128, 4096), F32)
    xb = ctx.sb("tp_b" + tag, (128, 4096), BF16)
    stg = ctx.sb("tp_s" + tag, (128, 4096), BF16)
    pT = ctx.ps("tp_p" + tag, (128, 128), BF16)
    idb = consts["c_idb"]
    for tg in range(n_tiles):
        ctx.op("sync", "dma_start", xt[:], rows_dram[tg * 128:(tg + 1) * 128, :])
        ctx.op("vector", "tensor_copy", xb[:], xt[:])
        for fc in range(32):
            ctx.op("tensor", "transpose", pT[:], xb[:, fc * 128:(fc + 1) * 128],
                   idb[:], inc=sems["t"].tick(1))
            ctx.op("vector", "tensor_copy", stg[:, fc * 128:(fc + 1) * 128],
                   pT[:])
        for g8 in range(4):
            ctx.op("gpsimd", "dma_start",
                   planes_bf[g8 * 8:(g8 + 1) * 8, :, tg * 128:(tg + 1) * 128]
                   .rearrange("fc d t -> d fc t"),
                   stg[:, g8 * 1024:(g8 + 1) * 1024]
                   .rearrange("d (fc t) -> d fc t", t=128))
    ctx.pop_scope()


def build_fused(upto="full"):
    nc = bass.Bass("TRN2", target_bir_lowering=False, debug=False)
    ctx = Ctx(nc)
    sems = make_sems(ctx)
    EI = "ExternalInput"
    x0_rows = nc.dram_tensor("x0_rows", [4096, 4096], F32, kind=EI)
    x0Tp = nc.dram_tensor("x0Tp", [32, 128, 4096], F32, kind=EI)
    x0_own = nc.dram_tensor("x0_own", [T_OWN, 4096], F32, kind=EI)
    wqkv = nc.dram_tensor("wqkv", [4096, 3072], F32, kind=EI)
    Wo_sl = nc.dram_tensor("Wo_sl", [2048, 4096], F32, kind=EI)
    WT = nc.dram_tensor("WT", [4096, 4096], F32, kind=EI)
    b_row = nc.dram_tensor("b_row", [1, 4096], F32, kind=EI)
    A4 = nc.dram_tensor("A4", [4096, 32], F32, kind=EI)
    LiT = nc.dram_tensor("LiT", [32, 8], F32, kind=EI)
    LiM = nc.dram_tensor("LiM", [8, 32], F32, kind=EI)
    Bm = nc.dram_tensor("Bm", [32, 4096], F32, kind=EI)
    Wcat = nc.dram_tensor("Wcat", [4096, 12288], F32, kind=EI)
    bcat = nc.dram_tensor("bcat", [1, 12288], F32, kind=EI)
    WoT = nc.dram_tensor("WoT", [4096, 4096], F32, kind=EI)
    bo_row = nc.dram_tensor("bo_row", [1, 4096], F32, kind=EI)
    Acat = nc.dram_tensor("Acat", [4096, 64], F32, kind=EI)
    Bcat = nc.dram_tensor("Bcat", [64, 4096], F32, kind=EI)
    WrT = nc.dram_tensor("WrT", [4096, 4], F32, kind=EI)
    qf = nc.dram_tensor("qf", [4096, 2048], F32)
    kf = nc.dram_tensor("kf", [4096, 512], F32)
    vf = nc.dram_tensor("vf", [4096, 512], F32)
    oT_send = nc.dram_tensor("oT_send", [2, 16, 128, T_OWN], BF16)
    rs_in = nc.dram_tensor("rs_in", [4096, 4096], BF16)
    rs_out = nc.dram_tensor("rs_out", [T_OWN, 4096], BF16)
    x1_rows = nc.dram_tensor("x1_rows", [T_OWN, 4096], F32)
    x1Tp = nc.dram_tensor("x1Tp", [32, 128, T_OWN], BF16)
    H_rows = nc.dram_tensor("H_rows", [T_OWN, 4096], F32)
    z_rows = nc.dram_tensor("z_rows", [T_OWN, 32], F32)
    ds_rows = nc.dram_tensor("ds_rows", [T_OWN, 8], F32)
    scales_loc = nc.dram_tensor("scales_loc", [1, 2], F32)
    scales_all = nc.dram_tensor("scales_all", [1, 16], F32)
    x2_rows = nc.dram_tensor("x2_rows", [T_OWN, 4096], F32)
    x2Tp = nc.dram_tensor("x2Tp", [32, 128, T_OWN], BF16)
    qT_loc = nc.dram_tensor("qT_loc", [32, 128, T_OWN], BF16)
    kT_loc = nc.dram_tensor("kT_loc", [32, 128, T_OWN], BF16)
    v_loc = nc.dram_tensor("v_loc", [32, T_OWN, 128], BF16)
    kT_rm = nc.dram_tensor("kT_rm", [2, 32, 128, T_OWN], BF16)
    v_rm = nc.dram_tensor("v_rm", [2, 32, T_OWN, 128], BF16)
    kmax_loc = nc.dram_tensor("kmax_loc", [1, 1], F32)
    kmax_all = nc.dram_tensor("kmax_all", [1, 8], F32)
    oT_l1 = nc.dram_tensor("oT_l1", [32, 128, T_OWN], BF16)
    x3_rows = nc.dram_tensor("x3_rows", [T_OWN, 4096], F32)
    x4_out = nc.dram_tensor("x4_out", [T_OWN, 4096], F32, kind="ExternalOutput")

    def _stop_here(ctx_, name, src_dram, shape):
        t = ctx_.sb("dbgcp" + name, (128, 4096), F32)
        for tg in range(NT_OWN):
            rows = slice(tg * 128, (tg + 1) * 128)
            ctx_.op("sync", "dma_start", t[:], src_dram[rows, :])
            ctx_.op("sync", "dma_start", x4_out[rows, :], t[:])
        a = ctx_._autosem("sync", "d")
        ctx_._need("sync", (a.h, a.n))

    with nc.Block() as block:
        consts = load_consts(ctx, FUSED_CONSTS, sems)

        stats0 = emit_stats(ctx, x0_rows, 32, sems, consts, tag="f0")
        ctx.push_scope()
        out_sb = [ctx.sb("f1o%d" % i, (128, 512), F32) for i in range(2)]
        epar = [0]

        def epi1(tg, o0, n_o, psum, tval):
            par = epar[0]
            epar[0] = 1 - par
            ctx.op("scalar", "activation", out_sb[par][:, 0:n_o],
                   psum[:, 0:n_o], AF.Copy, scale=stats0["rs"][:, tg:tg + 1],
                   inc=sems["e"].tick(1))
            if o0 < 2048:
                dest = qf[tg * 128:(tg + 1) * 128, o0:o0 + n_o]
            elif o0 < 2560:
                dest = kf[tg * 128:(tg + 1) * 128, 0:n_o]
            else:
                dest = vf[tg * 128:(tg + 1) * 128, 0:n_o]
            ctx.op("sync", "dma_start", dest, out_sb[par][:, 0:n_o])

        emit_proj(ctx, xT_planes=x0Tp, n_tok=4096, w_dram=wqkv, n_out=3072,
                  stats=stats0, sems=sems, consts=consts, epilogue=epi1,
                  tag="f1")
        ctx.pop_scope()

        def oT_writer(c, oT_stage):
            r, i = c & 1, c >> 1
            for g in range(2):
                ctx.op("gpsimd", "dma_start",
                       oT_send[r, g * 8:(g + 1) * 8, :, i * 128:(i + 1) * 128]
                       .rearrange("h d t -> d h t"),
                       oT_stage[:, g * 1024:(g + 1) * 1024]
                       .rearrange("d (h t) -> d h t", t=128))

        emit_mixer(ctx, qf=qf, kf=kf, vf=vf, oT_out=None, sems=sems,
                   consts=consts, tag="f", oT_writer=oT_writer)

        ctx.push_scope()
        out2 = [ctx.sb("f2o%d" % i, (128, 512), BF16) for i in range(2)]
        epar2 = [0]

        def epi2(tg, o0, n_o, psum, tval):
            par = epar2[0]
            epar2[0] = 1 - par
            rows = slice(tg * 128, (tg + 1) * 128)
            ctx.op("scalar", "activation", out2[par][:, 0:n_o],
                   psum[:, 0:n_o], AF.Copy, inc=sems["e"].tick(1))
            ctx.op("sync", "dma_start", rs_in[rows, o0:o0 + n_o],
                   out2[par][:, 0:n_o])

        def oT_loader(c2, xp, tb):
            hf, toff = divmod(tb * 512, 2048)
            for g in range(2):
                f0 = g * 8
                c2.op("gpsimd", "dma_start",
                      xp[:, f0 * 512:(f0 + 8) * 512]
                      .rearrange("p (fc t) -> p fc t", t=512),
                      oT_send[hf, f0:f0 + 8, :, toff:toff + 512]
                      .rearrange("fc p t -> p fc t"))

        emit_proj(ctx, xT_planes=None, n_tok=4096, w_dram=Wo_sl, n_out=4096,
                  stats=None, sems=sems, consts=consts, epilogue=epi2,
                  tag="f2", nfc=16, x_loader=oT_loader)
        ctx.pop_scope()
        ctx.op("gpsimd", "collective_compute", "ReduceScatter", ALU.add,
               replica_groups=RG_PAIR, ins=[rs_in.ap().opt()],
               outs=[rs_out.ap().opt()])
        ctx.push_scope()
        x0t = ctx.sb("f2x0", (128, 4096), F32)
        rst = ctx.sb("f2rs", (128, 4096), BF16)
        x1s = ctx.sb("f2x1", (128, 4096), F32)
        for tg in range(NT_OWN):
            rows = slice(tg * 128, (tg + 1) * 128)
            ctx.op("sync", "dma_start", x0t[:], x0_own[rows, :])
            ctx.op("sync", "dma_start", rst[:], rs_out[rows, :])
            ctx.op("vector", "tensor_add", x1s[:], x0t[:], rst[:])
            ctx.op("sync", "dma_start", x1_rows[rows, :], x1s[:])
        ctx.pop_scope()

        if upto == "x1":
            _stop_here(ctx, "x1", x1_rows, None)
            ctx.finalize(block)
            ctx.st.close()
            return nc

        emit_transpose_planes(ctx, x1_rows, x1Tp, NT_OWN, sems, consts, "f1T")
        statsB = emit_stats(ctx, x1_rows, NT_OWN, sems, consts, tag="fB")
        ctx.push_scope()
        hmax = ctx.sb("fbhm", (128, 1), F32)
        dmax = ctx.sb("fbdm", (128, 1), F32)
        tmp8 = ctx.sb("fbt8", (128, 8), F32)
        m8 = ctx.sb("fbm8", (128, 1), F32)
        ctx.op("gpsimd", "memset", hmax[:], 0.0)
        ctx.op("gpsimd", "memset", dmax[:], 0.0)
        outH = [ctx.sb("fbo%d" % i, (128, 512), F32) for i in range(2)]
        eparH = [0]

        def epiH(tg, o0, n_o, psum, tval):
            par = eparH[0]
            eparH[0] = 1 - par
            rows = slice(tg * 128, (tg + 1) * 128)
            ctx.op("scalar", "activation", outH[par][:, 0:n_o],
                   psum[:, 0:n_o], AF.Copy, scale=statsB["rs"][:, tg:tg + 1],
                   inc=sems["e"].tick(1))
            ctx.op("sync", "dma_start", H_rows[rows, o0:o0 + n_o],
                   outH[par][:, 0:n_o])
            if o0 == 0:
                ctx.op("vector", "tensor_scalar_mul", tmp8[:],
                       outH[par][:, 0:8], consts["c_odd"][:])
                ctx.op("vector", "reduce_max", m8[:], tmp8[:], axis=AX.X,
                       apply_absolute_value=True)
                ctx.op("vector", "tensor_max", hmax[:], hmax[:], m8[:])

        emit_proj(ctx, xT_planes=x1Tp, n_tok=T_OWN, w_dram=WT, n_out=4096,
                  stats=statsB, sems=sems, consts=consts, bias_dram=b_row,
                  epilogue=epiH, tag="fbH")
        z_sb = ctx.sb("fbz", (128, 32), F32)

        def epiZ(tg, o0, n_o, psum, tval):
            rows = slice(tg * 128, (tg + 1) * 128)
            ctx.op("scalar", "activation", z_sb[:, 0:n_o], psum[:, 0:n_o],
                   AF.Copy, scale=statsB["rs"][:, tg:tg + 1],
                   inc=sems["e"].tick(1))
            ctx.op("sync", "dma_start", z_rows[rows, 0:n_o], z_sb[:, 0:n_o])

        emit_proj(ctx, xT_planes=x1Tp, n_tok=T_OWN, w_dram=A4, n_out=32,
                  stats=statsB, sems=sems, consts=consts, epilogue=epiZ,
                  tag="fbz")
        zt = ctx.sb("fbzt", (128, 32), F32)
        ztb = ctx.sb("fbztb", (128, 32), BF16)
        zTb = ctx.sb("fbzT", (32, 128), BF16)
        litb = ctx.sb("fblit", (32, 8), BF16)
        ds_sb = ctx.sb("fbds", (128, 8), F32)
        pTz = ctx.ps("fbpT", (32, 128), BF16)
        pDz = ctx.ps("fbpD", (128, 8), F32)
        ctx.op("gpsimd", "dma_start", litb[:], LiT[:, :])
        for tg in range(NT_OWN):
            rows = slice(tg * 128, (tg + 1) * 128)
            ctx.op("sync", "dma_start", zt[:], z_rows[rows, :])
            ctx.op("vector", "tensor_copy", ztb[:], zt[:])
            ctx.op("tensor", "transpose", pTz[:], ztb[:], consts["c_idb"][:],
                   inc=sems["t"].tick(1))
            ctx.op("vector", "tensor_copy", zTb[:], pTz[:])
            ctx.op("tensor", "matmul", pDz[:], zTb[:], litb[:], start=True,
                   stop=True, inc=sems["t"].tick(1))
            ctx.op("vector", "tensor_copy", ds_sb[:], pDz[:])
            ctx.op("sync", "dma_start", ds_rows[rows, :], ds_sb[:])
            ctx.op("vector", "tensor_scalar_mul", tmp8[:], ds_sb[:],
                   consts["c_odd"][:])
            ctx.op("vector", "reduce_max", m8[:], tmp8[:], axis=AX.X,
                   apply_absolute_value=True)
            ctx.op("vector", "tensor_max", dmax[:], dmax[:], m8[:])
        pRs = ctx.ps("fbpR", (1, 128), F32)
        srow = ctx.sb("fbsr", (1, 128), F32)
        sc2 = ctx.sb("fbsc", (1, 2), F32)
        for i, acc in enumerate((hmax, dmax)):
            ctx.op("tensor", "transpose", pRs[:], acc[:], consts["c_idf"][:],
                   inc=sems["t"].tick(1))
            ctx.op("vector", "tensor_copy", srow[:], pRs[:])
            ctx.op("vector", "reduce_max", sc2[:, i:i + 1], srow[:], axis=AX.X)
        ctx.op("sync", "dma_start", scales_loc[:, :], sc2[:])
        ctx.pop_scope()

        ctx.op("gpsimd", "collective_compute", "AllGather", ALU.bypass,
               replica_groups=RG_ALL, ins=[scales_loc.ap().opt()],
               outs=[scales_all.ap().opt()])

        ctx.push_scope()
        _emit_gating(ctx, sems, consts, x1_rows, H_rows, z_rows, ds_rows,
                     scales_all, LiM, Bm, x2_rows)
        ctx.pop_scope()

        if upto == "x2":
            _stop_here(ctx, "x2", x2_rows, None)
            ctx.finalize(block)
            ctx.st.close()
            return nc

        emit_transpose_planes(ctx, x2_rows, x2Tp, NT_OWN, sems, consts, "f2T")
        statsC = emit_stats(ctx, x2_rows, NT_OWN, sems, consts, tag="fC")
        ctx.push_scope()
        outq = [ctx.sb("fco%d" % i, (128, 512), BF16) for i in range(2)]
        stgq = ctx.sb("fcst", (128, 128), BF16)
        sqs = ctx.sb("fcsq", (128, 128), F32)
        kn = ctx.sb("fckn", (128, 1), F32)
        knacc = ctx.sb("fcka", (128, 1), F32)
        pTq = ctx.ps("fcpT", (128, 128), BF16)
        ctx.op("gpsimd", "memset", knacc[:], 0.0)
        eparq = [0]

        def epiQ(tg, o0, n_o, psum, tval):
            par = eparq[0]
            eparq[0] = 1 - par
            rows = slice(tg * 128, (tg + 1) * 128)
            ctx.op("scalar", "activation", outq[par][:, 0:n_o],
                   psum[:, 0:n_o], AF.Copy, scale=statsC["rs"][:, tg:tg + 1],
                   inc=sems["e"].tick(1))
            which, off = divmod(o0, 4096)
            for j in range(4):
                h = off // 128 + j
                sl = slice(j * 128, (j + 1) * 128)
                if which < 2:
                    dst = (qT_loc, kT_loc)[which]
                    ctx.op("tensor", "transpose", pTq[:], outq[par][:, sl],
                           consts["c_idb"][:], inc=sems["t"].tick(1))
                    ctx.op("vector", "tensor_copy", stgq[:], pTq[:])
                    ctx.op("sync", "dma_start",
                           dst[h, :, tg * 128:(tg + 1) * 128], stgq[:])
                else:
                    ctx.op("sync", "dma_start",
                           v_loc[h, tg * 128:(tg + 1) * 128, :],
                           outq[par][:, sl])
                if which == 1:
                    ctx.op("scalar", "activation", sqs[:], outq[par][:, sl],
                           AF.Square, accum_out=kn[:])
                    ctx.op("vector", "tensor_max", knacc[:], knacc[:], kn[:])

        emit_proj(ctx, xT_planes=x2Tp, n_tok=T_OWN, w_dram=Wcat, n_out=12288,
                  stats=statsC, sems=sems, consts=consts, bias_dram=bcat,
                  epilogue=epiQ, tag="fc")
        pRk = ctx.ps("fcpR", (1, 128), F32)
        srk = ctx.sb("fcsr", (1, 128), F32)
        km1 = ctx.sb("fckm", (1, 1), F32)
        ctx.op("tensor", "transpose", pRk[:], knacc[:], consts["c_idf"][:],
               inc=sems["t"].tick(1))
        ctx.op("vector", "tensor_copy", srk[:], pRk[:])
        ctx.op("vector", "reduce_max", km1[:], srk[:], axis=AX.X)
        ctx.op("sync", "dma_start", kmax_loc[:, :], km1[:])
        ctx.pop_scope()

        ctx.op("gpsimd", "collective_compute", "AllGather", ALU.bypass,
               replica_groups=RG_ALL, ins=[kmax_loc.ap().opt()],
               outs=[kmax_all.ap().opt()])
        ctx.op("gpsimd", "collective_compute", "AllGather", ALU.bypass,
               replica_groups=RG_PAIR, ins=[kT_loc.ap().opt()],
               outs=[kT_rm.ap().opt()])
        ctx.op("gpsimd", "collective_compute", "AllGather", ALU.bypass,
               replica_groups=RG_PAIR, ins=[v_loc.ap().opt()],
               outs=[v_rm.ap().opt()])

        if upto == "qkv":
            _stop_here(ctx, "qkv", x2_rows, None)
            ctx.finalize(block)
            ctx.st.close()
            return nc

        emit_attn_l1(ctx, qT=qT_loc, kT=kT_rm, vpl=v_rm, kmax_all=kmax_all,
                     oT_l1=oT_l1, sems=sems, consts=consts, rank_major=True)

        if upto == "attn":
            _stop_here(ctx, "attn", x2_rows, None)
            ctx.finalize(block)
            ctx.st.close()
            return nc

        ctx.push_scope()
        x2t = ctx.sb("f5x2", (128, 512), F32)
        out5 = [ctx.sb("f5o%d" % i, (128, 512), F32) for i in range(2)]
        epar5 = [0]

        def epi5(tg, o0, n_o, psum, tval):
            par = epar5[0]
            epar5[0] = 1 - par
            rows = slice(tg * 128, (tg + 1) * 128)
            ctx.op("sync", "dma_start", x2t[:, 0:n_o], x2_rows[rows, o0:o0 + n_o])
            ctx.op("vector", "tensor_add", out5[par][:, 0:n_o],
                   psum[:, 0:n_o], x2t[:, 0:n_o], inc=sems["e"].tick(1))
            ctx.op("sync", "dma_start", x3_rows[rows, o0:o0 + n_o],
                   out5[par][:, 0:n_o])

        emit_proj(ctx, xT_planes=oT_l1, n_tok=T_OWN, w_dram=WoT, n_out=4096,
                  stats=None, sems=sems, consts=consts, bias_dram=bo_row,
                  epilogue=epi5, tag="f5o")
        ctx.pop_scope()

        if upto == "x3":
            _stop_here(ctx, "x3", x3_rows, None)
            ctx.finalize(block)
            ctx.st.close()
            return nc

        statsD = emit_stats(ctx, x3_rows, NT_OWN, sems, consts, tag="fD")
        ctx.push_scope()
        _emit_ffn1(ctx, sems, consts, statsD, x3_rows, Acat, Bcat, WrT, x4_out)
        a = ctx._autosem("sync", "d")
        ctx._need("sync", (a.h, a.n))
        ctx.pop_scope()
        ctx.finalize(block)
    ctx.st.close()
    return nc


def host_inputs_fused(inputs, core):
    b, p = core // 2, core % 2
    x0 = np.ascontiguousarray(inputs["L0_attn_hidden_states"][b])
    hs = slice(p * 2048, (p + 1) * 2048)
    ks = slice(p * 512, (p + 1) * 512)
    wqkv = np.concatenate([inputs["L0_attn_Wq"][:, hs],
                           inputs["L0_attn_Wk"][:, ks],
                           inputs["L0_attn_Wv"][:, ks]], axis=1)
    Wcat = np.concatenate([inputs["L1_attn_Wq"].T, inputs["L1_attn_Wk"].T,
                           inputs["L1_attn_Wv"].T], axis=1)
    bcat = np.concatenate([inputs["L1_attn_bq"], inputs["L1_attn_bk"],
                           inputs["L1_attn_bv"]])[None, :]
    d = {"x0_rows": x0,
         "x0Tp": np.ascontiguousarray(x0.T.reshape(32, 128, 4096)),
         "x0_own": np.ascontiguousarray(x0[own_rows(p)]),
         "wqkv": np.ascontiguousarray(wqkv),
         "Wo_sl": np.ascontiguousarray(
             inputs["L0_attn_Wo"][p * 2048:(p + 1) * 2048, :]),
         "WT": np.ascontiguousarray(inputs["L0_ffn_W"].T),
         "b_row": np.ascontiguousarray(inputs["L0_ffn_b"][None, :]),
         "A4": inputs["L0_ffn_A"],
         "LiT": np.ascontiguousarray(inputs["L0_ffn_LiMEs"].T),
         "LiM": inputs["L0_ffn_LiMEs"],
         "Bm": inputs["L0_ffn_Bmat"],
         "Wcat": np.ascontiguousarray(Wcat),
         "bcat": np.ascontiguousarray(bcat),
         "WoT": np.ascontiguousarray(inputs["L1_attn_Wo"].T),
         "bo_row": np.ascontiguousarray(inputs["L1_attn_bo"][None, :]),
         "Acat": np.ascontiguousarray(
             inputs["L1_ffn_A"].transpose(2, 0, 1).reshape(4096, 64)),
         "Bcat": np.ascontiguousarray(
             inputs["L1_ffn_B"].transpose(0, 2, 1).reshape(64, 4096)),
         "WrT": np.ascontiguousarray(inputs["L1_ffn_Wr"].T)}
    d.update({k: v for k, v in host_consts(p).items() if k in FUSED_CONSTS})
    return d


def kernel_fused(**inputs):
    inputs = {k: np.ascontiguousarray(np.asarray(v)) for k, v in inputs.items()}
    cores = list(range(NCORE))
    nc = _get("FUSED", build_fused)
    res = run_bass_kernel_spmd(nc, [host_inputs_fused(inputs, c)
                                    for c in cores], core_ids=cores)
    out = np.empty((B, N, DM), np.float32)
    for c in cores:
        out[c // 2][own_rows(c % 2)] = np.asarray(res.results[c]["x4_out"])
    return out
